# version 01
# baseline (speedup 1.0000x reference)
import numpy as np
from collections import defaultdict
from contextlib import ExitStack

import concourse.bass as bass
import concourse.mybir as mybir
from concourse.bass_utils import run_bass_kernel_spmd

F32 = mybir.dt.float32
BF16 = mybir.dt.bfloat16
AF = mybir.ActivationFunctionType
ALU = mybir.AluOpType
AX = mybir.AxisListType
ds = bass.ds

R = 8
D = 2048
S = 8192
NTOK = 1024
NH = 16
DL = 512
HID = 5632
NHC = 44
W1B = 6
G = 16
SCALE = float(1.0 / np.float32(np.sqrt(np.float32(128.0))))
NEG = -1.0e30
EPS_LN = 1.0e-5

NAT_MATS = ["Wq", "Wk", "Wv", "Wo", "wq1", "wk1", "wv1", "wo1"]
BLK = 128 * G * 128
W3BLK = 128 * NHC * 128

_packA_off, _oa = {}, 0
for _m in NAT_MATS:
    _packA_off[_m] = _oa
    _oa += 2 * BLK
_packA_off["WL"] = _oa; _oa += BLK
_packA_off["WT"] = _oa; _oa += 2 * BLK
PACKA_ELEMS = _oa
PACKA_ROWS = PACKA_ELEMS // 1024

_packB_off = {"W1Q": 0, "W2Q": W1B * BLK, "W3Q": 2 * W1B * BLK}
PACKB_ELEMS = 2 * W1B * BLK + 2 * W3BLK
PACKB_ROWS = PACKB_ELEMS // 1024


def natA(mat, o):
    r, lo = o // 2, o % 2
    return r * PACKA_ELEMS + _packA_off[mat] + lo * BLK


def wlA(which, o):
    return (o + 4 * which) * PACKA_ELEMS + _packA_off["WL"]


def wtA(o):
    r, lo = o // 2, o % 2
    return r * PACKA_ELEMS + _packA_off["WT"] + lo * BLK


def w12B(which, hc):
    if hc < 24:
        r, hb = hc // 6, hc % 6
    else:
        r, hb = 4 + (hc - 24) // 5, (hc - 24) % 5
    return r * PACKB_ELEMS + _packB_off["W1Q" if which == 0 else "W2Q"] + hb * BLK


def w12_rank_blocks(r):
    return (6 * r, 6) if r < 4 else (24 + 5 * (r - 4), 5)


def w3B(o):
    r, lo = o // 2, o % 2
    return r * PACKB_ELEMS + _packB_off["W3Q"] + lo * W3BLK


def rank_of_chunk(m):
    return (m, 0) if m < 8 else (15 - m, 1)


GIDX_COLS = {}
_gc = 0


def _gcol(key):
    global _gc
    if key not in GIDX_COLS:
        GIDX_COLS[key] = _gc
        _gc += 1
    return GIDX_COLS[key]


for _L in (0, 1):
    for _hi in (0, 1):
        for _m in range(16):
            _gcol(("k", _L, _hi, _m))
            _gcol(("v", _L, _hi, _m))
            if _L == 1:
                _gcol(("q", _L, _hi, _m))
        _gcol(("cmy", _L, _hi))
    for _jk in range(16):
        for _cc in (0, 1):
            _gcol(("ao", _L, _jk, _cc))
GIDX_N = _gc

KV0_ROWS = 2048 + 2048 + 2
KV1_ROWS = 2048 + 2048 + 2048 + 2
KVBASE = [
    {"q": None, "k": 0, "v": 2048, "c": 4096, "rows": KV0_ROWS},
    {"q": 0, "k": 2048, "v": 4096, "c": 6144, "rows": KV1_ROWS},
]


def ghead(jk):
    j, k = jk // 2, jk % 2
    return j if k == 0 else 15 - j


class Rec:
    ENGS = ("tensor", "vector", "scalar", "gpsimd", "sync")

    def __init__(self, sems):
        self.sems = sems
        self.ops = {e: [] for e in self.ENGS}
        self.cnt = defaultdict(int)

    def op(self, eng, fn, inc=None, n=1):
        if inc is None:
            self.ops[eng].append(fn)
            return None
        self.cnt[inc] += n
        tok = (inc, self.cnt[inc])
        sem = self.sems[inc]
        self.ops[eng].append(lambda e, fn=fn, s=sem, k=n: fn(e).then_inc(s, k))
        return tok

    def wait(self, eng, tok):
        if tok is None:
            return
        s, v = self.sems[tok[0]], tok[1]
        self.ops[eng].append(lambda e, s=s, v=v: e.wait_ge(s, v))

    def now(self, sem):
        return (sem, self.cnt[sem])

    def dma(self, eng, out, in_, sem=None, **kw):
        if sem is None:
            sem = {"sync": "dsp", "scalar": "dact", "gpsimd": "dgp"}[eng]
        return self.op(eng,
                       lambda e, o=out, i=in_, kw=kw: e.dma_start(
                           out=o, in_=i, **kw),
                       inc=sem, n=16)


def build_program(taps=(), upto=None):
    nc = bass.Bass("TRN2", target_bir_lowering=False, debug=False)

    def din(name, shape, dtype=F32):
        return nc.dram_tensor(name, list(shape), dtype, kind="ExternalInput").ap()

    x_own = din("x_own", (NTOK, D))
    p_nat = {m: din("p_" + m, (D, 256)) for m in NAT_MATS}
    p_wl = din("p_wl", (D, 128))
    p_WT = din("p_WT", (256, D))
    p_w1 = din("p_w1", (W1B * 128, D))
    p_w2 = din("p_w2", (W1B * 128, D))
    p_w3 = din("p_w3", (256, HID))
    bq_i = din("bq", (D,))
    bo_i = din("bo", (D,))
    bv_i = din("bv", (D,))
    fb_i = din("f_bias", (D,))
    rw1_i = din("rw1", (64, D))
    rb1_i = din("rb1", (64,))
    rw2_i = din("rw2", (32, 64))
    rb2_i = din("rb2", (32,))
    gates_i = din("gates", (16,))
    A_i = din("A_in", (256, D))
    B_i = din("B_in", (NH, D, 16))
    wqb_i = din("wq_b", (D,))
    wvb_i = din("wv_b", (D,))
    wvlb_i = din("wvl_b", (DL,))
    upk_i = din("upk_w", (NH, DL, 128))
    upv_i = din("upv_w", (NH, DL, 128))
    upvb_i = din("upv_b", (NH, 128))
    wob_i = din("wo_b", (D,))
    gidx_i = din("gidx", (128, GIDX_N), mybir.dt.int32)

    y_out = nc.dram_tensor("y_out", [NTOK, D], F32, kind="ExternalOutput").ap()
    tap_outs = {}
    for tname in taps:
        tap_outs[tname] = nc.dram_tensor(
            "tap_" + tname, [128, G * 1024], F32, kind="ExternalOutput").ap()

    packA_in = nc.dram_tensor("packA_in", [PACKA_ROWS, 1024], BF16).ap()
    packA_out = nc.dram_tensor("packA_out", [R * PACKA_ROWS, 1024], BF16).ap()
    packB_in = nc.dram_tensor("packB_in", [PACKB_ROWS, 1024], BF16).ap()
    packB_out = nc.dram_tensor("packB_out", [R * PACKB_ROWS, 1024], BF16).ap()
    gs_in = nc.dram_tensor("gs_in", [1, 16], F32).ap()
    gs_out = nc.dram_tensor("gs_out", [1, 16], F32).ap()
    kv_in = [nc.dram_tensor("kv0_in", [KV0_ROWS, 1024], BF16).ap(),
             nc.dram_tensor("kv1_in", [KV1_ROWS, 1024], BF16).ap()]
    kv_out = [nc.dram_tensor("kv0_out", [R * KV0_ROWS, 1024], BF16).ap(),
              nc.dram_tensor("kv1_out", [R * KV1_ROWS, 1024], BF16).ap()]
    ao_in = nc.dram_tensor("ao_in", [256, S], BF16).ap()
    ao_out = nc.dram_tensor("ao_out", [D, S], BF16).ap()
    abT = nc.dram_tensor("abT", [1536, 1024], BF16).ap()
    cscr = nc.dram_tensor("cscr", [2, 1024], BF16).ap()

    ctx = ExitStack()
    e = ctx.enter_context

    identf = e(nc.sbuf_tensor("identf", [128, 128], F32))
    identb = e(nc.sbuf_tensor("identb", [128, 128], BF16))
    stream = e(nc.sbuf_tensor("stream", [128, G * 1024], BF16))
    ybuf = e(nc.sbuf_tensor("ybuf", [128, G * 1024], BF16))
    qs = e(nc.sbuf_tensor("qs", [128, G * 1024], BF16))
    ksb = e(nc.sbuf_tensor("ksb", [128, 8192], BF16))
    vsb = e(nc.sbuf_tensor("vsb", [128, 8192], BF16))
    outT = e(nc.sbuf_tensor("outT", [128, G * 1024], BF16))
    ppre = e(nc.sbuf_tensor("ppre", [128, 1024], F32))
    pbf = e(nc.sbuf_tensor("pbf", [128, 1024], BF16))
    ascr = e(nc.sbuf_tensor("ascr", [128, 2560], F32))
    t1 = e(nc.sbuf_tensor("t1", [128, 1024], F32))
    wsa = e(nc.sbuf_tensor("wsa", [128, 4096], BF16))
    wsb = e(nc.sbuf_tensor("wsb", [128, 4096], BF16))
    absq = e(nc.sbuf_tensor("absq", [128, 512], BF16))
    smf = e(nc.sbuf_tensor("smf", [128, 512], F32))
    smb = e(nc.sbuf_tensor("smb", [128, 128], BF16))
    ebf = e(nc.sbuf_tensor("ebf", [128, 256], BF16))
    rwT = e(nc.sbuf_tensor("rwT", [128, 1088], BF16))
    gix = e(nc.sbuf_tensor("gix", [128, GIDX_N], mybir.dt.int32))

    xstg = wsa.ap().bitcast(F32)
    xstg2 = wsb.ap().bitcast(F32)
    qs32 = qs.ap().bitcast(F32)
    ksb32 = ksb.ap().bitcast(F32)
    vsb32 = vsb.ap().bitcast(F32)
    ysqv = pbf.ap()
    kvstg = ppre.ap().bitcast(BF16)
    outT32 = outT.ap().bitcast(F32)

    ck = smf[:, 0:16]
    bq_f = smf[:, 16:32]
    wqb_f = smf[:, 32:48]
    cb0 = smf[:, 48:64]
    cb1 = smf[:, 64:80]
    fb_f = smf[:, 80:96]
    cc1 = smf[:, 96:112]
    v1c = smf[:, 112:116]
    wvlb_f = smf[:, 116:120]
    upvb_f = smf[:, 120:136]
    bv_f = smf[:, 136:152]
    rgv = smf[:, 152:155]
    rgn = smf[:, 155:158]
    rb1_f = smf[:, 158:159]
    onescol_f = smf[:, 160:161]
    gacc = smf[:, 161:167]
    gtmp = smf[:, 167:168]
    gapd = smf[0:1, 168:176]
    gapsrc = smf[0:1, 176:184]
    ones_row = smf[0:1, 184:312]
    grow = smf[0:1, 312:328]
    g3a = smf[0:1, 328:331]
    g3b = smf[0:1, 331:334]
    g3c = smf[0:1, 334:337]
    bo_f = smf[:, 352:368]
    wob_f = smf[:, 368:384]
    wvb_f = smf[:, 384:400]
    rb2_row = smf[0:1, 400:416]
    gates_row = smf[0:1, 416:432]
    rb2g_row = smf[0:1, 432:448]

    cfeat = smb[:, 0:16]
    cmy = smb[:, 16:18]
    ones_b = smb[:, 18:19]
    cvb = smb[:, 20:36]
    wvb_b = smb[:, 36:52]
    v1cb = smb[:, 52:56]

    mhb = ascr[:, 0:512]
    rbb = ascr[:, 512:1024]
    mrow = ascr[0:1, 2048:2560]
    rstd_row = ascr[0:1, 0:1024]
    m2_row = ascr[0:1, 1024:2048]
    muK = ascr[0:1, 2048:2560]

    pband = [e(nc.psum_tensor(f"pb{i}", [128, 512], F32)) for i in range(8)]
    pb = [p.ap() for p in pband]
    pbb = [p.ap().bitcast(BF16) for p in pband]

    sems = {n: e(nc.semaphore(n))
            for n in ("pe", "dve", "act", "gp", "dsp", "dact", "dgp", "cc",
                      "dcn", "sta0", "sta1", "stb0", "stb1", "stc0", "stc1",
                      "std", "kvo0", "kvo1", "dup", "dld", "dao", "dcm",
                      "dkv", "dst0", "dst1", "dtp",
                      "cA", "cR", "cB", "ck0", "ck1", "cao")}
    rec = Rec(sems)

    def T(fn):
        return rec.op("tensor", fn, inc="pe")

    def V(fn):
        return rec.op("vector", fn, inc="dve")

    def Aop(fn):
        return rec.op("scalar", fn, inc="act")

    def Gop(fn):
        return rec.op("gpsimd", fn, inc="gp")

    def vgap():
        t = rec.now("dve")
        rec.wait("vector", t)
        return t

    def gwait():
        t = rec.now("gp")
        rec.wait("gpsimd", t)
        return t

    def barrier():
        toks = [rec.now(s) for s in sems]
        for eng in Rec.ENGS:
            for t in toks:
                rec.wait(eng, t)


    pA_flat = packA_in.rearrange("r c -> (r c)")
    pB_flat = packB_in.rearrange("r c -> (r c)")
    pAo_flat = packA_out.rearrange("r c -> (r c)")
    pBo_flat = packB_out.rearrange("r c -> (r c)")
    abT_flat = abT.rearrange("r c -> (r c)")
    kvi_flat = [k.rearrange("r c -> (r c)") for k in kv_in]
    kvo_flat = [k.rearrange("r c -> (r c)") for k in kv_out]
    kvo512 = [k.rearrange("r c -> (r c)").rearrange("(n b) -> n b", b=512)
              for k in kv_out]
    ao512 = ao_out.rearrange("r c -> (r c)").rearrange("(n b) -> n b", b=512)

    GATHER_SEM = {"k": "dld", "v": "dld", "q": "dld", "ao": "dao",
                  "cmy": "dcm"}

    def gather(out_ap, src512, key):
        col = GIDX_COLS[key]
        return rec.op("gpsimd", lambda e, o=out_ap, s=src512, col=col:
                      e.indirect_dma_start(
                          out=o, out_offset=None, in_=s,
                          in_offset=bass.IndirectOffsetOnAxis(
                              gix[:, col:col + 1], 0)),
                      inc=GATHER_SEM[key[0]], n=16)

    def packA_stage(elem_off, n):
        return pAo_flat[ds(elem_off, 128 * n)].rearrange("(p c) -> p c", p=128)

    def packB_stage(elem_off, n):
        return pBo_flat[ds(elem_off, 128 * n)].rearrange("(p c) -> p c", p=128)

    def ph_consts():
        def Gs(fn):
            t = Gop(fn)
            rec.wait("gpsimd", t)
            return t
        Gs(lambda e: e.memset(identf[:], 0.0))
        Gs(lambda e: e.affine_select(out=identf[:], in_=identf[:],
                                      compare_op=ALU.not_equal, fill=1.0, base=0,
                                      pattern=[[-1, 128]], channel_multiplier=1))
        Gs(lambda e: e.memset(identb[:], 0.0))
        Gs(lambda e: e.affine_select(out=identb[:], in_=identb[:],
                                      compare_op=ALU.not_equal, fill=1.0, base=0,
                                      pattern=[[-1, 128]], channel_multiplier=1))
        Gs(lambda e: e.memset(ones_b, 1.0))
        Gs(lambda e: e.memset(onescol_f, 1.0))
        Gs(lambda e: e.memset(ones_row, 1.0))
        Gs(lambda e: e.memset(gapsrc, 0.0))
        Gs(lambda e: e.memset(ebf[:], 0.0))
        for blk in range(2):
            sl = ebf[0:16, blk * 128:(blk + 1) * 128]
            Gs(lambda e, sl=sl: e.memset(sl, 1.0))
            Gs(lambda e, sl=sl, blk=blk: e.affine_select(
                out=sl, in_=sl, compare_op=ALU.is_ge, fill=0.0,
                base=128 * blk, pattern=[[1, 128]], channel_multiplier=-16))
            Gs(lambda e, sl=sl, blk=blk: e.affine_select(
                out=sl, in_=sl, compare_op=ALU.is_ge, fill=0.0,
                base=15 - 128 * blk, pattern=[[-1, 128]],
                channel_multiplier=16))
        for dst, src, p in ((bq_f, bq_i, 128), (wqb_f, wqb_i, 128),
                            (bv_f, bv_i, 128), (fb_f, fb_i, 128),
                            (bo_f, bo_i, 128), (wob_f, wob_i, 128),
                            (wvb_f, wvb_i, 128),
                            (upvb_f, upvb_i.rearrange("h d -> (h d)"), 128),
                            (wvlb_f, wvlb_i, 128),
                            (rb1_f, rb1_i, 64)):
            gg = src.shape[0] // p
            rec.dma("gpsimd", dst[0:p, 0:gg],
                    src.rearrange("(g p) -> p g", p=p), sem="dcn",
                    allow_slow_non_contiguous=True)
        rec.dma("gpsimd", rb2_row, rb2_i[ds(0, 16)].rearrange(
            "(o c) -> o c", o=1), sem="dcn")
        rec.dma("gpsimd", gates_row, gates_i.rearrange("(o c) -> o c", o=1),
                sem="dcn")
        rec.dma("gpsimd", gix[:], gidx_i[:, :], sem="dcn")
        gd = rec.now("dcn")
        rec.wait("vector", gd)
        V(lambda e: e.tensor_copy(wvb_b, wvb_f))
        V(lambda e: e.tensor_copy(cvb, bv_f))
        V(lambda e: e.tensor_tensor(rb2g_row, rb2_row, gates_row, ALU.add))
        vgap()
        V(lambda e: e.tensor_copy(rwT[64:65, 1024:1040], rb2g_row))
        barrier()

    def nat_prep_one(src_ap, dst_elems, nblocks):
        last = None
        for lo in range(nblocks):
            ld = rec.dma("sync",
                         xstg2[:, 0:2048].rearrange("p (g c) -> p g c", g=G),
                         src_ap[:, lo * 128:(lo + 1) * 128].rearrange(
                             "(g p) c -> p g c", p=128), sem="stb0")
            rec.wait("vector", ld)
            if last is not None:
                rec.wait("vector", last)
            cv = V(lambda e: e.tensor_copy(wsa[:, 0:2048], xstg2[:, 0:2048]))
            rec.wait("gpsimd", cv)
            dst = pA_flat[ds(dst_elems + lo * BLK, BLK)].rearrange(
                "(p c) -> p c", p=128)
            last = rec.dma("gpsimd", dst, wsa[:, 0:2048])
            rec.wait("gpsimd", last)
            rec.wait("sync", last)
        return last

    def ph_prep_A():
        for m in NAT_MATS:
            tok = nat_prep_one(p_nat[m], _packA_off[m], 2)
            rec.wait("sync", tok)
            rec.wait("vector", tok)
        tok = nat_prep_one(p_wl, _packA_off["WL"], 1)
        rec.wait("sync", tok)
        rec.wait("vector", tok)

        for lo in range(2):
            ld = rec.dma("sync", xstg2[:, 0:2048],
                         p_WT[lo * 128:(lo + 1) * 128, :], sem="stb0")
            rec.wait("vector", ld)
            cv = V(lambda e: e.tensor_copy(kvstg[:, 0:2048], xstg2[:, 0:2048]))
            rec.wait("tensor", cv)
            evs = []
            for g in range(G):
                slot = pbb[g % 4][:, 0:128]
                if g >= 4:
                    rec.wait("tensor", evs[g - 4])
                tt = T(lambda e, g=g, slot=slot: e.transpose(
                    slot, kvstg[:, g * 128:(g + 1) * 128], identb[:]))
                rec.wait("vector", tt)
                evs.append(V(lambda e, g=g, slot=slot: e.tensor_copy(
                    wsa[:, 2048 + g * 128:2048 + (g + 1) * 128], slot)))
            rec.wait("gpsimd", evs[-1])
            dst = pA_flat[ds(_packA_off["WT"] + lo * BLK, BLK)].rearrange(
                "(p c) -> p c", p=128)
            dt = rec.dma("gpsimd", dst, wsa[:, 2048:4096])
            rec.wait("gpsimd", dt)
            rec.wait("sync", dt)
            rec.wait("vector", dt)

        gtok = rec.now("dgp")
        rec.wait("gpsimd", gtok)
        agA = rec.op("gpsimd", lambda e: e.collective_compute(
            "AllGather", ALU.bypass, replica_groups=[list(range(R))],
            ins=[packA_in[:, :]], outs=[packA_out[:, :]]), inc="cA")
        return agA

    def ph_prep_AB():
        for erb in range(2):
            ld = rec.dma("sync", xstg2[:, 0:2048],
                         A_i[erb * 128:(erb + 1) * 128, :], sem="stb0")
            rec.wait("vector", ld)
            cv = V(lambda e: e.tensor_copy(kvstg[:, 0:2048], xstg2[:, 0:2048]))
            rec.wait("tensor", cv)
            evs = []
            for g in range(G):
                slot = pbb[g % 4][:, 0:128]
                if g >= 4:
                    rec.wait("tensor", evs[g - 4])
                tt = T(lambda e, g=g, slot=slot: e.transpose(
                    slot, kvstg[:, g * 128:(g + 1) * 128], identb[:]))
                rec.wait("vector", tt)
                evs.append(V(lambda e, g=g, slot=slot: e.tensor_copy(
                    wsa[:, 2048 + g * 128:2048 + (g + 1) * 128], slot)))
            rec.wait("gpsimd", evs[-1])
            dst = abT_flat[ds(erb * BLK, BLK)].rearrange("(p c) -> p c", p=128)
            dt = rec.dma("gpsimd", dst, wsa[:, 2048:4096])
            rec.wait("gpsimd", dt)
            rec.wait("sync", dt)
            rec.wait("vector", dt)

        Bf = B_i.rearrange("e o r -> (e o r)")
        for erb in range(2):
            srcv = Bf[ds(erb * 8 * D * 16, 8 * D * 16)].rearrange(
                "(e ob p r) -> p ob e r", e=8, ob=16, p=128)
            for ob in range(G):
                ld = rec.dma("sync", xstg2[:, ob * 128:(ob + 1) * 128]
                             .rearrange("p (e r) -> p e r", e=8),
                             srcv[:, ob, :, :], sem="stb0")
            rec.wait("vector", ld)
            cv = V(lambda e: e.tensor_copy(kvstg[:, 0:2048], xstg2[:, 0:2048]))
            rec.wait("tensor", cv)
            evs = []
            for ob in range(G):
                slot = pbb[ob % 4][:, 0:128]
                if ob >= 4:
                    rec.wait("tensor", evs[ob - 4])
                src_t = kvstg[:, ob * 128:(ob + 1) * 128]
                tt = T(lambda e, s=src_t, slot=slot: e.transpose(
                    slot, s, identb[:]))
                rec.wait("vector", tt)
                evs.append(V(lambda e, ob=ob, slot=slot: e.tensor_copy(
                    wsa[:, 2048 + ob * 128:2048 + (ob + 1) * 128], slot)))
            rec.wait("gpsimd", evs[-1])
            for ob in range(G):
                dst = abT_flat[ds(2 * BLK + ob * 32768 + erb * 16384, 16384)] \
                    .rearrange("(p c) -> p c", p=128)
                dt = rec.dma("gpsimd", dst,
                             wsa[:, 2048 + ob * 128:2048 + (ob + 1) * 128])
            rec.wait("gpsimd", dt)
            rec.wait("sync", dt)
            rec.wait("vector", dt)

        ld = rec.dma("sync", xstg2[0:64, 0:2048], rw1_i[:, :], sem="stb0")
        rec.wait("vector", ld)
        cv = V(lambda e: e.tensor_copy(kvstg[0:64, 0:2048], xstg2[0:64, 0:2048]))
        rec.wait("tensor", cv)
        evs = []
        for g in range(G):
            slot = pbb[g % 4][:, 0:128]
            if g >= 4:
                rec.wait("tensor", evs[g - 4])
            tt = T(lambda e, g=g, slot=slot: e.transpose(
                slot[:, 0:64], kvstg[0:64, g * 128:(g + 1) * 128],
                identb[0:64, 0:64]))
            rec.wait("vector", tt)
            evs.append(V(lambda e, g=g, slot=slot: e.tensor_copy(
                rwT[:, g * 64:(g + 1) * 64], slot[:, 0:64])))
        rec.wait("sync", cv)
        ld2 = rec.dma("sync", xstg2[0:32, 0:64], rw2_i[:, :], sem="stb0")
        rec.wait("vector", ld2)
        cv2 = V(lambda e: e.tensor_copy(kvstg[0:32, 0:64], xstg2[0:32, 0:64]))
        rec.wait("tensor", cv2)
        tt = T(lambda e: e.transpose(pbb[0][0:64, 256:288],
                                     kvstg[0:32, 0:64], identb[0:32, 0:32]))
        rec.wait("vector", tt)
        V(lambda e: e.tensor_copy(rwT[0:64, 1024:1056], pbb[0][0:64, 256:288]))
        barrier()

    def ph_prep_B():
        V(lambda e: e.memset(gacc, 0.0))
        blocks = []
        for b in range(W1B):
            blocks.append((0, p_w1[b * 128:(b + 1) * 128, :], D))
        for b in range(W1B):
            blocks.append((1, p_w2[b * 128:(b + 1) * 128, :], D))
        for lo in range(2):
            for piece in range(4):
                blocks.append((2, p_w3[lo * 128:(lo + 1) * 128,
                                       piece * 1408:(piece + 1) * 1408], 1408))
        cnt_wi = [0, 0, 0]
        rtok = None
        for wi, src, CW in blocks:
            if rtok is not None:
                rec.wait("sync", rtok)
            ld = rec.dma("sync", xstg2[:, 0:CW], src, sem="stb0")
            rec.wait("vector", ld)
            V(lambda e, CW=CW: e.reduce_sum(gtmp, xstg2[:, 0:CW], AX.X,
                                            apply_absolute_value=True))
            rtok = vgap()
            a = cnt_wi[wi] % 2
            b = (cnt_wi[wi] + 1) % 2
            V(lambda e, wi=wi, a=a, b=b: e.tensor_tensor(
                gacc[:, wi * 2 + b:wi * 2 + b + 1],
                gacc[:, wi * 2 + a:wi * 2 + a + 1], gtmp, ALU.add))
            vgap()
            cnt_wi[wi] += 1
        assert all(c % 2 == 0 for c in cnt_wi), cnt_wi
        pcom = vgap()
        rec.wait("tensor", pcom)
        gfin = gacc.rearrange("p (w t) -> p w t", t=2)[:, :, 0]
        mm = T(lambda e, gfin=gfin: e.matmul(pb[4][0:1, 0:3], onescol_f, gfin,
                                             start=True, stop=True))
        rec.wait("vector", mm)
        V(lambda e: e.memset(grow, 0.0))
        vgap()
        V(lambda e: e.tensor_copy(grow[:, 0:3], pb[4][0:1, 0:3]))
        gdone = vgap()
        rec.wait("gpsimd", gdone)
        d1 = rec.dma("gpsimd", gs_in[:, :], grow[:, 0:16])
        rec.wait("gpsimd", d1)
        ar = rec.op("gpsimd", lambda e: e.collective_compute(
            "AllReduce", ALU.add, replica_groups=[list(range(R))],
            ins=[gs_in[:, :]], outs=[gs_out[:, :]]), inc="cR")
        rec.wait("gpsimd", ar)
        d2 = rec.dma("gpsimd", grow[:, 0:16], gs_out[:, :])
        rec.wait("gpsimd", d2)
        rec.wait("vector", d2)
        NW = float(HID) * float(D)
        V(lambda e: e.tensor_scalar(g3a, grow[:, 0:3], 1.0 / NW, 1.0e-8,
                                    ALU.mult, ALU.add))
        vgap()
        V(lambda e: e.reciprocal(g3b, g3a))
        vgap()
        V(lambda e: e.tensor_scalar(g3c, g3b, -1.0, None, ALU.mult))
        gk = vgap()
        rec.wait("tensor", gk)
        mm1 = T(lambda e: e.matmul(pb[4][:, 0:3], ones_row, g3b,
                                   start=True, stop=True))
        mm2 = T(lambda e: e.matmul(pb[5][:, 0:3], ones_row, g3c,
                                   start=True, stop=True))
        rec.wait("vector", mm2)
        V(lambda e: e.tensor_copy(rgv, pb[4][:, 0:3]))
        cpr = V(lambda e: e.tensor_copy(rgn, pb[5][:, 0:3]))
        rec.wait("sync", cpr)

        def quant_into_wsa(src_slice, wi, CW):
            ld = rec.dma("sync", xstg2[:, 0:CW], src_slice, sem="stb0")
            rec.wait("vector", ld)
            vgap()
            V(lambda e, CW=CW, wi=wi: e.tensor_scalar(
                kvstg[:, 0:CW], xstg2[:, 0:CW], rgv[:, wi:wi + 1], 0.5,
                ALU.mult, ALU.is_ge))
            V(lambda e, CW=CW, wi=wi: e.tensor_scalar(
                outT[:, 0:CW], xstg2[:, 0:CW], rgn[:, wi:wi + 1], 0.5,
                ALU.mult, ALU.is_ge))
            vgap()
            return V(lambda e, CW=CW: e.tensor_tensor(
                wsa[:, 0:CW], kvstg[:, 0:CW], outT[:, 0:CW], ALU.subtract))

        for wi, src in ((0, p_w1), (1, p_w2)):
            for b in range(W1B):
                qv = quant_into_wsa(src[b * 128:(b + 1) * 128, :], wi, D)
                rec.wait("tensor", qv)
                evs = []
                for g in range(G):
                    slot = pbb[g % 4][:, 256:384]
                    if g >= 4:
                        rec.wait("tensor", evs[g - 4])
                    tt = T(lambda e, g=g, slot=slot: e.transpose(
                        slot, wsa[:, g * 128:(g + 1) * 128], identb[:]))
                    rec.wait("vector", tt)
                    evs.append(V(lambda e, g=g, slot=slot: e.tensor_copy(
                        wsa[:, 2048 + g * 128:2048 + (g + 1) * 128], slot)))
                rec.wait("gpsimd", evs[-1])
                dst = pB_flat[ds(_packB_off["W1Q" if wi == 0 else "W2Q"]
                                 + b * BLK, BLK)].rearrange("(p c) -> p c", p=128)
                dt = rec.dma("gpsimd", dst, wsa[:, 2048:4096])
                rec.wait("gpsimd", dt)
                rec.wait("sync", dt)
                rec.wait("vector", dt)

        for lo in range(2):
            for piece in range(4):
                cw = 11 * 128
                qv = quant_into_wsa(
                    p_w3[lo * 128:(lo + 1) * 128,
                         piece * cw:(piece + 1) * cw], 2, cw)
                rec.wait("tensor", qv)
                evs = []
                for t in range(11):
                    slot = pbb[t % 4][:, 256:384]
                    if t >= 4:
                        rec.wait("tensor", evs[t - 4])
                    tt = T(lambda e, t=t, slot=slot: e.transpose(
                        slot, wsa[:, t * 128:(t + 1) * 128], identb[:]))
                    rec.wait("vector", tt)
                    evs.append(V(lambda e, t=t, slot=slot: e.tensor_copy(
                        wsa[:, 2048 + t * 128:2048 + (t + 1) * 128], slot)))
                rec.wait("gpsimd", evs[-1])
                dst = pB_flat[ds(_packB_off["W3Q"] + lo * W3BLK, W3BLK)] \
                    .rearrange("(p hc j) -> p hc j", p=128, hc=NHC) \
                    [:, piece * 11:(piece + 1) * 11, :]
                dt = rec.dma("gpsimd", dst,
                             wsa[:, 2048:2048 + cw].rearrange(
                                 "p (hc j) -> p hc j", hc=11))
                rec.wait("gpsimd", dt)
                rec.wait("sync", dt)
                rec.wait("vector", dt)

        gtok = rec.now("dgp")
        rec.wait("gpsimd", gtok)
        agB = rec.op("gpsimd", lambda e: e.collective_compute(
            "AllGather", ALU.bypass, replica_groups=[list(range(R))],
            ins=[packB_in[:, :]], outs=[packB_out[:, :]]), inc="cB")
        return agB

    def ph_load_x():
        evs = []
        slot_ev = [None, None]
        for t in range(8):
            buf = xstg2 if t % 2 == 0 else xstg
            snm = "stb0" if t % 2 == 0 else "sta0"
            if slot_ev[t % 2] is not None:
                rec.wait("sync", slot_ev[t % 2])
            ld = rec.dma("sync", buf[:, 0:2048],
                         x_own[t * 128:(t + 1) * 128, :], sem=snm)
            rec.wait("tensor", ld)
            for g in range(G):
                slot = pb[g % 4][:, 0:128]
                idx = t * G + g
                if idx >= 4:
                    rec.wait("tensor", evs[idx - 4])
                tt = T(lambda e, buf=buf, g=g, slot=slot: e.transpose(
                    slot, buf[:, g * 128:(g + 1) * 128], identf[:]))
                rec.wait("vector", tt)
                evs.append(V(lambda e, g=g, t=t, slot=slot: e.tensor_copy(
                    stream[:, g * 1024 + t * 128:g * 1024 + (t + 1) * 128],
                    slot)))
            slot_ev[t % 2] = evs[-1]
        barrier()

    def proj16(addr_fn, rhs_fn, evac_fn, stage_cols=2048, nout=16,
               pre_tok=None, lhs_g_of=None):
        bank_tok = [None, None]
        stage_tok = [None, None]
        ld_tok = [None, None]
        rec.wait("sync", rec.now("pe"))
        rec.wait("tensor", rec.now("dve"))
        for o in range(nout):
            sbuf_half = o % 2
            stg = wsa[:, sbuf_half * 2048: sbuf_half * 2048 + stage_cols]
            if stage_tok[sbuf_half] is not None:
                rec.wait("sync", stage_tok[sbuf_half])
            if pre_tok is not None:
                rec.wait("sync", pre_tok)
                pre_tok = None
            ld_tok[sbuf_half] = rec.dma("sync", stg, addr_fn(o),
                                        sem="sta" + str(sbuf_half))
            mmtok = None
            for half in range(2):
                bnk = (2 * o + half) % 2
                bank = pb[bnk]
                if bank_tok[bnk] is not None:
                    rec.wait("tensor", bank_tok[bnk])
                rec.wait("tensor", ld_tok[sbuf_half])
                for g in range(G):
                    gl = lhs_g_of(g) if lhs_g_of else g
                    mmtok = T(lambda e, stg=stg, gl=gl, g=g, half=half,
                              bank=bank: e.matmul(
                        bank[:, 0:512], stg[:, gl * 128:(gl + 1) * 128],
                        rhs_fn(g, half), start=(g == 0), stop=(g == G - 1)))
                bank_tok[bnk] = evac_fn(o, half, bank, mmtok)
            stage_tok[sbuf_half] = bank_tok[(2 * o + 1) % 2]
        return bank_tok

    kv_dma_toks = []

    KVSTATE = {}

    def mk_evac_kv(layer, row_base, with_ck, bias_ap=None):
        state = KVSTATE

        def evac(o, half, bank, mmtok):
            buf = kvstg[:, (o % 2) * 1024:(o % 2) * 1024 + 1024]
            rec.wait("vector", mmtok)
            if (o % 2) in state:
                rec.wait("vector", state[o % 2])
            if bias_ap is not None:
                tok = V(lambda e, buf=buf, half=half, o=o, bank=bank:
                        e.tensor_scalar(buf[:, half * 512:(half + 1) * 512],
                                        bank[:, 0:512], bias_ap[:, o:o + 1],
                                        None, ALU.add))
            else:
                tok = V(lambda e, buf=buf, half=half, bank=bank: e.tensor_copy(
                    buf[:, half * 512:(half + 1) * 512], bank[:, 0:512]))
            if half == 1:
                if with_ck:
                    vgap()
                    V(lambda e, buf=buf, o=o: e.reduce_max(
                        ck[:, o:o + 1], buf, AX.X, apply_absolute_value=True))
                    tok = vgap()
                rec.wait("sync", tok)
                dt = rec.dma("sync",
                             kv_in[layer][row_base + o * 128:
                                          row_base + (o + 1) * 128, :], buf,
                             sem="kvo" + str(o % 2))
                state[o % 2] = dt
            return tok
        return evac

    def mk_evac_sbuf(dst_fn, bias_ap=None):
        def evac(o, half, bank, mmtok):
            rec.wait("vector", mmtok)
            if bias_ap is not None:
                return V(lambda e, o=o, half=half, bank=bank: e.tensor_scalar(
                    dst_fn(o, half), bank[:, 0:512], bias_ap[:, o:o + 1],
                    None, ALU.add))
            return V(lambda e, o=o, half=half, bank=bank: e.tensor_copy(
                dst_fn(o, half), bank[:, 0:512]))
        return evac

    qsv = qs.ap().rearrange("p (blk tl gg) -> p blk tl gg", blk=2, tl=512, gg=G)

    def stream_rhs(g, half):
        return stream[:, g * 1024 + half * 512: g * 1024 + (half + 1) * 512]

    def ph_proj_L0(agA_tok):
        rec.wait("sync", agA_tok)
        kv_dma_toks.clear()
        proj16(lambda o: packA_stage(natA("Wk", o), 2048), stream_rhs,
               mk_evac_kv(0, 0, with_ck=True))
        proj16(lambda o: packA_stage(natA("Wv", o), 2048), stream_rhs,
               mk_evac_sbuf(lambda o, half: outT[:, o * 1024 + half * 512:
                                                 o * 1024 + (half + 1) * 512]))
        vptok = rec.now("dve")
        rec.wait("tensor", vptok)
        rec.wait("sync", rec.now("kvo0"))
        rec.wait("sync", rec.now("kvo1"))
        prev_dma = [None, None]
        slot_ev = [None, None]
        for t in range(8):
            half = t % 2
            vrow = wsb[:, half * 2048:(half + 1) * 2048]
            if prev_dma[half] is not None:
                rec.wait("vector", prev_dma[half])
            evs = []
            for g in range(G):
                slot = pbb[2 + (g % 2)][:, 0:128]
                if g >= 2:
                    rec.wait("tensor", evs[g - 2])
                elif slot_ev[g % 2] is not None:
                    rec.wait("tensor", slot_ev[g % 2])
                tt = T(lambda e, g=g, t=t, slot=slot: e.transpose(
                    slot, outT[:, g * 1024 + t * 128:g * 1024 + (t + 1) * 128],
                    identb[:]))
                rec.wait("vector", tt)
                evs.append(V(lambda e, g=g, slot=slot, vrow=vrow: e.tensor_copy(
                    vrow[:, g * 128:(g + 1) * 128], slot)))
            slot_ev = [evs[-2], evs[-1]]
            rec.wait("sync", evs[-1])
            vreg0 = kvi_flat[0][ds(2048 * 1024, 2097152)].rearrange(
                "(p hf h sb j) -> p hf h sb j", p=128, hf=2, h=16, sb=4)
            dt = rec.dma("sync", vreg0[:, t // 4, :, t % 4, :],
                         vrow.rearrange("p (h j) -> p h j", h=16),
                         sem="kvo" + str(half))
            prev_dma[half] = dt
        cktok = rec.now("dve")
        rec.wait("gpsimd", cktok)
        dt = rec.dma("gpsimd", kvi_flat[0][ds(KVBASE[0]["c"] * 1024, 2048)]
                     .rearrange("(g p) -> p g", p=128), ck,
                     allow_slow_non_contiguous=True)
        rec.wait("gpsimd", dt)
        def q_evac(o, half, bank, mmtok):
            rec.wait("vector", mmtok)
            return V(lambda e, o=o, half=half, bank=bank: e.tensor_scalar(
                qsv[:, half, :, o], bank[:, 0:512], bq_f[:, o:o + 1],
                None, ALU.add))
        proj16(lambda o: packA_stage(natA("Wq", o), 2048), stream_rhs, q_evac)
        for sname in ("kvo0", "kvo1", "dkv", "dsp"):
            rec.wait("gpsimd", rec.now(sname))
        ag = rec.op("gpsimd", lambda e: e.collective_compute(
            "AllGather", ALU.bypass, replica_groups=[list(range(R))],
            ins=[kv_in[0][:, :]], outs=[kv_out[0][:, :]]), inc="ck0")
        return ag

    def ph_proj_L1(agB_tok):
        kv_dma_toks.clear()
        for dst, src in ((ksb, upk_i), (vsb, upv_i)):
            sf = src.rearrange("h d j -> (h d j)")
            ap = sf.rearrange("(h g p j) -> p h g j", h=NH, g=4, p=128)
            rec.dma("gpsimd", dst.ap().rearrange(
                "p (h g j) -> p h g j", h=NH, g=4), ap, sem="dup")
        up_tok = rec.now("dup")
        proj16(lambda o: packA_stage(natA("wq1", o), 2048), stream_rhs,
               mk_evac_kv(1, KVBASE[1]["q"], with_ck=False, bias_ap=wqb_f))
        proj16(lambda o: packA_stage(natA("wk1", o), 2048), stream_rhs,
               mk_evac_sbuf(lambda o, half: outT[:, o * 1024 + half * 512:
                                                 o * 1024 + (half + 1) * 512]))
        def k1_rhs(g, half):
            return outT[:, g * 1024 + half * 512: g * 1024 + (half + 1) * 512]
        proj16(lambda o: packA_stage(wlA(0, o), 2048), k1_rhs,
               mk_evac_sbuf(lambda o, half: qs[:, 8192 + o * 1024 + half * 512:
                                               8192 + o * 1024 + (half + 1) * 512]),
               nout=4)
        rec.wait("tensor", up_tok)
        rec.wait("tensor", rec.now("dve"))
        evk = mk_evac_kv(1, KVBASE[1]["k"], with_ck=True)
        bank_tok = [None, None]
        for h in range(NH):
            mmtok = None
            for half in range(2):
                bnk = (2 * h + half) % 2
                bank = pb[bnk]
                if bank_tok[bnk] is not None:
                    rec.wait("tensor", bank_tok[bnk])
                for g in range(4):
                    mmtok = T(lambda e, h=h, g=g, half=half, bank=bank:
                              e.matmul(bank[:, 0:512],
                                       ksb[:, h * 512 + g * 128:
                                           h * 512 + (g + 1) * 128],
                                       qs[:, 8192 + g * 1024 + half * 512:
                                          8192 + g * 1024 + (half + 1) * 512],
                                       start=(g == 0), stop=(g == 3)))
                bank_tok[bnk] = evk(h, half, bank, mmtok)
        proj16(lambda o: packA_stage(natA("wv1", o), 2048), stream_rhs,
               mk_evac_sbuf(lambda o, half: outT[:, o * 1024 + half * 512:
                                                 o * 1024 + (half + 1) * 512]))
        rec.wait("sync", rec.now("pe"))
        rec.wait("tensor", rec.now("dve"))
        vl_tok = [None, None]
        vlb_tok = [None, None]
        for o in range(4):
            sbuf_half = o % 2
            stg = wsa[:, sbuf_half * 2048:(sbuf_half + 1) * 2048]
            if vl_tok[sbuf_half] is not None:
                rec.wait("sync", vl_tok[sbuf_half])
            ld = rec.dma("sync", stg, packA_stage(wlA(1, o), 2048),
                         sem="sta" + str(sbuf_half))
            mmtok = None
            for half in range(2):
                bnk = (2 * o + half) % 2
                bank = pb[bnk]
                if vlb_tok[bnk] is not None:
                    rec.wait("tensor", vlb_tok[bnk])
                rec.wait("tensor", ld)
                for g in range(G):
                    mmtok = T(lambda e, stg=stg, g=g, half=half, o=o,
                              bank=bank: e.matmul(
                        bank[:, 0:512], stg[:, g * 128:(g + 1) * 128],
                        outT[:, g * 1024 + half * 512:
                             g * 1024 + (half + 1) * 512],
                        start=(g == 0), stop=(g == G - 1)))
                rec.wait("vector", mmtok)
                ev = V(lambda e, o=o, half=half, bank=bank: e.tensor_copy(
                    qs[:, 12288 + o * 1024 + half * 512:
                       12288 + o * 1024 + (half + 1) * 512], bank[:, 0:512]))
                vlb_tok[bnk] = ev
            rec.wait("tensor", rec.now("dve"))
            c_mm = None
            for g in range(G):
                c_mm = T(lambda e, stg=stg, g=g: e.matmul(
                    pb[5][:, 0:1], stg[:, g * 128:(g + 1) * 128],
                    wvb_b[:, g:g + 1], start=(g == 0), stop=(g == G - 1)))
            rec.wait("vector", c_mm)
            V(lambda e, o=o: e.tensor_scalar(
                v1c[:, o:o + 1], pb[5][:, 0:1], wvlb_f[:, o:o + 1],
                None, ALU.add))
            vgap()
            vl_tok[sbuf_half] = c_mm
        cst = V(lambda e: e.tensor_copy(v1cb, v1c))
        rec.wait("tensor", cst)
        rec.wait("tensor", rec.now("dve"))
        prev_ev = {}
        for h in range(NH):
            for t in range(8):
                bank = pb[(h * 8 + t) % 2]
                key = (h * 8 + t) % 2
                if key in prev_ev:
                    rec.wait("tensor", prev_ev[key])
                mmtok = None
                for g in range(4):
                    mmtok = T(lambda e, h=h, t=t, g=g, bank=bank: e.matmul(
                        bank[:, 0:128],
                        qs[:, 12288 + g * 1024 + t * 128:
                           12288 + g * 1024 + (t + 1) * 128],
                        vsb[:, h * 512 + g * 128: h * 512 + (g + 1) * 128],
                        start=(g == 0), stop=(g == 3)))
                rec.wait("vector", mmtok)
                prev_ev[key] = V(lambda e, h=h, t=t, bank=bank: e.tensor_copy(
                    outT[:, t * 2048 + h * 128: t * 2048 + (h + 1) * 128],
                    bank[:, 0:128]))
            rec.wait("tensor", rec.now("dve"))
            cm = None
            for g in range(4):
                cm = T(lambda e, h=h, g=g: e.matmul(
                    pb[5][:, 0:1],
                    vsb[:, h * 512 + g * 128: h * 512 + (g + 1) * 128],
                    v1cb[:, g:g + 1], start=(g == 0), stop=(g == 3)))
            rec.wait("vector", cm)
            V(lambda e, h=h: e.tensor_scalar(
                cc1[:, h:h + 1], pb[5][:, 0:1], upvb_f[:, h:h + 1],
                None, ALU.add))
            vgap()
        vtok = rec.now("dve")
        rec.wait("sync", vtok)
        vreg1 = kvi_flat[1][ds(KVBASE[1]["v"] * 1024, 2097152)].rearrange(
            "(p hf h sb j) -> p hf h sb j", p=128, hf=2, h=16, sb=4)
        for t in range(8):
            dt = rec.dma("sync", vreg1[:, t // 4, :, t % 4, :],
                         outT[:, t * 2048:(t + 1) * 2048].rearrange(
                             "p (h j) -> p h j", h=16), sem="dkv")
        cktok = rec.now("dve")
        rec.wait("gpsimd", cktok)
        dt = rec.dma("gpsimd", kvi_flat[1][ds(KVBASE[1]["c"] * 1024, 2048)]
                     .rearrange("(g p) -> p g", p=128), ck,
                     allow_slow_non_contiguous=True)
        rec.wait("gpsimd", dt)
        for sname in ("kvo0", "kvo1", "dkv", "dsp"):
            rec.wait("gpsimd", rec.now(sname))
        ag = rec.op("gpsimd", lambda e: e.collective_compute(
            "AllGather", ALU.bypass, replica_groups=[list(range(R))],
            ins=[kv_in[1][:, :]], outs=[kv_out[1][:, :]]), inc="ck1")
        return ag

    def ph_ccombine(layer, ag_tok):
        rec.wait("gpsimd", ag_tok)
        base = KVBASE[layer]["c"]
        rows = KVBASE[layer]["rows"]
        dts = []
        for r in range(R):
            dts.append(rec.dma(
                "gpsimd", absq[:, r * 16:(r + 1) * 16],
                kvo_flat[layer][ds((r * rows + base) * 1024, 2048)]
                .rearrange("(g p) -> p g", p=128), sem="dcm",
                allow_slow_non_contiguous=True))
        rec.wait("gpsimd", dts[-1])
        rec.wait("vector", dts[-1])
        V(lambda e: e.tensor_max(absq[:, 128:144], absq[:, 0:16], absq[:, 16:32]))
        V(lambda e: e.tensor_max(absq[:, 144:160], absq[:, 32:48], absq[:, 48:64]))
        V(lambda e: e.tensor_max(absq[:, 160:176], absq[:, 64:80], absq[:, 80:96]))
        V(lambda e: e.tensor_max(absq[:, 176:192], absq[:, 96:112], absq[:, 112:128]))
        vgap()
        V(lambda e: e.tensor_max(absq[:, 192:208], absq[:, 128:144], absq[:, 144:160]))
        V(lambda e: e.tensor_max(absq[:, 208:224], absq[:, 160:176], absq[:, 176:192]))
        vgap()
        cf = V(lambda e: e.tensor_max(cfeat, absq[:, 192:208], absq[:, 208:224]))
        cg = vgap()
        rec.wait("gpsimd", cg)
        cs_flat = cscr.rearrange("r c -> (r c)")
        st = rec.dma("gpsimd", cs_flat.rearrange("(g p) -> p g", p=128),
                     cfeat, sem="dcm", allow_slow_non_contiguous=True)
        rec.wait("gpsimd", st)
        cs1 = cs_flat.rearrange("(n b) -> n b", b=1)
        gather(cmy[:, 0:1], cs1, ("cmy", layer, 0))
        t2 = gather(cmy[:, 1:2], cs1, ("cmy", layer, 1))
        rec.wait("gpsimd", t2)
        return t2, rec.now("gp")

    def ph_attention(layer, ag_tok):
        barrier()
        dgp_tok, gp_tok = ph_ccombine(layer, ag_tok)
        rec.wait("tensor", dgp_tok)
        rec.wait("scalar", dgp_tok)
        rows = KVBASE[layer]["rows"]
        kb = KVBASE[layer]["k"]
        vb = KVBASE[layer]["v"]
        qb = KVBASE[layer]["q"]
        kvf = kvo_flat[layer]
        outv_toks = []
        last_unit_toks = []
        for hi in range(2):
            if last_unit_toks:
                for t in last_unit_toks:
                    rec.wait("gpsimd", t)
                last_unit_toks = []
            rec.wait("gpsimd", rec.now("act"))

            ktoks = []
            for m in range(16):
                ktoks.append(gather(ksb[:, m * 512:(m + 1) * 512],
                                    kvo512[layer], ("k", layer, hi, m)))
                if layer == 1:
                    ktoks.append(gather(qs[:, m * 512:(m + 1) * 512],
                                        kvo512[layer], ("q", layer, hi, m)))
                ktoks.append(gather(vsb[:, m * 512:(m + 1) * 512],
                                    kvo512[layer], ("v", layer, hi, m)))
            hload = rec.now("dld")
            rec.wait("tensor", hload)
            rec.wait("scalar", hload)

            sub_toks = {}
            exp_toks = {}
            pvl_toks = {}
            u_global = 0
            prev_final = None
            for qt in range(16):
                if layer == 0:
                    qtile = qs[:, hi * 8192 + qt * 512: hi * 8192 + (qt + 1) * 512]
                else:
                    qtile = qs[:, qt * 512:(qt + 1) * 512]
                ab = Aop(lambda e, qtile=qtile: e.activation(
                    absq[:, 0:512], qtile, AF.Abs))
                rec.wait("tensor", ab)
                mh = T(lambda e, hi=hi: e.matmul(
                    pb[5][0:1, 0:512], cmy[:, hi:hi + 1], absq[:, 0:512],
                    start=True, stop=True))
                rec.wait("vector", mh)
                V(lambda e: e.tensor_scalar(mrow[:, 0:512], pb[5][0:1, 0:512],
                                            SCALE, 1.0, ALU.mult, ALU.add))
                mg = vgap()
                rec.wait("tensor", mg)
                bc = T(lambda e: e.matmul(pb[4][:, 0:512], ones_row,
                                          mrow[:, 0:512], start=True, stop=True))
                rec.wait("vector", bc)
                mhcp = V(lambda e: e.tensor_copy(mhb, pb[4][:, 0:512]))
                vgap()
                nunits = 4 * (qt + 1)
                for uu in range(nunits):
                    u = u_global + uu
                    m, sb = uu // 4, uu % 4
                    diag = (m == qt)
                    par = u % 2
                    bank = pb[par]
                    if u - 2 in sub_toks:
                        rec.wait("tensor", sub_toks[u - 2])
                    qk = T(lambda e, uu=uu, bank=bank, qtile=qtile: e.matmul(
                        bank[:, 0:512], ksb[:, uu * 128:(uu + 1) * 128],
                        qtile, start=True, stop=True))
                    rec.wait("vector", qk)
                    if u - 2 in exp_toks:
                        rec.wait("vector", exp_toks[u - 2])
                    if uu == 0:
                        pass
                    st = V(lambda e, par=par, bank=bank: e.scalar_tensor_tensor(
                        ppre[:, par * 512:(par + 1) * 512], bank[:, 0:512],
                        SCALE, mhb, ALU.mult, ALU.subtract))
                    sub_toks[u] = st
                    rec.wait("scalar", st)
                    if u - 2 in pvl_toks:
                        rec.wait("scalar", pvl_toks[u - 2])
                    ex = Aop(lambda e, par=par: e.activation(
                        pbf[:, par * 512:(par + 1) * 512],
                        ppre[:, par * 512:(par + 1) * 512], AF.Exp))
                    exp_toks[u] = ex
                    ptok = ex
                    if diag:
                        rec.wait("gpsimd", ex)
                        mk = Gop(lambda e, par=par, sb=sb: e.affine_select(
                            out=pbf[:, par * 512:(par + 1) * 512],
                            in_=pbf[:, par * 512:(par + 1) * 512],
                            compare_op=ALU.is_ge, fill=0.0, base=-128 * sb,
                            pattern=[[1, 512]], channel_multiplier=-1))
                        ptok = mk
                        exp_toks[u] = mk
                    rec.wait("tensor", ptok)
                    if uu == 0 and prev_final is not None:
                        rec.wait("tensor", prev_final)
                    T(lambda e, uu=uu, par=par, nunits=nunits: e.matmul(
                        pb[2][:, 0:512], vsb[:, uu * 128:(uu + 1) * 128],
                        pbf[:, par * 512:(par + 1) * 512],
                        start=(uu == 0), stop=(uu == nunits - 1)))
                    pvl_toks[u] = T(lambda e, par=par, uu=uu, nunits=nunits:
                                    e.matmul(
                        pb[3][0:1, 0:512], ones_b,
                        pbf[:, par * 512:(par + 1) * 512],
                        start=(uu == 0), stop=(uu == nunits - 1)))
                u_global += nunits
                rec.wait("vector", pvl_toks[u_global - 1])
                V(lambda e: e.reciprocal(mrow[:, 0:512], pb[3][0:1, 0:512]))
                rg = vgap()
                rec.wait("tensor", rg)
                bc2 = T(lambda e: e.matmul(pb[4][:, 0:512], ones_row,
                                           mrow[:, 0:512], start=True,
                                           stop=True))
                rec.wait("vector", bc2)
                V(lambda e: e.tensor_copy(rbb, pb[4][:, 0:512]))
                vgap()
                fo = V(lambda e, hi=hi, qt=qt: e.tensor_tensor(
                    outT[:, hi * 8192 + qt * 512: hi * 8192 + (qt + 1) * 512],
                    pb[2][:, 0:512], rbb, ALU.mult))
                outv_toks.append(fo)
                prev_final = fo
            last_unit_toks = [rec.now("dve"), rec.now("pe")]
        return outv_toks

    def ph_wo(layer, outv_toks):
        for t in outv_toks[-2:]:
            rec.wait("sync", t)
        rec.wait("sync", rec.now("dve"))
        d1 = rec.dma("sync", ao_in[0:128, :], outT[:, 0:8192], sem="dkv")
        d2 = rec.dma("sync", ao_in[128:256, :], outT[:, 8192:16384], sem="dkv")
        rec.wait("gpsimd", rec.now("dkv"))
        ag = rec.op("gpsimd", lambda e: e.collective_compute(
            "AllGather", ALU.bypass, replica_groups=[list(range(R))],
            ins=[ao_in[:, :]], outs=[ao_out[:, :]]), inc="cao")
        rec.wait("gpsimd", ag)
        for jk in range(16):
            for cc in range(2):
                s2 = gather(outT[:, jk * 1024 + cc * 512:
                                 jk * 1024 + (cc + 1) * 512],
                            ao512, ("ao", layer, jk, cc))
        rec.wait("gpsimd", s2)
        rec.wait("tensor", s2)
        rec.wait("sync", s2)
        mat = "Wo" if layer == 0 else "wo1"
        bias_src = bo_f if layer == 0 else wob_f
        cbv = cb0 if layer == 0 else cb1
        if layer == 1:
            cv = V(lambda e: e.tensor_copy(cvb, cc1))
            rec.wait("tensor", cv)
        bank_tok = [None, None]
        stage_tok = [None, None]
        rec.wait("sync", rec.now("pe"))
        rec.wait("tensor", rec.now("dve"))
        for o in range(G):
            sh = o % 2
            stg = wsa[:, sh * 2048:(sh + 1) * 2048]
            if stage_tok[sh] is not None:
                rec.wait("sync", stage_tok[sh])
            ld = rec.dma("sync", stg, packA_stage(natA(mat, o), 2048),
                         sem="sta" + str(sh))
            rec.wait("tensor", ld)
            if o > 0:
                rec.wait("tensor", cb_ts)
            cm = None
            for g in range(G):
                cm = T(lambda e, stg=stg, g=g: e.matmul(
                    pb[5][:, 0:1], stg[:, g * 128:(g + 1) * 128],
                    cvb[:, g:g + 1], start=(g == 0), stop=(g == G - 1)))
            rec.wait("vector", cm)
            cb_ts = V(lambda e, o=o, cbv=cbv, bias_src=bias_src: e.tensor_scalar(
                cbv[:, o:o + 1], pb[5][:, 0:1], bias_src[:, o:o + 1],
                None, ALU.add))
            vgap()
            mmtok = None
            for half in range(2):
                bnk = (2 * o + half) % 2
                bank = pb[bnk]
                if bank_tok[bnk] is not None:
                    rec.wait("tensor", bank_tok[bnk])
                for jk in range(G):
                    gh = ghead(jk)
                    mmtok = T(lambda e, stg=stg, gh=gh, jk=jk, half=half,
                              bank=bank: e.matmul(
                        bank[:, 0:512], stg[:, gh * 128:(gh + 1) * 128],
                        outT[:, jk * 1024 + half * 512:
                             jk * 1024 + (half + 1) * 512],
                        start=(jk == 0), stop=(jk == G - 1)))
                rec.wait("vector", mmtok)
                bank_tok[bnk] = V(
                    lambda e, o=o, half=half, bank=bank, cbv=cbv:
                    e.scalar_tensor_tensor(
                        ybuf[:, o * 1024 + half * 512:
                             o * 1024 + (half + 1) * 512],
                        bank[:, 0:512], cbv[:, o:o + 1],
                        stream[:, o * 1024 + half * 512:
                               o * 1024 + (half + 1) * 512],
                        ALU.add, ALU.add))
            stage_tok[sh] = bank_tok[(2 * o + 1) % 2]

    def fview(g):
        if g < 8:
            return qs32[:, g * 1024:(g + 1) * 1024]
        if g < 12:
            return ksb32[:, (g - 8) * 1024:(g - 7) * 1024]
        return vsb32[:, (g - 12) * 1024:(g - 11) * 1024]

    def ph_ln(final=False):
        barrier()
        for half in range(2):
            sq_toks = []
            for g in range(G):
                if g >= 2:
                    rec.wait("scalar", sq_toks[g - 2])
                sq = Aop(lambda e, g=g, half=half: e.activation(
                    ysqv[:, (g % 2) * 512:(g % 2) * 512 + 512],
                    ybuf[:, g * 1024 + half * 512:
                         g * 1024 + (half + 1) * 512], AF.Square))
                rec.wait("tensor", sq)
                T(lambda e, g=g, half=half: e.matmul(
                    pb[6][0:1, 0:512], ones_b,
                    ybuf[:, g * 1024 + half * 512:
                         g * 1024 + (half + 1) * 512],
                    start=(g == 0), stop=(g == G - 1)))
                s2t = T(lambda e, g=g: e.matmul(
                    pb[7][0:1, 0:512], ones_b,
                    ysqv[:, (g % 2) * 512:(g % 2) * 512 + 512],
                    start=(g == 0), stop=(g == G - 1)))
                sq_toks.append(s2t)
            stt = sq_toks[-1]
            rec.wait("vector", stt)
            rec.wait("vector", rec.now("act"))
            r4 = pb[4][0:1, 0:512]
            r5 = pb[5][0:1, 0:512]
            Atmp = t1[0:1, 0:512]
            Btmp = t1[0:1, 512:1024]
            mu = muK[:, 0:512]
            V(lambda e: e.tensor_scalar(mu, pb[6][0:1, 0:512], 1.0 / D,
                                        None, ALU.mult))
            V(lambda e: e.tensor_scalar(Atmp, pb[7][0:1, 0:512], 1.0 / D,
                                        None, ALU.mult))
            vgap()
            V(lambda e, mu=mu: e.tensor_tensor(Btmp, mu, mu, ALU.mult))
            vgap()
            V(lambda e: e.tensor_tensor(r4, Atmp, Btmp, ALU.subtract))
            vgap()
            V(lambda e: e.tensor_scalar(Btmp, r4, EPS_LN, None, ALU.add))
            vr = vgap()
            rec.wait("scalar", vr)
            sd = Aop(lambda e: e.activation(r4, Btmp, AF.Sqrt))
            rec.wait("vector", sd)
            V(lambda e: e.reciprocal(Atmp, r4))
            vgap()
            V(lambda e: e.tensor_tensor(r4, Atmp, Atmp, ALU.mult))
            vgap()
            V(lambda e: e.tensor_tensor(r5, Btmp, r4, ALU.mult))
            vgap()
            V(lambda e: e.tensor_scalar(Btmp, r5, -0.5, 1.5, ALU.mult,
                                        ALU.add))
            vgap()
            V(lambda e, half=half: e.tensor_tensor(
                rstd_row[:, half * 512:(half + 1) * 512], Atmp, Btmp,
                ALU.mult))
            vgap()
            V(lambda e, half=half, mu=mu: e.tensor_tensor(
                m2_row[:, half * 512:(half + 1) * 512], mu,
                rstd_row[:, half * 512:(half + 1) * 512], ALU.mult))
            mfin = vgap()
            rec.wait("tensor", mfin)
            b1 = T(lambda e, half=half: e.matmul(
                pb[4][:, 0:512], ones_row,
                rstd_row[:, half * 512:(half + 1) * 512],
                start=True, stop=True))
            b2 = T(lambda e, half=half: e.matmul(
                pb[5][:, 0:512], ones_row,
                m2_row[:, half * 512:(half + 1) * 512],
                start=True, stop=True))
            rec.wait("vector", b2)
            zbuf = [t1[:, 0:512], t1[:, 512:1024]]
            for g in range(G):
                V(lambda e, g=g, half=half: e.tensor_tensor(
                    zbuf[g % 2], ybuf[:, g * 1024 + half * 512:
                                      g * 1024 + (half + 1) * 512],
                    pb[4][:, 0:512], ALU.mult))
                vgap()
                if final:
                    dst = fview(g)[:, half * 512:(half + 1) * 512]
                else:
                    dst = stream[:, g * 1024 + half * 512:
                                 g * 1024 + (half + 1) * 512]
                V(lambda e, g=g, dst=dst: e.tensor_tensor(
                    dst, zbuf[g % 2], pb[5][:, 0:512], ALU.subtract))
        barrier()

    def ph_ffn0(stop=None):
        hT = outT[0:64, 0:1024]
        Rt = outT32[:, 2048:2176]
        m1t = outT32[:, 2176:2184]
        m2t = outT32[:, 2184:2192]
        eqm = outT32[:, 2192:2320]
        msk = outT32[:, 2320:2448]
        expl = outT32[:, 2448:2576]
        gat = outT32[:, 2576:2704]
        den = outT32[:, 2704:2712]
        rcd = outT32[:, 2712:2720]
        cfb = outT[:, 5536:5536 + 2048]
        coT = outT[0:16, 7680:8704]
        xab = outT[:, 8704:8704 + 2048]
        rec.wait("vector", rec.now("pe"))
        onetok = V(lambda e: e.memset(outT[64:65, 0:1024], 1.0))
        for half in range(2):
            mm = None
            for g in range(G):
                mm = T(lambda e, g=g, half=half: e.matmul(
                    pb[4][0:64, 0:512], rwT[:, g * 64:(g + 1) * 64],
                    stream_rhs(g, half), start=(g == 0), stop=(g == G - 1)))
            rec.wait("vector", mm)
            V(lambda e, half=half: e.tensor_scalar(
                t1[0:64, 0:512], pb[4][0:64, 0:512], rb1_f[0:64, 0:1],
                None, ALU.add))
            sv = vgap()
            rec.wait("scalar", sv)
            hs = Aop(lambda e, half=half: e.activation(
                hT[:, half * 512:(half + 1) * 512], t1[0:64, 0:512], AF.Silu))
            rec.wait("tensor", hs)
        if stop == "router":
            return
        rec.wait("tensor", onetok)
        evs = []
        for tt in range(8):
            slot = pb[2 + (tt % 2)][:, 0:16]
            if tt >= 2:
                rec.wait("tensor", evs[tt - 2])
            tk = T(lambda e, tt=tt, slot=slot: e.matmul(
                slot, outT[0:65, tt * 128:(tt + 1) * 128],
                rwT[0:65, 1024:1040], start=True, stop=True))
            rec.wait("vector", tk)
            evs.append(V(lambda e, tt=tt, slot=slot: e.tensor_copy(
                Rt[:, tt * 16:(tt + 1) * 16], slot)))
        vgap()
        if stop == "ltr":
            return
        R3 = Rt.rearrange("p (t e) -> p t e", t=8)
        V(lambda e: e.reduce_max(m1t, R3, AX.X, negate=True))
        vgap()
        m1b = m1t.rearrange("p (t o) -> p t o", o=1).broadcast_to([128, 8, 16])
        V(lambda e: e.tensor_tensor(eqm.rearrange("p (t e) -> p t e", t=8),
                                    R3, m1b, ALU.add))
        vgap()
        V(lambda e: e.tensor_scalar(msk, eqm, 0.0, None, ALU.is_equal))
        vgap()
        V(lambda e: e.scalar_tensor_tensor(expl, msk, NEG, Rt, ALU.mult,
                                           ALU.add))
        vgap()
        V(lambda e: e.reduce_max(m2t, expl.rearrange("p (t e) -> p t e", t=8),
                                 AX.X))
        vgap()
        rec.wait("scalar", rec.now("dve"))
        ex = Aop(lambda e: e.activation(expl, eqm, AF.Exp))
        rec.wait("vector", ex)
        m2b = m2t.rearrange("p (t o) -> p t o", o=1).broadcast_to([128, 8, 16])
        V(lambda e: e.tensor_tensor(msk.rearrange("p (t e) -> p t e", t=8),
                                    R3, m2b, ALU.is_ge))
        vgap()
        V(lambda e: e.tensor_tensor(gat, expl, msk, ALU.mult))
        vgap()
        V(lambda e: e.reduce_sum(den, gat.rearrange("p (t e) -> p t e", t=8),
                                 AX.X))
        vgap()
        V(lambda e: e.reciprocal(rcd, den))
        vgap()
        if stop == "rmid":
            return
        denb = rcd.rearrange("p (t o) -> p t o", o=1).broadcast_to([128, 8, 16])
        vgap()
        cfo = V(lambda e: e.tensor_tensor(
            msk.rearrange("p (t e) -> p t e", t=8),
            gat.rearrange("p (t e) -> p t e", t=8), denb, ALU.mult))
        rec.wait("tensor", cfo)
        evs = []
        for tt in range(8):
            slot = pb[2 + (tt % 2)][0:16, 0:128]
            if tt >= 2:
                rec.wait("tensor", evs[tt - 2])
            tk = T(lambda e, tt=tt, slot=slot: e.transpose(
                slot, msk[:, tt * 16:(tt + 1) * 16], identf[:]))
            rec.wait("vector", tk)
            evs.append(V(lambda e, tt=tt, slot=slot: e.tensor_copy(
                coT[:, tt * 128:(tt + 1) * 128], slot)))
        ctok = vgap()
        if stop == "rmath":
            return
        rec.wait("tensor", ctok)
        cpt = None
        for erb in range(2):
            for half in range(2):
                if cpt is not None:
                    rec.wait("tensor", cpt)
                bm = T(lambda e, erb=erb, half=half: e.matmul(
                    pb[3][:, 0:512], ebf[0:16, erb * 128:(erb + 1) * 128],
                    coT[:, half * 512:(half + 1) * 512],
                    start=True, stop=True))
                rec.wait("vector", bm)
                cpt = V(lambda e, erb=erb, half=half: e.tensor_copy(
                    cfb[:, erb * 1024 + half * 512:
                        erb * 1024 + (half + 1) * 512], pb[3][:, 0:512]))
        vgap()
        if stop == "expand":
            return
        rec.wait("sync", rec.now("pe"))
        abtok = rec.dma("sync", wsb[:, 0:2048],
                        abT_flat[ds(0, BLK)].rearrange("(p c) -> p c", p=128),
                        sem="stb0")
        abtok2 = rec.dma("sync", wsb[:, 2048:4096],
                         abT_flat[ds(BLK, BLK)].rearrange("(p c) -> p c", p=128),
                         sem="stb1")
        cftok = rec.now("dve")
        xat = None
        rec.wait("tensor", abtok)
        rec.wait("tensor", abtok2)
        for erb in range(2):
            for half in range(2):
                if xat is not None:
                    rec.wait("tensor", xat)
                mm = None
                for g in range(G):
                    mm = T(lambda e, erb=erb, g=g, half=half: e.matmul(
                        pb[2][:, 0:512],
                        wsb[:, erb * 2048 + g * 128: erb * 2048 + (g + 1) * 128],
                        stream_rhs(g, half), start=(g == 0), stop=(g == G - 1)))
                rec.wait("vector", mm)
                xat = V(lambda e, erb=erb, half=half: e.tensor_tensor(
                    xab[:, erb * 1024 + half * 512:
                        erb * 1024 + (half + 1) * 512], pb[2][:, 0:512],
                    cfb[:, erb * 1024 + half * 512:
                        erb * 1024 + (half + 1) * 512], ALU.mult))
        wxtok = rec.now("dve")
        if stop == "xa":
            return
        rec.wait("sync", wxtok)
        btok = rec.dma("sync",
                       ksb[:, 0:4096].rearrange("p (o erb j) -> p o erb j",
                                                o=16, erb=2),
                       abT_flat[ds(2 * BLK, 2 * BLK)].rearrange(
                           "(o erb p j) -> p o erb j", p=128, o=16, erb=2),
                       sem="std")
        rec.wait("tensor", btok)
        rec.wait("tensor", wxtok)
        bank_tok = [None, None]
        stage_tok = [None, None]
        rec.wait("tensor", rec.now("dve"))
        for o in range(G):
            sh = o % 2
            stg = wsa[:, sh * 2048:(sh + 1) * 2048]
            if stage_tok[sh] is not None:
                rec.wait("sync", stage_tok[sh])
            ld = rec.dma("sync", stg, packA_stage(wtA(o), 2048),
                         sem="sta" + str(sh))
            rec.wait("tensor", ld)
            mmtok = None
            for half in range(2):
                bnk = (2 * o + half) % 2
                bank = pb[bnk]
                if bank_tok[bnk] is not None:
                    rec.wait("tensor", bank_tok[bnk])
                for g in range(G):
                    T(lambda e, stg=stg, g=g, half=half, bank=bank: e.matmul(
                        bank[:, 0:512], stg[:, g * 128:(g + 1) * 128],
                        stream_rhs(g, half), start=(g == 0), stop=False))
                for erb in range(2):
                    mmtok = T(lambda e, o=o, erb=erb, half=half, bank=bank:
                              e.matmul(
                        bank[:, 0:512],
                        ksb[:, o * 256 + erb * 128: o * 256 + (erb + 1) * 128],
                        xab[:, erb * 1024 + half * 512:
                            erb * 1024 + (half + 1) * 512],
                        start=False, stop=(erb == 1)))
                rec.wait("vector", mmtok)
                bank_tok[bnk] = V(
                    lambda e, o=o, half=half, bank=bank: e.scalar_tensor_tensor(
                        ybuf[:, o * 1024 + half * 512:
                             o * 1024 + (half + 1) * 512],
                        bank[:, 0:512], fb_f[:, o:o + 1],
                        stream[:, o * 1024 + half * 512:
                               o * 1024 + (half + 1) * 512],
                        ALU.add, ALU.add))
            stage_tok[sh] = bank_tok[(2 * o + 1) % 2]

    def hid_ap(hc, half):
        if hc < 16:
            return qs[:, hc * 1024 + half * 512: hc * 1024 + (half + 1) * 512]
        if hc < 24:
            c = (hc - 16) * 1024 + half * 512
            return ksb[:, c:c + 512]
        if hc < 32:
            c = (hc - 24) * 1024 + half * 512
            return vsb[:, c:c + 512]
        c = (hc - 32) * 1024 + half * 512
        return outT[:, c:c + 512]

    def ph_ffn1(agB_tok):
        rec.wait("sync", agB_tok)
        rec.wait("sync", rec.now("pe"))
        rec.wait("scalar", rec.now("pe"))
        rec.wait("tensor", rec.now("dve"))
        rec.wait("tensor", rec.now("act"))
        sig_tok = {}
        h_toks = {}
        w1tok = [None, None]
        w2tok = [None, None]
        for hc in range(NHC):
            sh = hc % 2
            stg1 = wsa[:, sh * 2048:(sh + 1) * 2048]
            stg2 = wsb[:, sh * 2048:(sh + 1) * 2048]
            if w1tok[sh] is not None:
                rec.wait("sync", w1tok[sh])
                rec.wait("scalar", w2tok[sh])
            l1 = rec.dma("sync", stg1, packB_stage(w12B(0, hc), 2048),
                         sem="sta" + str(sh))
            l2 = rec.dma("scalar", stg2, packB_stage(w12B(1, hc), 2048),
                         sem="stb" + str(sh))
            for half in range(2):
                prev = (hc, 0) if half == 1 else (hc - 1, 1)
                if prev in sig_tok:
                    rec.wait("tensor", sig_tok[prev])
                    rec.wait("tensor", h_toks[prev])
                    rec.wait("scalar", h_toks[prev])
                rec.wait("tensor", l1)
                mm1 = None
                for g in range(G):
                    mm1 = T(lambda e, stg1=stg1, g=g, half=half: e.matmul(
                        pb[0][:, 0:512], stg1[:, g * 128:(g + 1) * 128],
                        stream_rhs(g, half), start=(g == 0), stop=(g == G - 1)))
                rec.wait("tensor", l2)
                mm2 = None
                for g in range(G):
                    mm2 = T(lambda e, stg2=stg2, g=g, half=half: e.matmul(
                        pb[1][:, 0:512], stg2[:, g * 128:(g + 1) * 128],
                        stream_rhs(g, half), start=(g == 0), stop=(g == G - 1)))
                rec.wait("scalar", mm1)
                sg = Aop(lambda e, half=half: e.activation(
                    pbf[:, half * 512:(half + 1) * 512], pb[0][:, 0:512],
                    AF.Sigmoid))
                sig_tok[(hc, half)] = sg
                rec.wait("vector", mm2)
                rec.wait("vector", sg)
                h_toks[(hc, half)] = V(
                    lambda e, hc=hc, half=half: e.tensor_tensor(
                        hid_ap(hc, half), pbf[:, half * 512:(half + 1) * 512],
                        pb[1][:, 0:512], ALU.mult))
            w1tok[sh] = h_toks[(hc, 1)]
            w2tok[sh] = h_toks[(hc, 1)]
        hfin = rec.now("dve")
        rec.wait("sync", hfin)
        rec.wait("tensor", hfin)
        w3v = outT[:, 12288:16384]
        bank_tok = [None, None]
        st_tok = [None, None]
        for o in range(G):
            for qtr in range(4):
                sh = qtr % 2
                stg = w3v[:, sh * 2048: sh * 2048 + 1408]
                if st_tok[sh] is not None:
                    rec.wait("sync", st_tok[sh])
                ld = rec.dma("sync", stg,
                             packB_stage(w3B(o), NHC * 128).rearrange(
                                 "p (hc j) -> p hc j", hc=NHC)
                             [:, qtr * 11:(qtr + 1) * 11, :],
                             sem="stc" + str(sh))
                rec.wait("tensor", ld)
                mmt = None
                for hl in range(11):
                    hc = qtr * 11 + hl
                    for half in range(2):
                        bank = pb[2 + half]
                        if hc == 0 and bank_tok[half] is not None:
                            rec.wait("tensor", bank_tok[half])
                        mmt = T(lambda e, stg=stg, hl=hl, hc=hc, half=half,
                                bank=bank: e.matmul(
                            bank[:, 0:512], stg[:, hl * 128:(hl + 1) * 128],
                            hid_ap(hc, half), start=(hc == 0),
                            stop=(hc == NHC - 1)))
                st_tok[sh] = mmt
            rec.wait("vector", mmt)
            for half in range(2):
                bank = pb[2 + half]
                bank_tok[half] = V(
                    lambda e, o=o, half=half, bank=bank: e.tensor_tensor(
                        ybuf[:, o * 1024 + half * 512:
                             o * 1024 + (half + 1) * 512],
                        bank[:, 0:512],
                        stream[:, o * 1024 + half * 512:
                               o * 1024 + (half + 1) * 512], ALU.add))
        barrier()

    def ph_store():
        evs = []
        dts = []
        for t in range(8):
            buf = xstg if t % 2 == 0 else xstg2
            if t >= 2:
                rec.wait("vector", dts[t - 2])
                rec.wait("tensor", dts[t - 2])
            for g in range(G):
                slot = pb[g % 4][:, 0:128]
                idx = t * G + g
                if idx >= 4:
                    rec.wait("tensor", evs[idx - 4])
                tk = T(lambda e, g=g, t=t, slot=slot: e.transpose(
                    slot, fview(g)[:, t * 128:(t + 1) * 128], identf[:]))
                rec.wait("vector", tk)
                evs.append(V(lambda e, g=g, slot=slot, buf=buf: e.tensor_copy(
                    buf[:, g * 128:(g + 1) * 128], slot)))
            rec.wait("sync", evs[-1])
            dts.append(rec.dma("sync", y_out[t * 128:(t + 1) * 128, :], buf,
                               sem="dst" + str(t % 2)))
        tap_srcs = {"stream": (stream, G * 1024), "ybuf": (ybuf, G * 1024),
                    "qs": (qs, G * 1024), "ksb": (ksb, 8192),
                    "vsb": (vsb, 8192), "outT": (outT, G * 1024)}
        for tname in tap_outs:
            src, w = tap_srcs[tname]
            dts.append(rec.dma("gpsimd", tap_outs[tname][:, 0:w],
                               src[:, 0:w], sem="dtp"))
        for sname in ("dst0", "dst1", "dtp"):
            rec.wait("gpsimd", rec.now(sname))

    def run_phases(upto=None):
        steps = [
            ("consts", lambda st: st.__setitem__("agA", None) or ph_consts()),
        ]
        st = {}
        ph_consts()
        agA = ph_prep_A()
        ph_prep_AB()
        agB = ph_prep_B()
        ph_load_x()
        barrier()
        if upto == "prep":
            return
        ag0 = ph_proj_L0(agA)
        if upto == "proj0":
            barrier(); return
        ot = ph_attention(0, ag0)
        if upto == "attn0":
            barrier(); return
        ph_wo(0, ot)
        if upto == "wo0":
            barrier(); return
        ph_ln(final=False)
        if upto == "ln0":
            barrier(); return
        if upto is not None and upto.startswith("ffn0:"):
            ph_ffn0(stop=upto.split(":")[1])
            barrier(); return
        ph_ffn0()
        if upto == "ffn0":
            barrier(); return
        ph_ln(final=False)
        if upto == "ln1":
            barrier(); return
        ag1 = ph_proj_L1(agB)
        barrier()
        if upto == "proj1":
            return
        ot = ph_attention(1, ag1)
        if upto == "attn1":
            barrier(); return
        ph_wo(1, ot)
        if upto == "wo1":
            barrier(); return
        ph_ln(final=False)
        ph_ffn1(agB)
        if upto == "ffn1":
            barrier(); return
        ph_ln(final=True)
        ph_store()

    run_phases(upto)
    if upto is not None:
        for sname in ("dst0", "dst1", "dtp"):
            rec.wait("gpsimd", rec.now(sname))

    with nc.Block() as block:
        for engname, deco in (("tensor", block.tensor), ("vector", block.vector),
                              ("scalar", block.scalar), ("gpsimd", block.gpsimd),
                              ("sync", block.sync)):
            def runner(e, ops=rec.ops[engname]):
                for f in ops:
                    f(e)
            deco(runner)

    ctx.close()
    return nc


_NC_CACHE = {}


def _get_nc(tapkey=()):
    if tapkey not in _NC_CACHE:
        _NC_CACHE[tapkey] = build_program(taps=tapkey)
    return _NC_CACHE[tapkey]


def make_in_maps(inputs):
    ins = {k: np.ascontiguousarray(np.asarray(v)) for k, v in inputs.items()}
    x = ins["L0_attn_x"].reshape(S, D)
    in_maps = []
    for c in range(R):
        m = {}
        m["x_own"] = np.concatenate(
            [x[c * 512:(c + 1) * 512], x[(15 - c) * 512:(16 - c) * 512]], 0)
        m["p_Wq"] = ins["L0_attn_Wq"][:, c * 256:(c + 1) * 256]
        m["p_Wk"] = ins["L0_attn_Wk"][:, c * 256:(c + 1) * 256]
        m["p_Wv"] = ins["L0_attn_Wv"][:, c * 256:(c + 1) * 256]
        m["p_Wo"] = ins["L0_attn_Wo"][:, c * 256:(c + 1) * 256]
        m["p_wq1"] = ins["L1_attn_wq_w"][:, c * 256:(c + 1) * 256]
        m["p_wk1"] = ins["L1_attn_wk_w"][:, c * 256:(c + 1) * 256]
        m["p_wv1"] = ins["L1_attn_wv_w"][:, c * 256:(c + 1) * 256]
        m["p_wo1"] = ins["L1_attn_wo_w"][:, c * 256:(c + 1) * 256]
        if c < 4:
            m["p_wl"] = ins["L1_attn_wkl_w"][:, c * 128:(c + 1) * 128]
        else:
            m["p_wl"] = ins["L1_attn_wvl_w"][:, (c - 4) * 128:(c - 3) * 128]
        m["p_WT"] = ins["L0_ffn_W"][c * 256:(c + 1) * 256, :]
        st, nreal = w12_rank_blocks(c)
        w1s = ins["L1_ffn_w1"][st * 128:(st + nreal) * 128, :]
        w2s = ins["L1_ffn_w2"][st * 128:(st + nreal) * 128, :]
        if nreal < W1B:
            pad = np.zeros(((W1B - nreal) * 128, D), np.float32)
            w1s = np.concatenate([w1s, pad], 0)
            w2s = np.concatenate([w2s, pad], 0)
        m["p_w1"] = w1s
        m["p_w2"] = w2s
        m["p_w3"] = ins["L1_ffn_w3"][c * 256:(c + 1) * 256, :]
        m["bq"] = ins["L0_attn_bq"]
        m["bo"] = ins["L0_attn_bo"]
        m["bv"] = ins["L0_attn_bv"]
        m["f_bias"] = ins["L0_ffn_bias"]
        m["rw1"] = ins["L0_ffn_rw1"]
        m["rb1"] = ins["L0_ffn_rb1"]
        m["rw2"] = ins["L0_ffn_rw2"]
        m["rb2"] = np.ascontiguousarray(ins["L0_ffn_rb2"])
        m["gates"] = np.ascontiguousarray(ins["L0_ffn_gates"])
        m["A_in"] = ins["L0_ffn_A"].reshape(256, D)
        m["B_in"] = ins["L0_ffn_B"]
        m["wq_b"] = ins["L1_attn_wq_b"]
        m["wv_b"] = ins["L1_attn_wv_b"]
        m["wvl_b"] = ins["L1_attn_wvl_b"]
        m["upk_w"] = ins["L1_attn_upk_w"]
        m["upv_w"] = ins["L1_attn_upv_w"]
        m["upv_b"] = ins["L1_attn_upv_b"]
        m["wo_b"] = ins["L1_attn_wo_b"]
        gx = np.zeros((128, GIDX_N), np.int32)
        P = np.arange(128)
        rows_l = [KV0_ROWS, KV1_ROWS]
        for L in (0, 1):
            rws = rows_l[L]
            for hi in (0, 1):
                h = c if hi == 0 else 15 - c
                for mm2 in range(16):
                    rr, half = rank_of_chunk(mm2)
                    kb = KVBASE[L]["k"]
                    vb = KVBASE[L]["v"]
                    gx[:, GIDX_COLS[("k", L, hi, mm2)]] = \
                        (rr * rws + kb + h * 128 + P) * 2 + half
                    gx[:, GIDX_COLS[("v", L, hi, mm2)]] = \
                        (rr * rws + vb) * 2 + P * 32 + half * 16 + h
                    if L == 1:
                        qb = KVBASE[L]["q"]
                        gx[:, GIDX_COLS[("q", L, hi, mm2)]] = \
                            (rr * rws + qb + h * 128 + P) * 2 + half
                gx[:, GIDX_COLS[("cmy", L, hi)]] = h * 128 + P
            for jk in range(16):
                for cc2 in (0, 1):
                    chunk = c if cc2 == 0 else 15 - c
                    row = (jk // 2) * 256 + (jk % 2) * 128 + P
                    gx[:, GIDX_COLS[("ao", L, jk, cc2)]] = row * 16 + chunk
        m["gidx"] = gx
        in_maps.append({k: np.ascontiguousarray(v) for k, v in m.items()})
    return in_maps


def kernel(**inputs):
    nc = _get_nc(())
    in_maps = make_in_maps(inputs)
    res = run_bass_kernel_spmd(nc, in_maps, core_ids=list(range(R)))
    out = np.zeros((S, D), np.float32)
    for c in range(R):
        y = np.asarray(res.results[c]["y_out"])
        out[c * 512:(c + 1) * 512] = y[0:512]
        out[(15 - c) * 512:(16 - c) * 512] = y[512:1024]
    return out.reshape(1, S, D)


# version 02
# speedup vs baseline: 1.0727x; 1.0727x over previous
import numpy as np
from collections import defaultdict
from contextlib import ExitStack

import concourse.bass as bass
import concourse.mybir as mybir
from concourse.bass_utils import run_bass_kernel_spmd

F32 = mybir.dt.float32
BF16 = mybir.dt.bfloat16
AF = mybir.ActivationFunctionType
ALU = mybir.AluOpType
AX = mybir.AxisListType
ds = bass.ds

R = 8
D = 2048
S = 8192
NTOK = 1024
NH = 16
DL = 512
HID = 5632
NHC = 44
W1B = 6
G = 16
SCALE = float(1.0 / np.float32(np.sqrt(np.float32(128.0))))
NEG = -1.0e30
EPS_LN = 1.0e-5

NAT_MATS = ["Wq", "Wk", "Wv", "Wo", "wq1", "wk1", "wv1", "wo1"]
BLK = 128 * G * 128
W3BLK = 128 * NHC * 128

_packA_off, _oa = {}, 0
for _m in NAT_MATS:
    _packA_off[_m] = _oa
    _oa += 2 * BLK
_packA_off["WL"] = _oa; _oa += BLK
_packA_off["WT"] = _oa; _oa += 2 * BLK
PACKA_ELEMS = _oa
PACKA_ROWS = PACKA_ELEMS // 1024

_packB_off = {"W1Q": 0, "W2Q": W1B * BLK, "W3Q": 2 * W1B * BLK}
PACKB_ELEMS = 2 * W1B * BLK + 2 * W3BLK
PACKB_ROWS = PACKB_ELEMS // 1024


def natA(mat, o):
    r, lo = o // 2, o % 2
    return r * PACKA_ELEMS + _packA_off[mat] + lo * BLK


def wlA(which, o):
    return (o + 4 * which) * PACKA_ELEMS + _packA_off["WL"]


def wtA(o):
    r, lo = o // 2, o % 2
    return r * PACKA_ELEMS + _packA_off["WT"] + lo * BLK


def w12B(which, hc):
    if hc < 24:
        r, hb = hc // 6, hc % 6
    else:
        r, hb = 4 + (hc - 24) // 5, (hc - 24) % 5
    return r * PACKB_ELEMS + _packB_off["W1Q" if which == 0 else "W2Q"] + hb * BLK


def w12_rank_blocks(r):
    return (6 * r, 6) if r < 4 else (24 + 5 * (r - 4), 5)


def w3B(o):
    r, lo = o // 2, o % 2
    return r * PACKB_ELEMS + _packB_off["W3Q"] + lo * W3BLK


def rank_of_chunk(m):
    return (m, 0) if m < 8 else (15 - m, 1)


GIDX_COLS = {}
_gc = 0


def _gcol(key):
    global _gc
    if key not in GIDX_COLS:
        GIDX_COLS[key] = _gc
        _gc += 1
    return GIDX_COLS[key]


for _L in (0, 1):
    for _hi in (0, 1):
        for _m in range(16):
            _gcol(("k", _L, _hi, _m))
            _gcol(("v", _L, _hi, _m))
            if _L == 1:
                _gcol(("q", _L, _hi, _m))
        _gcol(("cmy", _L, _hi))
    for _jk in range(16):
        for _cc in (0, 1):
            _gcol(("ao", _L, _jk, _cc))
GIDX_N = _gc

KV0_ROWS = 2048 + 2048 + 2
KV1_ROWS = 2048 + 2048 + 2048 + 2
KVBASE = [
    {"q": None, "k": 0, "v": 2048, "c": 4096, "rows": KV0_ROWS},
    {"q": 0, "k": 2048, "v": 4096, "c": 6144, "rows": KV1_ROWS},
]


def ghead(jk):
    j, k = jk // 2, jk % 2
    return j if k == 0 else 15 - j


class Rec:
    ENGS = ("tensor", "vector", "scalar", "gpsimd", "sync")

    def __init__(self, sems):
        self.sems = sems
        self.ops = {e: [] for e in self.ENGS}
        self.cnt = defaultdict(int)

    def op(self, eng, fn, inc=None, n=1):
        if inc is None:
            self.ops[eng].append(fn)
            return None
        self.cnt[inc] += n
        tok = (inc, self.cnt[inc])
        sem = self.sems[inc]
        self.ops[eng].append(lambda e, fn=fn, s=sem, k=n: fn(e).then_inc(s, k))
        return tok

    def wait(self, eng, tok):
        if tok is None:
            return
        s, v = self.sems[tok[0]], tok[1]
        self.ops[eng].append(lambda e, s=s, v=v: e.wait_ge(s, v))

    def now(self, sem):
        return (sem, self.cnt[sem])

    def dma(self, eng, out, in_, sem=None, **kw):
        if sem is None:
            sem = {"sync": "dsp", "scalar": "dact", "gpsimd": "dgp"}[eng]
        return self.op(eng,
                       lambda e, o=out, i=in_, kw=kw: e.dma_start(
                           out=o, in_=i, **kw),
                       inc=sem, n=16)


PHASE_MARKS = []


def build_program(taps=(), upto=None):
    nc = bass.Bass("TRN2", target_bir_lowering=False, debug=False)

    def din(name, shape, dtype=F32):
        return nc.dram_tensor(name, list(shape), dtype, kind="ExternalInput").ap()

    x_own = din("x_own", (NTOK, D))
    p_nat = {m: din("p_" + m, (D, 256)) for m in NAT_MATS}
    p_wl = din("p_wl", (D, 128))
    p_WT = din("p_WT", (256, D))
    p_w1 = din("p_w1", (W1B * 128, D))
    p_w2 = din("p_w2", (W1B * 128, D))
    p_w3 = din("p_w3", (256, HID))
    bq_i = din("bq", (D,))
    bo_i = din("bo", (D,))
    bv_i = din("bv", (D,))
    fb_i = din("f_bias", (D,))
    rw1_i = din("rw1", (64, D))
    rb1_i = din("rb1", (64,))
    rw2_i = din("rw2", (32, 64))
    rb2_i = din("rb2", (32,))
    gates_i = din("gates", (16,))
    A_i = din("A_in", (256, D))
    B_i = din("B_in", (NH, D, 16))
    wqb_i = din("wq_b", (D,))
    wvb_i = din("wv_b", (D,))
    wvlb_i = din("wvl_b", (DL,))
    upk_i = din("upk_w", (NH, DL, 128))
    upv_i = din("upv_w", (NH, DL, 128))
    upvb_i = din("upv_b", (NH, 128))
    wob_i = din("wo_b", (D,))
    gidx_i = din("gidx", (128, GIDX_N), mybir.dt.int32)

    y_out = nc.dram_tensor("y_out", [NTOK, D], F32, kind="ExternalOutput").ap()
    tap_outs = {}
    for tname in taps:
        tap_outs[tname] = nc.dram_tensor(
            "tap_" + tname, [128, G * 1024], F32, kind="ExternalOutput").ap()

    packA_in = nc.dram_tensor("packA_in", [PACKA_ROWS, 1024], BF16).ap()
    packA_out = nc.dram_tensor("packA_out", [R * PACKA_ROWS, 1024], BF16,
                               addr_space="Shared").ap()
    packB_in = nc.dram_tensor("packB_in", [PACKB_ROWS, 1024], BF16).ap()
    packB_out = nc.dram_tensor("packB_out", [R * PACKB_ROWS, 1024], BF16,
                               addr_space="Shared").ap()
    gs_in = nc.dram_tensor("gs_in", [1, 16], F32).ap()
    gs_out = nc.dram_tensor("gs_out", [1, 16], F32).ap()
    kv_in = [nc.dram_tensor("kv0_in", [KV0_ROWS, 1024], BF16).ap(),
             nc.dram_tensor("kv1_in", [KV1_ROWS, 1024], BF16).ap()]
    kv_out = [nc.dram_tensor("kv0_out", [R * KV0_ROWS, 1024], BF16,
                             addr_space="Shared").ap(),
              nc.dram_tensor("kv1_out", [R * KV1_ROWS, 1024], BF16,
                             addr_space="Shared").ap()]
    ao_in = nc.dram_tensor("ao_in", [256, S], BF16).ap()
    ao_out = nc.dram_tensor("ao_out", [D, S], BF16,
                            addr_space="Shared").ap()
    abT = nc.dram_tensor("abT", [1536, 1024], BF16).ap()
    cscr = nc.dram_tensor("cscr", [2, 1024], BF16).ap()

    ctx = ExitStack()
    e = ctx.enter_context

    identf = e(nc.sbuf_tensor("identf", [128, 128], F32))
    identb = e(nc.sbuf_tensor("identb", [128, 128], BF16))
    stream = e(nc.sbuf_tensor("stream", [128, G * 1024], BF16))
    ybuf = e(nc.sbuf_tensor("ybuf", [128, G * 1024], BF16))
    qs = e(nc.sbuf_tensor("qs", [128, G * 1024], BF16))
    ksb = e(nc.sbuf_tensor("ksb", [128, 8192], BF16))
    vsb = e(nc.sbuf_tensor("vsb", [128, 8192], BF16))
    outT = e(nc.sbuf_tensor("outT", [128, G * 1024], BF16))
    ppre = e(nc.sbuf_tensor("ppre", [128, 1024], F32))
    pbf = e(nc.sbuf_tensor("pbf", [128, 1024], BF16))
    ascr = e(nc.sbuf_tensor("ascr", [128, 2560], F32))
    t1 = e(nc.sbuf_tensor("t1", [128, 1024], F32))
    wsa = e(nc.sbuf_tensor("wsa", [128, 4096], BF16))
    wsb = e(nc.sbuf_tensor("wsb", [128, 4096], BF16))
    absq = e(nc.sbuf_tensor("absq", [128, 512], BF16))
    smf = e(nc.sbuf_tensor("smf", [128, 512], F32))
    smb = e(nc.sbuf_tensor("smb", [128, 128], BF16))
    ebf = e(nc.sbuf_tensor("ebf", [128, 256], BF16))
    rwT = e(nc.sbuf_tensor("rwT", [128, 1088], BF16))
    gix = e(nc.sbuf_tensor("gix", [128, GIDX_N], mybir.dt.int32))

    xstg = wsa.ap().bitcast(F32)
    xstg2 = wsb.ap().bitcast(F32)
    qs32 = qs.ap().bitcast(F32)
    ksb32 = ksb.ap().bitcast(F32)
    vsb32 = vsb.ap().bitcast(F32)
    ysqv = pbf.ap()
    kvstg = ppre.ap().bitcast(BF16)
    outT32 = outT.ap().bitcast(F32)

    ck = smf[:, 0:16]
    bq_f = smf[:, 16:32]
    wqb_f = smf[:, 32:48]
    cb0 = smf[:, 48:64]
    cb1 = smf[:, 64:80]
    fb_f = smf[:, 80:96]
    cc1 = smf[:, 96:112]
    v1c = smf[:, 112:116]
    wvlb_f = smf[:, 116:120]
    upvb_f = smf[:, 120:136]
    bv_f = smf[:, 136:152]
    rgv = smf[:, 152:155]
    rgn = smf[:, 155:158]
    rb1_f = smf[:, 158:159]
    onescol_f = smf[:, 160:161]
    gacc = smf[:, 161:167]
    gtmp = smf[:, 167:168]
    gapd = smf[0:1, 168:176]
    gapsrc = smf[0:1, 176:184]
    ones_row = smf[0:1, 184:312]
    grow = smf[0:1, 312:328]
    g3a = smf[0:1, 328:331]
    g3b = smf[0:1, 331:334]
    g3c = smf[0:1, 334:337]
    bo_f = smf[:, 352:368]
    wob_f = smf[:, 368:384]
    wvb_f = smf[:, 384:400]
    rb2_row = smf[0:1, 400:416]
    gates_row = smf[0:1, 416:432]
    rb2g_row = smf[0:1, 432:448]

    cfeat = smb[:, 0:16]
    cmy = smb[:, 16:18]
    ones_b = smb[:, 18:19]
    cvb = smb[:, 20:36]
    wvb_b = smb[:, 36:52]
    v1cb = smb[:, 52:56]

    mhb = ascr[:, 0:512]
    rbb = ascr[:, 512:1024]
    mrow = ascr[0:1, 2048:2560]
    rstd_row = ascr[0:1, 0:1024]
    m2_row = ascr[0:1, 1024:2048]
    muK = ascr[0:1, 2048:2560]

    pband = [e(nc.psum_tensor(f"pb{i}", [128, 512], F32)) for i in range(8)]
    pb = [p.ap() for p in pband]
    pbb = [p.ap().bitcast(BF16) for p in pband]

    sems = {n: e(nc.semaphore(n))
            for n in ("pe", "dve", "act", "gp", "dsp", "dact", "dgp", "cc",
                      "dcn", "sta0", "sta1", "stb0", "stb1", "stc0", "stc1",
                      "std", "kvo0", "kvo1", "dup", "dld", "dao", "dcm",
                      "dkv", "dst0", "dst1", "dtp",
                      "cA", "cR", "cB", "ck0", "ck1", "cao")}
    rec = Rec(sems)

    def T(fn):
        return rec.op("tensor", fn, inc="pe")

    def V(fn):
        return rec.op("vector", fn, inc="dve")

    def Aop(fn):
        return rec.op("scalar", fn, inc="act")

    def Gop(fn):
        return rec.op("gpsimd", fn, inc="gp")

    def vgap():
        t = rec.now("dve")
        rec.wait("vector", t)
        return t

    def gwait():
        t = rec.now("gp")
        rec.wait("gpsimd", t)
        return t

    def barrier():
        toks = [rec.now(s) for s in sems]
        for eng in Rec.ENGS:
            for t in toks:
                rec.wait(eng, t)


    pA_flat = packA_in.rearrange("r c -> (r c)")
    pB_flat = packB_in.rearrange("r c -> (r c)")
    pAo_flat = packA_out.rearrange("r c -> (r c)")
    pBo_flat = packB_out.rearrange("r c -> (r c)")
    abT_flat = abT.rearrange("r c -> (r c)")
    kvi_flat = [k.rearrange("r c -> (r c)") for k in kv_in]
    kvo_flat = [k.rearrange("r c -> (r c)") for k in kv_out]
    kvo512 = [k.rearrange("r c -> (r c)").rearrange("(n b) -> n b", b=512)
              for k in kv_out]
    ao512 = ao_out.rearrange("r c -> (r c)").rearrange("(n b) -> n b", b=512)

    GATHER_SEM = {"k": "dld", "v": "dld", "q": "dld", "ao": "dao",
                  "cmy": "dcm"}

    def gather(out_ap, src512, key):
        col = GIDX_COLS[key]
        return rec.op("gpsimd", lambda e, o=out_ap, s=src512, col=col:
                      e.indirect_dma_start(
                          out=o, out_offset=None, in_=s,
                          in_offset=bass.IndirectOffsetOnAxis(
                              gix[:, col:col + 1], 0)),
                      inc=GATHER_SEM[key[0]], n=16)

    def packA_stage(elem_off, n):
        return pAo_flat[ds(elem_off, 128 * n)].rearrange("(p c) -> p c", p=128)

    def packB_stage(elem_off, n):
        return pBo_flat[ds(elem_off, 128 * n)].rearrange("(p c) -> p c", p=128)

    def ph_consts():
        def Gs(fn):
            t = Gop(fn)
            rec.wait("gpsimd", t)
            return t
        Gs(lambda e: e.memset(identf[:], 0.0))
        Gs(lambda e: e.affine_select(out=identf[:], in_=identf[:],
                                      compare_op=ALU.not_equal, fill=1.0, base=0,
                                      pattern=[[-1, 128]], channel_multiplier=1))
        Gs(lambda e: e.memset(identb[:], 0.0))
        Gs(lambda e: e.affine_select(out=identb[:], in_=identb[:],
                                      compare_op=ALU.not_equal, fill=1.0, base=0,
                                      pattern=[[-1, 128]], channel_multiplier=1))
        Gs(lambda e: e.memset(ones_b, 1.0))
        Gs(lambda e: e.memset(onescol_f, 1.0))
        Gs(lambda e: e.memset(ones_row, 1.0))
        Gs(lambda e: e.memset(gapsrc, 0.0))
        Gs(lambda e: e.memset(ebf[:], 0.0))
        for blk in range(2):
            sl = ebf[0:16, blk * 128:(blk + 1) * 128]
            Gs(lambda e, sl=sl: e.memset(sl, 1.0))
            Gs(lambda e, sl=sl, blk=blk: e.affine_select(
                out=sl, in_=sl, compare_op=ALU.is_ge, fill=0.0,
                base=128 * blk, pattern=[[1, 128]], channel_multiplier=-16))
            Gs(lambda e, sl=sl, blk=blk: e.affine_select(
                out=sl, in_=sl, compare_op=ALU.is_ge, fill=0.0,
                base=15 - 128 * blk, pattern=[[-1, 128]],
                channel_multiplier=16))
        for dst, src, p in ((bq_f, bq_i, 128), (wqb_f, wqb_i, 128),
                            (bv_f, bv_i, 128), (fb_f, fb_i, 128),
                            (bo_f, bo_i, 128), (wob_f, wob_i, 128),
                            (wvb_f, wvb_i, 128),
                            (upvb_f, upvb_i.rearrange("h d -> (h d)"), 128),
                            (wvlb_f, wvlb_i, 128),
                            (rb1_f, rb1_i, 64)):
            gg = src.shape[0] // p
            rec.dma("gpsimd", dst[0:p, 0:gg],
                    src.rearrange("(g p) -> p g", p=p), sem="dcn",
                    allow_slow_non_contiguous=True)
        rec.dma("gpsimd", rb2_row, rb2_i[ds(0, 16)].rearrange(
            "(o c) -> o c", o=1), sem="dcn")
        rec.dma("gpsimd", gates_row, gates_i.rearrange("(o c) -> o c", o=1),
                sem="dcn")
        rec.dma("gpsimd", gix[:], gidx_i[:, :], sem="dcn")
        gd = rec.now("dcn")
        rec.wait("vector", gd)
        V(lambda e: e.tensor_copy(wvb_b, wvb_f))
        V(lambda e: e.tensor_copy(cvb, bv_f))
        V(lambda e: e.tensor_tensor(rb2g_row, rb2_row, gates_row, ALU.add))
        vgap()
        V(lambda e: e.tensor_copy(rwT[64:65, 1024:1040], rb2g_row))
        barrier()

    def nat_prep_one(src_ap, dst_elems, nblocks):
        last = None
        for lo in range(nblocks):
            ld = rec.dma("sync",
                         xstg2[:, 0:2048].rearrange("p (g c) -> p g c", g=G),
                         src_ap[:, lo * 128:(lo + 1) * 128].rearrange(
                             "(g p) c -> p g c", p=128), sem="stb0")
            rec.wait("vector", ld)
            if last is not None:
                rec.wait("vector", last)
            cv = V(lambda e: e.tensor_copy(wsa[:, 0:2048], xstg2[:, 0:2048]))
            rec.wait("gpsimd", cv)
            dst = pA_flat[ds(dst_elems + lo * BLK, BLK)].rearrange(
                "(p c) -> p c", p=128)
            last = rec.dma("gpsimd", dst, wsa[:, 0:2048])
            rec.wait("gpsimd", last)
            rec.wait("sync", last)
        return last

    def ph_prep_A():
        for m in NAT_MATS:
            tok = nat_prep_one(p_nat[m], _packA_off[m], 2)
            rec.wait("sync", tok)
            rec.wait("vector", tok)
        tok = nat_prep_one(p_wl, _packA_off["WL"], 1)
        rec.wait("sync", tok)
        rec.wait("vector", tok)

        for lo in range(2):
            ld = rec.dma("sync", xstg2[:, 0:2048],
                         p_WT[lo * 128:(lo + 1) * 128, :], sem="stb0")
            rec.wait("vector", ld)
            cv = V(lambda e: e.tensor_copy(kvstg[:, 0:2048], xstg2[:, 0:2048]))
            rec.wait("tensor", cv)
            evs = []
            for g in range(G):
                slot = pbb[g % 4][:, 0:128]
                if g >= 4:
                    rec.wait("tensor", evs[g - 4])
                tt = T(lambda e, g=g, slot=slot: e.transpose(
                    slot, kvstg[:, g * 128:(g + 1) * 128], identb[:]))
                rec.wait("vector", tt)
                evs.append(V(lambda e, g=g, slot=slot: e.tensor_copy(
                    wsa[:, 2048 + g * 128:2048 + (g + 1) * 128], slot)))
            rec.wait("gpsimd", evs[-1])
            dst = pA_flat[ds(_packA_off["WT"] + lo * BLK, BLK)].rearrange(
                "(p c) -> p c", p=128)
            dt = rec.dma("gpsimd", dst, wsa[:, 2048:4096])
            rec.wait("gpsimd", dt)
            rec.wait("sync", dt)
            rec.wait("vector", dt)

        gtok = rec.now("dgp")
        rec.wait("gpsimd", gtok)
        agA = rec.op("gpsimd", lambda e: e.collective_compute(
            "AllGather", ALU.bypass, replica_groups=[list(range(R))],
            ins=[packA_in[:, :]], outs=[packA_out[:, :]]), inc="cA")
        return agA

    def ph_prep_AB():
        for erb in range(2):
            ld = rec.dma("sync", xstg2[:, 0:2048],
                         A_i[erb * 128:(erb + 1) * 128, :], sem="stb0")
            rec.wait("vector", ld)
            cv = V(lambda e: e.tensor_copy(kvstg[:, 0:2048], xstg2[:, 0:2048]))
            rec.wait("tensor", cv)
            evs = []
            for g in range(G):
                slot = pbb[g % 4][:, 0:128]
                if g >= 4:
                    rec.wait("tensor", evs[g - 4])
                tt = T(lambda e, g=g, slot=slot: e.transpose(
                    slot, kvstg[:, g * 128:(g + 1) * 128], identb[:]))
                rec.wait("vector", tt)
                evs.append(V(lambda e, g=g, slot=slot: e.tensor_copy(
                    wsa[:, 2048 + g * 128:2048 + (g + 1) * 128], slot)))
            rec.wait("gpsimd", evs[-1])
            dst = abT_flat[ds(erb * BLK, BLK)].rearrange("(p c) -> p c", p=128)
            dt = rec.dma("gpsimd", dst, wsa[:, 2048:4096])
            rec.wait("gpsimd", dt)
            rec.wait("sync", dt)
            rec.wait("vector", dt)

        Bf = B_i.rearrange("e o r -> (e o r)")
        for erb in range(2):
            srcv = Bf[ds(erb * 8 * D * 16, 8 * D * 16)].rearrange(
                "(e ob p r) -> p ob e r", e=8, ob=16, p=128)
            for ob in range(G):
                ld = rec.dma("sync", xstg2[:, ob * 128:(ob + 1) * 128]
                             .rearrange("p (e r) -> p e r", e=8),
                             srcv[:, ob, :, :], sem="stb0")
            rec.wait("vector", ld)
            cv = V(lambda e: e.tensor_copy(kvstg[:, 0:2048], xstg2[:, 0:2048]))
            rec.wait("tensor", cv)
            evs = []
            for ob in range(G):
                slot = pbb[ob % 4][:, 0:128]
                if ob >= 4:
                    rec.wait("tensor", evs[ob - 4])
                src_t = kvstg[:, ob * 128:(ob + 1) * 128]
                tt = T(lambda e, s=src_t, slot=slot: e.transpose(
                    slot, s, identb[:]))
                rec.wait("vector", tt)
                evs.append(V(lambda e, ob=ob, slot=slot: e.tensor_copy(
                    wsa[:, 2048 + ob * 128:2048 + (ob + 1) * 128], slot)))
            rec.wait("gpsimd", evs[-1])
            for ob in range(G):
                dst = abT_flat[ds(2 * BLK + ob * 32768 + erb * 16384, 16384)] \
                    .rearrange("(p c) -> p c", p=128)
                dt = rec.dma("gpsimd", dst,
                             wsa[:, 2048 + ob * 128:2048 + (ob + 1) * 128])
            rec.wait("gpsimd", dt)
            rec.wait("sync", dt)
            rec.wait("vector", dt)

        ld = rec.dma("sync", xstg2[0:64, 0:2048], rw1_i[:, :], sem="stb0")
        rec.wait("vector", ld)
        cv = V(lambda e: e.tensor_copy(kvstg[0:64, 0:2048], xstg2[0:64, 0:2048]))
        rec.wait("tensor", cv)
        evs = []
        for g in range(G):
            slot = pbb[g % 4][:, 0:128]
            if g >= 4:
                rec.wait("tensor", evs[g - 4])
            tt = T(lambda e, g=g, slot=slot: e.transpose(
                slot[:, 0:64], kvstg[0:64, g * 128:(g + 1) * 128],
                identb[0:64, 0:64]))
            rec.wait("vector", tt)
            evs.append(V(lambda e, g=g, slot=slot: e.tensor_copy(
                rwT[:, g * 64:(g + 1) * 64], slot[:, 0:64])))
        rec.wait("sync", cv)
        ld2 = rec.dma("sync", xstg2[0:32, 0:64], rw2_i[:, :], sem="stb0")
        rec.wait("vector", ld2)
        cv2 = V(lambda e: e.tensor_copy(kvstg[0:32, 0:64], xstg2[0:32, 0:64]))
        rec.wait("tensor", cv2)
        tt = T(lambda e: e.transpose(pbb[0][0:64, 256:288],
                                     kvstg[0:32, 0:64], identb[0:32, 0:32]))
        rec.wait("vector", tt)
        V(lambda e: e.tensor_copy(rwT[0:64, 1024:1056], pbb[0][0:64, 256:288]))
        rec.wait("sync", rec.now("dve"))

    def ph_prep_B():
        V(lambda e: e.memset(gacc, 0.0))
        blocks = []
        for b in range(W1B):
            blocks.append((0, p_w1[b * 128:(b + 1) * 128, :], D))
        for b in range(W1B):
            blocks.append((1, p_w2[b * 128:(b + 1) * 128, :], D))
        for lo in range(2):
            for piece in range(4):
                blocks.append((2, p_w3[lo * 128:(lo + 1) * 128,
                                       piece * 1408:(piece + 1) * 1408], 1408))
        cnt_wi = [0, 0, 0]
        rtok = None
        for wi, src, CW in blocks:
            if rtok is not None:
                rec.wait("sync", rtok)
            ld = rec.dma("sync", ksb32[:, 0:CW], src, sem="stb0")
            rec.wait("vector", ld)
            V(lambda e, CW=CW: e.reduce_sum(gtmp, ksb32[:, 0:CW], AX.X,
                                            apply_absolute_value=True))
            rtok = vgap()
            a = cnt_wi[wi] % 2
            b = (cnt_wi[wi] + 1) % 2
            V(lambda e, wi=wi, a=a, b=b: e.tensor_tensor(
                gacc[:, wi * 2 + b:wi * 2 + b + 1],
                gacc[:, wi * 2 + a:wi * 2 + a + 1], gtmp, ALU.add))
            vgap()
            cnt_wi[wi] += 1
        assert all(c % 2 == 0 for c in cnt_wi), cnt_wi
        pcom = vgap()
        rec.wait("tensor", pcom)
        gfin = gacc.rearrange("p (w t) -> p w t", t=2)[:, :, 0]
        mm = T(lambda e, gfin=gfin: e.matmul(pb[4][0:1, 0:3], onescol_f, gfin,
                                             start=True, stop=True))
        rec.wait("vector", mm)
        V(lambda e: e.memset(grow, 0.0))
        vgap()
        V(lambda e: e.tensor_copy(grow[:, 0:3], pb[4][0:1, 0:3]))
        gdone = vgap()
        rec.wait("gpsimd", gdone)
        d1 = rec.dma("gpsimd", gs_in[:, :], grow[:, 0:16])
        rec.wait("gpsimd", d1)
        ar = rec.op("gpsimd", lambda e: e.collective_compute(
            "AllReduce", ALU.add, replica_groups=[list(range(R))],
            ins=[gs_in[:, :]], outs=[gs_out[:, :]]), inc="cR")
        rec.wait("gpsimd", ar)
        d2 = rec.dma("gpsimd", grow[:, 0:16], gs_out[:, :])
        rec.wait("gpsimd", d2)
        rec.wait("vector", d2)
        NW = float(HID) * float(D)
        V(lambda e: e.tensor_scalar(g3a, grow[:, 0:3], 1.0 / NW, 1.0e-8,
                                    ALU.mult, ALU.add))
        vgap()
        V(lambda e: e.reciprocal(g3b, g3a))
        vgap()
        V(lambda e: e.tensor_scalar(g3c, g3b, -1.0, None, ALU.mult))
        gk = vgap()
        rec.wait("tensor", gk)
        mm1 = T(lambda e: e.matmul(pb[4][:, 0:3], ones_row, g3b,
                                   start=True, stop=True))
        mm2 = T(lambda e: e.matmul(pb[5][:, 0:3], ones_row, g3c,
                                   start=True, stop=True))
        rec.wait("vector", mm2)
        V(lambda e: e.tensor_copy(rgv, pb[4][:, 0:3]))
        cpr = V(lambda e: e.tensor_copy(rgn, pb[5][:, 0:3]))
        rec.wait("sync", cpr)

        def quant_into_vsb(src_slice, wi, CW):
            ld = rec.dma("sync", ksb32[:, 0:CW], src_slice, sem="stb0")
            rec.wait("vector", ld)
            vgap()
            V(lambda e, CW=CW, wi=wi: e.tensor_scalar(
                vsb[:, 0:CW], ksb32[:, 0:CW], rgv[:, wi:wi + 1], 0.5,
                ALU.mult, ALU.is_ge))
            V(lambda e, CW=CW, wi=wi: e.tensor_scalar(
                vsb[:, 2048:2048 + CW], ksb32[:, 0:CW], rgn[:, wi:wi + 1], 0.5,
                ALU.mult, ALU.is_ge))
            vgap()
            return V(lambda e, CW=CW: e.tensor_tensor(
                vsb[:, 4096:4096 + CW], vsb[:, 0:CW], vsb[:, 2048:2048 + CW],
                ALU.subtract))

        for wi, src in ((0, p_w1), (1, p_w2)):
            for b in range(W1B):
                qv = quant_into_vsb(src[b * 128:(b + 1) * 128, :], wi, D)
                rec.wait("tensor", qv)
                evs = []
                for g in range(G):
                    slot = pbb[4 + g % 4][:, 256:384]
                    if g >= 4:
                        rec.wait("tensor", evs[g - 4])
                    tt = T(lambda e, g=g, slot=slot: e.transpose(
                        slot, vsb[:, 4096 + g * 128:4096 + (g + 1) * 128],
                        identb[:]))
                    rec.wait("vector", tt)
                    evs.append(V(lambda e, g=g, slot=slot: e.tensor_copy(
                        vsb[:, 6144 + g * 128:6144 + (g + 1) * 128], slot)))
                rec.wait("gpsimd", evs[-1])
                dst = pB_flat[ds(_packB_off["W1Q" if wi == 0 else "W2Q"]
                                 + b * BLK, BLK)].rearrange("(p c) -> p c", p=128)
                dt = rec.dma("gpsimd", dst, vsb[:, 6144:8192])
                rec.wait("gpsimd", dt)
                rec.wait("sync", dt)
                rec.wait("vector", dt)

        for lo in range(2):
            for piece in range(4):
                cw = 11 * 128
                qv = quant_into_vsb(
                    p_w3[lo * 128:(lo + 1) * 128,
                         piece * cw:(piece + 1) * cw], 2, cw)
                rec.wait("tensor", qv)
                evs = []
                for t in range(11):
                    slot = pbb[4 + t % 4][:, 256:384]
                    if t >= 4:
                        rec.wait("tensor", evs[t - 4])
                    tt = T(lambda e, t=t, slot=slot: e.transpose(
                        slot, vsb[:, 4096 + t * 128:4096 + (t + 1) * 128],
                        identb[:]))
                    rec.wait("vector", tt)
                    evs.append(V(lambda e, t=t, slot=slot: e.tensor_copy(
                        vsb[:, 6144 + t * 128:6144 + (t + 1) * 128], slot)))
                rec.wait("gpsimd", evs[-1])
                dst = pB_flat[ds(_packB_off["W3Q"] + lo * W3BLK, W3BLK)] \
                    .rearrange("(p hc j) -> p hc j", p=128, hc=NHC) \
                    [:, piece * 11:(piece + 1) * 11, :]
                dt = rec.dma("gpsimd", dst,
                             vsb[:, 6144:6144 + cw].rearrange(
                                 "p (hc j) -> p hc j", hc=11))
                rec.wait("gpsimd", dt)
                rec.wait("sync", dt)
                rec.wait("vector", dt)

        gtok = rec.now("dgp")
        rec.wait("gpsimd", gtok)
        agB = rec.op("gpsimd", lambda e: e.collective_compute(
            "AllGather", ALU.bypass, replica_groups=[list(range(R))],
            ins=[packB_in[:, :]], outs=[packB_out[:, :]]), inc="cB")
        return agB

    def ph_load_x():
        evs = []
        slot_ev = [None, None]
        for t in range(8):
            buf = xstg2 if t % 2 == 0 else xstg
            snm = "stb0" if t % 2 == 0 else "sta0"
            if slot_ev[t % 2] is not None:
                rec.wait("sync", slot_ev[t % 2])
            ld = rec.dma("sync", buf[:, 0:2048],
                         x_own[t * 128:(t + 1) * 128, :], sem=snm)
            rec.wait("tensor", ld)
            for g in range(G):
                slot = pb[g % 4][:, 0:128]
                idx = t * G + g
                if idx >= 4:
                    rec.wait("tensor", evs[idx - 4])
                tt = T(lambda e, buf=buf, g=g, slot=slot: e.transpose(
                    slot, buf[:, g * 128:(g + 1) * 128], identf[:]))
                rec.wait("vector", tt)
                evs.append(V(lambda e, g=g, t=t, slot=slot: e.tensor_copy(
                    stream[:, g * 1024 + t * 128:g * 1024 + (t + 1) * 128],
                    slot)))
            slot_ev[t % 2] = evs[-1]

    def proj16(addr_fn, rhs_fn, evac_fn, stage_cols=2048, nout=16,
               pre_tok=None, lhs_g_of=None):
        bank_tok = [None, None]
        stage_tok = [None, None]
        ld_tok = [None, None]
        rec.wait("sync", rec.now("pe"))
        rec.wait("tensor", rec.now("dve"))
        for o in range(nout):
            sbuf_half = o % 2
            stg = wsa[:, sbuf_half * 2048: sbuf_half * 2048 + stage_cols]
            if stage_tok[sbuf_half] is not None:
                rec.wait("sync", stage_tok[sbuf_half])
            if pre_tok is not None:
                rec.wait("sync", pre_tok)
                pre_tok = None
            ld_tok[sbuf_half] = rec.dma("sync", stg, addr_fn(o),
                                        sem="sta" + str(sbuf_half))
            mmtok = None
            for half in range(2):
                bnk = (2 * o + half) % 2
                bank = pb[bnk]
                if bank_tok[bnk] is not None:
                    rec.wait("tensor", bank_tok[bnk])
                rec.wait("tensor", ld_tok[sbuf_half])
                for g in range(G):
                    gl = lhs_g_of(g) if lhs_g_of else g
                    mmtok = T(lambda e, stg=stg, gl=gl, g=g, half=half,
                              bank=bank: e.matmul(
                        bank[:, 0:512], stg[:, gl * 128:(gl + 1) * 128],
                        rhs_fn(g, half), start=(g == 0), stop=(g == G - 1)))
                bank_tok[bnk] = evac_fn(o, half, bank, mmtok)
            stage_tok[sbuf_half] = bank_tok[(2 * o + 1) % 2]
        return bank_tok

    kv_dma_toks = []

    KVSTATE = {}

    def mk_evac_kv(layer, row_base, with_ck, bias_ap=None):
        state = KVSTATE

        def evac(o, half, bank, mmtok):
            buf = kvstg[:, (o % 2) * 1024:(o % 2) * 1024 + 1024]
            rec.wait("vector", mmtok)
            if (o % 2) in state:
                rec.wait("vector", state[o % 2])
            if bias_ap is not None:
                tok = V(lambda e, buf=buf, half=half, o=o, bank=bank:
                        e.tensor_scalar(buf[:, half * 512:(half + 1) * 512],
                                        bank[:, 0:512], bias_ap[:, o:o + 1],
                                        None, ALU.add))
            else:
                tok = V(lambda e, buf=buf, half=half, bank=bank: e.tensor_copy(
                    buf[:, half * 512:(half + 1) * 512], bank[:, 0:512]))
            if half == 1:
                if with_ck:
                    vgap()
                    V(lambda e, buf=buf, o=o: e.reduce_max(
                        ck[:, o:o + 1], buf, AX.X, apply_absolute_value=True))
                    tok = vgap()
                rec.wait("sync", tok)
                dt = rec.dma("sync",
                             kv_in[layer][row_base + o * 128:
                                          row_base + (o + 1) * 128, :], buf,
                             sem="kvo" + str(o % 2))
                state[o % 2] = dt
            return tok
        return evac

    def mk_evac_sbuf(dst_fn, bias_ap=None):
        def evac(o, half, bank, mmtok):
            rec.wait("vector", mmtok)
            if bias_ap is not None:
                return V(lambda e, o=o, half=half, bank=bank: e.tensor_scalar(
                    dst_fn(o, half), bank[:, 0:512], bias_ap[:, o:o + 1],
                    None, ALU.add))
            return V(lambda e, o=o, half=half, bank=bank: e.tensor_copy(
                dst_fn(o, half), bank[:, 0:512]))
        return evac

    qsv = qs.ap().rearrange("p (blk tl gg) -> p blk tl gg", blk=2, tl=512, gg=G)

    def stream_rhs(g, half):
        return stream[:, g * 1024 + half * 512: g * 1024 + (half + 1) * 512]

    def ph_proj_L0(agA_tok):
        rec.wait("sync", agA_tok)
        kv_dma_toks.clear()
        proj16(lambda o: packA_stage(natA("Wk", o), 2048), stream_rhs,
               mk_evac_kv(0, 0, with_ck=True))
        proj16(lambda o: packA_stage(natA("Wv", o), 2048), stream_rhs,
               mk_evac_sbuf(lambda o, half: outT[:, o * 1024 + half * 512:
                                                 o * 1024 + (half + 1) * 512]))
        vptok = rec.now("dve")
        rec.wait("tensor", vptok)
        rec.wait("sync", rec.now("kvo0"))
        rec.wait("sync", rec.now("kvo1"))
        prev_dma = [None, None]
        slot_ev = [None, None]
        for t in range(8):
            half = t % 2
            vrow = wsb[:, half * 2048:(half + 1) * 2048]
            if prev_dma[half] is not None:
                rec.wait("vector", prev_dma[half])
            evs = []
            for g in range(G):
                slot = pbb[2 + (g % 2)][:, 0:128]
                if g >= 2:
                    rec.wait("tensor", evs[g - 2])
                elif slot_ev[g % 2] is not None:
                    rec.wait("tensor", slot_ev[g % 2])
                tt = T(lambda e, g=g, t=t, slot=slot: e.transpose(
                    slot, outT[:, g * 1024 + t * 128:g * 1024 + (t + 1) * 128],
                    identb[:]))
                rec.wait("vector", tt)
                evs.append(V(lambda e, g=g, slot=slot, vrow=vrow: e.tensor_copy(
                    vrow[:, g * 128:(g + 1) * 128], slot)))
            slot_ev = [evs[-2], evs[-1]]
            rec.wait("sync", evs[-1])
            vreg0 = kvi_flat[0][ds(2048 * 1024, 2097152)].rearrange(
                "(p hf h sb j) -> p hf h sb j", p=128, hf=2, h=16, sb=4)
            dt = rec.dma("sync", vreg0[:, t // 4, :, t % 4, :],
                         vrow.rearrange("p (h j) -> p h j", h=16),
                         sem="kvo" + str(half))
            prev_dma[half] = dt
        cktok = rec.now("dve")
        rec.wait("gpsimd", cktok)
        dt = rec.dma("gpsimd", kvi_flat[0][ds(KVBASE[0]["c"] * 1024, 2048)]
                     .rearrange("(g p) -> p g", p=128), ck,
                     allow_slow_non_contiguous=True)
        rec.wait("gpsimd", dt)
        def q_evac(o, half, bank, mmtok):
            rec.wait("vector", mmtok)
            return V(lambda e, o=o, half=half, bank=bank: e.tensor_scalar(
                qsv[:, half, :, o], bank[:, 0:512], bq_f[:, o:o + 1],
                None, ALU.add))
        proj16(lambda o: packA_stage(natA("Wq", o), 2048), stream_rhs, q_evac)
        rec.wait("gpsimd", rec.now("dve"))
        for sname in ("kvo0", "kvo1", "dkv", "dsp"):
            rec.wait("gpsimd", rec.now(sname))
        ag = rec.op("gpsimd", lambda e: e.collective_compute(
            "AllGather", ALU.bypass, replica_groups=[list(range(R))],
            ins=[kv_in[0][:, :]], outs=[kv_out[0][:, :]]), inc="ck0")
        return ag

    def ph_proj_L1(agB_tok):
        kv_dma_toks.clear()
        for dst, src in ((ksb, upk_i), (vsb, upv_i)):
            sf = src.rearrange("h d j -> (h d j)")
            ap = sf.rearrange("(h g p j) -> p h g j", h=NH, g=4, p=128)
            rec.dma("gpsimd", dst.ap().rearrange(
                "p (h g j) -> p h g j", h=NH, g=4), ap, sem="dup")
        up_tok = rec.now("dup")
        proj16(lambda o: packA_stage(natA("wq1", o), 2048), stream_rhs,
               mk_evac_kv(1, KVBASE[1]["q"], with_ck=False, bias_ap=wqb_f))
        proj16(lambda o: packA_stage(natA("wk1", o), 2048), stream_rhs,
               mk_evac_sbuf(lambda o, half: outT[:, o * 1024 + half * 512:
                                                 o * 1024 + (half + 1) * 512]))
        def k1_rhs(g, half):
            return outT[:, g * 1024 + half * 512: g * 1024 + (half + 1) * 512]
        proj16(lambda o: packA_stage(wlA(0, o), 2048), k1_rhs,
               mk_evac_sbuf(lambda o, half: qs[:, 8192 + o * 1024 + half * 512:
                                               8192 + o * 1024 + (half + 1) * 512]),
               nout=4)
        rec.wait("tensor", up_tok)
        rec.wait("tensor", rec.now("dve"))
        evk = mk_evac_kv(1, KVBASE[1]["k"], with_ck=True)
        bank_tok = [None, None]
        for h in range(NH):
            mmtok = None
            for half in range(2):
                bnk = (2 * h + half) % 2
                bank = pb[bnk]
                if bank_tok[bnk] is not None:
                    rec.wait("tensor", bank_tok[bnk])
                for g in range(4):
                    mmtok = T(lambda e, h=h, g=g, half=half, bank=bank:
                              e.matmul(bank[:, 0:512],
                                       ksb[:, h * 512 + g * 128:
                                           h * 512 + (g + 1) * 128],
                                       qs[:, 8192 + g * 1024 + half * 512:
                                          8192 + g * 1024 + (half + 1) * 512],
                                       start=(g == 0), stop=(g == 3)))
                bank_tok[bnk] = evk(h, half, bank, mmtok)
        proj16(lambda o: packA_stage(natA("wv1", o), 2048), stream_rhs,
               mk_evac_sbuf(lambda o, half: outT[:, o * 1024 + half * 512:
                                                 o * 1024 + (half + 1) * 512]))
        rec.wait("sync", rec.now("pe"))
        rec.wait("tensor", rec.now("dve"))
        vl_tok = [None, None]
        vlb_tok = [None, None]
        for o in range(4):
            sbuf_half = o % 2
            stg = wsa[:, sbuf_half * 2048:(sbuf_half + 1) * 2048]
            if vl_tok[sbuf_half] is not None:
                rec.wait("sync", vl_tok[sbuf_half])
            ld = rec.dma("sync", stg, packA_stage(wlA(1, o), 2048),
                         sem="sta" + str(sbuf_half))
            mmtok = None
            for half in range(2):
                bnk = (2 * o + half) % 2
                bank = pb[bnk]
                if vlb_tok[bnk] is not None:
                    rec.wait("tensor", vlb_tok[bnk])
                rec.wait("tensor", ld)
                for g in range(G):
                    mmtok = T(lambda e, stg=stg, g=g, half=half, o=o,
                              bank=bank: e.matmul(
                        bank[:, 0:512], stg[:, g * 128:(g + 1) * 128],
                        outT[:, g * 1024 + half * 512:
                             g * 1024 + (half + 1) * 512],
                        start=(g == 0), stop=(g == G - 1)))
                rec.wait("vector", mmtok)
                ev = V(lambda e, o=o, half=half, bank=bank: e.tensor_copy(
                    qs[:, 12288 + o * 1024 + half * 512:
                       12288 + o * 1024 + (half + 1) * 512], bank[:, 0:512]))
                vlb_tok[bnk] = ev
            rec.wait("tensor", rec.now("dve"))
            c_mm = None
            for g in range(G):
                c_mm = T(lambda e, stg=stg, g=g: e.matmul(
                    pb[5][:, 0:1], stg[:, g * 128:(g + 1) * 128],
                    wvb_b[:, g:g + 1], start=(g == 0), stop=(g == G - 1)))
            rec.wait("vector", c_mm)
            V(lambda e, o=o: e.tensor_scalar(
                v1c[:, o:o + 1], pb[5][:, 0:1], wvlb_f[:, o:o + 1],
                None, ALU.add))
            vgap()
            vl_tok[sbuf_half] = c_mm
        cst = V(lambda e: e.tensor_copy(v1cb, v1c))
        rec.wait("tensor", cst)
        rec.wait("tensor", rec.now("dve"))
        prev_ev = {}
        for h in range(NH):
            for t in range(8):
                bank = pb[(h * 8 + t) % 2]
                key = (h * 8 + t) % 2
                if key in prev_ev:
                    rec.wait("tensor", prev_ev[key])
                mmtok = None
                for g in range(4):
                    mmtok = T(lambda e, h=h, t=t, g=g, bank=bank: e.matmul(
                        bank[:, 0:128],
                        qs[:, 12288 + g * 1024 + t * 128:
                           12288 + g * 1024 + (t + 1) * 128],
                        vsb[:, h * 512 + g * 128: h * 512 + (g + 1) * 128],
                        start=(g == 0), stop=(g == 3)))
                rec.wait("vector", mmtok)
                prev_ev[key] = V(lambda e, h=h, t=t, bank=bank: e.tensor_copy(
                    outT[:, t * 2048 + h * 128: t * 2048 + (h + 1) * 128],
                    bank[:, 0:128]))
            rec.wait("tensor", rec.now("dve"))
            cm = None
            for g in range(4):
                cm = T(lambda e, h=h, g=g: e.matmul(
                    pb[5][:, 0:1],
                    vsb[:, h * 512 + g * 128: h * 512 + (g + 1) * 128],
                    v1cb[:, g:g + 1], start=(g == 0), stop=(g == 3)))
            rec.wait("vector", cm)
            V(lambda e, h=h: e.tensor_scalar(
                cc1[:, h:h + 1], pb[5][:, 0:1], upvb_f[:, h:h + 1],
                None, ALU.add))
            vgap()
        vtok = rec.now("dve")
        rec.wait("sync", vtok)
        vreg1 = kvi_flat[1][ds(KVBASE[1]["v"] * 1024, 2097152)].rearrange(
            "(p hf h sb j) -> p hf h sb j", p=128, hf=2, h=16, sb=4)
        for t in range(8):
            dt = rec.dma("sync", vreg1[:, t // 4, :, t % 4, :],
                         outT[:, t * 2048:(t + 1) * 2048].rearrange(
                             "p (h j) -> p h j", h=16), sem="dkv")
        cktok = rec.now("dve")
        rec.wait("gpsimd", cktok)
        dt = rec.dma("gpsimd", kvi_flat[1][ds(KVBASE[1]["c"] * 1024, 2048)]
                     .rearrange("(g p) -> p g", p=128), ck,
                     allow_slow_non_contiguous=True)
        rec.wait("gpsimd", dt)
        for sname in ("kvo0", "kvo1", "dkv", "dsp"):
            rec.wait("gpsimd", rec.now(sname))
        ag = rec.op("gpsimd", lambda e: e.collective_compute(
            "AllGather", ALU.bypass, replica_groups=[list(range(R))],
            ins=[kv_in[1][:, :]], outs=[kv_out[1][:, :]]), inc="ck1")
        return ag

    def ph_ccombine(layer, ag_tok):
        rec.wait("gpsimd", ag_tok)
        base = KVBASE[layer]["c"]
        rows = KVBASE[layer]["rows"]
        dts = []
        for r in range(R):
            dts.append(rec.dma(
                "gpsimd", absq[:, r * 16:(r + 1) * 16],
                kvo_flat[layer][ds((r * rows + base) * 1024, 2048)]
                .rearrange("(g p) -> p g", p=128), sem="dcm",
                allow_slow_non_contiguous=True))
        rec.wait("gpsimd", dts[-1])
        rec.wait("vector", dts[-1])
        V(lambda e: e.tensor_max(absq[:, 128:144], absq[:, 0:16], absq[:, 16:32]))
        V(lambda e: e.tensor_max(absq[:, 144:160], absq[:, 32:48], absq[:, 48:64]))
        V(lambda e: e.tensor_max(absq[:, 160:176], absq[:, 64:80], absq[:, 80:96]))
        V(lambda e: e.tensor_max(absq[:, 176:192], absq[:, 96:112], absq[:, 112:128]))
        vgap()
        V(lambda e: e.tensor_max(absq[:, 192:208], absq[:, 128:144], absq[:, 144:160]))
        V(lambda e: e.tensor_max(absq[:, 208:224], absq[:, 160:176], absq[:, 176:192]))
        vgap()
        cf = V(lambda e: e.tensor_max(cfeat, absq[:, 192:208], absq[:, 208:224]))
        cg = vgap()
        rec.wait("gpsimd", cg)
        cs_flat = cscr.rearrange("r c -> (r c)")
        st = rec.dma("gpsimd", cs_flat.rearrange("(g p) -> p g", p=128),
                     cfeat, sem="dcm", allow_slow_non_contiguous=True)
        rec.wait("gpsimd", st)
        cs1 = cs_flat.rearrange("(n b) -> n b", b=1)
        gather(cmy[:, 0:1], cs1, ("cmy", layer, 0))
        t2 = gather(cmy[:, 1:2], cs1, ("cmy", layer, 1))
        rec.wait("gpsimd", t2)
        return t2, rec.now("gp")

    def ph_attention(layer, ag_tok):
        dgp_tok, gp_tok = ph_ccombine(layer, ag_tok)
        rec.wait("tensor", dgp_tok)
        rec.wait("scalar", dgp_tok)
        rows = KVBASE[layer]["rows"]
        kb = KVBASE[layer]["k"]
        vb = KVBASE[layer]["v"]
        qb = KVBASE[layer]["q"]
        kvf = kvo_flat[layer]
        outv_toks = []
        last_unit_toks = []
        for hi in range(2):
            if last_unit_toks:
                for t in last_unit_toks:
                    rec.wait("gpsimd", t)
                last_unit_toks = []
            rec.wait("gpsimd", rec.now("act"))

            ktoks = []
            for m in range(16):
                ktoks.append(gather(ksb[:, m * 512:(m + 1) * 512],
                                    kvo512[layer], ("k", layer, hi, m)))
                if layer == 1:
                    ktoks.append(gather(qs[:, m * 512:(m + 1) * 512],
                                        kvo512[layer], ("q", layer, hi, m)))
                ktoks.append(gather(vsb[:, m * 512:(m + 1) * 512],
                                    kvo512[layer], ("v", layer, hi, m)))
            hload = rec.now("dld")
            rec.wait("tensor", hload)
            rec.wait("scalar", hload)

            sub_toks = {}
            exp_toks = {}
            pvl_toks = {}
            u_global = 0
            prev_final = None
            for qt in range(16):
                if layer == 0:
                    qtile = qs[:, hi * 8192 + qt * 512: hi * 8192 + (qt + 1) * 512]
                else:
                    qtile = qs[:, qt * 512:(qt + 1) * 512]
                ab = Aop(lambda e, qtile=qtile: e.activation(
                    absq[:, 0:512], qtile, AF.Abs))
                rec.wait("tensor", ab)
                mh = T(lambda e, hi=hi: e.matmul(
                    pb[5][0:1, 0:512], cmy[:, hi:hi + 1], absq[:, 0:512],
                    start=True, stop=True))
                rec.wait("vector", mh)
                V(lambda e: e.tensor_scalar(mrow[:, 0:512], pb[5][0:1, 0:512],
                                            SCALE, 1.0, ALU.mult, ALU.add))
                mg = vgap()
                rec.wait("tensor", mg)
                bc = T(lambda e: e.matmul(pb[4][:, 0:512], ones_row,
                                          mrow[:, 0:512], start=True, stop=True))
                rec.wait("vector", bc)
                mhcp = V(lambda e: e.tensor_copy(mhb, pb[4][:, 0:512]))
                vgap()
                nunits = 4 * (qt + 1)
                for uu in range(nunits):
                    u = u_global + uu
                    m, sb = uu // 4, uu % 4
                    diag = (m == qt)
                    par = u % 2
                    bank = pb[par]
                    if u - 2 in sub_toks:
                        rec.wait("tensor", sub_toks[u - 2])
                    qk = T(lambda e, uu=uu, bank=bank, qtile=qtile: e.matmul(
                        bank[:, 0:512], ksb[:, uu * 128:(uu + 1) * 128],
                        qtile, start=True, stop=True))
                    rec.wait("vector", qk)
                    if u - 2 in exp_toks:
                        rec.wait("vector", exp_toks[u - 2])
                    if uu == 0:
                        pass
                    st = V(lambda e, par=par, bank=bank: e.scalar_tensor_tensor(
                        ppre[:, par * 512:(par + 1) * 512], bank[:, 0:512],
                        SCALE, mhb, ALU.mult, ALU.subtract))
                    sub_toks[u] = st
                    rec.wait("scalar", st)
                    if u - 2 in pvl_toks:
                        rec.wait("scalar", pvl_toks[u - 2])
                    ex = Aop(lambda e, par=par: e.activation(
                        pbf[:, par * 512:(par + 1) * 512],
                        ppre[:, par * 512:(par + 1) * 512], AF.Exp))
                    exp_toks[u] = ex
                    ptok = ex
                    if diag:
                        rec.wait("gpsimd", ex)
                        mk = Gop(lambda e, par=par, sb=sb: e.affine_select(
                            out=pbf[:, par * 512:(par + 1) * 512],
                            in_=pbf[:, par * 512:(par + 1) * 512],
                            compare_op=ALU.is_ge, fill=0.0, base=-128 * sb,
                            pattern=[[1, 512]], channel_multiplier=-1))
                        ptok = mk
                        exp_toks[u] = mk
                    rec.wait("tensor", ptok)
                    if uu == 0 and prev_final is not None:
                        rec.wait("tensor", prev_final)
                    T(lambda e, uu=uu, par=par, nunits=nunits: e.matmul(
                        pb[2][:, 0:512], vsb[:, uu * 128:(uu + 1) * 128],
                        pbf[:, par * 512:(par + 1) * 512],
                        start=(uu == 0), stop=(uu == nunits - 1)))
                    pvl_toks[u] = T(lambda e, par=par, uu=uu, nunits=nunits:
                                    e.matmul(
                        pb[3][0:1, 0:512], ones_b,
                        pbf[:, par * 512:(par + 1) * 512],
                        start=(uu == 0), stop=(uu == nunits - 1)))
                u_global += nunits
                rec.wait("vector", pvl_toks[u_global - 1])
                V(lambda e: e.reciprocal(mrow[:, 0:512], pb[3][0:1, 0:512]))
                rg = vgap()
                rec.wait("tensor", rg)
                bc2 = T(lambda e: e.matmul(pb[4][:, 0:512], ones_row,
                                           mrow[:, 0:512], start=True,
                                           stop=True))
                rec.wait("vector", bc2)
                V(lambda e: e.tensor_copy(rbb, pb[4][:, 0:512]))
                vgap()
                fo = V(lambda e, hi=hi, qt=qt: e.tensor_tensor(
                    outT[:, hi * 8192 + qt * 512: hi * 8192 + (qt + 1) * 512],
                    pb[2][:, 0:512], rbb, ALU.mult))
                outv_toks.append(fo)
                prev_final = fo
            last_unit_toks = [rec.now("dve"), rec.now("pe")]
        return outv_toks

    def ph_wo(layer, outv_toks):
        for t in outv_toks[-2:]:
            rec.wait("sync", t)
        rec.wait("sync", rec.now("dve"))
        d1 = rec.dma("sync", ao_in[0:128, :], outT[:, 0:8192], sem="dkv")
        d2 = rec.dma("sync", ao_in[128:256, :], outT[:, 8192:16384], sem="dkv")
        rec.wait("gpsimd", rec.now("dkv"))
        ag = rec.op("gpsimd", lambda e: e.collective_compute(
            "AllGather", ALU.bypass, replica_groups=[list(range(R))],
            ins=[ao_in[:, :]], outs=[ao_out[:, :]]), inc="cao")
        rec.wait("gpsimd", ag)
        for jk in range(16):
            for cc in range(2):
                s2 = gather(outT[:, jk * 1024 + cc * 512:
                                 jk * 1024 + (cc + 1) * 512],
                            ao512, ("ao", layer, jk, cc))
        rec.wait("gpsimd", s2)
        rec.wait("tensor", s2)
        rec.wait("sync", s2)
        mat = "Wo" if layer == 0 else "wo1"
        bias_src = bo_f if layer == 0 else wob_f
        cbv = cb0 if layer == 0 else cb1
        if layer == 1:
            cv = V(lambda e: e.tensor_copy(cvb, cc1))
            rec.wait("tensor", cv)
        bank_tok = [None, None]
        stage_tok = [None, None]
        rec.wait("sync", rec.now("pe"))
        rec.wait("tensor", rec.now("dve"))
        for o in range(G):
            sh = o % 2
            stg = wsa[:, sh * 2048:(sh + 1) * 2048]
            if stage_tok[sh] is not None:
                rec.wait("sync", stage_tok[sh])
            ld = rec.dma("sync", stg, packA_stage(natA(mat, o), 2048),
                         sem="sta" + str(sh))
            rec.wait("tensor", ld)
            if o > 0:
                rec.wait("tensor", cb_ts)
            cm = None
            for g in range(G):
                cm = T(lambda e, stg=stg, g=g: e.matmul(
                    pb[5][:, 0:1], stg[:, g * 128:(g + 1) * 128],
                    cvb[:, g:g + 1], start=(g == 0), stop=(g == G - 1)))
            rec.wait("vector", cm)
            cb_ts = V(lambda e, o=o, cbv=cbv, bias_src=bias_src: e.tensor_scalar(
                cbv[:, o:o + 1], pb[5][:, 0:1], bias_src[:, o:o + 1],
                None, ALU.add))
            vgap()
            mmtok = None
            for half in range(2):
                bnk = (2 * o + half) % 2
                bank = pb[bnk]
                if bank_tok[bnk] is not None:
                    rec.wait("tensor", bank_tok[bnk])
                for jk in range(G):
                    gh = ghead(jk)
                    mmtok = T(lambda e, stg=stg, gh=gh, jk=jk, half=half,
                              bank=bank: e.matmul(
                        bank[:, 0:512], stg[:, gh * 128:(gh + 1) * 128],
                        outT[:, jk * 1024 + half * 512:
                             jk * 1024 + (half + 1) * 512],
                        start=(jk == 0), stop=(jk == G - 1)))
                rec.wait("vector", mmtok)
                bank_tok[bnk] = V(
                    lambda e, o=o, half=half, bank=bank, cbv=cbv:
                    e.scalar_tensor_tensor(
                        ybuf[:, o * 1024 + half * 512:
                             o * 1024 + (half + 1) * 512],
                        bank[:, 0:512], cbv[:, o:o + 1],
                        stream[:, o * 1024 + half * 512:
                               o * 1024 + (half + 1) * 512],
                        ALU.add, ALU.add))
            stage_tok[sh] = bank_tok[(2 * o + 1) % 2]

    def fview(g):
        if g < 8:
            return qs32[:, g * 1024:(g + 1) * 1024]
        if g < 12:
            return ksb32[:, (g - 8) * 1024:(g - 7) * 1024]
        return vsb32[:, (g - 12) * 1024:(g - 11) * 1024]

    def ph_ln(final=False):
        barrier()
        for half in range(2):
            sq_toks = []
            for g in range(G):
                if g >= 2:
                    rec.wait("scalar", sq_toks[g - 2])
                sq = Aop(lambda e, g=g, half=half: e.activation(
                    ysqv[:, (g % 2) * 512:(g % 2) * 512 + 512],
                    ybuf[:, g * 1024 + half * 512:
                         g * 1024 + (half + 1) * 512], AF.Square))
                rec.wait("tensor", sq)
                T(lambda e, g=g, half=half: e.matmul(
                    pb[6][0:1, 0:512], ones_b,
                    ybuf[:, g * 1024 + half * 512:
                         g * 1024 + (half + 1) * 512],
                    start=(g == 0), stop=(g == G - 1)))
                s2t = T(lambda e, g=g: e.matmul(
                    pb[7][0:1, 0:512], ones_b,
                    ysqv[:, (g % 2) * 512:(g % 2) * 512 + 512],
                    start=(g == 0), stop=(g == G - 1)))
                sq_toks.append(s2t)
            stt = sq_toks[-1]
            rec.wait("vector", stt)
            rec.wait("vector", rec.now("act"))
            r4 = pb[4][0:1, 0:512]
            r5 = pb[5][0:1, 0:512]
            Atmp = t1[0:1, 0:512]
            Btmp = t1[0:1, 512:1024]
            mu = muK[:, 0:512]
            V(lambda e: e.tensor_scalar(mu, pb[6][0:1, 0:512], 1.0 / D,
                                        None, ALU.mult))
            V(lambda e: e.tensor_scalar(Atmp, pb[7][0:1, 0:512], 1.0 / D,
                                        None, ALU.mult))
            vgap()
            V(lambda e, mu=mu: e.tensor_tensor(Btmp, mu, mu, ALU.mult))
            vgap()
            V(lambda e: e.tensor_tensor(r4, Atmp, Btmp, ALU.subtract))
            vgap()
            V(lambda e: e.tensor_scalar(Btmp, r4, EPS_LN, None, ALU.add))
            vr = vgap()
            rec.wait("scalar", vr)
            sd = Aop(lambda e: e.activation(r4, Btmp, AF.Sqrt))
            rec.wait("vector", sd)
            V(lambda e: e.reciprocal(Atmp, r4))
            vgap()
            V(lambda e: e.tensor_tensor(r4, Atmp, Atmp, ALU.mult))
            vgap()
            V(lambda e: e.tensor_tensor(r5, Btmp, r4, ALU.mult))
            vgap()
            V(lambda e: e.tensor_scalar(Btmp, r5, -0.5, 1.5, ALU.mult,
                                        ALU.add))
            vgap()
            V(lambda e, half=half: e.tensor_tensor(
                rstd_row[:, half * 512:(half + 1) * 512], Atmp, Btmp,
                ALU.mult))
            vgap()
            V(lambda e, half=half, mu=mu: e.tensor_tensor(
                m2_row[:, half * 512:(half + 1) * 512], mu,
                rstd_row[:, half * 512:(half + 1) * 512], ALU.mult))
            mfin = vgap()
            rec.wait("tensor", mfin)
            b1 = T(lambda e, half=half: e.matmul(
                pb[4][:, 0:512], ones_row,
                rstd_row[:, half * 512:(half + 1) * 512],
                start=True, stop=True))
            b2 = T(lambda e, half=half: e.matmul(
                pb[5][:, 0:512], ones_row,
                m2_row[:, half * 512:(half + 1) * 512],
                start=True, stop=True))
            rec.wait("vector", b2)
            zbuf = [t1[:, 0:512], t1[:, 512:1024]]
            for g in range(G):
                V(lambda e, g=g, half=half: e.tensor_tensor(
                    zbuf[g % 2], ybuf[:, g * 1024 + half * 512:
                                      g * 1024 + (half + 1) * 512],
                    pb[4][:, 0:512], ALU.mult))
                vgap()
                if final:
                    dst = fview(g)[:, half * 512:(half + 1) * 512]
                else:
                    dst = stream[:, g * 1024 + half * 512:
                                 g * 1024 + (half + 1) * 512]
                V(lambda e, g=g, dst=dst: e.tensor_tensor(
                    dst, zbuf[g % 2], pb[5][:, 0:512], ALU.subtract))
        barrier()

    def ph_ffn0(stop=None):
        hT = outT[0:64, 0:1024]
        Rt = outT32[:, 2048:2176]
        m1t = outT32[:, 2176:2184]
        m2t = outT32[:, 2184:2192]
        eqm = outT32[:, 2192:2320]
        msk = outT32[:, 2320:2448]
        expl = outT32[:, 2448:2576]
        gat = outT32[:, 2576:2704]
        den = outT32[:, 2704:2712]
        rcd = outT32[:, 2712:2720]
        cfb = outT[:, 5536:5536 + 2048]
        coT = outT[0:16, 7680:8704]
        xab = outT[:, 8704:8704 + 2048]
        rec.wait("vector", rec.now("pe"))
        onetok = V(lambda e: e.memset(outT[64:65, 0:1024], 1.0))
        for half in range(2):
            mm = None
            for g in range(G):
                mm = T(lambda e, g=g, half=half: e.matmul(
                    pb[4][0:64, 0:512], rwT[:, g * 64:(g + 1) * 64],
                    stream_rhs(g, half), start=(g == 0), stop=(g == G - 1)))
            rec.wait("vector", mm)
            V(lambda e, half=half: e.tensor_scalar(
                t1[0:64, 0:512], pb[4][0:64, 0:512], rb1_f[0:64, 0:1],
                None, ALU.add))
            sv = vgap()
            rec.wait("scalar", sv)
            hs = Aop(lambda e, half=half: e.activation(
                hT[:, half * 512:(half + 1) * 512], t1[0:64, 0:512], AF.Silu))
            rec.wait("tensor", hs)
        if stop == "router":
            return
        rec.wait("tensor", onetok)
        evs = []
        for tt in range(8):
            slot = pb[2 + (tt % 2)][:, 0:16]
            if tt >= 2:
                rec.wait("tensor", evs[tt - 2])
            tk = T(lambda e, tt=tt, slot=slot: e.matmul(
                slot, outT[0:65, tt * 128:(tt + 1) * 128],
                rwT[0:65, 1024:1040], start=True, stop=True))
            rec.wait("vector", tk)
            evs.append(V(lambda e, tt=tt, slot=slot: e.tensor_copy(
                Rt[:, tt * 16:(tt + 1) * 16], slot)))
        vgap()
        if stop == "ltr":
            return
        R3 = Rt.rearrange("p (t e) -> p t e", t=8)
        V(lambda e: e.reduce_max(m1t, R3, AX.X, negate=True))
        vgap()
        m1b = m1t.rearrange("p (t o) -> p t o", o=1).broadcast_to([128, 8, 16])
        V(lambda e: e.tensor_tensor(eqm.rearrange("p (t e) -> p t e", t=8),
                                    R3, m1b, ALU.add))
        vgap()
        V(lambda e: e.tensor_scalar(msk, eqm, 0.0, None, ALU.is_equal))
        vgap()
        V(lambda e: e.scalar_tensor_tensor(expl, msk, NEG, Rt, ALU.mult,
                                           ALU.add))
        vgap()
        V(lambda e: e.reduce_max(m2t, expl.rearrange("p (t e) -> p t e", t=8),
                                 AX.X))
        vgap()
        rec.wait("scalar", rec.now("dve"))
        ex = Aop(lambda e: e.activation(expl, eqm, AF.Exp))
        rec.wait("vector", ex)
        m2b = m2t.rearrange("p (t o) -> p t o", o=1).broadcast_to([128, 8, 16])
        V(lambda e: e.tensor_tensor(msk.rearrange("p (t e) -> p t e", t=8),
                                    R3, m2b, ALU.is_ge))
        vgap()
        V(lambda e: e.tensor_tensor(gat, expl, msk, ALU.mult))
        vgap()
        V(lambda e: e.reduce_sum(den, gat.rearrange("p (t e) -> p t e", t=8),
                                 AX.X))
        vgap()
        V(lambda e: e.reciprocal(rcd, den))
        vgap()
        if stop == "rmid":
            return
        denb = rcd.rearrange("p (t o) -> p t o", o=1).broadcast_to([128, 8, 16])
        vgap()
        cfo = V(lambda e: e.tensor_tensor(
            msk.rearrange("p (t e) -> p t e", t=8),
            gat.rearrange("p (t e) -> p t e", t=8), denb, ALU.mult))
        rec.wait("tensor", cfo)
        evs = []
        for tt in range(8):
            slot = pb[2 + (tt % 2)][0:16, 0:128]
            if tt >= 2:
                rec.wait("tensor", evs[tt - 2])
            tk = T(lambda e, tt=tt, slot=slot: e.transpose(
                slot, msk[:, tt * 16:(tt + 1) * 16], identf[:]))
            rec.wait("vector", tk)
            evs.append(V(lambda e, tt=tt, slot=slot: e.tensor_copy(
                coT[:, tt * 128:(tt + 1) * 128], slot)))
        ctok = vgap()
        if stop == "rmath":
            return
        rec.wait("tensor", ctok)
        cpt = None
        for erb in range(2):
            for half in range(2):
                if cpt is not None:
                    rec.wait("tensor", cpt)
                bm = T(lambda e, erb=erb, half=half: e.matmul(
                    pb[3][:, 0:512], ebf[0:16, erb * 128:(erb + 1) * 128],
                    coT[:, half * 512:(half + 1) * 512],
                    start=True, stop=True))
                rec.wait("vector", bm)
                cpt = V(lambda e, erb=erb, half=half: e.tensor_copy(
                    cfb[:, erb * 1024 + half * 512:
                        erb * 1024 + (half + 1) * 512], pb[3][:, 0:512]))
        vgap()
        if stop == "expand":
            return
        rec.wait("sync", rec.now("pe"))
        abtok = rec.dma("sync", wsb[:, 0:2048],
                        abT_flat[ds(0, BLK)].rearrange("(p c) -> p c", p=128),
                        sem="stb0")
        abtok2 = rec.dma("sync", wsb[:, 2048:4096],
                         abT_flat[ds(BLK, BLK)].rearrange("(p c) -> p c", p=128),
                         sem="stb1")
        cftok = rec.now("dve")
        xat = None
        rec.wait("tensor", abtok)
        rec.wait("tensor", abtok2)
        for erb in range(2):
            for half in range(2):
                if xat is not None:
                    rec.wait("tensor", xat)
                mm = None
                for g in range(G):
                    mm = T(lambda e, erb=erb, g=g, half=half: e.matmul(
                        pb[2][:, 0:512],
                        wsb[:, erb * 2048 + g * 128: erb * 2048 + (g + 1) * 128],
                        stream_rhs(g, half), start=(g == 0), stop=(g == G - 1)))
                rec.wait("vector", mm)
                xat = V(lambda e, erb=erb, half=half: e.tensor_tensor(
                    xab[:, erb * 1024 + half * 512:
                        erb * 1024 + (half + 1) * 512], pb[2][:, 0:512],
                    cfb[:, erb * 1024 + half * 512:
                        erb * 1024 + (half + 1) * 512], ALU.mult))
        wxtok = rec.now("dve")
        if stop == "xa":
            return
        rec.wait("sync", wxtok)
        btok = rec.dma("sync",
                       ksb[:, 0:4096].rearrange("p (o erb j) -> p o erb j",
                                                o=16, erb=2),
                       abT_flat[ds(2 * BLK, 2 * BLK)].rearrange(
                           "(o erb p j) -> p o erb j", p=128, o=16, erb=2),
                       sem="std")
        rec.wait("tensor", btok)
        rec.wait("tensor", wxtok)
        bank_tok = [None, None]
        stage_tok = [None, None]
        rec.wait("tensor", rec.now("dve"))
        for o in range(G):
            sh = o % 2
            stg = wsa[:, sh * 2048:(sh + 1) * 2048]
            if stage_tok[sh] is not None:
                rec.wait("sync", stage_tok[sh])
            ld = rec.dma("sync", stg, packA_stage(wtA(o), 2048),
                         sem="sta" + str(sh))
            rec.wait("tensor", ld)
            mmtok = None
            for half in range(2):
                bnk = (2 * o + half) % 2
                bank = pb[bnk]
                if bank_tok[bnk] is not None:
                    rec.wait("tensor", bank_tok[bnk])
                for g in range(G):
                    T(lambda e, stg=stg, g=g, half=half, bank=bank: e.matmul(
                        bank[:, 0:512], stg[:, g * 128:(g + 1) * 128],
                        stream_rhs(g, half), start=(g == 0), stop=False))
                for erb in range(2):
                    mmtok = T(lambda e, o=o, erb=erb, half=half, bank=bank:
                              e.matmul(
                        bank[:, 0:512],
                        ksb[:, o * 256 + erb * 128: o * 256 + (erb + 1) * 128],
                        xab[:, erb * 1024 + half * 512:
                            erb * 1024 + (half + 1) * 512],
                        start=False, stop=(erb == 1)))
                rec.wait("vector", mmtok)
                bank_tok[bnk] = V(
                    lambda e, o=o, half=half, bank=bank: e.scalar_tensor_tensor(
                        ybuf[:, o * 1024 + half * 512:
                             o * 1024 + (half + 1) * 512],
                        bank[:, 0:512], fb_f[:, o:o + 1],
                        stream[:, o * 1024 + half * 512:
                               o * 1024 + (half + 1) * 512],
                        ALU.add, ALU.add))
            stage_tok[sh] = bank_tok[(2 * o + 1) % 2]

    def hid_ap(hc, half):
        if hc < 16:
            return qs[:, hc * 1024 + half * 512: hc * 1024 + (half + 1) * 512]
        if hc < 24:
            c = (hc - 16) * 1024 + half * 512
            return ksb[:, c:c + 512]
        if hc < 32:
            c = (hc - 24) * 1024 + half * 512
            return vsb[:, c:c + 512]
        c = (hc - 32) * 1024 + half * 512
        return outT[:, c:c + 512]

    def ph_ffn1(agB_tok):
        rec.wait("sync", agB_tok)
        rec.wait("sync", rec.now("pe"))
        rec.wait("scalar", rec.now("pe"))
        rec.wait("tensor", rec.now("dve"))
        rec.wait("tensor", rec.now("act"))
        sig_tok = {}
        h_toks = {}
        w1tok = [None, None]
        w2tok = [None, None]
        for hc in range(NHC):
            sh = hc % 2
            stg1 = wsa[:, sh * 2048:(sh + 1) * 2048]
            stg2 = wsb[:, sh * 2048:(sh + 1) * 2048]
            if w1tok[sh] is not None:
                rec.wait("sync", w1tok[sh])
                rec.wait("scalar", w2tok[sh])
            l1 = rec.dma("sync", stg1, packB_stage(w12B(0, hc), 2048),
                         sem="sta" + str(sh))
            l2 = rec.dma("scalar", stg2, packB_stage(w12B(1, hc), 2048),
                         sem="stb" + str(sh))
            for half in range(2):
                prev = (hc, 0) if half == 1 else (hc - 1, 1)
                if prev in sig_tok:
                    rec.wait("tensor", sig_tok[prev])
                    rec.wait("tensor", h_toks[prev])
                    rec.wait("scalar", h_toks[prev])
                rec.wait("tensor", l1)
                mm1 = None
                for g in range(G):
                    mm1 = T(lambda e, stg1=stg1, g=g, half=half: e.matmul(
                        pb[0][:, 0:512], stg1[:, g * 128:(g + 1) * 128],
                        stream_rhs(g, half), start=(g == 0), stop=(g == G - 1)))
                rec.wait("tensor", l2)
                mm2 = None
                for g in range(G):
                    mm2 = T(lambda e, stg2=stg2, g=g, half=half: e.matmul(
                        pb[1][:, 0:512], stg2[:, g * 128:(g + 1) * 128],
                        stream_rhs(g, half), start=(g == 0), stop=(g == G - 1)))
                rec.wait("scalar", mm1)
                sg = Aop(lambda e, half=half: e.activation(
                    pbf[:, half * 512:(half + 1) * 512], pb[0][:, 0:512],
                    AF.Sigmoid))
                sig_tok[(hc, half)] = sg
                rec.wait("vector", mm2)
                rec.wait("vector", sg)
                h_toks[(hc, half)] = V(
                    lambda e, hc=hc, half=half: e.tensor_tensor(
                        hid_ap(hc, half), pbf[:, half * 512:(half + 1) * 512],
                        pb[1][:, 0:512], ALU.mult))
            w1tok[sh] = h_toks[(hc, 1)]
            w2tok[sh] = h_toks[(hc, 1)]
        hfin = rec.now("dve")
        rec.wait("sync", hfin)
        rec.wait("tensor", hfin)
        w3v = outT[:, 12288:16384]
        bank_tok = [None, None]
        st_tok = [None, None]
        for o in range(G):
            for qtr in range(4):
                sh = qtr % 2
                stg = w3v[:, sh * 2048: sh * 2048 + 1408]
                if st_tok[sh] is not None:
                    rec.wait("sync", st_tok[sh])
                ld = rec.dma("sync", stg,
                             packB_stage(w3B(o), NHC * 128).rearrange(
                                 "p (hc j) -> p hc j", hc=NHC)
                             [:, qtr * 11:(qtr + 1) * 11, :],
                             sem="stc" + str(sh))
                rec.wait("tensor", ld)
                mmt = None
                for hl in range(11):
                    hc = qtr * 11 + hl
                    for half in range(2):
                        bank = pb[2 + half]
                        if hc == 0 and bank_tok[half] is not None:
                            rec.wait("tensor", bank_tok[half])
                        mmt = T(lambda e, stg=stg, hl=hl, hc=hc, half=half,
                                bank=bank: e.matmul(
                            bank[:, 0:512], stg[:, hl * 128:(hl + 1) * 128],
                            hid_ap(hc, half), start=(hc == 0),
                            stop=(hc == NHC - 1)))
                st_tok[sh] = mmt
            rec.wait("vector", mmt)
            for half in range(2):
                bank = pb[2 + half]
                bank_tok[half] = V(
                    lambda e, o=o, half=half, bank=bank: e.tensor_tensor(
                        ybuf[:, o * 1024 + half * 512:
                             o * 1024 + (half + 1) * 512],
                        bank[:, 0:512],
                        stream[:, o * 1024 + half * 512:
                               o * 1024 + (half + 1) * 512], ALU.add))
        barrier()

    def ph_store():
        evs = []
        dts = []
        for t in range(8):
            buf = xstg if t % 2 == 0 else xstg2
            if t >= 2:
                rec.wait("vector", dts[t - 2])
                rec.wait("tensor", dts[t - 2])
            for g in range(G):
                slot = pb[g % 4][:, 0:128]
                idx = t * G + g
                if idx >= 4:
                    rec.wait("tensor", evs[idx - 4])
                tk = T(lambda e, g=g, t=t, slot=slot: e.transpose(
                    slot, fview(g)[:, t * 128:(t + 1) * 128], identf[:]))
                rec.wait("vector", tk)
                evs.append(V(lambda e, g=g, slot=slot, buf=buf: e.tensor_copy(
                    buf[:, g * 128:(g + 1) * 128], slot)))
            rec.wait("sync", evs[-1])
            dts.append(rec.dma("sync", y_out[t * 128:(t + 1) * 128, :], buf,
                               sem="dst" + str(t % 2)))
        tap_srcs = {"stream": (stream, G * 1024), "ybuf": (ybuf, G * 1024),
                    "qs": (qs, G * 1024), "ksb": (ksb, 8192),
                    "vsb": (vsb, 8192), "outT": (outT, G * 1024)}
        for tname in tap_outs:
            src, w = tap_srcs[tname]
            dts.append(rec.dma("gpsimd", tap_outs[tname][:, 0:w],
                               src[:, 0:w], sem="dtp"))
        for sname in ("dst0", "dst1", "dtp"):
            rec.wait("gpsimd", rec.now(sname))

    def mark(name):
        PHASE_MARKS.append((name, rec.cnt["pe"], rec.cnt["dve"],
                            rec.cnt["act"], rec.cnt["gp"]))

    def run_phases(upto=None):
        del PHASE_MARKS[:]
        mark("consts")
        ph_consts()
        mark("prep_A")
        agA = ph_prep_A()
        mark("prep_AB")
        ph_prep_AB()
        mark("prep_B")
        agB = ph_prep_B()
        mark("load_x")
        ph_load_x()
        if upto == "prep":
            barrier(); return
        mark("proj0")
        ag0 = ph_proj_L0(agA)
        if upto == "proj0":
            barrier(); return
        mark("attn0")
        ot = ph_attention(0, ag0)
        if upto == "attn0":
            barrier(); return
        mark("wo0")
        ph_wo(0, ot)
        if upto == "wo0":
            barrier(); return
        mark("ln0")
        ph_ln(final=False)
        if upto == "ln0":
            barrier(); return
        if upto is not None and upto.startswith("ffn0:"):
            ph_ffn0(stop=upto.split(":")[1])
            barrier(); return
        mark("ffn0")
        ph_ffn0()
        if upto == "ffn0":
            barrier(); return
        mark("ln1")
        ph_ln(final=False)
        if upto == "ln1":
            barrier(); return
        mark("proj1")
        ag1 = ph_proj_L1(agB)
        if upto == "proj1":
            barrier(); return
        mark("attn1")
        ot = ph_attention(1, ag1)
        if upto == "attn1":
            barrier(); return
        mark("wo1")
        ph_wo(1, ot)
        if upto == "wo1":
            barrier(); return
        mark("ln2")
        ph_ln(final=False)
        mark("ffn1")
        ph_ffn1(agB)
        if upto == "ffn1":
            barrier(); return
        mark("ln3")
        ph_ln(final=True)
        mark("store")
        ph_store()
        mark("end")

    run_phases(upto)
    if upto is not None:
        for sname in ("dst0", "dst1", "dtp"):
            rec.wait("gpsimd", rec.now(sname))

    with nc.Block() as block:
        for engname, deco in (("tensor", block.tensor), ("vector", block.vector),
                              ("scalar", block.scalar), ("gpsimd", block.gpsimd),
                              ("sync", block.sync)):
            def runner(e, ops=rec.ops[engname]):
                for f in ops:
                    f(e)
            deco(runner)

    ctx.close()
    return nc


_NC_CACHE = {}


def _get_nc(tapkey=()):
    if tapkey not in _NC_CACHE:
        _NC_CACHE[tapkey] = build_program(taps=tapkey)
    return _NC_CACHE[tapkey]


def make_in_maps(inputs):
    ins = {k: np.ascontiguousarray(np.asarray(v)) for k, v in inputs.items()}
    x = ins["L0_attn_x"].reshape(S, D)
    in_maps = []
    for c in range(R):
        m = {}
        m["x_own"] = np.concatenate(
            [x[c * 512:(c + 1) * 512], x[(15 - c) * 512:(16 - c) * 512]], 0)
        m["p_Wq"] = ins["L0_attn_Wq"][:, c * 256:(c + 1) * 256]
        m["p_Wk"] = ins["L0_attn_Wk"][:, c * 256:(c + 1) * 256]
        m["p_Wv"] = ins["L0_attn_Wv"][:, c * 256:(c + 1) * 256]
        m["p_Wo"] = ins["L0_attn_Wo"][:, c * 256:(c + 1) * 256]
        m["p_wq1"] = ins["L1_attn_wq_w"][:, c * 256:(c + 1) * 256]
        m["p_wk1"] = ins["L1_attn_wk_w"][:, c * 256:(c + 1) * 256]
        m["p_wv1"] = ins["L1_attn_wv_w"][:, c * 256:(c + 1) * 256]
        m["p_wo1"] = ins["L1_attn_wo_w"][:, c * 256:(c + 1) * 256]
        if c < 4:
            m["p_wl"] = ins["L1_attn_wkl_w"][:, c * 128:(c + 1) * 128]
        else:
            m["p_wl"] = ins["L1_attn_wvl_w"][:, (c - 4) * 128:(c - 3) * 128]
        m["p_WT"] = ins["L0_ffn_W"][c * 256:(c + 1) * 256, :]
        st, nreal = w12_rank_blocks(c)
        w1s = ins["L1_ffn_w1"][st * 128:(st + nreal) * 128, :]
        w2s = ins["L1_ffn_w2"][st * 128:(st + nreal) * 128, :]
        if nreal < W1B:
            pad = np.zeros(((W1B - nreal) * 128, D), np.float32)
            w1s = np.concatenate([w1s, pad], 0)
            w2s = np.concatenate([w2s, pad], 0)
        m["p_w1"] = w1s
        m["p_w2"] = w2s
        m["p_w3"] = ins["L1_ffn_w3"][c * 256:(c + 1) * 256, :]
        m["bq"] = ins["L0_attn_bq"]
        m["bo"] = ins["L0_attn_bo"]
        m["bv"] = ins["L0_attn_bv"]
        m["f_bias"] = ins["L0_ffn_bias"]
        m["rw1"] = ins["L0_ffn_rw1"]
        m["rb1"] = ins["L0_ffn_rb1"]
        m["rw2"] = ins["L0_ffn_rw2"]
        m["rb2"] = np.ascontiguousarray(ins["L0_ffn_rb2"])
        m["gates"] = np.ascontiguousarray(ins["L0_ffn_gates"])
        m["A_in"] = ins["L0_ffn_A"].reshape(256, D)
        m["B_in"] = ins["L0_ffn_B"]
        m["wq_b"] = ins["L1_attn_wq_b"]
        m["wv_b"] = ins["L1_attn_wv_b"]
        m["wvl_b"] = ins["L1_attn_wvl_b"]
        m["upk_w"] = ins["L1_attn_upk_w"]
        m["upv_w"] = ins["L1_attn_upv_w"]
        m["upv_b"] = ins["L1_attn_upv_b"]
        m["wo_b"] = ins["L1_attn_wo_b"]
        gx = np.zeros((128, GIDX_N), np.int32)
        P = np.arange(128)
        rows_l = [KV0_ROWS, KV1_ROWS]
        for L in (0, 1):
            rws = rows_l[L]
            for hi in (0, 1):
                h = c if hi == 0 else 15 - c
                for mm2 in range(16):
                    rr, half = rank_of_chunk(mm2)
                    kb = KVBASE[L]["k"]
                    vb = KVBASE[L]["v"]
                    gx[:, GIDX_COLS[("k", L, hi, mm2)]] = \
                        (rr * rws + kb + h * 128 + P) * 2 + half
                    gx[:, GIDX_COLS[("v", L, hi, mm2)]] = \
                        (rr * rws + vb) * 2 + P * 32 + half * 16 + h
                    if L == 1:
                        qb = KVBASE[L]["q"]
                        gx[:, GIDX_COLS[("q", L, hi, mm2)]] = \
                            (rr * rws + qb + h * 128 + P) * 2 + half
                gx[:, GIDX_COLS[("cmy", L, hi)]] = h * 128 + P
            for jk in range(16):
                for cc2 in (0, 1):
                    chunk = c if cc2 == 0 else 15 - c
                    row = (jk // 2) * 256 + (jk % 2) * 128 + P
                    gx[:, GIDX_COLS[("ao", L, jk, cc2)]] = row * 16 + chunk
        m["gidx"] = gx
        in_maps.append({k: np.ascontiguousarray(v) for k, v in m.items()})
    return in_maps


def kernel(**inputs):
    nc = _get_nc(())
    in_maps = make_in_maps(inputs)
    res = run_bass_kernel_spmd(nc, in_maps, core_ids=list(range(R)))
    out = np.zeros((S, D), np.float32)
    for c in range(R):
        y = np.asarray(res.results[c]["y_out"])
        out[c * 512:(c + 1) * 512] = y[0:512]
        out[(15 - c) * 512:(16 - c) * 512] = y[512:1024]
    return out.reshape(1, S, D)


# version 03
# speedup vs baseline: 1.6623x; 1.5497x over previous
import numpy as np
from collections import defaultdict
from contextlib import ExitStack

import concourse.bass as bass
import concourse.mybir as mybir
from concourse.bass_utils import run_bass_kernel_spmd

F32 = mybir.dt.float32
BF16 = mybir.dt.bfloat16
AF = mybir.ActivationFunctionType
ALU = mybir.AluOpType
AX = mybir.AxisListType
ds = bass.ds

R = 8
D = 2048
S = 8192
NTOK = 1024
NH = 16
DL = 512
HID = 5632
NHC = 44
W1B = 6
G = 16
SCALE = float(1.0 / np.float32(np.sqrt(np.float32(128.0))))
NEG = -1.0e30
EPS_LN = 1.0e-5

NAT_MATS = ["Wq", "Wk", "Wv", "Wo", "wq1", "wk1", "wv1", "wo1"]
BLK = 128 * G * 128
W3BLK = 128 * NHC * 128

_packA_off, _oa = {}, 0
for _m in NAT_MATS:
    _packA_off[_m] = _oa
    _oa += 2 * BLK
_packA_off["WL"] = _oa; _oa += BLK
_packA_off["WT"] = _oa; _oa += 2 * BLK
PACKA_ELEMS = _oa
PACKA_ROWS = PACKA_ELEMS // 1024

_packB_off = {"W1Q": 0, "W2Q": W1B * BLK, "W3Q": 2 * W1B * BLK}
PACKB_ELEMS = 2 * W1B * BLK + 2 * W3BLK
PACKB_ROWS = PACKB_ELEMS // 1024


def natA(mat, o):
    r, lo = o // 2, o % 2
    return r * PACKA_ELEMS + _packA_off[mat] + lo * BLK


def wlA(which, o):
    return (o + 4 * which) * PACKA_ELEMS + _packA_off["WL"]


def wtA(o):
    r, lo = o // 2, o % 2
    return r * PACKA_ELEMS + _packA_off["WT"] + lo * BLK


def w12B(which, hc):
    if hc < 24:
        r, hb = hc // 6, hc % 6
    else:
        r, hb = 4 + (hc - 24) // 5, (hc - 24) % 5
    return r * PACKB_ELEMS + _packB_off["W1Q" if which == 0 else "W2Q"] + hb * BLK


def w12_rank_blocks(r):
    return (6 * r, 6) if r < 4 else (24 + 5 * (r - 4), 5)


def w3B(o):
    r, lo = o // 2, o % 2
    return r * PACKB_ELEMS + _packB_off["W3Q"] + lo * W3BLK


def rank_of_chunk(m):
    return (m, 0) if m < 8 else (15 - m, 1)


GIDX_COLS = {}
_gc = 0


def _gcol(key):
    global _gc
    if key not in GIDX_COLS:
        GIDX_COLS[key] = _gc
        _gc += 1
    return GIDX_COLS[key]


for _L in (0, 1):
    for _hi in (0, 1):
        for _m in range(16):
            _gcol(("k", _L, _hi, _m))
            _gcol(("v", _L, _hi, _m))
            if _L == 1:
                _gcol(("q", _L, _hi, _m))
        _gcol(("cmy", _L, _hi))
    for _jk in range(16):
        for _cc in (0, 1):
            _gcol(("ao", _L, _jk, _cc))
GIDX_N = _gc

KV0_ROWS = 2048 + 2048 + 2
KV1_ROWS = 2048 + 2048 + 2048 + 2
KVBASE = [
    {"q": None, "k": 0, "v": 2048, "c": 4096, "rows": KV0_ROWS},
    {"q": 0, "k": 2048, "v": 4096, "c": 6144, "rows": KV1_ROWS},
]


def ghead(jk):
    j, k = jk // 2, jk % 2
    return j if k == 0 else 15 - j


class Rec:
    ENGS = ("tensor", "vector", "scalar", "gpsimd", "sync")

    def __init__(self, sems):
        self.sems = sems
        self.ops = {e: [] for e in self.ENGS}
        self.cnt = defaultdict(int)

    def op(self, eng, fn, inc=None, n=1):
        if inc is None:
            self.ops[eng].append(fn)
            return None
        self.cnt[inc] += n
        tok = (inc, self.cnt[inc])
        sem = self.sems[inc]
        self.ops[eng].append(lambda e, fn=fn, s=sem, k=n: fn(e).then_inc(s, k))
        return tok

    def wait(self, eng, tok):
        if tok is None:
            return
        s, v = self.sems[tok[0]], tok[1]
        self.ops[eng].append(lambda e, s=s, v=v: e.wait_ge(s, v))

    def now(self, sem):
        return (sem, self.cnt[sem])

    def dma(self, eng, out, in_, sem=None, **kw):
        if sem is None:
            sem = {"sync": "dsp", "scalar": "dact", "gpsimd": "dgp"}[eng]
        return self.op(eng,
                       lambda e, o=out, i=in_, kw=kw: e.dma_start(
                           out=o, in_=i, **kw),
                       inc=sem, n=16)


PHASE_MARKS = []


def build_program(taps=(), upto=None):
    nc = bass.Bass("TRN2", target_bir_lowering=False, debug=False)

    def din(name, shape, dtype=F32):
        return nc.dram_tensor(name, list(shape), dtype, kind="ExternalInput").ap()

    x_own = din("x_own", (NTOK, D))
    p_nat = {m: din("p_" + m, (D, 256)) for m in NAT_MATS}
    p_wl = din("p_wl", (D, 128))
    p_WT = din("p_WT", (256, D))
    p_w1 = din("p_w1", (W1B * 128, D))
    p_w2 = din("p_w2", (W1B * 128, D))
    p_w3 = din("p_w3", (256, HID))
    bq_i = din("bq", (D,))
    bo_i = din("bo", (D,))
    bv_i = din("bv", (D,))
    fb_i = din("f_bias", (D,))
    rw1_i = din("rw1", (64, D))
    rb1_i = din("rb1", (64,))
    rw2_i = din("rw2", (32, 64))
    rb2_i = din("rb2", (32,))
    gates_i = din("gates", (16,))
    A_i = din("A_in", (256, D))
    B_i = din("B_in", (NH, D, 16))
    wqb_i = din("wq_b", (D,))
    wvb_i = din("wv_b", (D,))
    wvlb_i = din("wvl_b", (DL,))
    upk_i = din("upk_w", (NH, DL, 128))
    upv_i = din("upv_w", (NH, DL, 128))
    upvb_i = din("upv_b", (NH, 128))
    wob_i = din("wo_b", (D,))
    gidx_i = din("gidx", (128, GIDX_N), mybir.dt.int32)

    y_out = nc.dram_tensor("y_out", [NTOK, D], F32, kind="ExternalOutput").ap()
    tap_outs = {}
    for tname in taps:
        tap_outs[tname] = nc.dram_tensor(
            "tap_" + tname, [128, G * 1024], F32, kind="ExternalOutput").ap()

    packA_in = nc.dram_tensor("packA_in", [PACKA_ROWS, 1024], BF16).ap()
    packA_out = nc.dram_tensor("packA_out", [R * PACKA_ROWS, 1024], BF16,
                               addr_space="Shared").ap()
    packB_in = nc.dram_tensor("packB_in", [PACKB_ROWS, 1024], BF16).ap()
    packB_out = nc.dram_tensor("packB_out", [R * PACKB_ROWS, 1024], BF16,
                               addr_space="Shared").ap()
    gs_in = nc.dram_tensor("gs_in", [1, 16], F32).ap()
    gs_out = nc.dram_tensor("gs_out", [1, 16], F32).ap()
    kv_in = [nc.dram_tensor("kv0_in", [KV0_ROWS, 1024], BF16).ap(),
             nc.dram_tensor("kv1_in", [KV1_ROWS, 1024], BF16).ap()]
    kv_out = [nc.dram_tensor("kv0_out", [R * KV0_ROWS, 1024], BF16,
                             addr_space="Shared").ap(),
              nc.dram_tensor("kv1_out", [R * KV1_ROWS, 1024], BF16,
                             addr_space="Shared").ap()]
    ao_in = nc.dram_tensor("ao_in", [256, S], BF16).ap()
    ao_out = nc.dram_tensor("ao_out", [D, S], BF16,
                            addr_space="Shared").ap()
    abT = nc.dram_tensor("abT", [1536, 1024], BF16).ap()
    cscr = nc.dram_tensor("cscr", [2, 1024], BF16).ap()

    ctx = ExitStack()
    e = ctx.enter_context

    identf = e(nc.sbuf_tensor("identf", [128, 128], F32))
    identb = e(nc.sbuf_tensor("identb", [128, 128], BF16))
    stream = e(nc.sbuf_tensor("stream", [128, G * 1024], BF16))
    ybuf = e(nc.sbuf_tensor("ybuf", [128, G * 1024], BF16))
    qs = e(nc.sbuf_tensor("qs", [128, G * 1024], BF16))
    ksb = e(nc.sbuf_tensor("ksb", [128, 8192], BF16))
    vsb = e(nc.sbuf_tensor("vsb", [128, 8192], BF16))
    outT = e(nc.sbuf_tensor("outT", [128, G * 1024], BF16))
    ppre = e(nc.sbuf_tensor("ppre", [128, 1024], F32))
    pbf = e(nc.sbuf_tensor("pbf", [128, 1024], BF16))
    ascr = e(nc.sbuf_tensor("ascr", [128, 2560], F32))
    t1 = e(nc.sbuf_tensor("t1", [128, 1024], F32))
    wsa = e(nc.sbuf_tensor("wsa", [128, 4096], BF16))
    wsb = e(nc.sbuf_tensor("wsb", [128, 4096], BF16))
    absq = e(nc.sbuf_tensor("absq", [128, 512], BF16))
    smf = e(nc.sbuf_tensor("smf", [128, 512], F32))
    smb = e(nc.sbuf_tensor("smb", [128, 128], BF16))
    ebf = e(nc.sbuf_tensor("ebf", [128, 256], BF16))
    rwT = e(nc.sbuf_tensor("rwT", [128, 1088], BF16))
    gix = e(nc.sbuf_tensor("gix", [128, GIDX_N], mybir.dt.int32))

    xstg = wsa.ap().bitcast(F32)
    xstg2 = wsb.ap().bitcast(F32)
    qs32 = qs.ap().bitcast(F32)
    ksb32 = ksb.ap().bitcast(F32)
    vsb32 = vsb.ap().bitcast(F32)
    ysqv = pbf.ap()
    kvstg = ppre.ap().bitcast(BF16)
    outT32 = outT.ap().bitcast(F32)
    t1b = t1.ap().bitcast(BF16)

    ck = smf[:, 0:16]
    bq_f = smf[:, 16:32]
    wqb_f = smf[:, 32:48]
    cb0 = smf[:, 48:64]
    cb1 = smf[:, 64:80]
    fb_f = smf[:, 80:96]
    cc1 = smf[:, 96:112]
    v1c = smf[:, 112:116]
    wvlb_f = smf[:, 116:120]
    upvb_f = smf[:, 120:136]
    bv_f = smf[:, 136:152]
    rgv = smf[:, 152:155]
    rgn = smf[:, 155:158]
    rb1_f = smf[:, 158:159]
    onescol_f = smf[:, 160:161]
    gacc = smf[:, 161:167]
    gtmp = smf[:, 167:168]
    gapd = smf[0:1, 168:176]
    gapsrc = smf[0:1, 176:184]
    ones_row = smf[0:1, 184:312]
    grow = smf[0:1, 312:328]
    g3a = smf[0:1, 328:331]
    g3b = smf[0:1, 331:334]
    g3c = smf[0:1, 334:337]
    bo_f = smf[:, 352:368]
    wob_f = smf[:, 368:384]
    wvb_f = smf[:, 384:400]
    rb2_row = smf[0:1, 400:416]
    gates_row = smf[0:1, 416:432]
    rb2g_row = smf[0:1, 432:448]

    cfeat = smb[:, 0:16]
    cmy = smb[:, 16:18]
    ones_b = smb[:, 18:19]
    cvb = smb[:, 20:36]
    wvb_b = smb[:, 36:52]
    v1cb = smb[:, 52:56]

    mhb = ascr[:, 0:512]
    rbb = ascr[:, 512:1024]
    mrow = ascr[0:1, 2048:2560]
    rstd_row = ascr[0:1, 0:1024]
    m2_row = ascr[0:1, 1024:2048]
    muK = ascr[0:1, 2048:2560]

    pband = [e(nc.psum_tensor(f"pb{i}", [128, 512], F32)) for i in range(8)]
    pb = [p.ap() for p in pband]
    pbb = [p.ap().bitcast(BF16) for p in pband]

    sems = {n: e(nc.semaphore(n))
            for n in ("pe", "dve", "act", "gp", "dsp", "dact", "dgp", "cc",
                      "dcn", "sta0", "sta1", "stb0", "stb1", "stc0", "stc1",
                      "std", "kvo0", "kvo1", "dup", "dld",
                      "dao", "dcm", "dkv", "dst0", "dst1", "dtp",
                      "cA", "cR", "cB", "ck0", "ck1", "cao")
            + tuple(f"dl{m}" for m in range(16))}
    rec = Rec(sems)

    def T(fn):
        return rec.op("tensor", fn, inc="pe")

    def V(fn):
        return rec.op("vector", fn, inc="dve")

    def Aop(fn):
        return rec.op("scalar", fn, inc="act")

    def Gop(fn):
        return rec.op("gpsimd", fn, inc="gp")

    def vgap():
        t = rec.now("dve")
        rec.wait("vector", t)
        return t

    def gwait():
        t = rec.now("gp")
        rec.wait("gpsimd", t)
        return t

    def barrier():
        toks = [rec.now(s) for s in sems]
        for eng in Rec.ENGS:
            for t in toks:
                rec.wait(eng, t)


    pA_flat = packA_in.rearrange("r c -> (r c)")
    pB_flat = packB_in.rearrange("r c -> (r c)")
    pAo_flat = packA_out.rearrange("r c -> (r c)")
    pBo_flat = packB_out.rearrange("r c -> (r c)")
    abT_flat = abT.rearrange("r c -> (r c)")
    kvi_flat = [k.rearrange("r c -> (r c)") for k in kv_in]
    kvo_flat = [k.rearrange("r c -> (r c)") for k in kv_out]
    kvo512 = [k.rearrange("r c -> (r c)").rearrange("(n b) -> n b", b=512)
              for k in kv_out]
    ao512 = ao_out.rearrange("r c -> (r c)").rearrange("(n b) -> n b", b=512)

    GATHER_SEM = {"k": "dld", "v": "dld", "q": "dld", "ao": "dao",
                  "cmy": "dcm"}

    def gather(out_ap, src512, key, sem=None):
        col = GIDX_COLS[key]
        return rec.op("gpsimd", lambda e, o=out_ap, s=src512, col=col:
                      e.indirect_dma_start(
                          out=o, out_offset=None, in_=s,
                          in_offset=bass.IndirectOffsetOnAxis(
                              gix[:, col:col + 1], 0)),
                      inc=sem or GATHER_SEM[key[0]], n=16)

    def packA_stage(elem_off, n):
        return pAo_flat[ds(elem_off, 128 * n)].rearrange("(p c) -> p c", p=128)

    def packB_stage(elem_off, n):
        return pBo_flat[ds(elem_off, 128 * n)].rearrange("(p c) -> p c", p=128)

    def ph_consts():
        def Gs(fn):
            t = Gop(fn)
            rec.wait("gpsimd", t)
            return t
        Gs(lambda e: e.memset(identf[:], 0.0))
        Gs(lambda e: e.affine_select(out=identf[:], in_=identf[:],
                                      compare_op=ALU.not_equal, fill=1.0, base=0,
                                      pattern=[[-1, 128]], channel_multiplier=1))
        Gs(lambda e: e.memset(identb[:], 0.0))
        Gs(lambda e: e.affine_select(out=identb[:], in_=identb[:],
                                      compare_op=ALU.not_equal, fill=1.0, base=0,
                                      pattern=[[-1, 128]], channel_multiplier=1))
        Gs(lambda e: e.memset(ones_b, 1.0))
        Gs(lambda e: e.memset(onescol_f, 1.0))
        Gs(lambda e: e.memset(ones_row, 1.0))
        Gs(lambda e: e.memset(gapsrc, 0.0))
        Gs(lambda e: e.memset(ebf[:], 0.0))
        for blk in range(2):
            sl = ebf[0:16, blk * 128:(blk + 1) * 128]
            Gs(lambda e, sl=sl: e.memset(sl, 1.0))
            Gs(lambda e, sl=sl, blk=blk: e.affine_select(
                out=sl, in_=sl, compare_op=ALU.is_ge, fill=0.0,
                base=128 * blk, pattern=[[1, 128]], channel_multiplier=-16))
            Gs(lambda e, sl=sl, blk=blk: e.affine_select(
                out=sl, in_=sl, compare_op=ALU.is_ge, fill=0.0,
                base=15 - 128 * blk, pattern=[[-1, 128]],
                channel_multiplier=16))
        for dst, src, p in ((bq_f, bq_i, 128), (wqb_f, wqb_i, 128),
                            (bv_f, bv_i, 128), (fb_f, fb_i, 128),
                            (bo_f, bo_i, 128), (wob_f, wob_i, 128),
                            (wvb_f, wvb_i, 128),
                            (upvb_f, upvb_i.rearrange("h d -> (h d)"), 128),
                            (wvlb_f, wvlb_i, 128),
                            (rb1_f, rb1_i, 64)):
            gg = src.shape[0] // p
            rec.dma("gpsimd", dst[0:p, 0:gg],
                    src.rearrange("(g p) -> p g", p=p), sem="dcn",
                    allow_slow_non_contiguous=True)
        rec.dma("gpsimd", rb2_row, rb2_i[ds(0, 16)].rearrange(
            "(o c) -> o c", o=1), sem="dcn")
        rec.dma("gpsimd", gates_row, gates_i.rearrange("(o c) -> o c", o=1),
                sem="dcn")
        rec.dma("gpsimd", gix[:], gidx_i[:, :], sem="dcn")
        gd = rec.now("dcn")
        rec.wait("vector", gd)
        V(lambda e: e.tensor_copy(wvb_b, wvb_f))
        V(lambda e: e.tensor_copy(cvb, bv_f))
        V(lambda e: e.tensor_tensor(rb2g_row, rb2_row, gates_row, ALU.add))
        vgap()
        V(lambda e: e.tensor_copy(rwT[64:65, 1024:1040], rb2g_row))
        barrier()

    def nat_prep_one(src_ap, dst_elems, nblocks):
        last = None
        for lo in range(nblocks):
            ld = rec.dma("sync",
                         xstg2[:, 0:2048].rearrange("p (g c) -> p g c", g=G),
                         src_ap[:, lo * 128:(lo + 1) * 128].rearrange(
                             "(g p) c -> p g c", p=128), sem="stb0")
            rec.wait("vector", ld)
            if last is not None:
                rec.wait("vector", last)
            cv = V(lambda e: e.tensor_copy(wsa[:, 0:2048], xstg2[:, 0:2048]))
            rec.wait("gpsimd", cv)
            dst = pA_flat[ds(dst_elems + lo * BLK, BLK)].rearrange(
                "(p c) -> p c", p=128)
            last = rec.dma("gpsimd", dst, wsa[:, 0:2048])
            rec.wait("gpsimd", last)
            rec.wait("sync", last)
        return last

    def ph_prep_A():
        for m in NAT_MATS:
            tok = nat_prep_one(p_nat[m], _packA_off[m], 2)
            rec.wait("sync", tok)
            rec.wait("vector", tok)
        tok = nat_prep_one(p_wl, _packA_off["WL"], 1)
        rec.wait("sync", tok)
        rec.wait("vector", tok)

        for lo in range(2):
            ld = rec.dma("sync", xstg2[:, 0:2048],
                         p_WT[lo * 128:(lo + 1) * 128, :], sem="stb0")
            rec.wait("vector", ld)
            cv = V(lambda e: e.tensor_copy(kvstg[:, 0:2048], xstg2[:, 0:2048]))
            rec.wait("tensor", cv)
            evs = []
            for g in range(G):
                slot = pbb[g % 4][:, 0:128]
                if g >= 4:
                    rec.wait("tensor", evs[g - 4])
                tt = T(lambda e, g=g, slot=slot: e.transpose(
                    slot, kvstg[:, g * 128:(g + 1) * 128], identb[:]))
                rec.wait("vector", tt)
                evs.append(V(lambda e, g=g, slot=slot: e.tensor_copy(
                    wsa[:, 2048 + g * 128:2048 + (g + 1) * 128], slot)))
            rec.wait("gpsimd", evs[-1])
            dst = pA_flat[ds(_packA_off["WT"] + lo * BLK, BLK)].rearrange(
                "(p c) -> p c", p=128)
            dt = rec.dma("gpsimd", dst, wsa[:, 2048:4096])
            rec.wait("gpsimd", dt)
            rec.wait("sync", dt)
            rec.wait("vector", dt)

        gtok = rec.now("dgp")
        rec.wait("gpsimd", gtok)
        agA = rec.op("gpsimd", lambda e: e.collective_compute(
            "AllGather", ALU.bypass, replica_groups=[list(range(R))],
            ins=[packA_in[:, :]], outs=[packA_out[:, :]]), inc="cA")
        return agA

    def ph_prep_AB():
        for erb in range(2):
            ld = rec.dma("sync", xstg2[:, 0:2048],
                         A_i[erb * 128:(erb + 1) * 128, :], sem="stb0")
            rec.wait("vector", ld)
            cv = V(lambda e: e.tensor_copy(kvstg[:, 0:2048], xstg2[:, 0:2048]))
            rec.wait("tensor", cv)
            evs = []
            for g in range(G):
                slot = pbb[g % 4][:, 0:128]
                if g >= 4:
                    rec.wait("tensor", evs[g - 4])
                tt = T(lambda e, g=g, slot=slot: e.transpose(
                    slot, kvstg[:, g * 128:(g + 1) * 128], identb[:]))
                rec.wait("vector", tt)
                evs.append(V(lambda e, g=g, slot=slot: e.tensor_copy(
                    wsa[:, 2048 + g * 128:2048 + (g + 1) * 128], slot)))
            rec.wait("gpsimd", evs[-1])
            dst = abT_flat[ds(erb * BLK, BLK)].rearrange("(p c) -> p c", p=128)
            dt = rec.dma("gpsimd", dst, wsa[:, 2048:4096])
            rec.wait("gpsimd", dt)
            rec.wait("sync", dt)
            rec.wait("vector", dt)

        Bf = B_i.rearrange("e o r -> (e o r)")
        for erb in range(2):
            srcv = Bf[ds(erb * 8 * D * 16, 8 * D * 16)].rearrange(
                "(e ob p r) -> p ob e r", e=8, ob=16, p=128)
            for ob in range(G):
                ld = rec.dma("sync", xstg2[:, ob * 128:(ob + 1) * 128]
                             .rearrange("p (e r) -> p e r", e=8),
                             srcv[:, ob, :, :], sem="stb0")
            rec.wait("vector", ld)
            cv = V(lambda e: e.tensor_copy(kvstg[:, 0:2048], xstg2[:, 0:2048]))
            rec.wait("tensor", cv)
            evs = []
            for ob in range(G):
                slot = pbb[ob % 4][:, 0:128]
                if ob >= 4:
                    rec.wait("tensor", evs[ob - 4])
                src_t = kvstg[:, ob * 128:(ob + 1) * 128]
                tt = T(lambda e, s=src_t, slot=slot: e.transpose(
                    slot, s, identb[:]))
                rec.wait("vector", tt)
                evs.append(V(lambda e, ob=ob, slot=slot: e.tensor_copy(
                    wsa[:, 2048 + ob * 128:2048 + (ob + 1) * 128], slot)))
            rec.wait("gpsimd", evs[-1])
            for ob in range(G):
                dst = abT_flat[ds(2 * BLK + ob * 32768 + erb * 16384, 16384)] \
                    .rearrange("(p c) -> p c", p=128)
                dt = rec.dma("gpsimd", dst,
                             wsa[:, 2048 + ob * 128:2048 + (ob + 1) * 128])
            rec.wait("gpsimd", dt)
            rec.wait("sync", dt)
            rec.wait("vector", dt)

        ld = rec.dma("sync", xstg2[0:64, 0:2048], rw1_i[:, :], sem="stb0")
        rec.wait("vector", ld)
        cv = V(lambda e: e.tensor_copy(kvstg[0:64, 0:2048], xstg2[0:64, 0:2048]))
        rec.wait("tensor", cv)
        evs = []
        for g in range(G):
            slot = pbb[g % 4][:, 0:128]
            if g >= 4:
                rec.wait("tensor", evs[g - 4])
            tt = T(lambda e, g=g, slot=slot: e.transpose(
                slot[:, 0:64], kvstg[0:64, g * 128:(g + 1) * 128],
                identb[0:64, 0:64]))
            rec.wait("vector", tt)
            evs.append(V(lambda e, g=g, slot=slot: e.tensor_copy(
                rwT[:, g * 64:(g + 1) * 64], slot[:, 0:64])))
        rec.wait("sync", cv)
        ld2 = rec.dma("sync", xstg2[0:32, 0:64], rw2_i[:, :], sem="stb0")
        rec.wait("vector", ld2)
        cv2 = V(lambda e: e.tensor_copy(kvstg[0:32, 0:64], xstg2[0:32, 0:64]))
        rec.wait("tensor", cv2)
        tt = T(lambda e: e.transpose(pbb[0][0:64, 256:288],
                                     kvstg[0:32, 0:64], identb[0:32, 0:32]))
        rec.wait("vector", tt)
        V(lambda e: e.tensor_copy(rwT[0:64, 1024:1056], pbb[0][0:64, 256:288]))
        rec.wait("sync", rec.now("dve"))

    def ph_prep_B():
        V(lambda e: e.memset(gacc, 0.0))
        blocks = []
        for b in range(W1B):
            blocks.append((0, p_w1[b * 128:(b + 1) * 128, :], D))
        for b in range(W1B):
            blocks.append((1, p_w2[b * 128:(b + 1) * 128, :], D))
        for lo in range(2):
            for piece in range(4):
                blocks.append((2, p_w3[lo * 128:(lo + 1) * 128,
                                       piece * 1408:(piece + 1) * 1408], 1408))
        cnt_wi = [0, 0, 0]
        rtok = None
        for wi, src, CW in blocks:
            if rtok is not None:
                rec.wait("sync", rtok)
            ld = rec.dma("sync", ksb32[:, 0:CW], src, sem="stb0")
            rec.wait("vector", ld)
            V(lambda e, CW=CW: e.reduce_sum(gtmp, ksb32[:, 0:CW], AX.X,
                                            apply_absolute_value=True))
            rtok = vgap()
            a = cnt_wi[wi] % 2
            b = (cnt_wi[wi] + 1) % 2
            V(lambda e, wi=wi, a=a, b=b: e.tensor_tensor(
                gacc[:, wi * 2 + b:wi * 2 + b + 1],
                gacc[:, wi * 2 + a:wi * 2 + a + 1], gtmp, ALU.add))
            vgap()
            cnt_wi[wi] += 1
        assert all(c % 2 == 0 for c in cnt_wi), cnt_wi
        pcom = vgap()
        rec.wait("tensor", pcom)
        gfin = gacc.rearrange("p (w t) -> p w t", t=2)[:, :, 0]
        mm = T(lambda e, gfin=gfin: e.matmul(pb[4][0:1, 0:3], onescol_f, gfin,
                                             start=True, stop=True))
        rec.wait("vector", mm)
        V(lambda e: e.memset(grow, 0.0))
        vgap()
        V(lambda e: e.tensor_copy(grow[:, 0:3], pb[4][0:1, 0:3]))
        gdone = vgap()
        rec.wait("gpsimd", gdone)
        d1 = rec.dma("gpsimd", gs_in[:, :], grow[:, 0:16])
        rec.wait("gpsimd", d1)
        ar = rec.op("gpsimd", lambda e: e.collective_compute(
            "AllReduce", ALU.add, replica_groups=[list(range(R))],
            ins=[gs_in[:, :]], outs=[gs_out[:, :]]), inc="cR")
        rec.wait("gpsimd", ar)
        d2 = rec.dma("gpsimd", grow[:, 0:16], gs_out[:, :])
        rec.wait("gpsimd", d2)
        rec.wait("vector", d2)
        NW = float(HID) * float(D)
        V(lambda e: e.tensor_scalar(g3a, grow[:, 0:3], 1.0 / NW, 1.0e-8,
                                    ALU.mult, ALU.add))
        vgap()
        V(lambda e: e.reciprocal(g3b, g3a))
        vgap()
        V(lambda e: e.tensor_scalar(g3c, g3b, -1.0, None, ALU.mult))
        gk = vgap()
        rec.wait("tensor", gk)
        mm1 = T(lambda e: e.matmul(pb[4][:, 0:3], ones_row, g3b,
                                   start=True, stop=True))
        mm2 = T(lambda e: e.matmul(pb[5][:, 0:3], ones_row, g3c,
                                   start=True, stop=True))
        rec.wait("vector", mm2)
        V(lambda e: e.tensor_copy(rgv, pb[4][:, 0:3]))
        cpr = V(lambda e: e.tensor_copy(rgn, pb[5][:, 0:3]))
        rec.wait("sync", cpr)

        def quant_into_vsb(src_slice, wi, CW):
            ld = rec.dma("sync", ksb32[:, 0:CW], src_slice, sem="stb0")
            rec.wait("vector", ld)
            vgap()
            V(lambda e, CW=CW, wi=wi: e.tensor_scalar(
                vsb[:, 0:CW], ksb32[:, 0:CW], rgv[:, wi:wi + 1], 0.5,
                ALU.mult, ALU.is_ge))
            V(lambda e, CW=CW, wi=wi: e.tensor_scalar(
                vsb[:, 2048:2048 + CW], ksb32[:, 0:CW], rgn[:, wi:wi + 1], 0.5,
                ALU.mult, ALU.is_ge))
            vgap()
            return V(lambda e, CW=CW: e.tensor_tensor(
                vsb[:, 4096:4096 + CW], vsb[:, 0:CW], vsb[:, 2048:2048 + CW],
                ALU.subtract))

        for wi, src in ((0, p_w1), (1, p_w2)):
            for b in range(W1B):
                qv = quant_into_vsb(src[b * 128:(b + 1) * 128, :], wi, D)
                rec.wait("tensor", qv)
                evs = []
                for g in range(G):
                    slot = pbb[4 + g % 4][:, 256:384]
                    if g >= 4:
                        rec.wait("tensor", evs[g - 4])
                    tt = T(lambda e, g=g, slot=slot: e.transpose(
                        slot, vsb[:, 4096 + g * 128:4096 + (g + 1) * 128],
                        identb[:]))
                    rec.wait("vector", tt)
                    evs.append(V(lambda e, g=g, slot=slot: e.tensor_copy(
                        vsb[:, 6144 + g * 128:6144 + (g + 1) * 128], slot)))
                rec.wait("gpsimd", evs[-1])
                dst = pB_flat[ds(_packB_off["W1Q" if wi == 0 else "W2Q"]
                                 + b * BLK, BLK)].rearrange("(p c) -> p c", p=128)
                dt = rec.dma("gpsimd", dst, vsb[:, 6144:8192])
                rec.wait("gpsimd", dt)
                rec.wait("sync", dt)
                rec.wait("vector", dt)

        for lo in range(2):
            for piece in range(4):
                cw = 11 * 128
                qv = quant_into_vsb(
                    p_w3[lo * 128:(lo + 1) * 128,
                         piece * cw:(piece + 1) * cw], 2, cw)
                rec.wait("tensor", qv)
                evs = []
                for t in range(11):
                    slot = pbb[4 + t % 4][:, 256:384]
                    if t >= 4:
                        rec.wait("tensor", evs[t - 4])
                    tt = T(lambda e, t=t, slot=slot: e.transpose(
                        slot, vsb[:, 4096 + t * 128:4096 + (t + 1) * 128],
                        identb[:]))
                    rec.wait("vector", tt)
                    evs.append(V(lambda e, t=t, slot=slot: e.tensor_copy(
                        vsb[:, 6144 + t * 128:6144 + (t + 1) * 128], slot)))
                rec.wait("gpsimd", evs[-1])
                dst = pB_flat[ds(_packB_off["W3Q"] + lo * W3BLK, W3BLK)] \
                    .rearrange("(p hc j) -> p hc j", p=128, hc=NHC) \
                    [:, piece * 11:(piece + 1) * 11, :]
                dt = rec.dma("gpsimd", dst,
                             vsb[:, 6144:6144 + cw].rearrange(
                                 "p (hc j) -> p hc j", hc=11))
                rec.wait("gpsimd", dt)
                rec.wait("sync", dt)
                rec.wait("vector", dt)

        gtok = rec.now("dgp")
        rec.wait("gpsimd", gtok)
        agB = rec.op("gpsimd", lambda e: e.collective_compute(
            "AllGather", ALU.bypass, replica_groups=[list(range(R))],
            ins=[packB_in[:, :]], outs=[packB_out[:, :]]), inc="cB")
        return agB

    def ph_load_x():
        evs = []
        slot_ev = [None, None]
        for t in range(8):
            buf = xstg2 if t % 2 == 0 else xstg
            snm = "stb0" if t % 2 == 0 else "sta0"
            if slot_ev[t % 2] is not None:
                rec.wait("sync", slot_ev[t % 2])
            ld = rec.dma("sync", buf[:, 0:2048],
                         x_own[t * 128:(t + 1) * 128, :], sem=snm)
            rec.wait("tensor", ld)
            for g in range(G):
                slot = pb[g % 4][:, 0:128]
                idx = t * G + g
                if idx >= 4:
                    rec.wait("tensor", evs[idx - 4])
                tt = T(lambda e, buf=buf, g=g, slot=slot: e.transpose(
                    slot, buf[:, g * 128:(g + 1) * 128], identf[:]))
                rec.wait("vector", tt)
                evs.append(V(lambda e, g=g, t=t, slot=slot: e.tensor_copy(
                    stream[:, g * 1024 + t * 128:g * 1024 + (t + 1) * 128],
                    slot)))
            slot_ev[t % 2] = evs[-1]

    def proj16(addr_fn, rhs_fn, evac_fn, stage_cols=2048, nout=16,
               pre_tok=None, lhs_g_of=None):
        bank_tok = [None, None]
        stage_tok = [None, None]
        ld_tok = [None, None]
        rec.wait("sync", rec.now("pe"))
        rec.wait("tensor", rec.now("dve"))
        for o in range(nout):
            sbuf_half = o % 2
            stg = wsa[:, sbuf_half * 2048: sbuf_half * 2048 + stage_cols]
            if stage_tok[sbuf_half] is not None:
                rec.wait("sync", stage_tok[sbuf_half])
            if pre_tok is not None:
                rec.wait("sync", pre_tok)
                pre_tok = None
            ld_tok[sbuf_half] = rec.dma("sync", stg, addr_fn(o),
                                        sem="sta" + str(sbuf_half))
            mmtok = None
            for half in range(2):
                bnk = (2 * o + half) % 2
                bank = pb[bnk]
                if bank_tok[bnk] is not None:
                    rec.wait("tensor", bank_tok[bnk])
                rec.wait("tensor", ld_tok[sbuf_half])
                for g in range(G):
                    gl = lhs_g_of(g) if lhs_g_of else g
                    mmtok = T(lambda e, stg=stg, gl=gl, g=g, half=half,
                              bank=bank: e.matmul(
                        bank[:, 0:512], stg[:, gl * 128:(gl + 1) * 128],
                        rhs_fn(g, half), start=(g == 0), stop=(g == G - 1)))
                bank_tok[bnk] = evac_fn(o, half, bank, mmtok)
            stage_tok[sbuf_half] = bank_tok[(2 * o + 1) % 2]
        return bank_tok

    kv_dma_toks = []

    KVSTATE = {}

    def mk_evac_kv(layer, row_base, with_ck, bias_ap=None):
        state = KVSTATE

        def evac(o, half, bank, mmtok):
            buf = kvstg[:, (o % 2) * 1024:(o % 2) * 1024 + 1024]
            rec.wait("vector", mmtok)
            if (o % 2) in state:
                rec.wait("vector", state[o % 2])
            if bias_ap is not None:
                tok = V(lambda e, buf=buf, half=half, o=o, bank=bank:
                        e.tensor_scalar(buf[:, half * 512:(half + 1) * 512],
                                        bank[:, 0:512], bias_ap[:, o:o + 1],
                                        None, ALU.add))
            else:
                tok = V(lambda e, buf=buf, half=half, bank=bank: e.tensor_copy(
                    buf[:, half * 512:(half + 1) * 512], bank[:, 0:512]))
            if half == 1:
                if with_ck:
                    vgap()
                    V(lambda e, buf=buf, o=o: e.reduce_max(
                        ck[:, o:o + 1], buf, AX.X, apply_absolute_value=True))
                    tok = vgap()
                rec.wait("sync", tok)
                dt = rec.dma("sync",
                             kv_in[layer][row_base + o * 128:
                                          row_base + (o + 1) * 128, :], buf,
                             sem="kvo" + str(o % 2))
                state[o % 2] = dt
            return tok
        return evac

    def mk_evac_sbuf(dst_fn, bias_ap=None):
        def evac(o, half, bank, mmtok):
            rec.wait("vector", mmtok)
            if bias_ap is not None:
                return V(lambda e, o=o, half=half, bank=bank: e.tensor_scalar(
                    dst_fn(o, half), bank[:, 0:512], bias_ap[:, o:o + 1],
                    None, ALU.add))
            return V(lambda e, o=o, half=half, bank=bank: e.tensor_copy(
                dst_fn(o, half), bank[:, 0:512]))
        return evac

    qsv = qs.ap().rearrange("p (blk tl gg) -> p blk tl gg", blk=2, tl=512, gg=G)

    def stream_rhs(g, half):
        return stream[:, g * 1024 + half * 512: g * 1024 + (half + 1) * 512]

    def ph_proj_L0(agA_tok):
        rec.wait("sync", agA_tok)
        kv_dma_toks.clear()
        proj16(lambda o: packA_stage(natA("Wk", o), 2048), stream_rhs,
               mk_evac_kv(0, 0, with_ck=True))
        proj16(lambda o: packA_stage(natA("Wv", o), 2048), stream_rhs,
               mk_evac_sbuf(lambda o, half: outT[:, o * 1024 + half * 512:
                                                 o * 1024 + (half + 1) * 512]))
        vptok = rec.now("dve")
        rec.wait("tensor", vptok)
        rec.wait("sync", rec.now("kvo0"))
        rec.wait("sync", rec.now("kvo1"))
        prev_dma = [None, None]
        slot_ev = [None, None]
        for t in range(8):
            half = t % 2
            vrow = wsb[:, half * 2048:(half + 1) * 2048]
            if prev_dma[half] is not None:
                rec.wait("vector", prev_dma[half])
            evs = []
            for g in range(G):
                slot = pbb[2 + (g % 2)][:, 0:128]
                if g >= 2:
                    rec.wait("tensor", evs[g - 2])
                elif slot_ev[g % 2] is not None:
                    rec.wait("tensor", slot_ev[g % 2])
                tt = T(lambda e, g=g, t=t, slot=slot: e.transpose(
                    slot, outT[:, g * 1024 + t * 128:g * 1024 + (t + 1) * 128],
                    identb[:]))
                rec.wait("vector", tt)
                evs.append(V(lambda e, g=g, slot=slot, vrow=vrow: e.tensor_copy(
                    vrow[:, g * 128:(g + 1) * 128], slot)))
            slot_ev = [evs[-2], evs[-1]]
            rec.wait("sync", evs[-1])
            vreg0 = kvi_flat[0][ds(2048 * 1024, 2097152)].rearrange(
                "(p hf h sb j) -> p hf h sb j", p=128, hf=2, h=16, sb=4)
            dt = rec.dma("sync", vreg0[:, t // 4, :, t % 4, :],
                         vrow.rearrange("p (h j) -> p h j", h=16),
                         sem="kvo" + str(half))
            prev_dma[half] = dt
        cktok = rec.now("dve")
        rec.wait("gpsimd", cktok)
        dt = rec.dma("gpsimd", kvi_flat[0][ds(KVBASE[0]["c"] * 1024, 2048)]
                     .rearrange("(g p) -> p g", p=128), ck,
                     allow_slow_non_contiguous=True)
        rec.wait("gpsimd", dt)
        def q_evac(o, half, bank, mmtok):
            rec.wait("vector", mmtok)
            return V(lambda e, o=o, half=half, bank=bank: e.tensor_scalar(
                qsv[:, half, :, o], bank[:, 0:512], bq_f[:, o:o + 1],
                None, ALU.add))
        proj16(lambda o: packA_stage(natA("Wq", o), 2048), stream_rhs, q_evac)
        rec.wait("gpsimd", rec.now("dve"))
        for sname in ("kvo0", "kvo1", "dkv", "dsp"):
            rec.wait("gpsimd", rec.now(sname))
        ag = rec.op("gpsimd", lambda e: e.collective_compute(
            "AllGather", ALU.bypass, replica_groups=[list(range(R))],
            ins=[kv_in[0][:, :]], outs=[kv_out[0][:, :]]), inc="ck0")
        return ag

    def ph_proj_L1(agB_tok):
        kv_dma_toks.clear()
        for dst, src in ((ksb, upk_i), (vsb, upv_i)):
            sf = src.rearrange("h d j -> (h d j)")
            ap = sf.rearrange("(h g p j) -> p h g j", h=NH, g=4, p=128)
            rec.dma("gpsimd", dst.ap().rearrange(
                "p (h g j) -> p h g j", h=NH, g=4), ap, sem="dup")
        up_tok = rec.now("dup")
        proj16(lambda o: packA_stage(natA("wq1", o), 2048), stream_rhs,
               mk_evac_kv(1, KVBASE[1]["q"], with_ck=False, bias_ap=wqb_f))
        proj16(lambda o: packA_stage(natA("wk1", o), 2048), stream_rhs,
               mk_evac_sbuf(lambda o, half: outT[:, o * 1024 + half * 512:
                                                 o * 1024 + (half + 1) * 512]))
        def k1_rhs(g, half):
            return outT[:, g * 1024 + half * 512: g * 1024 + (half + 1) * 512]
        proj16(lambda o: packA_stage(wlA(0, o), 2048), k1_rhs,
               mk_evac_sbuf(lambda o, half: qs[:, 8192 + o * 1024 + half * 512:
                                               8192 + o * 1024 + (half + 1) * 512]),
               nout=4)
        rec.wait("tensor", up_tok)
        rec.wait("tensor", rec.now("dve"))
        evk = mk_evac_kv(1, KVBASE[1]["k"], with_ck=True)
        bank_tok = [None, None]
        for h in range(NH):
            mmtok = None
            for half in range(2):
                bnk = (2 * h + half) % 2
                bank = pb[bnk]
                if bank_tok[bnk] is not None:
                    rec.wait("tensor", bank_tok[bnk])
                for g in range(4):
                    mmtok = T(lambda e, h=h, g=g, half=half, bank=bank:
                              e.matmul(bank[:, 0:512],
                                       ksb[:, h * 512 + g * 128:
                                           h * 512 + (g + 1) * 128],
                                       qs[:, 8192 + g * 1024 + half * 512:
                                          8192 + g * 1024 + (half + 1) * 512],
                                       start=(g == 0), stop=(g == 3)))
                bank_tok[bnk] = evk(h, half, bank, mmtok)
        proj16(lambda o: packA_stage(natA("wv1", o), 2048), stream_rhs,
               mk_evac_sbuf(lambda o, half: outT[:, o * 1024 + half * 512:
                                                 o * 1024 + (half + 1) * 512]))
        rec.wait("sync", rec.now("pe"))
        rec.wait("tensor", rec.now("dve"))
        vl_tok = [None, None]
        vlb_tok = [None, None]
        for o in range(4):
            sbuf_half = o % 2
            stg = wsa[:, sbuf_half * 2048:(sbuf_half + 1) * 2048]
            if vl_tok[sbuf_half] is not None:
                rec.wait("sync", vl_tok[sbuf_half])
            ld = rec.dma("sync", stg, packA_stage(wlA(1, o), 2048),
                         sem="sta" + str(sbuf_half))
            mmtok = None
            for half in range(2):
                bnk = (2 * o + half) % 2
                bank = pb[bnk]
                if vlb_tok[bnk] is not None:
                    rec.wait("tensor", vlb_tok[bnk])
                rec.wait("tensor", ld)
                for g in range(G):
                    mmtok = T(lambda e, stg=stg, g=g, half=half, o=o,
                              bank=bank: e.matmul(
                        bank[:, 0:512], stg[:, g * 128:(g + 1) * 128],
                        outT[:, g * 1024 + half * 512:
                             g * 1024 + (half + 1) * 512],
                        start=(g == 0), stop=(g == G - 1)))
                rec.wait("vector", mmtok)
                ev = V(lambda e, o=o, half=half, bank=bank: e.tensor_copy(
                    qs[:, 12288 + o * 1024 + half * 512:
                       12288 + o * 1024 + (half + 1) * 512], bank[:, 0:512]))
                vlb_tok[bnk] = ev
            rec.wait("tensor", rec.now("dve"))
            c_mm = None
            for g in range(G):
                c_mm = T(lambda e, stg=stg, g=g: e.matmul(
                    pb[5][:, 0:1], stg[:, g * 128:(g + 1) * 128],
                    wvb_b[:, g:g + 1], start=(g == 0), stop=(g == G - 1)))
            rec.wait("vector", c_mm)
            V(lambda e, o=o: e.tensor_scalar(
                v1c[:, o:o + 1], pb[5][:, 0:1], wvlb_f[:, o:o + 1],
                None, ALU.add))
            vgap()
            vl_tok[sbuf_half] = c_mm
        cst = V(lambda e: e.tensor_copy(v1cb, v1c))
        rec.wait("tensor", cst)
        rec.wait("tensor", rec.now("dve"))
        prev_ev = {}
        for h in range(NH):
            for t in range(8):
                bank = pb[(h * 8 + t) % 2]
                key = (h * 8 + t) % 2
                if key in prev_ev:
                    rec.wait("tensor", prev_ev[key])
                mmtok = None
                for g in range(4):
                    mmtok = T(lambda e, h=h, t=t, g=g, bank=bank: e.matmul(
                        bank[:, 0:128],
                        qs[:, 12288 + g * 1024 + t * 128:
                           12288 + g * 1024 + (t + 1) * 128],
                        vsb[:, h * 512 + g * 128: h * 512 + (g + 1) * 128],
                        start=(g == 0), stop=(g == 3)))
                rec.wait("vector", mmtok)
                prev_ev[key] = V(lambda e, h=h, t=t, bank=bank: e.tensor_copy(
                    outT[:, t * 2048 + h * 128: t * 2048 + (h + 1) * 128],
                    bank[:, 0:128]))
            rec.wait("tensor", rec.now("dve"))
            cm = None
            for g in range(4):
                cm = T(lambda e, h=h, g=g: e.matmul(
                    pb[5][:, 0:1],
                    vsb[:, h * 512 + g * 128: h * 512 + (g + 1) * 128],
                    v1cb[:, g:g + 1], start=(g == 0), stop=(g == 3)))
            rec.wait("vector", cm)
            V(lambda e, h=h: e.tensor_scalar(
                cc1[:, h:h + 1], pb[5][:, 0:1], upvb_f[:, h:h + 1],
                None, ALU.add))
            vgap()
        vtok = rec.now("dve")
        rec.wait("sync", vtok)
        vreg1 = kvi_flat[1][ds(KVBASE[1]["v"] * 1024, 2097152)].rearrange(
            "(p hf h sb j) -> p hf h sb j", p=128, hf=2, h=16, sb=4)
        for t in range(8):
            dt = rec.dma("sync", vreg1[:, t // 4, :, t % 4, :],
                         outT[:, t * 2048:(t + 1) * 2048].rearrange(
                             "p (h j) -> p h j", h=16), sem="dkv")
        cktok = rec.now("dve")
        rec.wait("gpsimd", cktok)
        dt = rec.dma("gpsimd", kvi_flat[1][ds(KVBASE[1]["c"] * 1024, 2048)]
                     .rearrange("(g p) -> p g", p=128), ck,
                     allow_slow_non_contiguous=True)
        rec.wait("gpsimd", dt)
        for sname in ("kvo0", "kvo1", "dkv", "dsp"):
            rec.wait("gpsimd", rec.now(sname))
        ag = rec.op("gpsimd", lambda e: e.collective_compute(
            "AllGather", ALU.bypass, replica_groups=[list(range(R))],
            ins=[kv_in[1][:, :]], outs=[kv_out[1][:, :]]), inc="ck1")
        return ag

    def ph_ccombine(layer, ag_tok):
        rec.wait("gpsimd", ag_tok)
        base = KVBASE[layer]["c"]
        rows = KVBASE[layer]["rows"]
        dts = []
        for r in range(R):
            dts.append(rec.dma(
                "gpsimd", absq[:, r * 16:(r + 1) * 16],
                kvo_flat[layer][ds((r * rows + base) * 1024, 2048)]
                .rearrange("(g p) -> p g", p=128), sem="dcm",
                allow_slow_non_contiguous=True))
        rec.wait("gpsimd", dts[-1])
        rec.wait("vector", dts[-1])
        V(lambda e: e.tensor_max(absq[:, 128:144], absq[:, 0:16], absq[:, 16:32]))
        V(lambda e: e.tensor_max(absq[:, 144:160], absq[:, 32:48], absq[:, 48:64]))
        V(lambda e: e.tensor_max(absq[:, 160:176], absq[:, 64:80], absq[:, 80:96]))
        V(lambda e: e.tensor_max(absq[:, 176:192], absq[:, 96:112], absq[:, 112:128]))
        vgap()
        V(lambda e: e.tensor_max(absq[:, 192:208], absq[:, 128:144], absq[:, 144:160]))
        V(lambda e: e.tensor_max(absq[:, 208:224], absq[:, 160:176], absq[:, 176:192]))
        vgap()
        cf = V(lambda e: e.tensor_max(cfeat, absq[:, 192:208], absq[:, 208:224]))
        cg = vgap()
        rec.wait("gpsimd", cg)
        cs_flat = cscr.rearrange("r c -> (r c)")
        st = rec.dma("gpsimd", cs_flat.rearrange("(g p) -> p g", p=128),
                     cfeat, sem="dcm", allow_slow_non_contiguous=True)
        rec.wait("gpsimd", st)
        cs1 = cs_flat.rearrange("(n b) -> n b", b=1)
        gather(cmy[:, 0:1], cs1, ("cmy", layer, 0))
        t2 = gather(cmy[:, 1:2], cs1, ("cmy", layer, 1))
        rec.wait("gpsimd", t2)
        return t2, rec.now("gp")

    def ph_attention(layer, ag_tok):
        dgp_tok, gp_tok = ph_ccombine(layer, ag_tok)
        rec.wait("tensor", dgp_tok)
        rec.wait("scalar", dgp_tok)
        SC = [pb[0], pb[1], pb[2]]
        PPRE3 = [ppre[:, 0:512], ppre[:, 512:1024], ascr[:, 1024:1536]]
        PBF3 = [pbf[:, 0:512], pbf[:, 512:1024], t1b[:, 0:512]]
        PVB = [pb[3], pb[6]]
        MHB = [ascr[:, 0:512], ascr[:, 512:1024]]
        MROW = [ascr[0:1, 1536:2048], t1[0:1, 512:1024]]
        rrow = ascr[0:1, 2048:2560]
        rbbv = ascr[:, 2048:2560]
        outv_toks = []
        state = {}

        def qtile_of(hi, qt):
            if layer == 0:
                return qs[:, hi * 8192 + qt * 512: hi * 8192 + (qt + 1) * 512]
            return qs[:, qt * 512:(qt + 1) * 512]

        def emit_gather_group(hi, m):
            sem = f"dl{m}"
            gather(ksb[:, m * 512:(m + 1) * 512], kvo512[layer],
                   ("k", layer, hi, m), sem=sem)
            if layer == 1:
                gather(qs[:, m * 512:(m + 1) * 512], kvo512[layer],
                       ("q", layer, hi, m), sem=sem)
            gather(vsb[:, m * 512:(m + 1) * 512], kvo512[layer],
                   ("v", layer, hi, m), sem=sem)
            return rec.now(sem)

        def mk_prologue(hi, qt, gtok):
            par = qt % 2
            qtile = qtile_of(hi, qt)
            toks = {}

            def p_ab():
                if layer == 1:
                    rec.wait("scalar", gtok[qt])
                if "mh" in state:
                    rec.wait("scalar", state["mh"])
                toks["ab"] = Aop(lambda e: e.activation(
                    absq[:, 0:512], qtile, AF.Abs))

            def p_mh():
                rec.wait("tensor", toks["ab"])
                if "mts" in state:
                    rec.wait("tensor", state["mts"])
                mh = T(lambda e: e.matmul(
                    pb[5][0:1, 0:512], cmy[:, hi:hi + 1], absq[:, 0:512],
                    start=True, stop=True))
                state["mh"] = mh
                rec.wait("vector", mh)
                if ("bc", par) in state:
                    rec.wait("vector", state[("bc", par)])
                V(lambda e: e.tensor_scalar(
                    MROW[par], pb[5][0:1, 0:512], SCALE, 1.0,
                    ALU.mult, ALU.add))
                toks["mg"] = vgap()
                state["mts"] = toks["mg"]

            def p_bc():
                rec.wait("tensor", toks["mg"])
                if "mhcp" in state:
                    rec.wait("tensor", state["mhcp"])
                bc = T(lambda e: e.matmul(
                    pb[4][:, 0:512], ones_row, MROW[par],
                    start=True, stop=True))
                state[("bc", par)] = bc
                rec.wait("vector", bc)
                V(lambda e: e.tensor_copy(MHB[par], pb[4][:, 0:512]))
                state["mhcp"] = vgap()

            return [p_ab, p_mh, p_bc]

        next_gtok = []
        for hi in range(2):
            if hi == 0:
                gtok = []
                for m in range(16):
                    gtok.append(emit_gather_group(hi, m))
            else:
                gtok = next_gtok
                next_gtok = []

            units = []
            for qt in range(16):
                n = 4 * (qt + 1)
                for uu in range(n):
                    units.append((qt, uu, uu // 4, uu % 4,
                                  uu // 4 == qt, uu == 0, uu == n - 1))
            NU = len(units)
            qk_toks = {}
            sub_toks = {}
            exa_toks = {}
            ptoks = {}
            pvl_toks = {}
            fo_toks = {}
            recip_tok = None
            last_m = [-1]

            def emit_qk(j):
                qtj, uuj, mj, sbj, diagj, firstj, lastj = units[j]
                slot = SC[j % 3]
                if j >= 3:
                    rec.wait("tensor", sub_toks[j - 3])
                if mj > last_m[0]:
                    rec.wait("tensor", gtok[mj])
                    last_m[0] = mj
                qtile = qtile_of(hi, qtj)
                if layer == 1 and firstj:
                    rec.wait("tensor", gtok[qtj])
                qk_toks[j] = T(lambda e, uuj=uuj, slot=slot, qtile=qtile:
                               e.matmul(slot[:, 0:512],
                                        ksb[:, uuj * 128:(uuj + 1) * 128],
                                        qtile, start=True, stop=True))

            pro = mk_prologue(hi, 0, gtok)
            for p in pro:
                p()
            pending_pro = None
            pro_pos = []
            emit_qk(0)
            if NU > 1:
                emit_qk(1)
            for j, (qt, uu, m, sb, diag, first, last) in enumerate(units):
                nunits = 4 * (qt + 1)
                if first and qt < 15:
                    pending_pro = mk_prologue(hi, qt + 1, gtok)
                    p0 = max(0, nunits - 6)
                    p1 = max(p0 + 1, nunits - 4)
                    p2 = max(p1 + 1, nunits - 2)
                    pro_pos = [p0, p1, p2]
                if pending_pro and uu in pro_pos:
                    pi = pro_pos.index(uu)
                    pending_pro[pi]()
                    if pi == 2:
                        pending_pro = None
                if j + 2 < NU:
                    emit_qk(j + 2)
                rec.wait("vector", qk_toks[j])
                if j - 3 in exa_toks:
                    rec.wait("vector", exa_toks[j - 3])
                st = V(lambda e, j=j, qt=qt: e.scalar_tensor_tensor(
                    PPRE3[j % 3], SC[j % 3][:, 0:512], SCALE, MHB[qt % 2],
                    ALU.mult, ALU.subtract))
                sub_toks[j] = st
                rec.wait("scalar", st)
                if j - 3 in pvl_toks:
                    rec.wait("scalar", pvl_toks[j - 3])
                ex = Aop(lambda e, j=j: e.activation(
                    PBF3[j % 3], PPRE3[j % 3], AF.Exp))
                exa_toks[j] = ex
                ptok = ex
                if diag:
                    rec.wait("gpsimd", ex)
                    ptok = Gop(lambda e, j=j, sb=sb: e.affine_select(
                        out=PBF3[j % 3], in_=PBF3[j % 3],
                        compare_op=ALU.is_ge, fill=0.0, base=-128 * sb,
                        pattern=[[1, 512]], channel_multiplier=-1))
                ptoks[j] = ptok
                rec.wait("tensor", ptok)
                if first:
                    if qt - 2 in fo_toks:
                        rec.wait("tensor", fo_toks[qt - 2])
                    if recip_tok is not None:
                        rec.wait("tensor", recip_tok)
                T(lambda e, j=j, uu=uu, qt=qt, first=first, last=last:
                  e.matmul(PVB[qt % 2][:, 0:512],
                           vsb[:, uu * 128:(uu + 1) * 128], PBF3[j % 3],
                           start=first, stop=last))
                pvl = T(lambda e, j=j, first=first, last=last: e.matmul(
                    pb[7][0:1, 0:512], ones_b, PBF3[j % 3],
                    start=first, stop=last))
                pvl_toks[j] = pvl
                if hi == 0 and qt == 15 and sb == 3:
                    rec.wait("gpsimd", pvl)
                    if layer == 1:
                        rec.wait("gpsimd", exa_toks[j])
                    next_gtok.append(emit_gather_group(1, m))
                if last:
                    rec.wait("vector", pvl)
                    V(lambda e: e.reciprocal(rrow, pb[7][0:1, 0:512]))
                    rg = vgap()
                    recip_tok = rg
                    rec.wait("tensor", rg)
                    slot = SC[j % 3]
                    bc2 = T(lambda e, slot=slot: e.matmul(
                        slot[:, 0:512], ones_row, rrow,
                        start=True, stop=True))
                    rec.wait("vector", bc2)
                    rbc = V(lambda e, slot=slot: e.tensor_copy(
                        rbbv, slot[:, 0:512]))
                    sub_toks[j] = rbc
                    vgap()
                    fo = V(lambda e, hi=hi, qt=qt: e.tensor_tensor(
                        outT[:, hi * 8192 + qt * 512:
                             hi * 8192 + (qt + 1) * 512],
                        PVB[qt % 2][:, 0:512], rbbv, ALU.mult))
                    fo_toks[qt] = fo
                    outv_toks.append(fo)
        return outv_toks

    def ph_wo(layer, outv_toks):
        for t in outv_toks[-2:]:
            rec.wait("sync", t)
        rec.wait("sync", rec.now("dve"))
        d1 = rec.dma("sync", ao_in[0:128, :], outT[:, 0:8192], sem="dkv")
        d2 = rec.dma("sync", ao_in[128:256, :], outT[:, 8192:16384], sem="dkv")
        rec.wait("gpsimd", rec.now("dkv"))
        ag = rec.op("gpsimd", lambda e: e.collective_compute(
            "AllGather", ALU.bypass, replica_groups=[list(range(R))],
            ins=[ao_in[:, :]], outs=[ao_out[:, :]]), inc="cao")
        rec.wait("gpsimd", ag)
        for jk in range(16):
            for cc in range(2):
                s2 = gather(outT[:, jk * 1024 + cc * 512:
                                 jk * 1024 + (cc + 1) * 512],
                            ao512, ("ao", layer, jk, cc))
        rec.wait("gpsimd", s2)
        rec.wait("tensor", s2)
        rec.wait("sync", s2)
        mat = "Wo" if layer == 0 else "wo1"
        bias_src = bo_f if layer == 0 else wob_f
        cbv = cb0 if layer == 0 else cb1
        if layer == 1:
            cv = V(lambda e: e.tensor_copy(cvb, cc1))
            rec.wait("tensor", cv)
        bank_tok = [None, None]
        stage_tok = [None, None]
        rec.wait("sync", rec.now("pe"))
        rec.wait("tensor", rec.now("dve"))
        for o in range(G):
            sh = o % 2
            stg = wsa[:, sh * 2048:(sh + 1) * 2048]
            if stage_tok[sh] is not None:
                rec.wait("sync", stage_tok[sh])
            ld = rec.dma("sync", stg, packA_stage(natA(mat, o), 2048),
                         sem="sta" + str(sh))
            rec.wait("tensor", ld)
            if o > 0:
                rec.wait("tensor", cb_ts)
            cm = None
            for g in range(G):
                cm = T(lambda e, stg=stg, g=g: e.matmul(
                    pb[5][:, 0:1], stg[:, g * 128:(g + 1) * 128],
                    cvb[:, g:g + 1], start=(g == 0), stop=(g == G - 1)))
            rec.wait("vector", cm)
            cb_ts = V(lambda e, o=o, cbv=cbv, bias_src=bias_src: e.tensor_scalar(
                cbv[:, o:o + 1], pb[5][:, 0:1], bias_src[:, o:o + 1],
                None, ALU.add))
            vgap()
            mmtok = None
            for half in range(2):
                bnk = (2 * o + half) % 2
                bank = pb[bnk]
                if bank_tok[bnk] is not None:
                    rec.wait("tensor", bank_tok[bnk])
                for jk in range(G):
                    gh = ghead(jk)
                    mmtok = T(lambda e, stg=stg, gh=gh, jk=jk, half=half,
                              bank=bank: e.matmul(
                        bank[:, 0:512], stg[:, gh * 128:(gh + 1) * 128],
                        outT[:, jk * 1024 + half * 512:
                             jk * 1024 + (half + 1) * 512],
                        start=(jk == 0), stop=(jk == G - 1)))
                rec.wait("vector", mmtok)
                bank_tok[bnk] = V(
                    lambda e, o=o, half=half, bank=bank, cbv=cbv:
                    e.scalar_tensor_tensor(
                        ybuf[:, o * 1024 + half * 512:
                             o * 1024 + (half + 1) * 512],
                        bank[:, 0:512], cbv[:, o:o + 1],
                        stream[:, o * 1024 + half * 512:
                               o * 1024 + (half + 1) * 512],
                        ALU.add, ALU.add))
            stage_tok[sh] = bank_tok[(2 * o + 1) % 2]

    def fview(g):
        if g < 8:
            return qs32[:, g * 1024:(g + 1) * 1024]
        if g < 12:
            return ksb32[:, (g - 8) * 1024:(g - 7) * 1024]
        return vsb32[:, (g - 12) * 1024:(g - 11) * 1024]

    def ph_ln(final=False):
        barrier()
        for half in range(2):
            sq_toks = []
            for g in range(G):
                if g >= 2:
                    rec.wait("scalar", sq_toks[g - 2])
                sq = Aop(lambda e, g=g, half=half: e.activation(
                    ysqv[:, (g % 2) * 512:(g % 2) * 512 + 512],
                    ybuf[:, g * 1024 + half * 512:
                         g * 1024 + (half + 1) * 512], AF.Square))
                rec.wait("tensor", sq)
                T(lambda e, g=g, half=half: e.matmul(
                    pb[6][0:1, 0:512], ones_b,
                    ybuf[:, g * 1024 + half * 512:
                         g * 1024 + (half + 1) * 512],
                    start=(g == 0), stop=(g == G - 1)))
                s2t = T(lambda e, g=g: e.matmul(
                    pb[7][0:1, 0:512], ones_b,
                    ysqv[:, (g % 2) * 512:(g % 2) * 512 + 512],
                    start=(g == 0), stop=(g == G - 1)))
                sq_toks.append(s2t)
            stt = sq_toks[-1]
            rec.wait("vector", stt)
            rec.wait("vector", rec.now("act"))
            r4 = pb[4][0:1, 0:512]
            r5 = pb[5][0:1, 0:512]
            Atmp = t1[0:1, 0:512]
            Btmp = t1[0:1, 512:1024]
            mu = muK[:, 0:512]
            V(lambda e: e.tensor_scalar(mu, pb[6][0:1, 0:512], 1.0 / D,
                                        None, ALU.mult))
            V(lambda e: e.tensor_scalar(Atmp, pb[7][0:1, 0:512], 1.0 / D,
                                        None, ALU.mult))
            vgap()
            V(lambda e, mu=mu: e.tensor_tensor(Btmp, mu, mu, ALU.mult))
            vgap()
            V(lambda e: e.tensor_tensor(r4, Atmp, Btmp, ALU.subtract))
            vgap()
            V(lambda e: e.tensor_scalar(Btmp, r4, EPS_LN, None, ALU.add))
            vr = vgap()
            rec.wait("scalar", vr)
            sd = Aop(lambda e: e.activation(r4, Btmp, AF.Sqrt))
            rec.wait("vector", sd)
            V(lambda e: e.reciprocal(Atmp, r4))
            vgap()
            V(lambda e: e.tensor_tensor(r4, Atmp, Atmp, ALU.mult))
            vgap()
            V(lambda e: e.tensor_tensor(r5, Btmp, r4, ALU.mult))
            vgap()
            V(lambda e: e.tensor_scalar(Btmp, r5, -0.5, 1.5, ALU.mult,
                                        ALU.add))
            vgap()
            V(lambda e, half=half: e.tensor_tensor(
                rstd_row[:, half * 512:(half + 1) * 512], Atmp, Btmp,
                ALU.mult))
            vgap()
            V(lambda e, half=half, mu=mu: e.tensor_tensor(
                m2_row[:, half * 512:(half + 1) * 512], mu,
                rstd_row[:, half * 512:(half + 1) * 512], ALU.mult))
            mfin = vgap()
            rec.wait("tensor", mfin)
            b1 = T(lambda e, half=half: e.matmul(
                pb[4][:, 0:512], ones_row,
                rstd_row[:, half * 512:(half + 1) * 512],
                start=True, stop=True))
            b2 = T(lambda e, half=half: e.matmul(
                pb[5][:, 0:512], ones_row,
                m2_row[:, half * 512:(half + 1) * 512],
                start=True, stop=True))
            rec.wait("vector", b2)
            zbuf = [t1[:, 0:512], t1[:, 512:1024]]
            for g in range(G):
                V(lambda e, g=g, half=half: e.tensor_tensor(
                    zbuf[g % 2], ybuf[:, g * 1024 + half * 512:
                                      g * 1024 + (half + 1) * 512],
                    pb[4][:, 0:512], ALU.mult))
                vgap()
                if final:
                    dst = fview(g)[:, half * 512:(half + 1) * 512]
                else:
                    dst = stream[:, g * 1024 + half * 512:
                                 g * 1024 + (half + 1) * 512]
                V(lambda e, g=g, dst=dst: e.tensor_tensor(
                    dst, zbuf[g % 2], pb[5][:, 0:512], ALU.subtract))
        barrier()

    def ph_ffn0(stop=None):
        hT = outT[0:64, 0:1024]
        Rt = outT32[:, 2048:2176]
        m1t = outT32[:, 2176:2184]
        m2t = outT32[:, 2184:2192]
        eqm = outT32[:, 2192:2320]
        msk = outT32[:, 2320:2448]
        expl = outT32[:, 2448:2576]
        gat = outT32[:, 2576:2704]
        den = outT32[:, 2704:2712]
        rcd = outT32[:, 2712:2720]
        cfb = outT[:, 5536:5536 + 2048]
        coT = outT[0:16, 7680:8704]
        xab = outT[:, 8704:8704 + 2048]
        rec.wait("vector", rec.now("pe"))
        onetok = V(lambda e: e.memset(outT[64:65, 0:1024], 1.0))
        for half in range(2):
            mm = None
            for g in range(G):
                mm = T(lambda e, g=g, half=half: e.matmul(
                    pb[4][0:64, 0:512], rwT[:, g * 64:(g + 1) * 64],
                    stream_rhs(g, half), start=(g == 0), stop=(g == G - 1)))
            rec.wait("vector", mm)
            V(lambda e, half=half: e.tensor_scalar(
                t1[0:64, 0:512], pb[4][0:64, 0:512], rb1_f[0:64, 0:1],
                None, ALU.add))
            sv = vgap()
            rec.wait("scalar", sv)
            hs = Aop(lambda e, half=half: e.activation(
                hT[:, half * 512:(half + 1) * 512], t1[0:64, 0:512], AF.Silu))
            rec.wait("tensor", hs)
        if stop == "router":
            return
        rec.wait("tensor", onetok)
        evs = []
        for tt in range(8):
            slot = pb[2 + (tt % 2)][:, 0:16]
            if tt >= 2:
                rec.wait("tensor", evs[tt - 2])
            tk = T(lambda e, tt=tt, slot=slot: e.matmul(
                slot, outT[0:65, tt * 128:(tt + 1) * 128],
                rwT[0:65, 1024:1040], start=True, stop=True))
            rec.wait("vector", tk)
            evs.append(V(lambda e, tt=tt, slot=slot: e.tensor_copy(
                Rt[:, tt * 16:(tt + 1) * 16], slot)))
        vgap()
        if stop == "ltr":
            return
        R3 = Rt.rearrange("p (t e) -> p t e", t=8)
        V(lambda e: e.reduce_max(m1t, R3, AX.X, negate=True))
        vgap()
        m1b = m1t.rearrange("p (t o) -> p t o", o=1).broadcast_to([128, 8, 16])
        V(lambda e: e.tensor_tensor(eqm.rearrange("p (t e) -> p t e", t=8),
                                    R3, m1b, ALU.add))
        vgap()
        V(lambda e: e.tensor_scalar(msk, eqm, 0.0, None, ALU.is_equal))
        vgap()
        V(lambda e: e.scalar_tensor_tensor(expl, msk, NEG, Rt, ALU.mult,
                                           ALU.add))
        vgap()
        V(lambda e: e.reduce_max(m2t, expl.rearrange("p (t e) -> p t e", t=8),
                                 AX.X))
        vgap()
        rec.wait("scalar", rec.now("dve"))
        ex = Aop(lambda e: e.activation(expl, eqm, AF.Exp))
        rec.wait("vector", ex)
        m2b = m2t.rearrange("p (t o) -> p t o", o=1).broadcast_to([128, 8, 16])
        V(lambda e: e.tensor_tensor(msk.rearrange("p (t e) -> p t e", t=8),
                                    R3, m2b, ALU.is_ge))
        vgap()
        V(lambda e: e.tensor_tensor(gat, expl, msk, ALU.mult))
        vgap()
        V(lambda e: e.reduce_sum(den, gat.rearrange("p (t e) -> p t e", t=8),
                                 AX.X))
        vgap()
        V(lambda e: e.reciprocal(rcd, den))
        vgap()
        if stop == "rmid":
            return
        denb = rcd.rearrange("p (t o) -> p t o", o=1).broadcast_to([128, 8, 16])
        vgap()
        cfo = V(lambda e: e.tensor_tensor(
            msk.rearrange("p (t e) -> p t e", t=8),
            gat.rearrange("p (t e) -> p t e", t=8), denb, ALU.mult))
        rec.wait("tensor", cfo)
        evs = []
        for tt in range(8):
            slot = pb[2 + (tt % 2)][0:16, 0:128]
            if tt >= 2:
                rec.wait("tensor", evs[tt - 2])
            tk = T(lambda e, tt=tt, slot=slot: e.transpose(
                slot, msk[:, tt * 16:(tt + 1) * 16], identf[:]))
            rec.wait("vector", tk)
            evs.append(V(lambda e, tt=tt, slot=slot: e.tensor_copy(
                coT[:, tt * 128:(tt + 1) * 128], slot)))
        ctok = vgap()
        if stop == "rmath":
            return
        rec.wait("tensor", ctok)
        cpt = None
        for erb in range(2):
            for half in range(2):
                if cpt is not None:
                    rec.wait("tensor", cpt)
                bm = T(lambda e, erb=erb, half=half: e.matmul(
                    pb[3][:, 0:512], ebf[0:16, erb * 128:(erb + 1) * 128],
                    coT[:, half * 512:(half + 1) * 512],
                    start=True, stop=True))
                rec.wait("vector", bm)
                cpt = V(lambda e, erb=erb, half=half: e.tensor_copy(
                    cfb[:, erb * 1024 + half * 512:
                        erb * 1024 + (half + 1) * 512], pb[3][:, 0:512]))
        vgap()
        if stop == "expand":
            return
        rec.wait("sync", rec.now("pe"))
        abtok = rec.dma("sync", wsb[:, 0:2048],
                        abT_flat[ds(0, BLK)].rearrange("(p c) -> p c", p=128),
                        sem="stb0")
        abtok2 = rec.dma("sync", wsb[:, 2048:4096],
                         abT_flat[ds(BLK, BLK)].rearrange("(p c) -> p c", p=128),
                         sem="stb1")
        cftok = rec.now("dve")
        xat = None
        rec.wait("tensor", abtok)
        rec.wait("tensor", abtok2)
        for erb in range(2):
            for half in range(2):
                if xat is not None:
                    rec.wait("tensor", xat)
                mm = None
                for g in range(G):
                    mm = T(lambda e, erb=erb, g=g, half=half: e.matmul(
                        pb[2][:, 0:512],
                        wsb[:, erb * 2048 + g * 128: erb * 2048 + (g + 1) * 128],
                        stream_rhs(g, half), start=(g == 0), stop=(g == G - 1)))
                rec.wait("vector", mm)
                xat = V(lambda e, erb=erb, half=half: e.tensor_tensor(
                    xab[:, erb * 1024 + half * 512:
                        erb * 1024 + (half + 1) * 512], pb[2][:, 0:512],
                    cfb[:, erb * 1024 + half * 512:
                        erb * 1024 + (half + 1) * 512], ALU.mult))
        wxtok = rec.now("dve")
        if stop == "xa":
            return
        rec.wait("sync", wxtok)
        btok = rec.dma("sync",
                       ksb[:, 0:4096].rearrange("p (o erb j) -> p o erb j",
                                                o=16, erb=2),
                       abT_flat[ds(2 * BLK, 2 * BLK)].rearrange(
                           "(o erb p j) -> p o erb j", p=128, o=16, erb=2),
                       sem="std")
        rec.wait("tensor", btok)
        rec.wait("tensor", wxtok)
        bank_tok = [None, None]
        stage_tok = [None, None]
        rec.wait("tensor", rec.now("dve"))
        for o in range(G):
            sh = o % 2
            stg = wsa[:, sh * 2048:(sh + 1) * 2048]
            if stage_tok[sh] is not None:
                rec.wait("sync", stage_tok[sh])
            ld = rec.dma("sync", stg, packA_stage(wtA(o), 2048),
                         sem="sta" + str(sh))
            rec.wait("tensor", ld)
            mmtok = None
            for half in range(2):
                bnk = (2 * o + half) % 2
                bank = pb[bnk]
                if bank_tok[bnk] is not None:
                    rec.wait("tensor", bank_tok[bnk])
                for g in range(G):
                    T(lambda e, stg=stg, g=g, half=half, bank=bank: e.matmul(
                        bank[:, 0:512], stg[:, g * 128:(g + 1) * 128],
                        stream_rhs(g, half), start=(g == 0), stop=False))
                for erb in range(2):
                    mmtok = T(lambda e, o=o, erb=erb, half=half, bank=bank:
                              e.matmul(
                        bank[:, 0:512],
                        ksb[:, o * 256 + erb * 128: o * 256 + (erb + 1) * 128],
                        xab[:, erb * 1024 + half * 512:
                            erb * 1024 + (half + 1) * 512],
                        start=False, stop=(erb == 1)))
                rec.wait("vector", mmtok)
                bank_tok[bnk] = V(
                    lambda e, o=o, half=half, bank=bank: e.scalar_tensor_tensor(
                        ybuf[:, o * 1024 + half * 512:
                             o * 1024 + (half + 1) * 512],
                        bank[:, 0:512], fb_f[:, o:o + 1],
                        stream[:, o * 1024 + half * 512:
                               o * 1024 + (half + 1) * 512],
                        ALU.add, ALU.add))
            stage_tok[sh] = bank_tok[(2 * o + 1) % 2]

    def hid_ap(hc, half):
        if hc < 16:
            return qs[:, hc * 1024 + half * 512: hc * 1024 + (half + 1) * 512]
        if hc < 24:
            c = (hc - 16) * 1024 + half * 512
            return ksb[:, c:c + 512]
        if hc < 32:
            c = (hc - 24) * 1024 + half * 512
            return vsb[:, c:c + 512]
        c = (hc - 32) * 1024 + half * 512
        return outT[:, c:c + 512]

    def ph_ffn1(agB_tok):
        rec.wait("sync", agB_tok)
        rec.wait("sync", rec.now("pe"))
        rec.wait("scalar", rec.now("pe"))
        rec.wait("tensor", rec.now("dve"))
        rec.wait("tensor", rec.now("act"))
        sig_tok = {}
        h_toks = {}
        w1tok = [None, None]
        w2tok = [None, None]
        for hc in range(NHC):
            sh = hc % 2
            stg1 = wsa[:, sh * 2048:(sh + 1) * 2048]
            stg2 = wsb[:, sh * 2048:(sh + 1) * 2048]
            if w1tok[sh] is not None:
                rec.wait("sync", w1tok[sh])
                rec.wait("scalar", w2tok[sh])
            l1 = rec.dma("sync", stg1, packB_stage(w12B(0, hc), 2048),
                         sem="sta" + str(sh))
            l2 = rec.dma("scalar", stg2, packB_stage(w12B(1, hc), 2048),
                         sem="stb" + str(sh))
            for half in range(2):
                prev = (hc, 0) if half == 1 else (hc - 1, 1)
                if prev in sig_tok:
                    rec.wait("tensor", sig_tok[prev])
                    rec.wait("tensor", h_toks[prev])
                    rec.wait("scalar", h_toks[prev])
                rec.wait("tensor", l1)
                mm1 = None
                for g in range(G):
                    mm1 = T(lambda e, stg1=stg1, g=g, half=half: e.matmul(
                        pb[0][:, 0:512], stg1[:, g * 128:(g + 1) * 128],
                        stream_rhs(g, half), start=(g == 0), stop=(g == G - 1)))
                rec.wait("tensor", l2)
                mm2 = None
                for g in range(G):
                    mm2 = T(lambda e, stg2=stg2, g=g, half=half: e.matmul(
                        pb[1][:, 0:512], stg2[:, g * 128:(g + 1) * 128],
                        stream_rhs(g, half), start=(g == 0), stop=(g == G - 1)))
                rec.wait("scalar", mm1)
                sg = Aop(lambda e, half=half: e.activation(
                    pbf[:, half * 512:(half + 1) * 512], pb[0][:, 0:512],
                    AF.Sigmoid))
                sig_tok[(hc, half)] = sg
                rec.wait("vector", mm2)
                rec.wait("vector", sg)
                h_toks[(hc, half)] = V(
                    lambda e, hc=hc, half=half: e.tensor_tensor(
                        hid_ap(hc, half), pbf[:, half * 512:(half + 1) * 512],
                        pb[1][:, 0:512], ALU.mult))
            w1tok[sh] = h_toks[(hc, 1)]
            w2tok[sh] = h_toks[(hc, 1)]
        hfin = rec.now("dve")
        rec.wait("sync", hfin)
        rec.wait("tensor", hfin)
        w3v = outT[:, 12288:16384]
        bank_tok = [None, None]
        st_tok = [None, None]
        for o in range(G):
            for qtr in range(4):
                sh = qtr % 2
                stg = w3v[:, sh * 2048: sh * 2048 + 1408]
                if st_tok[sh] is not None:
                    rec.wait("sync", st_tok[sh])
                ld = rec.dma("sync", stg,
                             packB_stage(w3B(o), NHC * 128).rearrange(
                                 "p (hc j) -> p hc j", hc=NHC)
                             [:, qtr * 11:(qtr + 1) * 11, :],
                             sem="stc" + str(sh))
                rec.wait("tensor", ld)
                mmt = None
                for hl in range(11):
                    hc = qtr * 11 + hl
                    for half in range(2):
                        bank = pb[2 + half]
                        if hc == 0 and bank_tok[half] is not None:
                            rec.wait("tensor", bank_tok[half])
                        mmt = T(lambda e, stg=stg, hl=hl, hc=hc, half=half,
                                bank=bank: e.matmul(
                            bank[:, 0:512], stg[:, hl * 128:(hl + 1) * 128],
                            hid_ap(hc, half), start=(hc == 0),
                            stop=(hc == NHC - 1)))
                st_tok[sh] = mmt
            rec.wait("vector", mmt)
            for half in range(2):
                bank = pb[2 + half]
                bank_tok[half] = V(
                    lambda e, o=o, half=half, bank=bank: e.tensor_tensor(
                        ybuf[:, o * 1024 + half * 512:
                             o * 1024 + (half + 1) * 512],
                        bank[:, 0:512],
                        stream[:, o * 1024 + half * 512:
                               o * 1024 + (half + 1) * 512], ALU.add))
        barrier()

    def ph_store():
        evs = []
        dts = []
        for t in range(8):
            buf = xstg if t % 2 == 0 else xstg2
            if t >= 2:
                rec.wait("vector", dts[t - 2])
                rec.wait("tensor", dts[t - 2])
            for g in range(G):
                slot = pb[g % 4][:, 0:128]
                idx = t * G + g
                if idx >= 4:
                    rec.wait("tensor", evs[idx - 4])
                tk = T(lambda e, g=g, t=t, slot=slot: e.transpose(
                    slot, fview(g)[:, t * 128:(t + 1) * 128], identf[:]))
                rec.wait("vector", tk)
                evs.append(V(lambda e, g=g, slot=slot, buf=buf: e.tensor_copy(
                    buf[:, g * 128:(g + 1) * 128], slot)))
            rec.wait("sync", evs[-1])
            dts.append(rec.dma("sync", y_out[t * 128:(t + 1) * 128, :], buf,
                               sem="dst" + str(t % 2)))
        tap_srcs = {"stream": (stream, G * 1024), "ybuf": (ybuf, G * 1024),
                    "qs": (qs, G * 1024), "ksb": (ksb, 8192),
                    "vsb": (vsb, 8192), "outT": (outT, G * 1024)}
        for tname in tap_outs:
            src, w = tap_srcs[tname]
            dts.append(rec.dma("gpsimd", tap_outs[tname][:, 0:w],
                               src[:, 0:w], sem="dtp"))
        for sname in ("dst0", "dst1", "dtp"):
            rec.wait("gpsimd", rec.now(sname))

    def mark(name):
        PHASE_MARKS.append((name, rec.cnt["pe"], rec.cnt["dve"],
                            rec.cnt["act"], rec.cnt["gp"]))

    def run_phases(upto=None):
        del PHASE_MARKS[:]
        mark("consts")
        ph_consts()
        mark("prep_A")
        agA = ph_prep_A()
        mark("prep_AB")
        ph_prep_AB()
        mark("prep_B")
        agB = ph_prep_B()
        mark("load_x")
        ph_load_x()
        if upto == "prep":
            barrier(); return
        mark("proj0")
        ag0 = ph_proj_L0(agA)
        if upto == "proj0":
            barrier(); return
        mark("attn0")
        ot = ph_attention(0, ag0)
        if upto == "attn0":
            barrier(); return
        mark("wo0")
        ph_wo(0, ot)
        if upto == "wo0":
            barrier(); return
        mark("ln0")
        ph_ln(final=False)
        if upto == "ln0":
            barrier(); return
        if upto is not None and upto.startswith("ffn0:"):
            ph_ffn0(stop=upto.split(":")[1])
            barrier(); return
        mark("ffn0")
        ph_ffn0()
        if upto == "ffn0":
            barrier(); return
        mark("ln1")
        ph_ln(final=False)
        if upto == "ln1":
            barrier(); return
        mark("proj1")
        ag1 = ph_proj_L1(agB)
        if upto == "proj1":
            barrier(); return
        mark("attn1")
        ot = ph_attention(1, ag1)
        if upto == "attn1":
            barrier(); return
        mark("wo1")
        ph_wo(1, ot)
        if upto == "wo1":
            barrier(); return
        mark("ln2")
        ph_ln(final=False)
        mark("ffn1")
        ph_ffn1(agB)
        if upto == "ffn1":
            barrier(); return
        mark("ln3")
        ph_ln(final=True)
        mark("store")
        ph_store()
        mark("end")

    run_phases(upto)
    if upto is not None:
        for sname in ("dst0", "dst1", "dtp"):
            rec.wait("gpsimd", rec.now(sname))

    with nc.Block() as block:
        for engname, deco in (("tensor", block.tensor), ("vector", block.vector),
                              ("scalar", block.scalar), ("gpsimd", block.gpsimd),
                              ("sync", block.sync)):
            def runner(e, ops=rec.ops[engname]):
                for f in ops:
                    f(e)
            deco(runner)

    ctx.close()
    return nc


_NC_CACHE = {}


def _get_nc(tapkey=()):
    if tapkey not in _NC_CACHE:
        _NC_CACHE[tapkey] = build_program(taps=tapkey)
    return _NC_CACHE[tapkey]


def make_in_maps(inputs):
    ins = {k: np.ascontiguousarray(np.asarray(v)) for k, v in inputs.items()}
    x = ins["L0_attn_x"].reshape(S, D)
    in_maps = []
    for c in range(R):
        m = {}
        m["x_own"] = np.concatenate(
            [x[c * 512:(c + 1) * 512], x[(15 - c) * 512:(16 - c) * 512]], 0)
        m["p_Wq"] = ins["L0_attn_Wq"][:, c * 256:(c + 1) * 256]
        m["p_Wk"] = ins["L0_attn_Wk"][:, c * 256:(c + 1) * 256]
        m["p_Wv"] = ins["L0_attn_Wv"][:, c * 256:(c + 1) * 256]
        m["p_Wo"] = ins["L0_attn_Wo"][:, c * 256:(c + 1) * 256]
        m["p_wq1"] = ins["L1_attn_wq_w"][:, c * 256:(c + 1) * 256]
        m["p_wk1"] = ins["L1_attn_wk_w"][:, c * 256:(c + 1) * 256]
        m["p_wv1"] = ins["L1_attn_wv_w"][:, c * 256:(c + 1) * 256]
        m["p_wo1"] = ins["L1_attn_wo_w"][:, c * 256:(c + 1) * 256]
        if c < 4:
            m["p_wl"] = ins["L1_attn_wkl_w"][:, c * 128:(c + 1) * 128]
        else:
            m["p_wl"] = ins["L1_attn_wvl_w"][:, (c - 4) * 128:(c - 3) * 128]
        m["p_WT"] = ins["L0_ffn_W"][c * 256:(c + 1) * 256, :]
        st, nreal = w12_rank_blocks(c)
        w1s = ins["L1_ffn_w1"][st * 128:(st + nreal) * 128, :]
        w2s = ins["L1_ffn_w2"][st * 128:(st + nreal) * 128, :]
        if nreal < W1B:
            pad = np.zeros(((W1B - nreal) * 128, D), np.float32)
            w1s = np.concatenate([w1s, pad], 0)
            w2s = np.concatenate([w2s, pad], 0)
        m["p_w1"] = w1s
        m["p_w2"] = w2s
        m["p_w3"] = ins["L1_ffn_w3"][c * 256:(c + 1) * 256, :]
        m["bq"] = ins["L0_attn_bq"]
        m["bo"] = ins["L0_attn_bo"]
        m["bv"] = ins["L0_attn_bv"]
        m["f_bias"] = ins["L0_ffn_bias"]
        m["rw1"] = ins["L0_ffn_rw1"]
        m["rb1"] = ins["L0_ffn_rb1"]
        m["rw2"] = ins["L0_ffn_rw2"]
        m["rb2"] = np.ascontiguousarray(ins["L0_ffn_rb2"])
        m["gates"] = np.ascontiguousarray(ins["L0_ffn_gates"])
        m["A_in"] = ins["L0_ffn_A"].reshape(256, D)
        m["B_in"] = ins["L0_ffn_B"]
        m["wq_b"] = ins["L1_attn_wq_b"]
        m["wv_b"] = ins["L1_attn_wv_b"]
        m["wvl_b"] = ins["L1_attn_wvl_b"]
        m["upk_w"] = ins["L1_attn_upk_w"]
        m["upv_w"] = ins["L1_attn_upv_w"]
        m["upv_b"] = ins["L1_attn_upv_b"]
        m["wo_b"] = ins["L1_attn_wo_b"]
        gx = np.zeros((128, GIDX_N), np.int32)
        P = np.arange(128)
        rows_l = [KV0_ROWS, KV1_ROWS]
        for L in (0, 1):
            rws = rows_l[L]
            for hi in (0, 1):
                h = c if hi == 0 else 15 - c
                for mm2 in range(16):
                    rr, half = rank_of_chunk(mm2)
                    kb = KVBASE[L]["k"]
                    vb = KVBASE[L]["v"]
                    gx[:, GIDX_COLS[("k", L, hi, mm2)]] = \
                        (rr * rws + kb + h * 128 + P) * 2 + half
                    gx[:, GIDX_COLS[("v", L, hi, mm2)]] = \
                        (rr * rws + vb) * 2 + P * 32 + half * 16 + h
                    if L == 1:
                        qb = KVBASE[L]["q"]
                        gx[:, GIDX_COLS[("q", L, hi, mm2)]] = \
                            (rr * rws + qb + h * 128 + P) * 2 + half
                gx[:, GIDX_COLS[("cmy", L, hi)]] = h * 128 + P
            for jk in range(16):
                for cc2 in (0, 1):
                    chunk = c if cc2 == 0 else 15 - c
                    row = (jk // 2) * 256 + (jk % 2) * 128 + P
                    gx[:, GIDX_COLS[("ao", L, jk, cc2)]] = row * 16 + chunk
        m["gidx"] = gx
        in_maps.append({k: np.ascontiguousarray(v) for k, v in m.items()})
    return in_maps


def kernel(**inputs):
    nc = _get_nc(())
    in_maps = make_in_maps(inputs)
    res = run_bass_kernel_spmd(nc, in_maps, core_ids=list(range(R)))
    out = np.zeros((S, D), np.float32)
    for c in range(R):
        y = np.asarray(res.results[c]["y_out"])
        out[c * 512:(c + 1) * 512] = y[0:512]
        out[(15 - c) * 512:(16 - c) * 512] = y[512:1024]
    return out.reshape(1, S, D)


# version 04
# speedup vs baseline: 1.6646x; 1.0014x over previous
import numpy as np
from collections import defaultdict
from contextlib import ExitStack

import concourse.bass as bass
import concourse.mybir as mybir
from concourse.bass_utils import run_bass_kernel_spmd

F32 = mybir.dt.float32
BF16 = mybir.dt.bfloat16
AF = mybir.ActivationFunctionType
ALU = mybir.AluOpType
AX = mybir.AxisListType
ds = bass.ds

R = 8
D = 2048
S = 8192
NTOK = 1024
NH = 16
DL = 512
HID = 5632
NHC = 44
W1B = 6
G = 16
SCALE = float(1.0 / np.float32(np.sqrt(np.float32(128.0))))
NEG = -1.0e30
EPS_LN = 1.0e-5

NAT_MATS = ["Wq", "Wk", "Wv", "Wo", "wq1", "wk1", "wv1", "wo1"]
BLK = 128 * G * 128
W3BLK = 128 * NHC * 128

_packA_off, _oa = {}, 0
for _m in NAT_MATS:
    _packA_off[_m] = _oa
    _oa += 2 * BLK
_packA_off["WL"] = _oa; _oa += BLK
_packA_off["WT"] = _oa; _oa += 2 * BLK
PACKA_ELEMS = _oa
PACKA_ROWS = PACKA_ELEMS // 1024

_packB_off = {"W1Q": 0, "W2Q": W1B * BLK, "W3Q": 2 * W1B * BLK}
PACKB_ELEMS = 2 * W1B * BLK + 2 * W3BLK
PACKB_ROWS = PACKB_ELEMS // 1024


def natA(mat, o):
    r, lo = o // 2, o % 2
    return r * PACKA_ELEMS + _packA_off[mat] + lo * BLK


def wlA(which, o):
    return (o + 4 * which) * PACKA_ELEMS + _packA_off["WL"]


def wtA(o):
    r, lo = o // 2, o % 2
    return r * PACKA_ELEMS + _packA_off["WT"] + lo * BLK


def w12B(which, hc):
    if hc < 24:
        r, hb = hc // 6, hc % 6
    else:
        r, hb = 4 + (hc - 24) // 5, (hc - 24) % 5
    return r * PACKB_ELEMS + _packB_off["W1Q" if which == 0 else "W2Q"] + hb * BLK


def w12_rank_blocks(r):
    return (6 * r, 6) if r < 4 else (24 + 5 * (r - 4), 5)


def w3B(o):
    r, lo = o // 2, o % 2
    return r * PACKB_ELEMS + _packB_off["W3Q"] + lo * W3BLK


def rank_of_chunk(m):
    return (m, 0) if m < 8 else (15 - m, 1)


GIDX_COLS = {}
_gc = 0


def _gcol(key):
    global _gc
    if key not in GIDX_COLS:
        GIDX_COLS[key] = _gc
        _gc += 1
    return GIDX_COLS[key]


for _L in (0, 1):
    for _hi in (0, 1):
        for _m in range(16):
            _gcol(("k", _L, _hi, _m))
            _gcol(("v", _L, _hi, _m))
            if _L == 1:
                _gcol(("q", _L, _hi, _m))
        _gcol(("cmy", _L, _hi))
    for _jk in range(16):
        for _cc in (0, 1):
            _gcol(("ao", _L, _jk, _cc))
GIDX_N = _gc

KV0_ROWS = 2048 + 2048 + 2
KV1_ROWS = 2048 + 2048 + 2048 + 2
KVBASE = [
    {"q": None, "k": 0, "v": 2048, "c": 4096, "rows": KV0_ROWS},
    {"q": 0, "k": 2048, "v": 4096, "c": 6144, "rows": KV1_ROWS},
]


def ghead(jk):
    j, k = jk // 2, jk % 2
    return j if k == 0 else 15 - j


class Rec:
    ENGS = ("tensor", "vector", "scalar", "gpsimd", "sync")

    def __init__(self, sems):
        self.sems = sems
        self.ops = {e: [] for e in self.ENGS}
        self.cnt = defaultdict(int)

    def op(self, eng, fn, inc=None, n=1):
        if inc is None:
            self.ops[eng].append(fn)
            return None
        self.cnt[inc] += n
        tok = (inc, self.cnt[inc])
        sem = self.sems[inc]
        self.ops[eng].append(lambda e, fn=fn, s=sem, k=n: fn(e).then_inc(s, k))
        return tok

    def wait(self, eng, tok):
        if tok is None:
            return
        s, v = self.sems[tok[0]], tok[1]
        self.ops[eng].append(lambda e, s=s, v=v: e.wait_ge(s, v))

    def now(self, sem):
        return (sem, self.cnt[sem])

    def dma(self, eng, out, in_, sem=None, **kw):
        if sem is None:
            sem = {"sync": "dsp", "scalar": "dact", "gpsimd": "dgp"}[eng]
        return self.op(eng,
                       lambda e, o=out, i=in_, kw=kw: e.dma_start(
                           out=o, in_=i, **kw),
                       inc=sem, n=16)


PHASE_MARKS = []


def build_program(taps=(), upto=None):
    nc = bass.Bass("TRN2", target_bir_lowering=False, debug=False)

    def din(name, shape, dtype=F32):
        return nc.dram_tensor(name, list(shape), dtype, kind="ExternalInput").ap()

    x_own = din("x_own", (NTOK, D))
    p_nat = {m: din("p_" + m, (D, 256)) for m in NAT_MATS}
    p_wl = din("p_wl", (D, 128))
    p_WT = din("p_WT", (256, D))
    p_w1 = din("p_w1", (W1B * 128, D))
    p_w2 = din("p_w2", (W1B * 128, D))
    p_w3 = din("p_w3", (256, HID))
    bq_i = din("bq", (D,))
    bo_i = din("bo", (D,))
    bv_i = din("bv", (D,))
    fb_i = din("f_bias", (D,))
    rw1_i = din("rw1", (64, D))
    rb1_i = din("rb1", (64,))
    rw2_i = din("rw2", (32, 64))
    rb2_i = din("rb2", (32,))
    gates_i = din("gates", (16,))
    A_i = din("A_in", (256, D))
    B_i = din("B_in", (NH, D, 16))
    wqb_i = din("wq_b", (D,))
    wvb_i = din("wv_b", (D,))
    wvlb_i = din("wvl_b", (DL,))
    upk_i = din("upk_w", (NH, DL, 128))
    upv_i = din("upv_w", (NH, DL, 128))
    upvb_i = din("upv_b", (NH, 128))
    wob_i = din("wo_b", (D,))
    gidx_i = din("gidx", (128, GIDX_N), mybir.dt.int32)

    y_out = nc.dram_tensor("y_out", [NTOK, D], F32, kind="ExternalOutput").ap()
    tap_outs = {}
    for tname in taps:
        tap_outs[tname] = nc.dram_tensor(
            "tap_" + tname, [128, G * 1024], F32, kind="ExternalOutput").ap()

    packA_in = nc.dram_tensor("packA_in", [PACKA_ROWS, 1024], BF16).ap()
    packA_out = nc.dram_tensor("packA_out", [R * PACKA_ROWS, 1024], BF16,
                               addr_space="Shared").ap()
    packB_in = nc.dram_tensor("packB_in", [PACKB_ROWS, 1024], BF16).ap()
    packB_out = nc.dram_tensor("packB_out", [R * PACKB_ROWS, 1024], BF16,
                               addr_space="Shared").ap()
    gs_in = nc.dram_tensor("gs_in", [1, 16], F32).ap()
    gs_out = nc.dram_tensor("gs_out", [1, 16], F32).ap()
    kv_in = [nc.dram_tensor("kv0_in", [KV0_ROWS, 1024], BF16).ap(),
             nc.dram_tensor("kv1_in", [KV1_ROWS, 1024], BF16).ap()]
    kv_out = [nc.dram_tensor("kv0_out", [R * KV0_ROWS, 1024], BF16,
                             addr_space="Shared").ap(),
              nc.dram_tensor("kv1_out", [R * KV1_ROWS, 1024], BF16,
                             addr_space="Shared").ap()]
    ao_in = nc.dram_tensor("ao_in", [256, S], BF16).ap()
    ao_out = nc.dram_tensor("ao_out", [D, S], BF16,
                            addr_space="Shared").ap()
    abT = nc.dram_tensor("abT", [1536, 1024], BF16).ap()
    cscr = nc.dram_tensor("cscr", [2, 1024], BF16).ap()

    ctx = ExitStack()
    e = ctx.enter_context

    identf = e(nc.sbuf_tensor("identf", [128, 128], F32))
    identb = e(nc.sbuf_tensor("identb", [128, 128], BF16))
    stream = e(nc.sbuf_tensor("stream", [128, G * 1024], BF16))
    ybuf = e(nc.sbuf_tensor("ybuf", [128, G * 1024], BF16))
    qs = e(nc.sbuf_tensor("qs", [128, G * 1024], BF16))
    ksb = e(nc.sbuf_tensor("ksb", [128, 8192], BF16))
    vsb = e(nc.sbuf_tensor("vsb", [128, 8192], BF16))
    outT = e(nc.sbuf_tensor("outT", [128, G * 1024], BF16))
    ppre = e(nc.sbuf_tensor("ppre", [128, 1024], F32))
    pbf = e(nc.sbuf_tensor("pbf", [128, 1024], BF16))
    ascr = e(nc.sbuf_tensor("ascr", [128, 2560], F32))
    t1 = e(nc.sbuf_tensor("t1", [128, 1024], F32))
    wsa = e(nc.sbuf_tensor("wsa", [128, 4096], BF16))
    wsb = e(nc.sbuf_tensor("wsb", [128, 4096], BF16))
    absq = e(nc.sbuf_tensor("absq", [128, 512], BF16))
    smf = e(nc.sbuf_tensor("smf", [128, 512], F32))
    smb = e(nc.sbuf_tensor("smb", [128, 128], BF16))
    ebf = e(nc.sbuf_tensor("ebf", [128, 256], BF16))
    rwT = e(nc.sbuf_tensor("rwT", [128, 1088], BF16))
    gix = e(nc.sbuf_tensor("gix", [128, GIDX_N], mybir.dt.int32))

    xstg = wsa.ap().bitcast(F32)
    xstg2 = wsb.ap().bitcast(F32)
    qs32 = qs.ap().bitcast(F32)
    ksb32 = ksb.ap().bitcast(F32)
    vsb32 = vsb.ap().bitcast(F32)
    ysqv = pbf.ap()
    kvstg = ppre.ap().bitcast(BF16)
    outT32 = outT.ap().bitcast(F32)
    t1b = t1.ap().bitcast(BF16)

    ck = smf[:, 0:16]
    bq_f = smf[:, 16:32]
    wqb_f = smf[:, 32:48]
    cb0 = smf[:, 48:64]
    cb1 = smf[:, 64:80]
    fb_f = smf[:, 80:96]
    cc1 = smf[:, 96:112]
    v1c = smf[:, 112:116]
    wvlb_f = smf[:, 116:120]
    upvb_f = smf[:, 120:136]
    bv_f = smf[:, 136:152]
    rgv = smf[:, 152:155]
    rgn = smf[:, 155:158]
    rb1_f = smf[:, 158:159]
    onescol_f = smf[:, 160:161]
    gacc = smf[:, 161:167]
    gtmp = smf[:, 167:168]
    gapd = smf[0:1, 168:176]
    gapsrc = smf[0:1, 176:184]
    ones_row = smf[0:1, 184:312]
    grow = smf[0:1, 312:328]
    g3a = smf[0:1, 328:331]
    g3b = smf[0:1, 331:334]
    g3c = smf[0:1, 334:337]
    bo_f = smf[:, 352:368]
    wob_f = smf[:, 368:384]
    wvb_f = smf[:, 384:400]
    rb2_row = smf[0:1, 400:416]
    gates_row = smf[0:1, 416:432]
    rb2g_row = smf[0:1, 432:448]

    cfeat = smb[:, 0:16]
    cmy = smb[:, 16:18]
    ones_b = smb[:, 18:19]
    cvb = smb[:, 20:36]
    wvb_b = smb[:, 36:52]
    v1cb = smb[:, 52:56]

    mhb = ascr[:, 0:512]
    rbb = ascr[:, 512:1024]
    mrow = ascr[0:1, 2048:2560]
    rstd_row = ascr[0:1, 0:1024]
    m2_row = ascr[0:1, 1024:2048]
    muK = ascr[0:1, 2048:2560]

    pband = [e(nc.psum_tensor(f"pb{i}", [128, 512], F32)) for i in range(8)]
    pb = [p.ap() for p in pband]
    pbb = [p.ap().bitcast(BF16) for p in pband]

    sems = {n: e(nc.semaphore(n))
            for n in ("pe", "dve", "act", "gp", "dsp", "dact", "dgp", "cc",
                      "dcn", "sta0", "sta1", "stb0", "stb1", "stc0", "stc1",
                      "std", "kvo0", "kvo1", "dup", "dld",
                      "dao", "dcm", "dkv", "dst0", "dst1", "dtp",
                      "cA", "cR", "cB", "ck0", "ck1", "cao")
            + tuple(f"dl{m}" for m in range(16))}
    rec = Rec(sems)

    def T(fn):
        return rec.op("tensor", fn, inc="pe")

    def V(fn):
        return rec.op("vector", fn, inc="dve")

    def Aop(fn):
        return rec.op("scalar", fn, inc="act")

    def Gop(fn):
        return rec.op("gpsimd", fn, inc="gp")

    def vgap():
        t = rec.now("dve")
        rec.wait("vector", t)
        return t

    def gwait():
        t = rec.now("gp")
        rec.wait("gpsimd", t)
        return t

    def barrier():
        toks = [rec.now(s) for s in sems]
        for eng in Rec.ENGS:
            for t in toks:
                rec.wait(eng, t)


    pA_flat = packA_in.rearrange("r c -> (r c)")
    pB_flat = packB_in.rearrange("r c -> (r c)")
    pAo_flat = packA_out.rearrange("r c -> (r c)")
    pBo_flat = packB_out.rearrange("r c -> (r c)")
    abT_flat = abT.rearrange("r c -> (r c)")
    kvi_flat = [k.rearrange("r c -> (r c)") for k in kv_in]
    kvo_flat = [k.rearrange("r c -> (r c)") for k in kv_out]
    kvo512 = [k.rearrange("r c -> (r c)").rearrange("(n b) -> n b", b=512)
              for k in kv_out]
    ao512 = ao_out.rearrange("r c -> (r c)").rearrange("(n b) -> n b", b=512)

    GATHER_SEM = {"k": "dld", "v": "dld", "q": "dld", "ao": "dao",
                  "cmy": "dcm"}

    def gather(out_ap, src512, key, sem=None):
        col = GIDX_COLS[key]
        return rec.op("gpsimd", lambda e, o=out_ap, s=src512, col=col:
                      e.indirect_dma_start(
                          out=o, out_offset=None, in_=s,
                          in_offset=bass.IndirectOffsetOnAxis(
                              gix[:, col:col + 1], 0)),
                      inc=sem or GATHER_SEM[key[0]], n=16)

    def packA_stage(elem_off, n):
        return pAo_flat[ds(elem_off, 128 * n)].rearrange("(p c) -> p c", p=128)

    def packB_stage(elem_off, n):
        return pBo_flat[ds(elem_off, 128 * n)].rearrange("(p c) -> p c", p=128)

    def ph_consts():
        def Gs(fn):
            t = Gop(fn)
            rec.wait("gpsimd", t)
            return t
        Gs(lambda e: e.memset(identf[:], 0.0))
        Gs(lambda e: e.affine_select(out=identf[:], in_=identf[:],
                                      compare_op=ALU.not_equal, fill=1.0, base=0,
                                      pattern=[[-1, 128]], channel_multiplier=1))
        Gs(lambda e: e.memset(identb[:], 0.0))
        Gs(lambda e: e.affine_select(out=identb[:], in_=identb[:],
                                      compare_op=ALU.not_equal, fill=1.0, base=0,
                                      pattern=[[-1, 128]], channel_multiplier=1))
        Gs(lambda e: e.memset(ones_b, 1.0))
        Gs(lambda e: e.memset(onescol_f, 1.0))
        Gs(lambda e: e.memset(ones_row, 1.0))
        Gs(lambda e: e.memset(gapsrc, 0.0))
        Gs(lambda e: e.memset(ebf[:], 0.0))
        for blk in range(2):
            sl = ebf[0:16, blk * 128:(blk + 1) * 128]
            Gs(lambda e, sl=sl: e.memset(sl, 1.0))
            Gs(lambda e, sl=sl, blk=blk: e.affine_select(
                out=sl, in_=sl, compare_op=ALU.is_ge, fill=0.0,
                base=128 * blk, pattern=[[1, 128]], channel_multiplier=-16))
            Gs(lambda e, sl=sl, blk=blk: e.affine_select(
                out=sl, in_=sl, compare_op=ALU.is_ge, fill=0.0,
                base=15 - 128 * blk, pattern=[[-1, 128]],
                channel_multiplier=16))
        for dst, src, p in ((bq_f, bq_i, 128), (wqb_f, wqb_i, 128),
                            (bv_f, bv_i, 128), (fb_f, fb_i, 128),
                            (bo_f, bo_i, 128), (wob_f, wob_i, 128),
                            (wvb_f, wvb_i, 128),
                            (upvb_f, upvb_i.rearrange("h d -> (h d)"), 128),
                            (wvlb_f, wvlb_i, 128),
                            (rb1_f, rb1_i, 64)):
            gg = src.shape[0] // p
            rec.dma("gpsimd", dst[0:p, 0:gg],
                    src.rearrange("(g p) -> p g", p=p), sem="dcn",
                    allow_slow_non_contiguous=True)
        rec.dma("gpsimd", rb2_row, rb2_i[ds(0, 16)].rearrange(
            "(o c) -> o c", o=1), sem="dcn")
        rec.dma("gpsimd", gates_row, gates_i.rearrange("(o c) -> o c", o=1),
                sem="dcn")
        rec.dma("gpsimd", gix[:], gidx_i[:, :], sem="dcn")
        gd = rec.now("dcn")
        rec.wait("vector", gd)
        V(lambda e: e.tensor_copy(wvb_b, wvb_f))
        V(lambda e: e.tensor_copy(cvb, bv_f))
        V(lambda e: e.tensor_tensor(rb2g_row, rb2_row, gates_row, ALU.add))
        vgap()
        V(lambda e: e.tensor_copy(rwT[64:65, 1024:1040], rb2g_row))
        barrier()

    def nat_prep_one(src_ap, dst_elems, nblocks):
        last = None
        for lo in range(nblocks):
            ld = rec.dma("sync",
                         xstg2[:, 0:2048].rearrange("p (g c) -> p g c", g=G),
                         src_ap[:, lo * 128:(lo + 1) * 128].rearrange(
                             "(g p) c -> p g c", p=128), sem="stb0")
            rec.wait("vector", ld)
            if last is not None:
                rec.wait("vector", last)
            cv = V(lambda e: e.tensor_copy(wsa[:, 0:2048], xstg2[:, 0:2048]))
            rec.wait("gpsimd", cv)
            dst = pA_flat[ds(dst_elems + lo * BLK, BLK)].rearrange(
                "(p c) -> p c", p=128)
            last = rec.dma("gpsimd", dst, wsa[:, 0:2048])
            rec.wait("gpsimd", last)
            rec.wait("sync", last)
        return last

    def ph_prep_A():
        for m in NAT_MATS:
            tok = nat_prep_one(p_nat[m], _packA_off[m], 2)
            rec.wait("sync", tok)
            rec.wait("vector", tok)
        tok = nat_prep_one(p_wl, _packA_off["WL"], 1)
        rec.wait("sync", tok)
        rec.wait("vector", tok)

        for lo in range(2):
            ld = rec.dma("sync", xstg2[:, 0:2048],
                         p_WT[lo * 128:(lo + 1) * 128, :], sem="stb0")
            rec.wait("vector", ld)
            cv = V(lambda e: e.tensor_copy(kvstg[:, 0:2048], xstg2[:, 0:2048]))
            rec.wait("tensor", cv)
            evs = []
            for g in range(G):
                slot = pbb[g % 4][:, 0:128]
                if g >= 4:
                    rec.wait("tensor", evs[g - 4])
                tt = T(lambda e, g=g, slot=slot: e.transpose(
                    slot, kvstg[:, g * 128:(g + 1) * 128], identb[:]))
                rec.wait("vector", tt)
                evs.append(V(lambda e, g=g, slot=slot: e.tensor_copy(
                    wsa[:, 2048 + g * 128:2048 + (g + 1) * 128], slot)))
            rec.wait("gpsimd", evs[-1])
            dst = pA_flat[ds(_packA_off["WT"] + lo * BLK, BLK)].rearrange(
                "(p c) -> p c", p=128)
            dt = rec.dma("gpsimd", dst, wsa[:, 2048:4096])
            rec.wait("gpsimd", dt)
            rec.wait("sync", dt)
            rec.wait("vector", dt)

        gtok = rec.now("dgp")
        rec.wait("gpsimd", gtok)
        agA = rec.op("gpsimd", lambda e: e.collective_compute(
            "AllGather", ALU.bypass, replica_groups=[list(range(R))],
            ins=[packA_in[:, :]], outs=[packA_out[:, :]]), inc="cA")
        return agA

    def ph_prep_AB():
        for erb in range(2):
            ld = rec.dma("sync", xstg2[:, 0:2048],
                         A_i[erb * 128:(erb + 1) * 128, :], sem="stb0")
            rec.wait("vector", ld)
            cv = V(lambda e: e.tensor_copy(kvstg[:, 0:2048], xstg2[:, 0:2048]))
            rec.wait("tensor", cv)
            evs = []
            for g in range(G):
                slot = pbb[g % 4][:, 0:128]
                if g >= 4:
                    rec.wait("tensor", evs[g - 4])
                tt = T(lambda e, g=g, slot=slot: e.transpose(
                    slot, kvstg[:, g * 128:(g + 1) * 128], identb[:]))
                rec.wait("vector", tt)
                evs.append(V(lambda e, g=g, slot=slot: e.tensor_copy(
                    wsa[:, 2048 + g * 128:2048 + (g + 1) * 128], slot)))
            rec.wait("gpsimd", evs[-1])
            dst = abT_flat[ds(erb * BLK, BLK)].rearrange("(p c) -> p c", p=128)
            dt = rec.dma("gpsimd", dst, wsa[:, 2048:4096])
            rec.wait("gpsimd", dt)
            rec.wait("sync", dt)
            rec.wait("vector", dt)

        Bf = B_i.rearrange("e o r -> (e o r)")
        for erb in range(2):
            srcv = Bf[ds(erb * 8 * D * 16, 8 * D * 16)].rearrange(
                "(e ob p r) -> p ob e r", e=8, ob=16, p=128)
            for ob in range(G):
                ld = rec.dma("sync", xstg2[:, ob * 128:(ob + 1) * 128]
                             .rearrange("p (e r) -> p e r", e=8),
                             srcv[:, ob, :, :], sem="stb0")
            rec.wait("vector", ld)
            cv = V(lambda e: e.tensor_copy(kvstg[:, 0:2048], xstg2[:, 0:2048]))
            rec.wait("tensor", cv)
            evs = []
            for ob in range(G):
                slot = pbb[ob % 4][:, 0:128]
                if ob >= 4:
                    rec.wait("tensor", evs[ob - 4])
                src_t = kvstg[:, ob * 128:(ob + 1) * 128]
                tt = T(lambda e, s=src_t, slot=slot: e.transpose(
                    slot, s, identb[:]))
                rec.wait("vector", tt)
                evs.append(V(lambda e, ob=ob, slot=slot: e.tensor_copy(
                    wsa[:, 2048 + ob * 128:2048 + (ob + 1) * 128], slot)))
            rec.wait("gpsimd", evs[-1])
            for ob in range(G):
                dst = abT_flat[ds(2 * BLK + ob * 32768 + erb * 16384, 16384)] \
                    .rearrange("(p c) -> p c", p=128)
                dt = rec.dma("gpsimd", dst,
                             wsa[:, 2048 + ob * 128:2048 + (ob + 1) * 128])
            rec.wait("gpsimd", dt)
            rec.wait("sync", dt)
            rec.wait("vector", dt)

        ld = rec.dma("sync", xstg2[0:64, 0:2048], rw1_i[:, :], sem="stb0")
        rec.wait("vector", ld)
        cv = V(lambda e: e.tensor_copy(kvstg[0:64, 0:2048], xstg2[0:64, 0:2048]))
        rec.wait("tensor", cv)
        evs = []
        for g in range(G):
            slot = pbb[g % 4][:, 0:128]
            if g >= 4:
                rec.wait("tensor", evs[g - 4])
            tt = T(lambda e, g=g, slot=slot: e.transpose(
                slot[:, 0:64], kvstg[0:64, g * 128:(g + 1) * 128],
                identb[0:64, 0:64]))
            rec.wait("vector", tt)
            evs.append(V(lambda e, g=g, slot=slot: e.tensor_copy(
                rwT[:, g * 64:(g + 1) * 64], slot[:, 0:64])))
        rec.wait("sync", cv)
        ld2 = rec.dma("sync", xstg2[0:32, 0:64], rw2_i[:, :], sem="stb0")
        rec.wait("vector", ld2)
        cv2 = V(lambda e: e.tensor_copy(kvstg[0:32, 0:64], xstg2[0:32, 0:64]))
        rec.wait("tensor", cv2)
        tt = T(lambda e: e.transpose(pbb[0][0:64, 256:288],
                                     kvstg[0:32, 0:64], identb[0:32, 0:32]))
        rec.wait("vector", tt)
        V(lambda e: e.tensor_copy(rwT[0:64, 1024:1056], pbb[0][0:64, 256:288]))
        rec.wait("sync", rec.now("dve"))

    def ph_prep_B():
        V(lambda e: e.memset(gacc, 0.0))
        blocks = []
        for b in range(W1B):
            blocks.append((0, p_w1[b * 128:(b + 1) * 128, :], D))
        for b in range(W1B):
            blocks.append((1, p_w2[b * 128:(b + 1) * 128, :], D))
        for lo in range(2):
            for piece in range(4):
                blocks.append((2, p_w3[lo * 128:(lo + 1) * 128,
                                       piece * 1408:(piece + 1) * 1408], 1408))
        cnt_wi = [0, 0, 0]
        rtok = None
        for wi, src, CW in blocks:
            if rtok is not None:
                rec.wait("sync", rtok)
            ld = rec.dma("sync", ksb32[:, 0:CW], src, sem="stb0")
            rec.wait("vector", ld)
            V(lambda e, CW=CW: e.reduce_sum(gtmp, ksb32[:, 0:CW], AX.X,
                                            apply_absolute_value=True))
            rtok = vgap()
            a = cnt_wi[wi] % 2
            b = (cnt_wi[wi] + 1) % 2
            V(lambda e, wi=wi, a=a, b=b: e.tensor_tensor(
                gacc[:, wi * 2 + b:wi * 2 + b + 1],
                gacc[:, wi * 2 + a:wi * 2 + a + 1], gtmp, ALU.add))
            vgap()
            cnt_wi[wi] += 1
        assert all(c % 2 == 0 for c in cnt_wi), cnt_wi
        pcom = vgap()
        rec.wait("tensor", pcom)
        gfin = gacc.rearrange("p (w t) -> p w t", t=2)[:, :, 0]
        mm = T(lambda e, gfin=gfin: e.matmul(pb[4][0:1, 0:3], onescol_f, gfin,
                                             start=True, stop=True))
        rec.wait("vector", mm)
        V(lambda e: e.memset(grow, 0.0))
        vgap()
        V(lambda e: e.tensor_copy(grow[:, 0:3], pb[4][0:1, 0:3]))
        gdone = vgap()
        rec.wait("gpsimd", gdone)
        d1 = rec.dma("gpsimd", gs_in[:, :], grow[:, 0:16])
        rec.wait("gpsimd", d1)
        ar = rec.op("gpsimd", lambda e: e.collective_compute(
            "AllReduce", ALU.add, replica_groups=[list(range(R))],
            ins=[gs_in[:, :]], outs=[gs_out[:, :]]), inc="cR")
        rec.wait("gpsimd", ar)
        d2 = rec.dma("gpsimd", grow[:, 0:16], gs_out[:, :])
        rec.wait("gpsimd", d2)
        rec.wait("vector", d2)
        NW = float(HID) * float(D)
        V(lambda e: e.tensor_scalar(g3a, grow[:, 0:3], 1.0 / NW, 1.0e-8,
                                    ALU.mult, ALU.add))
        vgap()
        V(lambda e: e.reciprocal(g3b, g3a))
        vgap()
        V(lambda e: e.tensor_scalar(g3c, g3b, -1.0, None, ALU.mult))
        gk = vgap()
        rec.wait("tensor", gk)
        mm1 = T(lambda e: e.matmul(pb[4][:, 0:3], ones_row, g3b,
                                   start=True, stop=True))
        mm2 = T(lambda e: e.matmul(pb[5][:, 0:3], ones_row, g3c,
                                   start=True, stop=True))
        rec.wait("vector", mm2)
        V(lambda e: e.tensor_copy(rgv, pb[4][:, 0:3]))
        cpr = V(lambda e: e.tensor_copy(rgn, pb[5][:, 0:3]))
        rec.wait("sync", cpr)

        def quant_into_vsb(src_slice, wi, CW):
            ld = rec.dma("sync", ksb32[:, 0:CW], src_slice, sem="stb0")
            rec.wait("vector", ld)
            vgap()
            V(lambda e, CW=CW, wi=wi: e.tensor_scalar(
                vsb[:, 0:CW], ksb32[:, 0:CW], rgv[:, wi:wi + 1], 0.5,
                ALU.mult, ALU.is_ge))
            V(lambda e, CW=CW, wi=wi: e.tensor_scalar(
                vsb[:, 2048:2048 + CW], ksb32[:, 0:CW], rgn[:, wi:wi + 1], 0.5,
                ALU.mult, ALU.is_ge))
            vgap()
            return V(lambda e, CW=CW: e.tensor_tensor(
                vsb[:, 4096:4096 + CW], vsb[:, 0:CW], vsb[:, 2048:2048 + CW],
                ALU.subtract))

        for wi, src in ((0, p_w1), (1, p_w2)):
            for b in range(W1B):
                qv = quant_into_vsb(src[b * 128:(b + 1) * 128, :], wi, D)
                rec.wait("tensor", qv)
                evs = []
                for g in range(G):
                    slot = pbb[4 + g % 4][:, 256:384]
                    if g >= 4:
                        rec.wait("tensor", evs[g - 4])
                    tt = T(lambda e, g=g, slot=slot: e.transpose(
                        slot, vsb[:, 4096 + g * 128:4096 + (g + 1) * 128],
                        identb[:]))
                    rec.wait("vector", tt)
                    evs.append(V(lambda e, g=g, slot=slot: e.tensor_copy(
                        vsb[:, 6144 + g * 128:6144 + (g + 1) * 128], slot)))
                rec.wait("gpsimd", evs[-1])
                dst = pB_flat[ds(_packB_off["W1Q" if wi == 0 else "W2Q"]
                                 + b * BLK, BLK)].rearrange("(p c) -> p c", p=128)
                dt = rec.dma("gpsimd", dst, vsb[:, 6144:8192])
                rec.wait("gpsimd", dt)
                rec.wait("sync", dt)
                rec.wait("vector", dt)

        for lo in range(2):
            for piece in range(4):
                cw = 11 * 128
                qv = quant_into_vsb(
                    p_w3[lo * 128:(lo + 1) * 128,
                         piece * cw:(piece + 1) * cw], 2, cw)
                rec.wait("tensor", qv)
                evs = []
                for t in range(11):
                    slot = pbb[4 + t % 4][:, 256:384]
                    if t >= 4:
                        rec.wait("tensor", evs[t - 4])
                    tt = T(lambda e, t=t, slot=slot: e.transpose(
                        slot, vsb[:, 4096 + t * 128:4096 + (t + 1) * 128],
                        identb[:]))
                    rec.wait("vector", tt)
                    evs.append(V(lambda e, t=t, slot=slot: e.tensor_copy(
                        vsb[:, 6144 + t * 128:6144 + (t + 1) * 128], slot)))
                rec.wait("gpsimd", evs[-1])
                dst = pB_flat[ds(_packB_off["W3Q"] + lo * W3BLK, W3BLK)] \
                    .rearrange("(p hc j) -> p hc j", p=128, hc=NHC) \
                    [:, piece * 11:(piece + 1) * 11, :]
                dt = rec.dma("gpsimd", dst,
                             vsb[:, 6144:6144 + cw].rearrange(
                                 "p (hc j) -> p hc j", hc=11))
                rec.wait("gpsimd", dt)
                rec.wait("sync", dt)
                rec.wait("vector", dt)

        gtok = rec.now("dgp")
        rec.wait("gpsimd", gtok)
        agB = rec.op("gpsimd", lambda e: e.collective_compute(
            "AllGather", ALU.bypass, replica_groups=[list(range(R))],
            ins=[packB_in[:, :]], outs=[packB_out[:, :]]), inc="cB")
        return agB

    def ph_load_x():
        evs = []
        slot_ev = [None, None]
        for t in range(8):
            buf = xstg2 if t % 2 == 0 else xstg
            snm = "stb0" if t % 2 == 0 else "sta0"
            if slot_ev[t % 2] is not None:
                rec.wait("sync", slot_ev[t % 2])
            ld = rec.dma("sync", buf[:, 0:2048],
                         x_own[t * 128:(t + 1) * 128, :], sem=snm)
            rec.wait("tensor", ld)
            for g in range(G):
                slot = pb[g % 4][:, 0:128]
                idx = t * G + g
                if idx >= 4:
                    rec.wait("tensor", evs[idx - 4])
                tt = T(lambda e, buf=buf, g=g, slot=slot: e.transpose(
                    slot, buf[:, g * 128:(g + 1) * 128], identf[:]))
                rec.wait("vector", tt)
                evs.append(V(lambda e, g=g, t=t, slot=slot: e.tensor_copy(
                    stream[:, g * 1024 + t * 128:g * 1024 + (t + 1) * 128],
                    slot)))
            slot_ev[t % 2] = evs[-1]

    def proj16(addr_fn, rhs_fn, evac_fn, stage_cols=2048, nout=16,
               pre_tok=None, lhs_g_of=None):
        bank_tok = [None, None, None, None]
        stage_tok = [None, None]
        ld_tok = [None, None]
        rec.wait("sync", rec.now("pe"))
        rec.wait("tensor", rec.now("dve"))
        for o in range(nout):
            sbuf_half = o % 2
            stg = wsa[:, sbuf_half * 2048: sbuf_half * 2048 + stage_cols]
            if stage_tok[sbuf_half] is not None:
                rec.wait("sync", stage_tok[sbuf_half])
            if pre_tok is not None:
                rec.wait("sync", pre_tok)
                pre_tok = None
            ld_tok[sbuf_half] = rec.dma("sync", stg, addr_fn(o),
                                        sem="sta" + str(sbuf_half))
            mmtok = None
            for half in range(2):
                bnk = (2 * o + half) % 4
                bank = pb[bnk]
                if bank_tok[bnk] is not None:
                    rec.wait("tensor", bank_tok[bnk])
                rec.wait("tensor", ld_tok[sbuf_half])
                for g in range(G):
                    gl = lhs_g_of(g) if lhs_g_of else g
                    mmtok = T(lambda e, stg=stg, gl=gl, g=g, half=half,
                              bank=bank: e.matmul(
                        bank[:, 0:512], stg[:, gl * 128:(gl + 1) * 128],
                        rhs_fn(g, half), start=(g == 0), stop=(g == G - 1)))
                bank_tok[bnk] = evac_fn(o, half, bank, mmtok)
            stage_tok[sbuf_half] = bank_tok[(2 * o + 1) % 4]
        return bank_tok

    kv_dma_toks = []

    KVSTATE = {}

    def mk_evac_kv(layer, row_base, with_ck, bias_ap=None):
        state = KVSTATE

        def evac(o, half, bank, mmtok):
            buf = kvstg[:, (o % 2) * 1024:(o % 2) * 1024 + 1024]
            rec.wait("vector", mmtok)
            if (o % 2) in state:
                rec.wait("vector", state[o % 2])
            if bias_ap is not None:
                tok = V(lambda e, buf=buf, half=half, o=o, bank=bank:
                        e.tensor_scalar(buf[:, half * 512:(half + 1) * 512],
                                        bank[:, 0:512], bias_ap[:, o:o + 1],
                                        None, ALU.add))
            else:
                tok = V(lambda e, buf=buf, half=half, bank=bank: e.tensor_copy(
                    buf[:, half * 512:(half + 1) * 512], bank[:, 0:512]))
            if half == 1:
                if with_ck:
                    vgap()
                    V(lambda e, buf=buf, o=o: e.reduce_max(
                        ck[:, o:o + 1], buf, AX.X, apply_absolute_value=True))
                    tok = vgap()
                rec.wait("sync", tok)
                dt = rec.dma("sync",
                             kv_in[layer][row_base + o * 128:
                                          row_base + (o + 1) * 128, :], buf,
                             sem="kvo" + str(o % 2))
                state[o % 2] = dt
            return tok
        return evac

    def mk_evac_sbuf(dst_fn, bias_ap=None):
        def evac(o, half, bank, mmtok):
            rec.wait("vector", mmtok)
            if bias_ap is not None:
                return V(lambda e, o=o, half=half, bank=bank: e.tensor_scalar(
                    dst_fn(o, half), bank[:, 0:512], bias_ap[:, o:o + 1],
                    None, ALU.add))
            return V(lambda e, o=o, half=half, bank=bank: e.tensor_copy(
                dst_fn(o, half), bank[:, 0:512]))
        return evac

    qsv = qs.ap().rearrange("p (blk tl gg) -> p blk tl gg", blk=2, tl=512, gg=G)

    def stream_rhs(g, half):
        return stream[:, g * 1024 + half * 512: g * 1024 + (half + 1) * 512]

    def ph_proj_L0(agA_tok):
        rec.wait("sync", agA_tok)
        kv_dma_toks.clear()
        proj16(lambda o: packA_stage(natA("Wk", o), 2048), stream_rhs,
               mk_evac_kv(0, 0, with_ck=True))
        proj16(lambda o: packA_stage(natA("Wv", o), 2048), stream_rhs,
               mk_evac_sbuf(lambda o, half: outT[:, o * 1024 + half * 512:
                                                 o * 1024 + (half + 1) * 512]))
        vptok = rec.now("dve")
        rec.wait("tensor", vptok)
        rec.wait("sync", rec.now("kvo0"))
        rec.wait("sync", rec.now("kvo1"))
        prev_dma = [None, None]
        slot_ev = [None, None]
        for t in range(8):
            half = t % 2
            vrow = wsb[:, half * 2048:(half + 1) * 2048]
            if prev_dma[half] is not None:
                rec.wait("vector", prev_dma[half])
            evs = []
            for g in range(G):
                slot = pbb[2 + (g % 2)][:, 0:128]
                if g >= 2:
                    rec.wait("tensor", evs[g - 2])
                elif slot_ev[g % 2] is not None:
                    rec.wait("tensor", slot_ev[g % 2])
                tt = T(lambda e, g=g, t=t, slot=slot: e.transpose(
                    slot, outT[:, g * 1024 + t * 128:g * 1024 + (t + 1) * 128],
                    identb[:]))
                rec.wait("vector", tt)
                evs.append(V(lambda e, g=g, slot=slot, vrow=vrow: e.tensor_copy(
                    vrow[:, g * 128:(g + 1) * 128], slot)))
            slot_ev = [evs[-2], evs[-1]]
            rec.wait("sync", evs[-1])
            vreg0 = kvi_flat[0][ds(2048 * 1024, 2097152)].rearrange(
                "(p hf h sb j) -> p hf h sb j", p=128, hf=2, h=16, sb=4)
            dt = rec.dma("sync", vreg0[:, t // 4, :, t % 4, :],
                         vrow.rearrange("p (h j) -> p h j", h=16),
                         sem="kvo" + str(half))
            prev_dma[half] = dt
        cktok = rec.now("dve")
        rec.wait("gpsimd", cktok)
        dt = rec.dma("gpsimd", kvi_flat[0][ds(KVBASE[0]["c"] * 1024, 2048)]
                     .rearrange("(g p) -> p g", p=128), ck,
                     allow_slow_non_contiguous=True)
        rec.wait("gpsimd", dt)
        def q_evac(o, half, bank, mmtok):
            rec.wait("vector", mmtok)
            return V(lambda e, o=o, half=half, bank=bank: e.tensor_scalar(
                qsv[:, half, :, o], bank[:, 0:512], bq_f[:, o:o + 1],
                None, ALU.add))
        proj16(lambda o: packA_stage(natA("Wq", o), 2048), stream_rhs, q_evac)
        rec.wait("gpsimd", rec.now("dve"))
        for sname in ("kvo0", "kvo1", "dkv", "dsp"):
            rec.wait("gpsimd", rec.now(sname))
        ag = rec.op("gpsimd", lambda e: e.collective_compute(
            "AllGather", ALU.bypass, replica_groups=[list(range(R))],
            ins=[kv_in[0][:, :]], outs=[kv_out[0][:, :]]), inc="ck0")
        return ag

    def ph_proj_L1(agB_tok):
        kv_dma_toks.clear()
        for dst, src in ((ksb, upk_i), (vsb, upv_i)):
            sf = src.rearrange("h d j -> (h d j)")
            ap = sf.rearrange("(h g p j) -> p h g j", h=NH, g=4, p=128)
            rec.dma("gpsimd", dst.ap().rearrange(
                "p (h g j) -> p h g j", h=NH, g=4), ap, sem="dup")
        up_tok = rec.now("dup")
        proj16(lambda o: packA_stage(natA("wq1", o), 2048), stream_rhs,
               mk_evac_kv(1, KVBASE[1]["q"], with_ck=False, bias_ap=wqb_f))
        proj16(lambda o: packA_stage(natA("wk1", o), 2048), stream_rhs,
               mk_evac_sbuf(lambda o, half: outT[:, o * 1024 + half * 512:
                                                 o * 1024 + (half + 1) * 512]))
        def k1_rhs(g, half):
            return outT[:, g * 1024 + half * 512: g * 1024 + (half + 1) * 512]
        proj16(lambda o: packA_stage(wlA(0, o), 2048), k1_rhs,
               mk_evac_sbuf(lambda o, half: qs[:, 8192 + o * 1024 + half * 512:
                                               8192 + o * 1024 + (half + 1) * 512]),
               nout=4)
        rec.wait("tensor", up_tok)
        rec.wait("tensor", rec.now("dve"))
        evk = mk_evac_kv(1, KVBASE[1]["k"], with_ck=True)
        bank_tok = [None, None]
        for h in range(NH):
            mmtok = None
            for half in range(2):
                bnk = (2 * h + half) % 2
                bank = pb[bnk]
                if bank_tok[bnk] is not None:
                    rec.wait("tensor", bank_tok[bnk])
                for g in range(4):
                    mmtok = T(lambda e, h=h, g=g, half=half, bank=bank:
                              e.matmul(bank[:, 0:512],
                                       ksb[:, h * 512 + g * 128:
                                           h * 512 + (g + 1) * 128],
                                       qs[:, 8192 + g * 1024 + half * 512:
                                          8192 + g * 1024 + (half + 1) * 512],
                                       start=(g == 0), stop=(g == 3)))
                bank_tok[bnk] = evk(h, half, bank, mmtok)
        proj16(lambda o: packA_stage(natA("wv1", o), 2048), stream_rhs,
               mk_evac_sbuf(lambda o, half: outT[:, o * 1024 + half * 512:
                                                 o * 1024 + (half + 1) * 512]))
        rec.wait("sync", rec.now("pe"))
        rec.wait("tensor", rec.now("dve"))
        vl_tok = [None, None]
        vlb_tok = [None, None]
        for o in range(4):
            sbuf_half = o % 2
            stg = wsa[:, sbuf_half * 2048:(sbuf_half + 1) * 2048]
            if vl_tok[sbuf_half] is not None:
                rec.wait("sync", vl_tok[sbuf_half])
            ld = rec.dma("sync", stg, packA_stage(wlA(1, o), 2048),
                         sem="sta" + str(sbuf_half))
            mmtok = None
            for half in range(2):
                bnk = (2 * o + half) % 2
                bank = pb[bnk]
                if vlb_tok[bnk] is not None:
                    rec.wait("tensor", vlb_tok[bnk])
                rec.wait("tensor", ld)
                for g in range(G):
                    mmtok = T(lambda e, stg=stg, g=g, half=half, o=o,
                              bank=bank: e.matmul(
                        bank[:, 0:512], stg[:, g * 128:(g + 1) * 128],
                        outT[:, g * 1024 + half * 512:
                             g * 1024 + (half + 1) * 512],
                        start=(g == 0), stop=(g == G - 1)))
                rec.wait("vector", mmtok)
                ev = V(lambda e, o=o, half=half, bank=bank: e.tensor_copy(
                    qs[:, 12288 + o * 1024 + half * 512:
                       12288 + o * 1024 + (half + 1) * 512], bank[:, 0:512]))
                vlb_tok[bnk] = ev
            rec.wait("tensor", rec.now("dve"))
            c_mm = None
            for g in range(G):
                c_mm = T(lambda e, stg=stg, g=g: e.matmul(
                    pb[5][:, 0:1], stg[:, g * 128:(g + 1) * 128],
                    wvb_b[:, g:g + 1], start=(g == 0), stop=(g == G - 1)))
            rec.wait("vector", c_mm)
            V(lambda e, o=o: e.tensor_scalar(
                v1c[:, o:o + 1], pb[5][:, 0:1], wvlb_f[:, o:o + 1],
                None, ALU.add))
            vgap()
            vl_tok[sbuf_half] = c_mm
        cst = V(lambda e: e.tensor_copy(v1cb, v1c))
        rec.wait("tensor", cst)
        rec.wait("tensor", rec.now("dve"))
        prev_ev = {}
        for h in range(NH):
            for t in range(8):
                bank = pb[(h * 8 + t) % 2]
                key = (h * 8 + t) % 2
                if key in prev_ev:
                    rec.wait("tensor", prev_ev[key])
                mmtok = None
                for g in range(4):
                    mmtok = T(lambda e, h=h, t=t, g=g, bank=bank: e.matmul(
                        bank[:, 0:128],
                        qs[:, 12288 + g * 1024 + t * 128:
                           12288 + g * 1024 + (t + 1) * 128],
                        vsb[:, h * 512 + g * 128: h * 512 + (g + 1) * 128],
                        start=(g == 0), stop=(g == 3)))
                rec.wait("vector", mmtok)
                prev_ev[key] = V(lambda e, h=h, t=t, bank=bank: e.tensor_copy(
                    outT[:, t * 2048 + h * 128: t * 2048 + (h + 1) * 128],
                    bank[:, 0:128]))
            rec.wait("tensor", rec.now("dve"))
            cm = None
            for g in range(4):
                cm = T(lambda e, h=h, g=g: e.matmul(
                    pb[5][:, 0:1],
                    vsb[:, h * 512 + g * 128: h * 512 + (g + 1) * 128],
                    v1cb[:, g:g + 1], start=(g == 0), stop=(g == 3)))
            rec.wait("vector", cm)
            V(lambda e, h=h: e.tensor_scalar(
                cc1[:, h:h + 1], pb[5][:, 0:1], upvb_f[:, h:h + 1],
                None, ALU.add))
            vgap()
        vtok = rec.now("dve")
        rec.wait("sync", vtok)
        vreg1 = kvi_flat[1][ds(KVBASE[1]["v"] * 1024, 2097152)].rearrange(
            "(p hf h sb j) -> p hf h sb j", p=128, hf=2, h=16, sb=4)
        for t in range(8):
            dt = rec.dma("sync", vreg1[:, t // 4, :, t % 4, :],
                         outT[:, t * 2048:(t + 1) * 2048].rearrange(
                             "p (h j) -> p h j", h=16), sem="dkv")
        cktok = rec.now("dve")
        rec.wait("gpsimd", cktok)
        dt = rec.dma("gpsimd", kvi_flat[1][ds(KVBASE[1]["c"] * 1024, 2048)]
                     .rearrange("(g p) -> p g", p=128), ck,
                     allow_slow_non_contiguous=True)
        rec.wait("gpsimd", dt)
        for sname in ("kvo0", "kvo1", "dkv", "dsp"):
            rec.wait("gpsimd", rec.now(sname))
        ag = rec.op("gpsimd", lambda e: e.collective_compute(
            "AllGather", ALU.bypass, replica_groups=[list(range(R))],
            ins=[kv_in[1][:, :]], outs=[kv_out[1][:, :]]), inc="ck1")
        return ag

    def ph_ccombine(layer, ag_tok):
        rec.wait("gpsimd", ag_tok)
        base = KVBASE[layer]["c"]
        rows = KVBASE[layer]["rows"]
        dts = []
        for r in range(R):
            dts.append(rec.dma(
                "gpsimd", absq[:, r * 16:(r + 1) * 16],
                kvo_flat[layer][ds((r * rows + base) * 1024, 2048)]
                .rearrange("(g p) -> p g", p=128), sem="dcm",
                allow_slow_non_contiguous=True))
        rec.wait("gpsimd", dts[-1])
        rec.wait("vector", dts[-1])
        V(lambda e: e.tensor_max(absq[:, 128:144], absq[:, 0:16], absq[:, 16:32]))
        V(lambda e: e.tensor_max(absq[:, 144:160], absq[:, 32:48], absq[:, 48:64]))
        V(lambda e: e.tensor_max(absq[:, 160:176], absq[:, 64:80], absq[:, 80:96]))
        V(lambda e: e.tensor_max(absq[:, 176:192], absq[:, 96:112], absq[:, 112:128]))
        vgap()
        V(lambda e: e.tensor_max(absq[:, 192:208], absq[:, 128:144], absq[:, 144:160]))
        V(lambda e: e.tensor_max(absq[:, 208:224], absq[:, 160:176], absq[:, 176:192]))
        vgap()
        cf = V(lambda e: e.tensor_max(cfeat, absq[:, 192:208], absq[:, 208:224]))
        cg = vgap()
        rec.wait("gpsimd", cg)
        cs_flat = cscr.rearrange("r c -> (r c)")
        st = rec.dma("gpsimd", cs_flat.rearrange("(g p) -> p g", p=128),
                     cfeat, sem="dcm", allow_slow_non_contiguous=True)
        rec.wait("gpsimd", st)
        cs1 = cs_flat.rearrange("(n b) -> n b", b=1)
        gather(cmy[:, 0:1], cs1, ("cmy", layer, 0))
        t2 = gather(cmy[:, 1:2], cs1, ("cmy", layer, 1))
        rec.wait("gpsimd", t2)
        return t2, rec.now("gp")

    def ph_attention(layer, ag_tok):
        dgp_tok, gp_tok = ph_ccombine(layer, ag_tok)
        rec.wait("tensor", dgp_tok)
        rec.wait("scalar", dgp_tok)
        SC = [pb[0], pb[1], pb[2]]
        PPRE3 = [ppre[:, 0:512], ppre[:, 512:1024], ascr[:, 1024:1536]]
        PBF3 = [pbf[:, 0:512], pbf[:, 512:1024], t1b[:, 0:512]]
        PVB = [pb[3], pb[6]]
        MHB = [ascr[:, 0:512], ascr[:, 512:1024]]
        MROW = [ascr[0:1, 1536:2048], t1[0:1, 512:1024]]
        rrow = ascr[0:1, 2048:2560]
        rbbv = ascr[:, 2048:2560]
        outv_toks = []
        state = {}

        def qtile_of(hi, qt):
            if layer == 0:
                return qs[:, hi * 8192 + qt * 512: hi * 8192 + (qt + 1) * 512]
            return qs[:, qt * 512:(qt + 1) * 512]

        def emit_gather_group(hi, m):
            sem = f"dl{m}"
            gather(ksb[:, m * 512:(m + 1) * 512], kvo512[layer],
                   ("k", layer, hi, m), sem=sem)
            if layer == 1:
                gather(qs[:, m * 512:(m + 1) * 512], kvo512[layer],
                       ("q", layer, hi, m), sem=sem)
            gather(vsb[:, m * 512:(m + 1) * 512], kvo512[layer],
                   ("v", layer, hi, m), sem=sem)
            return rec.now(sem)

        def mk_prologue(hi, qt, gtok):
            par = qt % 2
            qtile = qtile_of(hi, qt)
            toks = {}

            def p_ab():
                if layer == 1:
                    rec.wait("scalar", gtok[qt])
                if "mh" in state:
                    rec.wait("scalar", state["mh"])
                toks["ab"] = Aop(lambda e: e.activation(
                    absq[:, 0:512], qtile, AF.Abs))

            def p_mh():
                rec.wait("tensor", toks["ab"])
                if "mts" in state:
                    rec.wait("tensor", state["mts"])
                mh = T(lambda e: e.matmul(
                    pb[5][0:1, 0:512], cmy[:, hi:hi + 1], absq[:, 0:512],
                    start=True, stop=True))
                state["mh"] = mh
                rec.wait("vector", mh)
                if ("bc", par) in state:
                    rec.wait("vector", state[("bc", par)])
                V(lambda e: e.tensor_scalar(
                    MROW[par], pb[5][0:1, 0:512], SCALE, 1.0,
                    ALU.mult, ALU.add))
                toks["mg"] = vgap()
                state["mts"] = toks["mg"]

            def p_bc():
                rec.wait("tensor", toks["mg"])
                if "mhcp" in state:
                    rec.wait("tensor", state["mhcp"])
                bc = T(lambda e: e.matmul(
                    pb[4][:, 0:512], ones_row, MROW[par],
                    start=True, stop=True))
                state[("bc", par)] = bc
                rec.wait("vector", bc)
                V(lambda e: e.tensor_copy(MHB[par], pb[4][:, 0:512]))
                state["mhcp"] = vgap()

            return [p_ab, p_mh, p_bc]

        next_gtok = []
        for hi in range(2):
            if hi == 0:
                gtok = []
                for m in range(16):
                    gtok.append(emit_gather_group(hi, m))
            else:
                gtok = next_gtok
                next_gtok = []

            units = []
            for qt in range(16):
                n = 4 * (qt + 1)
                for uu in range(n):
                    units.append((qt, uu, uu // 4, uu % 4,
                                  uu // 4 == qt, uu == 0, uu == n - 1))
            NU = len(units)
            qk_toks = {}
            sub_toks = {}
            exa_toks = {}
            ptoks = {}
            pvl_toks = {}
            fo_toks = {}
            recip_tok = None
            last_m = [-1]

            def emit_qk(j):
                qtj, uuj, mj, sbj, diagj, firstj, lastj = units[j]
                slot = SC[j % 3]
                if j >= 3:
                    rec.wait("tensor", sub_toks[j - 3])
                if mj > last_m[0]:
                    rec.wait("tensor", gtok[mj])
                    last_m[0] = mj
                qtile = qtile_of(hi, qtj)
                if layer == 1 and firstj:
                    rec.wait("tensor", gtok[qtj])
                qk_toks[j] = T(lambda e, uuj=uuj, slot=slot, qtile=qtile:
                               e.matmul(slot[:, 0:512],
                                        ksb[:, uuj * 128:(uuj + 1) * 128],
                                        qtile, start=True, stop=True))

            pro = mk_prologue(hi, 0, gtok)
            for p in pro:
                p()
            pending_pro = None
            pro_pos = []
            emit_qk(0)
            if NU > 1:
                emit_qk(1)
            for j, (qt, uu, m, sb, diag, first, last) in enumerate(units):
                nunits = 4 * (qt + 1)
                if first and qt < 15:
                    pending_pro = mk_prologue(hi, qt + 1, gtok)
                    p0 = max(0, nunits - 6)
                    p1 = max(p0 + 1, nunits - 4)
                    p2 = max(p1 + 1, nunits - 2)
                    pro_pos = [p0, p1, p2]
                if pending_pro and uu in pro_pos:
                    pi = pro_pos.index(uu)
                    pending_pro[pi]()
                    if pi == 2:
                        pending_pro = None
                if j + 2 < NU:
                    emit_qk(j + 2)
                rec.wait("vector", qk_toks[j])
                if j - 3 in exa_toks:
                    rec.wait("vector", exa_toks[j - 3])
                st = V(lambda e, j=j, qt=qt: e.scalar_tensor_tensor(
                    PPRE3[j % 3], SC[j % 3][:, 0:512], SCALE, MHB[qt % 2],
                    ALU.mult, ALU.subtract))
                sub_toks[j] = st
                rec.wait("scalar", st)
                if j - 3 in pvl_toks:
                    rec.wait("scalar", pvl_toks[j - 3])
                ex = Aop(lambda e, j=j: e.activation(
                    PBF3[j % 3], PPRE3[j % 3], AF.Exp))
                exa_toks[j] = ex
                ptok = ex
                if diag:
                    rec.wait("gpsimd", ex)
                    ptok = Gop(lambda e, j=j, sb=sb: e.affine_select(
                        out=PBF3[j % 3], in_=PBF3[j % 3],
                        compare_op=ALU.is_ge, fill=0.0, base=-128 * sb,
                        pattern=[[1, 512]], channel_multiplier=-1))
                ptoks[j] = ptok
                rec.wait("tensor", ptok)
                if first:
                    if qt - 2 in fo_toks:
                        rec.wait("tensor", fo_toks[qt - 2])
                    if recip_tok is not None:
                        rec.wait("tensor", recip_tok)
                T(lambda e, j=j, uu=uu, qt=qt, first=first, last=last:
                  e.matmul(PVB[qt % 2][:, 0:512],
                           vsb[:, uu * 128:(uu + 1) * 128], PBF3[j % 3],
                           start=first, stop=last))
                pvl = T(lambda e, j=j, first=first, last=last: e.matmul(
                    pb[7][0:1, 0:512], ones_b, PBF3[j % 3],
                    start=first, stop=last))
                pvl_toks[j] = pvl
                if hi == 0 and qt == 15 and sb == 3:
                    rec.wait("gpsimd", pvl)
                    if layer == 1:
                        rec.wait("gpsimd", exa_toks[j])
                    next_gtok.append(emit_gather_group(1, m))
                if last:
                    rec.wait("vector", pvl)
                    V(lambda e: e.reciprocal(rrow, pb[7][0:1, 0:512]))
                    rg = vgap()
                    recip_tok = rg
                    rec.wait("tensor", rg)
                    slot = SC[j % 3]
                    bc2 = T(lambda e, slot=slot: e.matmul(
                        slot[:, 0:512], ones_row, rrow,
                        start=True, stop=True))
                    rec.wait("vector", bc2)
                    rbc = V(lambda e, slot=slot: e.tensor_copy(
                        rbbv, slot[:, 0:512]))
                    sub_toks[j] = rbc
                    vgap()
                    fo = V(lambda e, hi=hi, qt=qt: e.tensor_tensor(
                        outT[:, hi * 8192 + qt * 512:
                             hi * 8192 + (qt + 1) * 512],
                        PVB[qt % 2][:, 0:512], rbbv, ALU.mult))
                    fo_toks[qt] = fo
                    outv_toks.append(fo)
        return outv_toks

    def ph_wo(layer, outv_toks):
        for t in outv_toks[-2:]:
            rec.wait("sync", t)
        rec.wait("sync", rec.now("dve"))
        d1 = rec.dma("sync", ao_in[0:128, :], outT[:, 0:8192], sem="dkv")
        d2 = rec.dma("sync", ao_in[128:256, :], outT[:, 8192:16384], sem="dkv")
        rec.wait("gpsimd", rec.now("dkv"))
        ag = rec.op("gpsimd", lambda e: e.collective_compute(
            "AllGather", ALU.bypass, replica_groups=[list(range(R))],
            ins=[ao_in[:, :]], outs=[ao_out[:, :]]), inc="cao")
        rec.wait("gpsimd", ag)
        for jk in range(16):
            for cc in range(2):
                s2 = gather(outT[:, jk * 1024 + cc * 512:
                                 jk * 1024 + (cc + 1) * 512],
                            ao512, ("ao", layer, jk, cc))
        rec.wait("gpsimd", s2)
        rec.wait("tensor", s2)
        rec.wait("sync", s2)
        mat = "Wo" if layer == 0 else "wo1"
        bias_src = bo_f if layer == 0 else wob_f
        cbv = cb0 if layer == 0 else cb1
        if layer == 1:
            cv = V(lambda e: e.tensor_copy(cvb, cc1))
            rec.wait("tensor", cv)
        bank_tok = [None, None]
        stage_tok = [None, None]
        rec.wait("sync", rec.now("pe"))
        rec.wait("tensor", rec.now("dve"))
        for o in range(G):
            sh = o % 2
            stg = wsa[:, sh * 2048:(sh + 1) * 2048]
            if stage_tok[sh] is not None:
                rec.wait("sync", stage_tok[sh])
            ld = rec.dma("sync", stg, packA_stage(natA(mat, o), 2048),
                         sem="sta" + str(sh))
            rec.wait("tensor", ld)
            if o > 0:
                rec.wait("tensor", cb_ts)
            cm = None
            for g in range(G):
                cm = T(lambda e, stg=stg, g=g: e.matmul(
                    pb[5][:, 0:1], stg[:, g * 128:(g + 1) * 128],
                    cvb[:, g:g + 1], start=(g == 0), stop=(g == G - 1)))
            rec.wait("vector", cm)
            cb_ts = V(lambda e, o=o, cbv=cbv, bias_src=bias_src: e.tensor_scalar(
                cbv[:, o:o + 1], pb[5][:, 0:1], bias_src[:, o:o + 1],
                None, ALU.add))
            vgap()
            mmtok = None
            for half in range(2):
                bnk = (2 * o + half) % 2
                bank = pb[bnk]
                if bank_tok[bnk] is not None:
                    rec.wait("tensor", bank_tok[bnk])
                for jk in range(G):
                    gh = ghead(jk)
                    mmtok = T(lambda e, stg=stg, gh=gh, jk=jk, half=half,
                              bank=bank: e.matmul(
                        bank[:, 0:512], stg[:, gh * 128:(gh + 1) * 128],
                        outT[:, jk * 1024 + half * 512:
                             jk * 1024 + (half + 1) * 512],
                        start=(jk == 0), stop=(jk == G - 1)))
                rec.wait("vector", mmtok)
                bank_tok[bnk] = V(
                    lambda e, o=o, half=half, bank=bank, cbv=cbv:
                    e.scalar_tensor_tensor(
                        ybuf[:, o * 1024 + half * 512:
                             o * 1024 + (half + 1) * 512],
                        bank[:, 0:512], cbv[:, o:o + 1],
                        stream[:, o * 1024 + half * 512:
                               o * 1024 + (half + 1) * 512],
                        ALU.add, ALU.add))
            stage_tok[sh] = bank_tok[(2 * o + 1) % 2]

    def fview(g):
        if g < 8:
            return qs32[:, g * 1024:(g + 1) * 1024]
        if g < 12:
            return ksb32[:, (g - 8) * 1024:(g - 7) * 1024]
        return vsb32[:, (g - 12) * 1024:(g - 11) * 1024]

    def ph_ln(final=False):
        barrier()
        for half in range(2):
            sq_toks = []
            for g in range(G):
                if g >= 2:
                    rec.wait("scalar", sq_toks[g - 2])
                sq = Aop(lambda e, g=g, half=half: e.activation(
                    ysqv[:, (g % 2) * 512:(g % 2) * 512 + 512],
                    ybuf[:, g * 1024 + half * 512:
                         g * 1024 + (half + 1) * 512], AF.Square))
                rec.wait("tensor", sq)
                T(lambda e, g=g, half=half: e.matmul(
                    pb[6][0:1, 0:512], ones_b,
                    ybuf[:, g * 1024 + half * 512:
                         g * 1024 + (half + 1) * 512],
                    start=(g == 0), stop=(g == G - 1)))
                s2t = T(lambda e, g=g: e.matmul(
                    pb[7][0:1, 0:512], ones_b,
                    ysqv[:, (g % 2) * 512:(g % 2) * 512 + 512],
                    start=(g == 0), stop=(g == G - 1)))
                sq_toks.append(s2t)
            stt = sq_toks[-1]
            rec.wait("vector", stt)
            rec.wait("vector", rec.now("act"))
            r4 = pb[4][0:1, 0:512]
            r5 = pb[5][0:1, 0:512]
            Atmp = t1[0:1, 0:512]
            Btmp = t1[0:1, 512:1024]
            mu = muK[:, 0:512]
            V(lambda e: e.tensor_scalar(mu, pb[6][0:1, 0:512], 1.0 / D,
                                        None, ALU.mult))
            V(lambda e: e.tensor_scalar(Atmp, pb[7][0:1, 0:512], 1.0 / D,
                                        None, ALU.mult))
            vgap()
            V(lambda e, mu=mu: e.tensor_tensor(Btmp, mu, mu, ALU.mult))
            vgap()
            V(lambda e: e.tensor_tensor(r4, Atmp, Btmp, ALU.subtract))
            vgap()
            V(lambda e: e.tensor_scalar(Btmp, r4, EPS_LN, None, ALU.add))
            vr = vgap()
            rec.wait("scalar", vr)
            sd = Aop(lambda e: e.activation(r4, Btmp, AF.Sqrt))
            rec.wait("vector", sd)
            V(lambda e: e.reciprocal(Atmp, r4))
            vgap()
            V(lambda e: e.tensor_tensor(r4, Atmp, Atmp, ALU.mult))
            vgap()
            V(lambda e: e.tensor_tensor(r5, Btmp, r4, ALU.mult))
            vgap()
            V(lambda e: e.tensor_scalar(Btmp, r5, -0.5, 1.5, ALU.mult,
                                        ALU.add))
            vgap()
            V(lambda e, half=half: e.tensor_tensor(
                rstd_row[:, half * 512:(half + 1) * 512], Atmp, Btmp,
                ALU.mult))
            vgap()
            V(lambda e, half=half, mu=mu: e.tensor_tensor(
                m2_row[:, half * 512:(half + 1) * 512], mu,
                rstd_row[:, half * 512:(half + 1) * 512], ALU.mult))
            mfin = vgap()
            rec.wait("tensor", mfin)
            b1 = T(lambda e, half=half: e.matmul(
                pb[4][:, 0:512], ones_row,
                rstd_row[:, half * 512:(half + 1) * 512],
                start=True, stop=True))
            b2 = T(lambda e, half=half: e.matmul(
                pb[5][:, 0:512], ones_row,
                m2_row[:, half * 512:(half + 1) * 512],
                start=True, stop=True))
            rec.wait("vector", b2)
            zbuf = [t1[:, 0:512], t1[:, 512:1024]]
            for g in range(G):
                V(lambda e, g=g, half=half: e.tensor_tensor(
                    zbuf[g % 2], ybuf[:, g * 1024 + half * 512:
                                      g * 1024 + (half + 1) * 512],
                    pb[4][:, 0:512], ALU.mult))
                vgap()
                if final:
                    dst = fview(g)[:, half * 512:(half + 1) * 512]
                else:
                    dst = stream[:, g * 1024 + half * 512:
                                 g * 1024 + (half + 1) * 512]
                V(lambda e, g=g, dst=dst: e.tensor_tensor(
                    dst, zbuf[g % 2], pb[5][:, 0:512], ALU.subtract))
        barrier()

    def ph_ffn0(stop=None):
        hT = outT[0:64, 0:1024]
        Rt = outT32[:, 2048:2176]
        m1t = outT32[:, 2176:2184]
        m2t = outT32[:, 2184:2192]
        eqm = outT32[:, 2192:2320]
        msk = outT32[:, 2320:2448]
        expl = outT32[:, 2448:2576]
        gat = outT32[:, 2576:2704]
        den = outT32[:, 2704:2712]
        rcd = outT32[:, 2712:2720]
        cfb = outT[:, 5536:5536 + 2048]
        coT = outT[0:16, 7680:8704]
        xab = outT[:, 8704:8704 + 2048]
        rec.wait("vector", rec.now("pe"))
        onetok = V(lambda e: e.memset(outT[64:65, 0:1024], 1.0))
        for half in range(2):
            mm = None
            for g in range(G):
                mm = T(lambda e, g=g, half=half: e.matmul(
                    pb[4][0:64, 0:512], rwT[:, g * 64:(g + 1) * 64],
                    stream_rhs(g, half), start=(g == 0), stop=(g == G - 1)))
            rec.wait("vector", mm)
            V(lambda e, half=half: e.tensor_scalar(
                t1[0:64, 0:512], pb[4][0:64, 0:512], rb1_f[0:64, 0:1],
                None, ALU.add))
            sv = vgap()
            rec.wait("scalar", sv)
            hs = Aop(lambda e, half=half: e.activation(
                hT[:, half * 512:(half + 1) * 512], t1[0:64, 0:512], AF.Silu))
            rec.wait("tensor", hs)
        if stop == "router":
            return
        rec.wait("tensor", onetok)
        evs = []
        for tt in range(8):
            slot = pb[2 + (tt % 2)][:, 0:16]
            if tt >= 2:
                rec.wait("tensor", evs[tt - 2])
            tk = T(lambda e, tt=tt, slot=slot: e.matmul(
                slot, outT[0:65, tt * 128:(tt + 1) * 128],
                rwT[0:65, 1024:1040], start=True, stop=True))
            rec.wait("vector", tk)
            evs.append(V(lambda e, tt=tt, slot=slot: e.tensor_copy(
                Rt[:, tt * 16:(tt + 1) * 16], slot)))
        vgap()
        if stop == "ltr":
            return
        R3 = Rt.rearrange("p (t e) -> p t e", t=8)
        V(lambda e: e.reduce_max(m1t, R3, AX.X, negate=True))
        vgap()
        m1b = m1t.rearrange("p (t o) -> p t o", o=1).broadcast_to([128, 8, 16])
        V(lambda e: e.tensor_tensor(eqm.rearrange("p (t e) -> p t e", t=8),
                                    R3, m1b, ALU.add))
        vgap()
        V(lambda e: e.tensor_scalar(msk, eqm, 0.0, None, ALU.is_equal))
        vgap()
        V(lambda e: e.scalar_tensor_tensor(expl, msk, NEG, Rt, ALU.mult,
                                           ALU.add))
        vgap()
        V(lambda e: e.reduce_max(m2t, expl.rearrange("p (t e) -> p t e", t=8),
                                 AX.X))
        vgap()
        rec.wait("scalar", rec.now("dve"))
        ex = Aop(lambda e: e.activation(expl, eqm, AF.Exp))
        rec.wait("vector", ex)
        m2b = m2t.rearrange("p (t o) -> p t o", o=1).broadcast_to([128, 8, 16])
        V(lambda e: e.tensor_tensor(msk.rearrange("p (t e) -> p t e", t=8),
                                    R3, m2b, ALU.is_ge))
        vgap()
        V(lambda e: e.tensor_tensor(gat, expl, msk, ALU.mult))
        vgap()
        V(lambda e: e.reduce_sum(den, gat.rearrange("p (t e) -> p t e", t=8),
                                 AX.X))
        vgap()
        V(lambda e: e.reciprocal(rcd, den))
        vgap()
        if stop == "rmid":
            return
        denb = rcd.rearrange("p (t o) -> p t o", o=1).broadcast_to([128, 8, 16])
        vgap()
        cfo = V(lambda e: e.tensor_tensor(
            msk.rearrange("p (t e) -> p t e", t=8),
            gat.rearrange("p (t e) -> p t e", t=8), denb, ALU.mult))
        rec.wait("tensor", cfo)
        evs = []
        for tt in range(8):
            slot = pb[2 + (tt % 2)][0:16, 0:128]
            if tt >= 2:
                rec.wait("tensor", evs[tt - 2])
            tk = T(lambda e, tt=tt, slot=slot: e.transpose(
                slot, msk[:, tt * 16:(tt + 1) * 16], identf[:]))
            rec.wait("vector", tk)
            evs.append(V(lambda e, tt=tt, slot=slot: e.tensor_copy(
                coT[:, tt * 128:(tt + 1) * 128], slot)))
        ctok = vgap()
        if stop == "rmath":
            return
        rec.wait("tensor", ctok)
        cpt = None
        for erb in range(2):
            for half in range(2):
                if cpt is not None:
                    rec.wait("tensor", cpt)
                bm = T(lambda e, erb=erb, half=half: e.matmul(
                    pb[3][:, 0:512], ebf[0:16, erb * 128:(erb + 1) * 128],
                    coT[:, half * 512:(half + 1) * 512],
                    start=True, stop=True))
                rec.wait("vector", bm)
                cpt = V(lambda e, erb=erb, half=half: e.tensor_copy(
                    cfb[:, erb * 1024 + half * 512:
                        erb * 1024 + (half + 1) * 512], pb[3][:, 0:512]))
        vgap()
        if stop == "expand":
            return
        rec.wait("sync", rec.now("pe"))
        abtok = rec.dma("sync", wsb[:, 0:2048],
                        abT_flat[ds(0, BLK)].rearrange("(p c) -> p c", p=128),
                        sem="stb0")
        abtok2 = rec.dma("sync", wsb[:, 2048:4096],
                         abT_flat[ds(BLK, BLK)].rearrange("(p c) -> p c", p=128),
                         sem="stb1")
        cftok = rec.now("dve")
        xat = None
        rec.wait("tensor", abtok)
        rec.wait("tensor", abtok2)
        for erb in range(2):
            for half in range(2):
                if xat is not None:
                    rec.wait("tensor", xat)
                mm = None
                for g in range(G):
                    mm = T(lambda e, erb=erb, g=g, half=half: e.matmul(
                        pb[2][:, 0:512],
                        wsb[:, erb * 2048 + g * 128: erb * 2048 + (g + 1) * 128],
                        stream_rhs(g, half), start=(g == 0), stop=(g == G - 1)))
                rec.wait("vector", mm)
                xat = V(lambda e, erb=erb, half=half: e.tensor_tensor(
                    xab[:, erb * 1024 + half * 512:
                        erb * 1024 + (half + 1) * 512], pb[2][:, 0:512],
                    cfb[:, erb * 1024 + half * 512:
                        erb * 1024 + (half + 1) * 512], ALU.mult))
        wxtok = rec.now("dve")
        if stop == "xa":
            return
        rec.wait("sync", wxtok)
        btok = rec.dma("sync",
                       ksb[:, 0:4096].rearrange("p (o erb j) -> p o erb j",
                                                o=16, erb=2),
                       abT_flat[ds(2 * BLK, 2 * BLK)].rearrange(
                           "(o erb p j) -> p o erb j", p=128, o=16, erb=2),
                       sem="std")
        rec.wait("tensor", btok)
        rec.wait("tensor", wxtok)
        bank_tok = [None, None]
        stage_tok = [None, None]
        rec.wait("tensor", rec.now("dve"))
        for o in range(G):
            sh = o % 2
            stg = wsa[:, sh * 2048:(sh + 1) * 2048]
            if stage_tok[sh] is not None:
                rec.wait("sync", stage_tok[sh])
            ld = rec.dma("sync", stg, packA_stage(wtA(o), 2048),
                         sem="sta" + str(sh))
            rec.wait("tensor", ld)
            mmtok = None
            for half in range(2):
                bnk = (2 * o + half) % 2
                bank = pb[bnk]
                if bank_tok[bnk] is not None:
                    rec.wait("tensor", bank_tok[bnk])
                for g in range(G):
                    T(lambda e, stg=stg, g=g, half=half, bank=bank: e.matmul(
                        bank[:, 0:512], stg[:, g * 128:(g + 1) * 128],
                        stream_rhs(g, half), start=(g == 0), stop=False))
                for erb in range(2):
                    mmtok = T(lambda e, o=o, erb=erb, half=half, bank=bank:
                              e.matmul(
                        bank[:, 0:512],
                        ksb[:, o * 256 + erb * 128: o * 256 + (erb + 1) * 128],
                        xab[:, erb * 1024 + half * 512:
                            erb * 1024 + (half + 1) * 512],
                        start=False, stop=(erb == 1)))
                rec.wait("vector", mmtok)
                bank_tok[bnk] = V(
                    lambda e, o=o, half=half, bank=bank: e.scalar_tensor_tensor(
                        ybuf[:, o * 1024 + half * 512:
                             o * 1024 + (half + 1) * 512],
                        bank[:, 0:512], fb_f[:, o:o + 1],
                        stream[:, o * 1024 + half * 512:
                               o * 1024 + (half + 1) * 512],
                        ALU.add, ALU.add))
            stage_tok[sh] = bank_tok[(2 * o + 1) % 2]

    def hid_ap(hc, half):
        if hc < 16:
            return qs[:, hc * 1024 + half * 512: hc * 1024 + (half + 1) * 512]
        if hc < 24:
            c = (hc - 16) * 1024 + half * 512
            return ksb[:, c:c + 512]
        if hc < 32:
            c = (hc - 24) * 1024 + half * 512
            return vsb[:, c:c + 512]
        c = (hc - 32) * 1024 + half * 512
        return outT[:, c:c + 512]

    def ph_ffn1(agB_tok):
        rec.wait("sync", agB_tok)
        rec.wait("sync", rec.now("pe"))
        rec.wait("scalar", rec.now("pe"))
        rec.wait("tensor", rec.now("dve"))
        rec.wait("tensor", rec.now("act"))
        sig_tok = {}
        h_toks = {}
        w1tok = [None, None]
        w2tok = [None, None]
        for hc in range(NHC):
            sh = hc % 2
            stg1 = wsa[:, sh * 2048:(sh + 1) * 2048]
            stg2 = wsb[:, sh * 2048:(sh + 1) * 2048]
            if w1tok[sh] is not None:
                rec.wait("sync", w1tok[sh])
                rec.wait("scalar", w2tok[sh])
            l1 = rec.dma("sync", stg1, packB_stage(w12B(0, hc), 2048),
                         sem="sta" + str(sh))
            l2 = rec.dma("scalar", stg2, packB_stage(w12B(1, hc), 2048),
                         sem="stb" + str(sh))
            b1 = pb[2 * (hc % 2)]
            b2 = pb[2 * (hc % 2) + 1]
            for half in range(2):
                prev = (hc, 0) if half == 1 else (hc - 2, 1)
                if prev in sig_tok:
                    rec.wait("tensor", sig_tok[prev])
                    rec.wait("tensor", h_toks[prev])
                if (hc - 1, half) in h_toks:
                    rec.wait("scalar", h_toks[(hc - 1, half)])
                rec.wait("tensor", l1)
                mm1 = None
                for g in range(G):
                    mm1 = T(lambda e, stg1=stg1, g=g, half=half, b1=b1:
                            e.matmul(
                        b1[:, 0:512], stg1[:, g * 128:(g + 1) * 128],
                        stream_rhs(g, half), start=(g == 0), stop=(g == G - 1)))
                rec.wait("tensor", l2)
                mm2 = None
                for g in range(G):
                    mm2 = T(lambda e, stg2=stg2, g=g, half=half, b2=b2:
                            e.matmul(
                        b2[:, 0:512], stg2[:, g * 128:(g + 1) * 128],
                        stream_rhs(g, half), start=(g == 0), stop=(g == G - 1)))
                rec.wait("scalar", mm1)
                sg = Aop(lambda e, half=half, b1=b1: e.activation(
                    pbf[:, half * 512:(half + 1) * 512], b1[:, 0:512],
                    AF.Sigmoid))
                sig_tok[(hc, half)] = sg
                rec.wait("vector", mm2)
                rec.wait("vector", sg)
                h_toks[(hc, half)] = V(
                    lambda e, hc=hc, half=half, b2=b2: e.tensor_tensor(
                        hid_ap(hc, half), pbf[:, half * 512:(half + 1) * 512],
                        b2[:, 0:512], ALU.mult))
            w1tok[sh] = h_toks[(hc, 1)]
            w2tok[sh] = h_toks[(hc, 1)]
        hfin = rec.now("dve")
        rec.wait("sync", hfin)
        rec.wait("tensor", hfin)
        w3v = outT[:, 12288:16384]
        bank_tok = [None, None]
        st_tok = [None, None]
        for o in range(G):
            for qtr in range(4):
                sh = qtr % 2
                stg = w3v[:, sh * 2048: sh * 2048 + 1408]
                if st_tok[sh] is not None:
                    rec.wait("sync", st_tok[sh])
                ld = rec.dma("sync", stg,
                             packB_stage(w3B(o), NHC * 128).rearrange(
                                 "p (hc j) -> p hc j", hc=NHC)
                             [:, qtr * 11:(qtr + 1) * 11, :],
                             sem="stc" + str(sh))
                rec.wait("tensor", ld)
                mmt = None
                for hl in range(11):
                    hc = qtr * 11 + hl
                    for half in range(2):
                        bank = pb[2 + half]
                        if hc == 0 and bank_tok[half] is not None:
                            rec.wait("tensor", bank_tok[half])
                        mmt = T(lambda e, stg=stg, hl=hl, hc=hc, half=half,
                                bank=bank: e.matmul(
                            bank[:, 0:512], stg[:, hl * 128:(hl + 1) * 128],
                            hid_ap(hc, half), start=(hc == 0),
                            stop=(hc == NHC - 1)))
                st_tok[sh] = mmt
            rec.wait("vector", mmt)
            for half in range(2):
                bank = pb[2 + half]
                bank_tok[half] = V(
                    lambda e, o=o, half=half, bank=bank: e.tensor_tensor(
                        ybuf[:, o * 1024 + half * 512:
                             o * 1024 + (half + 1) * 512],
                        bank[:, 0:512],
                        stream[:, o * 1024 + half * 512:
                               o * 1024 + (half + 1) * 512], ALU.add))
        barrier()

    def ph_store():
        evs = []
        dts = []
        for t in range(8):
            buf = xstg if t % 2 == 0 else xstg2
            if t >= 2:
                rec.wait("vector", dts[t - 2])
                rec.wait("tensor", dts[t - 2])
            for g in range(G):
                slot = pb[g % 4][:, 0:128]
                idx = t * G + g
                if idx >= 4:
                    rec.wait("tensor", evs[idx - 4])
                tk = T(lambda e, g=g, t=t, slot=slot: e.transpose(
                    slot, fview(g)[:, t * 128:(t + 1) * 128], identf[:]))
                rec.wait("vector", tk)
                evs.append(V(lambda e, g=g, slot=slot, buf=buf: e.tensor_copy(
                    buf[:, g * 128:(g + 1) * 128], slot)))
            rec.wait("sync", evs[-1])
            dts.append(rec.dma("sync", y_out[t * 128:(t + 1) * 128, :], buf,
                               sem="dst" + str(t % 2)))
        tap_srcs = {"stream": (stream, G * 1024), "ybuf": (ybuf, G * 1024),
                    "qs": (qs, G * 1024), "ksb": (ksb, 8192),
                    "vsb": (vsb, 8192), "outT": (outT, G * 1024)}
        for tname in tap_outs:
            src, w = tap_srcs[tname]
            dts.append(rec.dma("gpsimd", tap_outs[tname][:, 0:w],
                               src[:, 0:w], sem="dtp"))
        for sname in ("dst0", "dst1", "dtp"):
            rec.wait("gpsimd", rec.now(sname))

    def mark(name):
        PHASE_MARKS.append((name, rec.cnt["pe"], rec.cnt["dve"],
                            rec.cnt["act"], rec.cnt["gp"]))

    def run_phases(upto=None):
        del PHASE_MARKS[:]
        mark("consts")
        ph_consts()
        mark("prep_A")
        agA = ph_prep_A()
        mark("prep_AB")
        ph_prep_AB()
        mark("prep_B")
        agB = ph_prep_B()
        mark("load_x")
        ph_load_x()
        if upto == "prep":
            barrier(); return
        mark("proj0")
        ag0 = ph_proj_L0(agA)
        if upto == "proj0":
            barrier(); return
        mark("attn0")
        ot = ph_attention(0, ag0)
        if upto == "attn0":
            barrier(); return
        mark("wo0")
        ph_wo(0, ot)
        if upto == "wo0":
            barrier(); return
        mark("ln0")
        ph_ln(final=False)
        if upto == "ln0":
            barrier(); return
        if upto is not None and upto.startswith("ffn0:"):
            ph_ffn0(stop=upto.split(":")[1])
            barrier(); return
        mark("ffn0")
        ph_ffn0()
        if upto == "ffn0":
            barrier(); return
        mark("ln1")
        ph_ln(final=False)
        if upto == "ln1":
            barrier(); return
        mark("proj1")
        ag1 = ph_proj_L1(agB)
        if upto == "proj1":
            barrier(); return
        mark("attn1")
        ot = ph_attention(1, ag1)
        if upto == "attn1":
            barrier(); return
        mark("wo1")
        ph_wo(1, ot)
        if upto == "wo1":
            barrier(); return
        mark("ln2")
        ph_ln(final=False)
        mark("ffn1")
        ph_ffn1(agB)
        if upto == "ffn1":
            barrier(); return
        mark("ln3")
        ph_ln(final=True)
        mark("store")
        ph_store()
        mark("end")

    run_phases(upto)
    if upto is not None:
        for sname in ("dst0", "dst1", "dtp"):
            rec.wait("gpsimd", rec.now(sname))

    with nc.Block() as block:
        for engname, deco in (("tensor", block.tensor), ("vector", block.vector),
                              ("scalar", block.scalar), ("gpsimd", block.gpsimd),
                              ("sync", block.sync)):
            def runner(e, ops=rec.ops[engname]):
                for f in ops:
                    f(e)
            deco(runner)

    ctx.close()
    return nc


_NC_CACHE = {}


def _get_nc(tapkey=()):
    if tapkey not in _NC_CACHE:
        _NC_CACHE[tapkey] = build_program(taps=tapkey)
    return _NC_CACHE[tapkey]


def make_in_maps(inputs):
    ins = {k: np.ascontiguousarray(np.asarray(v)) for k, v in inputs.items()}
    x = ins["L0_attn_x"].reshape(S, D)
    in_maps = []
    for c in range(R):
        m = {}
        m["x_own"] = np.concatenate(
            [x[c * 512:(c + 1) * 512], x[(15 - c) * 512:(16 - c) * 512]], 0)
        m["p_Wq"] = ins["L0_attn_Wq"][:, c * 256:(c + 1) * 256]
        m["p_Wk"] = ins["L0_attn_Wk"][:, c * 256:(c + 1) * 256]
        m["p_Wv"] = ins["L0_attn_Wv"][:, c * 256:(c + 1) * 256]
        m["p_Wo"] = ins["L0_attn_Wo"][:, c * 256:(c + 1) * 256]
        m["p_wq1"] = ins["L1_attn_wq_w"][:, c * 256:(c + 1) * 256]
        m["p_wk1"] = ins["L1_attn_wk_w"][:, c * 256:(c + 1) * 256]
        m["p_wv1"] = ins["L1_attn_wv_w"][:, c * 256:(c + 1) * 256]
        m["p_wo1"] = ins["L1_attn_wo_w"][:, c * 256:(c + 1) * 256]
        if c < 4:
            m["p_wl"] = ins["L1_attn_wkl_w"][:, c * 128:(c + 1) * 128]
        else:
            m["p_wl"] = ins["L1_attn_wvl_w"][:, (c - 4) * 128:(c - 3) * 128]
        m["p_WT"] = ins["L0_ffn_W"][c * 256:(c + 1) * 256, :]
        st, nreal = w12_rank_blocks(c)
        w1s = ins["L1_ffn_w1"][st * 128:(st + nreal) * 128, :]
        w2s = ins["L1_ffn_w2"][st * 128:(st + nreal) * 128, :]
        if nreal < W1B:
            pad = np.zeros(((W1B - nreal) * 128, D), np.float32)
            w1s = np.concatenate([w1s, pad], 0)
            w2s = np.concatenate([w2s, pad], 0)
        m["p_w1"] = w1s
        m["p_w2"] = w2s
        m["p_w3"] = ins["L1_ffn_w3"][c * 256:(c + 1) * 256, :]
        m["bq"] = ins["L0_attn_bq"]
        m["bo"] = ins["L0_attn_bo"]
        m["bv"] = ins["L0_attn_bv"]
        m["f_bias"] = ins["L0_ffn_bias"]
        m["rw1"] = ins["L0_ffn_rw1"]
        m["rb1"] = ins["L0_ffn_rb1"]
        m["rw2"] = ins["L0_ffn_rw2"]
        m["rb2"] = np.ascontiguousarray(ins["L0_ffn_rb2"])
        m["gates"] = np.ascontiguousarray(ins["L0_ffn_gates"])
        m["A_in"] = ins["L0_ffn_A"].reshape(256, D)
        m["B_in"] = ins["L0_ffn_B"]
        m["wq_b"] = ins["L1_attn_wq_b"]
        m["wv_b"] = ins["L1_attn_wv_b"]
        m["wvl_b"] = ins["L1_attn_wvl_b"]
        m["upk_w"] = ins["L1_attn_upk_w"]
        m["upv_w"] = ins["L1_attn_upv_w"]
        m["upv_b"] = ins["L1_attn_upv_b"]
        m["wo_b"] = ins["L1_attn_wo_b"]
        gx = np.zeros((128, GIDX_N), np.int32)
        P = np.arange(128)
        rows_l = [KV0_ROWS, KV1_ROWS]
        for L in (0, 1):
            rws = rows_l[L]
            for hi in (0, 1):
                h = c if hi == 0 else 15 - c
                for mm2 in range(16):
                    rr, half = rank_of_chunk(mm2)
                    kb = KVBASE[L]["k"]
                    vb = KVBASE[L]["v"]
                    gx[:, GIDX_COLS[("k", L, hi, mm2)]] = \
                        (rr * rws + kb + h * 128 + P) * 2 + half
                    gx[:, GIDX_COLS[("v", L, hi, mm2)]] = \
                        (rr * rws + vb) * 2 + P * 32 + half * 16 + h
                    if L == 1:
                        qb = KVBASE[L]["q"]
                        gx[:, GIDX_COLS[("q", L, hi, mm2)]] = \
                            (rr * rws + qb + h * 128 + P) * 2 + half
                gx[:, GIDX_COLS[("cmy", L, hi)]] = h * 128 + P
            for jk in range(16):
                for cc2 in (0, 1):
                    chunk = c if cc2 == 0 else 15 - c
                    row = (jk // 2) * 256 + (jk % 2) * 128 + P
                    gx[:, GIDX_COLS[("ao", L, jk, cc2)]] = row * 16 + chunk
        m["gidx"] = gx
        in_maps.append({k: np.ascontiguousarray(v) for k, v in m.items()})
    return in_maps


def kernel(**inputs):
    nc = _get_nc(())
    in_maps = make_in_maps(inputs)
    res = run_bass_kernel_spmd(nc, in_maps, core_ids=list(range(R)))
    out = np.zeros((S, D), np.float32)
    for c in range(R):
        y = np.asarray(res.results[c]["y_out"])
        out[c * 512:(c + 1) * 512] = y[0:512]
        out[(15 - c) * 512:(16 - c) * 512] = y[512:1024]
    return out.reshape(1, S, D)


# version 05 (final)
# speedup vs baseline: 1.6738x; 1.0055x over previous
import numpy as np
from collections import defaultdict
from contextlib import ExitStack

import concourse.bass as bass
import concourse.mybir as mybir
from concourse.bass_utils import run_bass_kernel_spmd

F32 = mybir.dt.float32
BF16 = mybir.dt.bfloat16
AF = mybir.ActivationFunctionType
ALU = mybir.AluOpType
AX = mybir.AxisListType
ds = bass.ds

R = 8
D = 2048
S = 8192
NTOK = 1024
NH = 16
DL = 512
HID = 5632
NHC = 44
W1B = 6
G = 16
SCALE = float(1.0 / np.float32(np.sqrt(np.float32(128.0))))
NEG = -1.0e30
EPS_LN = 1.0e-5

NAT_MATS = ["Wq", "Wk", "Wv", "Wo", "wq1", "wk1", "wv1", "wo1"]
BLK = 128 * G * 128
W3BLK = 128 * NHC * 128

_packA_off, _oa = {}, 0
for _m in NAT_MATS:
    _packA_off[_m] = _oa
    _oa += 2 * BLK
_packA_off["WL"] = _oa; _oa += BLK
_packA_off["WT"] = _oa; _oa += 2 * BLK
PACKA_ELEMS = _oa
PACKA_ROWS = PACKA_ELEMS // 1024

_packB_off = {"W1Q": 0, "W2Q": W1B * BLK, "W3Q": 2 * W1B * BLK}
PACKB_ELEMS = 2 * W1B * BLK + 2 * W3BLK
PACKB_ROWS = PACKB_ELEMS // 1024


def natA(mat, o):
    r, lo = o // 2, o % 2
    return r * PACKA_ELEMS + _packA_off[mat] + lo * BLK


def wlA(which, o):
    return (o + 4 * which) * PACKA_ELEMS + _packA_off["WL"]


def wtA(o):
    r, lo = o // 2, o % 2
    return r * PACKA_ELEMS + _packA_off["WT"] + lo * BLK


def w12B(which, hc):
    if hc < 24:
        r, hb = hc // 6, hc % 6
    else:
        r, hb = 4 + (hc - 24) // 5, (hc - 24) % 5
    return r * PACKB_ELEMS + _packB_off["W1Q" if which == 0 else "W2Q"] + hb * BLK


def w12_rank_blocks(r):
    return (6 * r, 6) if r < 4 else (24 + 5 * (r - 4), 5)


def w3B(o):
    r, lo = o // 2, o % 2
    return r * PACKB_ELEMS + _packB_off["W3Q"] + lo * W3BLK


def rank_of_chunk(m):
    return (m, 0) if m < 8 else (15 - m, 1)


GIDX_COLS = {}
_gc = 0


def _gcol(key):
    global _gc
    if key not in GIDX_COLS:
        GIDX_COLS[key] = _gc
        _gc += 1
    return GIDX_COLS[key]


for _L in (0, 1):
    for _hi in (0, 1):
        for _m in range(16):
            _gcol(("k", _L, _hi, _m))
            _gcol(("v", _L, _hi, _m))
            if _L == 1:
                _gcol(("q", _L, _hi, _m))
        _gcol(("cmy", _L, _hi))
    for _jk in range(16):
        for _cc in (0, 1):
            _gcol(("ao", _L, _jk, _cc))
GIDX_N = _gc

KV0_ROWS = 2048 + 2048 + 2
KV1_ROWS = 2048 + 2048 + 2048 + 2
KVBASE = [
    {"q": None, "k": 0, "v": 2048, "c": 4096, "rows": KV0_ROWS},
    {"q": 0, "k": 2048, "v": 4096, "c": 6144, "rows": KV1_ROWS},
]


def ghead(jk):
    j, k = jk // 2, jk % 2
    return j if k == 0 else 15 - j


class Rec:
    ENGS = ("tensor", "vector", "scalar", "gpsimd", "sync")

    def __init__(self, sems):
        self.sems = sems
        self.ops = {e: [] for e in self.ENGS}
        self.cnt = defaultdict(int)

    def op(self, eng, fn, inc=None, n=1):
        if inc is None:
            self.ops[eng].append(fn)
            return None
        self.cnt[inc] += n
        tok = (inc, self.cnt[inc])
        sem = self.sems[inc]
        self.ops[eng].append(lambda e, fn=fn, s=sem, k=n: fn(e).then_inc(s, k))
        return tok

    def wait(self, eng, tok):
        if tok is None:
            return
        s, v = self.sems[tok[0]], tok[1]
        self.ops[eng].append(lambda e, s=s, v=v: e.wait_ge(s, v))

    def now(self, sem):
        return (sem, self.cnt[sem])

    def dma(self, eng, out, in_, sem=None, **kw):
        if sem is None:
            sem = {"sync": "dsp", "scalar": "dact", "gpsimd": "dgp"}[eng]
        return self.op(eng,
                       lambda e, o=out, i=in_, kw=kw: e.dma_start(
                           out=o, in_=i, **kw),
                       inc=sem, n=16)


PHASE_MARKS = []


def build_program(taps=(), upto=None):
    nc = bass.Bass("TRN2", target_bir_lowering=False, debug=False)

    def din(name, shape, dtype=F32):
        return nc.dram_tensor(name, list(shape), dtype, kind="ExternalInput").ap()

    x_own = din("x_own", (NTOK, D))
    p_nat = {m: din("p_" + m, (D, 256)) for m in NAT_MATS}
    p_wl = din("p_wl", (D, 128))
    p_WT = din("p_WT", (256, D))
    p_w1 = din("p_w1", (W1B * 128, D))
    p_w2 = din("p_w2", (W1B * 128, D))
    p_w3 = din("p_w3", (256, HID))
    bq_i = din("bq", (D,))
    bo_i = din("bo", (D,))
    bv_i = din("bv", (D,))
    fb_i = din("f_bias", (D,))
    rw1_i = din("rw1", (64, D))
    rb1_i = din("rb1", (64,))
    rw2_i = din("rw2", (32, 64))
    rb2_i = din("rb2", (32,))
    gates_i = din("gates", (16,))
    A_i = din("A_in", (256, D))
    B_i = din("B_in", (NH, D, 16))
    wqb_i = din("wq_b", (D,))
    wvb_i = din("wv_b", (D,))
    wvlb_i = din("wvl_b", (DL,))
    upk_i = din("upk_w", (NH, DL, 128))
    upv_i = din("upv_w", (NH, DL, 128))
    upvb_i = din("upv_b", (NH, 128))
    wob_i = din("wo_b", (D,))
    gidx_i = din("gidx", (128, GIDX_N), mybir.dt.int32)

    y_out = nc.dram_tensor("y_out", [NTOK, D], F32, kind="ExternalOutput").ap()
    tap_outs = {}
    for tname in taps:
        tap_outs[tname] = nc.dram_tensor(
            "tap_" + tname, [128, G * 1024], F32, kind="ExternalOutput").ap()

    packA_in = nc.dram_tensor("packA_in", [PACKA_ROWS, 1024], BF16).ap()
    packA_out = nc.dram_tensor("packA_out", [R * PACKA_ROWS, 1024], BF16,
                               addr_space="Shared").ap()
    packB_in = nc.dram_tensor("packB_in", [PACKB_ROWS, 1024], BF16).ap()
    packB_out = nc.dram_tensor("packB_out", [R * PACKB_ROWS, 1024], BF16,
                               addr_space="Shared").ap()
    gs_in = nc.dram_tensor("gs_in", [1, 16], F32).ap()
    gs_out = nc.dram_tensor("gs_out", [1, 16], F32).ap()
    kv_in = [nc.dram_tensor("kv0_in", [KV0_ROWS, 1024], BF16).ap(),
             nc.dram_tensor("kv1_in", [KV1_ROWS, 1024], BF16).ap()]
    kv_out = [nc.dram_tensor("kv0_out", [R * KV0_ROWS, 1024], BF16,
                             addr_space="Shared").ap(),
              nc.dram_tensor("kv1_out", [R * KV1_ROWS, 1024], BF16,
                             addr_space="Shared").ap()]
    ao_in = nc.dram_tensor("ao_in", [256, S], BF16).ap()
    ao_out = nc.dram_tensor("ao_out", [D, S], BF16,
                            addr_space="Shared").ap()
    abT = nc.dram_tensor("abT", [1536, 1024], BF16).ap()
    cscr = nc.dram_tensor("cscr", [2, 1024], BF16).ap()

    ctx = ExitStack()
    e = ctx.enter_context

    identf = e(nc.sbuf_tensor("identf", [128, 128], F32))
    identb = e(nc.sbuf_tensor("identb", [128, 128], BF16))
    stream = e(nc.sbuf_tensor("stream", [128, G * 1024], BF16))
    ybuf = e(nc.sbuf_tensor("ybuf", [128, G * 1024], BF16))
    qs = e(nc.sbuf_tensor("qs", [128, G * 1024], BF16))
    ksb = e(nc.sbuf_tensor("ksb", [128, 8192], BF16))
    vsb = e(nc.sbuf_tensor("vsb", [128, 8192], BF16))
    outT = e(nc.sbuf_tensor("outT", [128, G * 1024], BF16))
    ppre = e(nc.sbuf_tensor("ppre", [128, 1024], F32))
    pbf = e(nc.sbuf_tensor("pbf", [128, 1024], BF16))
    ascr = e(nc.sbuf_tensor("ascr", [128, 2560], F32))
    t1 = e(nc.sbuf_tensor("t1", [128, 1024], F32))
    wsa = e(nc.sbuf_tensor("wsa", [128, 4096], BF16))
    wsb = e(nc.sbuf_tensor("wsb", [128, 4096], BF16))
    absq = e(nc.sbuf_tensor("absq", [128, 512], BF16))
    smf = e(nc.sbuf_tensor("smf", [128, 512], F32))
    smb = e(nc.sbuf_tensor("smb", [128, 128], BF16))
    ebf = e(nc.sbuf_tensor("ebf", [128, 256], BF16))
    rwT = e(nc.sbuf_tensor("rwT", [128, 1088], BF16))
    gix = e(nc.sbuf_tensor("gix", [128, GIDX_N], mybir.dt.int32))

    xstg = wsa.ap().bitcast(F32)
    xstg2 = wsb.ap().bitcast(F32)
    qs32 = qs.ap().bitcast(F32)
    ksb32 = ksb.ap().bitcast(F32)
    vsb32 = vsb.ap().bitcast(F32)
    ysqv = pbf.ap()
    kvstg = ppre.ap().bitcast(BF16)
    outT32 = outT.ap().bitcast(F32)
    t1b = t1.ap().bitcast(BF16)

    ck = smf[:, 0:16]
    bq_f = smf[:, 16:32]
    wqb_f = smf[:, 32:48]
    cb0 = smf[:, 48:64]
    cb1 = smf[:, 64:80]
    fb_f = smf[:, 80:96]
    cc1 = smf[:, 96:112]
    v1c = smf[:, 112:116]
    wvlb_f = smf[:, 116:120]
    upvb_f = smf[:, 120:136]
    bv_f = smf[:, 136:152]
    rgv = smf[:, 152:155]
    rgn = smf[:, 155:158]
    rb1_f = smf[:, 158:159]
    onescol_f = smf[:, 160:161]
    gacc = smf[:, 161:167]
    gtmp = smf[:, 167:168]
    gapd = smf[0:1, 168:176]
    gapsrc = smf[0:1, 176:184]
    ones_row = smf[0:1, 184:312]
    grow = smf[0:1, 312:328]
    g3a = smf[0:1, 328:331]
    g3b = smf[0:1, 331:334]
    g3c = smf[0:1, 334:337]
    bo_f = smf[:, 352:368]
    wob_f = smf[:, 368:384]
    wvb_f = smf[:, 384:400]
    rb2_row = smf[0:1, 400:416]
    gates_row = smf[0:1, 416:432]
    rb2g_row = smf[0:1, 432:448]

    cfeat = smb[:, 0:16]
    cmy = smb[:, 16:18]
    ones_b = smb[:, 18:19]
    cvb = smb[:, 20:36]
    wvb_b = smb[:, 36:52]
    v1cb = smb[:, 52:56]

    mhb = ascr[:, 0:512]
    rbb = ascr[:, 512:1024]
    mrow = ascr[0:1, 2048:2560]
    rstd_row = ascr[0:1, 0:1024]
    m2_row = ascr[0:1, 1024:2048]
    muK = ascr[0:1, 2048:2560]

    pband = [e(nc.psum_tensor(f"pb{i}", [128, 512], F32)) for i in range(8)]
    pb = [p.ap() for p in pband]
    pbb = [p.ap().bitcast(BF16) for p in pband]

    sems = {n: e(nc.semaphore(n))
            for n in ("pe", "dve", "act", "gp", "dsp", "dact", "dgp", "cc",
                      "dcn", "sta0", "sta1", "stb0", "stb1", "stc0", "stc1",
                      "std", "kvo0", "kvo1", "dup", "dld",
                      "dao", "dcm", "dkv", "dst0", "dst1", "dtp",
                      "cA", "cR", "cB", "ck0", "ck1", "cao")
            + tuple(f"dl{m}" for m in range(16))}
    rec = Rec(sems)

    def T(fn):
        return rec.op("tensor", fn, inc="pe")

    def V(fn):
        return rec.op("vector", fn, inc="dve")

    def Aop(fn):
        return rec.op("scalar", fn, inc="act")

    def Gop(fn):
        return rec.op("gpsimd", fn, inc="gp")

    def vgap():
        t = rec.now("dve")
        rec.wait("vector", t)
        return t

    def gwait():
        t = rec.now("gp")
        rec.wait("gpsimd", t)
        return t

    def barrier():
        toks = [rec.now(s) for s in sems]
        for eng in Rec.ENGS:
            for t in toks:
                rec.wait(eng, t)


    pA_flat = packA_in.rearrange("r c -> (r c)")
    pB_flat = packB_in.rearrange("r c -> (r c)")
    pAo_flat = packA_out.rearrange("r c -> (r c)")
    pBo_flat = packB_out.rearrange("r c -> (r c)")
    abT_flat = abT.rearrange("r c -> (r c)")
    kvi_flat = [k.rearrange("r c -> (r c)") for k in kv_in]
    kvo_flat = [k.rearrange("r c -> (r c)") for k in kv_out]
    kvo512 = [k.rearrange("r c -> (r c)").rearrange("(n b) -> n b", b=512)
              for k in kv_out]
    ao512 = ao_out.rearrange("r c -> (r c)").rearrange("(n b) -> n b", b=512)

    GATHER_SEM = {"k": "dld", "v": "dld", "q": "dld", "ao": "dao",
                  "cmy": "dcm"}

    def gather(out_ap, src512, key, sem=None):
        col = GIDX_COLS[key]
        return rec.op("gpsimd", lambda e, o=out_ap, s=src512, col=col:
                      e.indirect_dma_start(
                          out=o, out_offset=None, in_=s,
                          in_offset=bass.IndirectOffsetOnAxis(
                              gix[:, col:col + 1], 0)),
                      inc=sem or GATHER_SEM[key[0]], n=16)

    def packA_stage(elem_off, n):
        return pAo_flat[ds(elem_off, 128 * n)].rearrange("(p c) -> p c", p=128)

    def packB_stage(elem_off, n):
        return pBo_flat[ds(elem_off, 128 * n)].rearrange("(p c) -> p c", p=128)

    def ph_consts():
        def Gs(fn):
            t = Gop(fn)
            rec.wait("gpsimd", t)
            return t
        Gs(lambda e: e.memset(identf[:], 0.0))
        Gs(lambda e: e.affine_select(out=identf[:], in_=identf[:],
                                      compare_op=ALU.not_equal, fill=1.0, base=0,
                                      pattern=[[-1, 128]], channel_multiplier=1))
        Gs(lambda e: e.memset(identb[:], 0.0))
        Gs(lambda e: e.affine_select(out=identb[:], in_=identb[:],
                                      compare_op=ALU.not_equal, fill=1.0, base=0,
                                      pattern=[[-1, 128]], channel_multiplier=1))
        Gs(lambda e: e.memset(ones_b, 1.0))
        Gs(lambda e: e.memset(onescol_f, 1.0))
        Gs(lambda e: e.memset(ones_row, 1.0))
        Gs(lambda e: e.memset(gapsrc, 0.0))
        Gs(lambda e: e.memset(ebf[:], 0.0))
        for blk in range(2):
            sl = ebf[0:16, blk * 128:(blk + 1) * 128]
            Gs(lambda e, sl=sl: e.memset(sl, 1.0))
            Gs(lambda e, sl=sl, blk=blk: e.affine_select(
                out=sl, in_=sl, compare_op=ALU.is_ge, fill=0.0,
                base=128 * blk, pattern=[[1, 128]], channel_multiplier=-16))
            Gs(lambda e, sl=sl, blk=blk: e.affine_select(
                out=sl, in_=sl, compare_op=ALU.is_ge, fill=0.0,
                base=15 - 128 * blk, pattern=[[-1, 128]],
                channel_multiplier=16))
        for dst, src, p in ((bq_f, bq_i, 128), (wqb_f, wqb_i, 128),
                            (bv_f, bv_i, 128), (fb_f, fb_i, 128),
                            (bo_f, bo_i, 128), (wob_f, wob_i, 128),
                            (wvb_f, wvb_i, 128),
                            (upvb_f, upvb_i.rearrange("h d -> (h d)"), 128),
                            (wvlb_f, wvlb_i, 128),
                            (rb1_f, rb1_i, 64)):
            gg = src.shape[0] // p
            rec.dma("gpsimd", dst[0:p, 0:gg],
                    src.rearrange("(g p) -> p g", p=p), sem="dcn",
                    allow_slow_non_contiguous=True)
        rec.dma("gpsimd", rb2_row, rb2_i[ds(0, 16)].rearrange(
            "(o c) -> o c", o=1), sem="dcn")
        rec.dma("gpsimd", gates_row, gates_i.rearrange("(o c) -> o c", o=1),
                sem="dcn")
        rec.dma("gpsimd", gix[:], gidx_i[:, :], sem="dcn")
        gd = rec.now("dcn")
        rec.wait("vector", gd)
        V(lambda e: e.tensor_copy(wvb_b, wvb_f))
        V(lambda e: e.tensor_copy(cvb, bv_f))
        V(lambda e: e.tensor_tensor(rb2g_row, rb2_row, gates_row, ALU.add))
        vgap()
        V(lambda e: e.tensor_copy(rwT[64:65, 1024:1040], rb2g_row))
        barrier()

    def nat_prep_one(src_ap, dst_elems, nblocks):
        last = None
        for lo in range(nblocks):
            ld = rec.dma("sync",
                         xstg2[:, 0:2048].rearrange("p (g c) -> p g c", g=G),
                         src_ap[:, lo * 128:(lo + 1) * 128].rearrange(
                             "(g p) c -> p g c", p=128), sem="stb0")
            rec.wait("vector", ld)
            if last is not None:
                rec.wait("vector", last)
            cv = V(lambda e: e.tensor_copy(wsa[:, 0:2048], xstg2[:, 0:2048]))
            rec.wait("gpsimd", cv)
            dst = pA_flat[ds(dst_elems + lo * BLK, BLK)].rearrange(
                "(p c) -> p c", p=128)
            last = rec.dma("gpsimd", dst, wsa[:, 0:2048])
            rec.wait("gpsimd", last)
            rec.wait("sync", last)
        return last

    def ph_prep_A():
        for m in NAT_MATS:
            tok = nat_prep_one(p_nat[m], _packA_off[m], 2)
            rec.wait("sync", tok)
            rec.wait("vector", tok)
        tok = nat_prep_one(p_wl, _packA_off["WL"], 1)
        rec.wait("sync", tok)
        rec.wait("vector", tok)

        for lo in range(2):
            ld = rec.dma("sync", xstg2[:, 0:2048],
                         p_WT[lo * 128:(lo + 1) * 128, :], sem="stb0")
            rec.wait("vector", ld)
            cv = V(lambda e: e.tensor_copy(kvstg[:, 0:2048], xstg2[:, 0:2048]))
            rec.wait("tensor", cv)
            evs = []
            for g in range(G):
                slot = pbb[g % 4][:, 0:128]
                if g >= 4:
                    rec.wait("tensor", evs[g - 4])
                tt = T(lambda e, g=g, slot=slot: e.transpose(
                    slot, kvstg[:, g * 128:(g + 1) * 128], identb[:]))
                rec.wait("vector", tt)
                evs.append(V(lambda e, g=g, slot=slot: e.tensor_copy(
                    wsa[:, 2048 + g * 128:2048 + (g + 1) * 128], slot)))
            rec.wait("gpsimd", evs[-1])
            dst = pA_flat[ds(_packA_off["WT"] + lo * BLK, BLK)].rearrange(
                "(p c) -> p c", p=128)
            dt = rec.dma("gpsimd", dst, wsa[:, 2048:4096])
            rec.wait("gpsimd", dt)
            rec.wait("sync", dt)
            rec.wait("vector", dt)

        gtok = rec.now("dgp")
        rec.wait("gpsimd", gtok)
        agA = rec.op("gpsimd", lambda e: e.collective_compute(
            "AllGather", ALU.bypass, replica_groups=[list(range(R))],
            ins=[packA_in[:, :]], outs=[packA_out[:, :]]), inc="cA")
        return agA

    def ph_prep_AB():
        for erb in range(2):
            ld = rec.dma("sync", xstg2[:, 0:2048],
                         A_i[erb * 128:(erb + 1) * 128, :], sem="stb0")
            rec.wait("vector", ld)
            cv = V(lambda e: e.tensor_copy(kvstg[:, 0:2048], xstg2[:, 0:2048]))
            rec.wait("tensor", cv)
            evs = []
            for g in range(G):
                slot = pbb[g % 4][:, 0:128]
                if g >= 4:
                    rec.wait("tensor", evs[g - 4])
                tt = T(lambda e, g=g, slot=slot: e.transpose(
                    slot, kvstg[:, g * 128:(g + 1) * 128], identb[:]))
                rec.wait("vector", tt)
                evs.append(V(lambda e, g=g, slot=slot: e.tensor_copy(
                    wsa[:, 2048 + g * 128:2048 + (g + 1) * 128], slot)))
            rec.wait("gpsimd", evs[-1])
            dst = abT_flat[ds(erb * BLK, BLK)].rearrange("(p c) -> p c", p=128)
            dt = rec.dma("gpsimd", dst, wsa[:, 2048:4096])
            rec.wait("gpsimd", dt)
            rec.wait("sync", dt)
            rec.wait("vector", dt)

        Bf = B_i.rearrange("e o r -> (e o r)")
        for erb in range(2):
            srcv = Bf[ds(erb * 8 * D * 16, 8 * D * 16)].rearrange(
                "(e ob p r) -> p ob e r", e=8, ob=16, p=128)
            for ob in range(G):
                ld = rec.dma("sync", xstg2[:, ob * 128:(ob + 1) * 128]
                             .rearrange("p (e r) -> p e r", e=8),
                             srcv[:, ob, :, :], sem="stb0")
            rec.wait("vector", ld)
            cv = V(lambda e: e.tensor_copy(kvstg[:, 0:2048], xstg2[:, 0:2048]))
            rec.wait("tensor", cv)
            evs = []
            for ob in range(G):
                slot = pbb[ob % 4][:, 0:128]
                if ob >= 4:
                    rec.wait("tensor", evs[ob - 4])
                src_t = kvstg[:, ob * 128:(ob + 1) * 128]
                tt = T(lambda e, s=src_t, slot=slot: e.transpose(
                    slot, s, identb[:]))
                rec.wait("vector", tt)
                evs.append(V(lambda e, ob=ob, slot=slot: e.tensor_copy(
                    wsa[:, 2048 + ob * 128:2048 + (ob + 1) * 128], slot)))
            rec.wait("gpsimd", evs[-1])
            for ob in range(G):
                dst = abT_flat[ds(2 * BLK + ob * 32768 + erb * 16384, 16384)] \
                    .rearrange("(p c) -> p c", p=128)
                dt = rec.dma("gpsimd", dst,
                             wsa[:, 2048 + ob * 128:2048 + (ob + 1) * 128])
            rec.wait("gpsimd", dt)
            rec.wait("sync", dt)
            rec.wait("vector", dt)

        ld = rec.dma("sync", xstg2[0:64, 0:2048], rw1_i[:, :], sem="stb0")
        rec.wait("vector", ld)
        cv = V(lambda e: e.tensor_copy(kvstg[0:64, 0:2048], xstg2[0:64, 0:2048]))
        rec.wait("tensor", cv)
        evs = []
        for g in range(G):
            slot = pbb[g % 4][:, 0:128]
            if g >= 4:
                rec.wait("tensor", evs[g - 4])
            tt = T(lambda e, g=g, slot=slot: e.transpose(
                slot[:, 0:64], kvstg[0:64, g * 128:(g + 1) * 128],
                identb[0:64, 0:64]))
            rec.wait("vector", tt)
            evs.append(V(lambda e, g=g, slot=slot: e.tensor_copy(
                rwT[:, g * 64:(g + 1) * 64], slot[:, 0:64])))
        rec.wait("sync", cv)
        ld2 = rec.dma("sync", xstg2[0:32, 0:64], rw2_i[:, :], sem="stb0")
        rec.wait("vector", ld2)
        cv2 = V(lambda e: e.tensor_copy(kvstg[0:32, 0:64], xstg2[0:32, 0:64]))
        rec.wait("tensor", cv2)
        tt = T(lambda e: e.transpose(pbb[0][0:64, 256:288],
                                     kvstg[0:32, 0:64], identb[0:32, 0:32]))
        rec.wait("vector", tt)
        V(lambda e: e.tensor_copy(rwT[0:64, 1024:1056], pbb[0][0:64, 256:288]))
        rec.wait("sync", rec.now("dve"))

    def ph_prep_B():
        V(lambda e: e.memset(gacc, 0.0))
        blocks = []
        for b in range(W1B):
            blocks.append((0, p_w1[b * 128:(b + 1) * 128, :], D))
        for b in range(W1B):
            blocks.append((1, p_w2[b * 128:(b + 1) * 128, :], D))
        for lo in range(2):
            for piece in range(4):
                blocks.append((2, p_w3[lo * 128:(lo + 1) * 128,
                                       piece * 1408:(piece + 1) * 1408], 1408))
        cnt_wi = [0, 0, 0]
        rtok = None
        for wi, src, CW in blocks:
            if rtok is not None:
                rec.wait("sync", rtok)
            ld = rec.dma("sync", ksb32[:, 0:CW], src, sem="stb0")
            rec.wait("vector", ld)
            V(lambda e, CW=CW: e.reduce_sum(gtmp, ksb32[:, 0:CW], AX.X,
                                            apply_absolute_value=True))
            rtok = vgap()
            a = cnt_wi[wi] % 2
            b = (cnt_wi[wi] + 1) % 2
            V(lambda e, wi=wi, a=a, b=b: e.tensor_tensor(
                gacc[:, wi * 2 + b:wi * 2 + b + 1],
                gacc[:, wi * 2 + a:wi * 2 + a + 1], gtmp, ALU.add))
            vgap()
            cnt_wi[wi] += 1
        assert all(c % 2 == 0 for c in cnt_wi), cnt_wi
        pcom = vgap()
        rec.wait("tensor", pcom)
        gfin = gacc.rearrange("p (w t) -> p w t", t=2)[:, :, 0]
        mm = T(lambda e, gfin=gfin: e.matmul(pb[4][0:1, 0:3], onescol_f, gfin,
                                             start=True, stop=True))
        rec.wait("vector", mm)
        V(lambda e: e.memset(grow, 0.0))
        vgap()
        V(lambda e: e.tensor_copy(grow[:, 0:3], pb[4][0:1, 0:3]))
        gdone = vgap()
        rec.wait("gpsimd", gdone)
        d1 = rec.dma("gpsimd", gs_in[:, :], grow[:, 0:16])
        rec.wait("gpsimd", d1)
        ar = rec.op("gpsimd", lambda e: e.collective_compute(
            "AllReduce", ALU.add, replica_groups=[list(range(R))],
            ins=[gs_in[:, :]], outs=[gs_out[:, :]]), inc="cR")
        rec.wait("gpsimd", ar)
        d2 = rec.dma("gpsimd", grow[:, 0:16], gs_out[:, :])
        rec.wait("gpsimd", d2)
        rec.wait("vector", d2)
        NW = float(HID) * float(D)
        V(lambda e: e.tensor_scalar(g3a, grow[:, 0:3], 1.0 / NW, 1.0e-8,
                                    ALU.mult, ALU.add))
        vgap()
        V(lambda e: e.reciprocal(g3b, g3a))
        vgap()
        V(lambda e: e.tensor_scalar(g3c, g3b, -1.0, None, ALU.mult))
        gk = vgap()
        rec.wait("tensor", gk)
        mm1 = T(lambda e: e.matmul(pb[4][:, 0:3], ones_row, g3b,
                                   start=True, stop=True))
        mm2 = T(lambda e: e.matmul(pb[5][:, 0:3], ones_row, g3c,
                                   start=True, stop=True))
        rec.wait("vector", mm2)
        V(lambda e: e.tensor_copy(rgv, pb[4][:, 0:3]))
        cpr = V(lambda e: e.tensor_copy(rgn, pb[5][:, 0:3]))
        rec.wait("sync", cpr)

        def quant_into_vsb(src_slice, wi, CW):
            ld = rec.dma("sync", ksb32[:, 0:CW], src_slice, sem="stb0")
            rec.wait("vector", ld)
            vgap()
            V(lambda e, CW=CW, wi=wi: e.tensor_scalar(
                vsb[:, 0:CW], ksb32[:, 0:CW], rgv[:, wi:wi + 1], 0.5,
                ALU.mult, ALU.is_ge))
            V(lambda e, CW=CW, wi=wi: e.tensor_scalar(
                vsb[:, 2048:2048 + CW], ksb32[:, 0:CW], rgn[:, wi:wi + 1], 0.5,
                ALU.mult, ALU.is_ge))
            vgap()
            return V(lambda e, CW=CW: e.tensor_tensor(
                vsb[:, 4096:4096 + CW], vsb[:, 0:CW], vsb[:, 2048:2048 + CW],
                ALU.subtract))

        for wi, src in ((0, p_w1), (1, p_w2)):
            for b in range(W1B):
                qv = quant_into_vsb(src[b * 128:(b + 1) * 128, :], wi, D)
                rec.wait("tensor", qv)
                evs = []
                for g in range(G):
                    slot = pbb[4 + g % 4][:, 256:384]
                    if g >= 4:
                        rec.wait("tensor", evs[g - 4])
                    tt = T(lambda e, g=g, slot=slot: e.transpose(
                        slot, vsb[:, 4096 + g * 128:4096 + (g + 1) * 128],
                        identb[:]))
                    rec.wait("vector", tt)
                    evs.append(V(lambda e, g=g, slot=slot: e.tensor_copy(
                        vsb[:, 6144 + g * 128:6144 + (g + 1) * 128], slot)))
                rec.wait("gpsimd", evs[-1])
                dst = pB_flat[ds(_packB_off["W1Q" if wi == 0 else "W2Q"]
                                 + b * BLK, BLK)].rearrange("(p c) -> p c", p=128)
                dt = rec.dma("gpsimd", dst, vsb[:, 6144:8192])
                rec.wait("gpsimd", dt)
                rec.wait("sync", dt)
                rec.wait("vector", dt)

        for lo in range(2):
            for piece in range(4):
                cw = 11 * 128
                qv = quant_into_vsb(
                    p_w3[lo * 128:(lo + 1) * 128,
                         piece * cw:(piece + 1) * cw], 2, cw)
                rec.wait("tensor", qv)
                evs = []
                for t in range(11):
                    slot = pbb[4 + t % 4][:, 256:384]
                    if t >= 4:
                        rec.wait("tensor", evs[t - 4])
                    tt = T(lambda e, t=t, slot=slot: e.transpose(
                        slot, vsb[:, 4096 + t * 128:4096 + (t + 1) * 128],
                        identb[:]))
                    rec.wait("vector", tt)
                    evs.append(V(lambda e, t=t, slot=slot: e.tensor_copy(
                        vsb[:, 6144 + t * 128:6144 + (t + 1) * 128], slot)))
                rec.wait("gpsimd", evs[-1])
                dst = pB_flat[ds(_packB_off["W3Q"] + lo * W3BLK, W3BLK)] \
                    .rearrange("(p hc j) -> p hc j", p=128, hc=NHC) \
                    [:, piece * 11:(piece + 1) * 11, :]
                dt = rec.dma("gpsimd", dst,
                             vsb[:, 6144:6144 + cw].rearrange(
                                 "p (hc j) -> p hc j", hc=11))
                rec.wait("gpsimd", dt)
                rec.wait("sync", dt)
                rec.wait("vector", dt)

        gtok = rec.now("dgp")
        rec.wait("gpsimd", gtok)
        agB = rec.op("gpsimd", lambda e: e.collective_compute(
            "AllGather", ALU.bypass, replica_groups=[list(range(R))],
            ins=[packB_in[:, :]], outs=[packB_out[:, :]]), inc="cB")
        return agB

    def ph_load_x():
        evs = []
        slot_ev = [None, None]
        for t in range(8):
            buf = xstg2 if t % 2 == 0 else xstg
            snm = "stb0" if t % 2 == 0 else "sta0"
            if slot_ev[t % 2] is not None:
                rec.wait("sync", slot_ev[t % 2])
            ld = rec.dma("sync", buf[:, 0:2048],
                         x_own[t * 128:(t + 1) * 128, :], sem=snm)
            rec.wait("tensor", ld)
            for g in range(G):
                slot = pb[g % 4][:, 0:128]
                idx = t * G + g
                if idx >= 4:
                    rec.wait("tensor", evs[idx - 4])
                tt = T(lambda e, buf=buf, g=g, slot=slot: e.transpose(
                    slot, buf[:, g * 128:(g + 1) * 128], identf[:]))
                rec.wait("vector", tt)
                evs.append(V(lambda e, g=g, t=t, slot=slot: e.tensor_copy(
                    stream[:, g * 1024 + t * 128:g * 1024 + (t + 1) * 128],
                    slot)))
            slot_ev[t % 2] = evs[-1]

    def proj16(addr_fn, rhs_fn, evac_fn, stage_cols=2048, nout=16,
               pre_tok=None, lhs_g_of=None):
        bank_tok = [None, None, None, None]
        stage_tok = [None, None]
        ld_tok = [None, None]
        rec.wait("sync", rec.now("pe"))
        rec.wait("tensor", rec.now("dve"))
        for o in range(nout):
            sbuf_half = o % 2
            stg = wsa[:, sbuf_half * 2048: sbuf_half * 2048 + stage_cols]
            if stage_tok[sbuf_half] is not None:
                rec.wait("sync", stage_tok[sbuf_half])
            if pre_tok is not None:
                rec.wait("sync", pre_tok)
                pre_tok = None
            ld_tok[sbuf_half] = rec.dma("sync", stg, addr_fn(o),
                                        sem="sta" + str(sbuf_half))
            mmtok = None
            for half in range(2):
                bnk = (2 * o + half) % 4
                bank = pb[bnk]
                if bank_tok[bnk] is not None:
                    rec.wait("tensor", bank_tok[bnk])
                rec.wait("tensor", ld_tok[sbuf_half])
                for g in range(G):
                    gl = lhs_g_of(g) if lhs_g_of else g
                    mmtok = T(lambda e, stg=stg, gl=gl, g=g, half=half,
                              bank=bank: e.matmul(
                        bank[:, 0:512], stg[:, gl * 128:(gl + 1) * 128],
                        rhs_fn(g, half), start=(g == 0), stop=(g == G - 1)))
                bank_tok[bnk] = evac_fn(o, half, bank, mmtok)
            stage_tok[sbuf_half] = bank_tok[(2 * o + 1) % 4]
        return bank_tok

    kv_dma_toks = []

    KVSTATE = {}

    def mk_evac_kv(layer, row_base, with_ck, bias_ap=None):
        state = KVSTATE

        def evac(o, half, bank, mmtok):
            buf = kvstg[:, (o % 2) * 1024:(o % 2) * 1024 + 1024]
            rec.wait("vector", mmtok)
            if (o % 2) in state:
                rec.wait("vector", state[o % 2])
            if bias_ap is not None:
                tok = V(lambda e, buf=buf, half=half, o=o, bank=bank:
                        e.tensor_scalar(buf[:, half * 512:(half + 1) * 512],
                                        bank[:, 0:512], bias_ap[:, o:o + 1],
                                        None, ALU.add))
            else:
                tok = V(lambda e, buf=buf, half=half, bank=bank: e.tensor_copy(
                    buf[:, half * 512:(half + 1) * 512], bank[:, 0:512]))
            if half == 1:
                if with_ck:
                    vgap()
                    V(lambda e, buf=buf, o=o: e.reduce_max(
                        ck[:, o:o + 1], buf, AX.X, apply_absolute_value=True))
                    tok = vgap()
                rec.wait("sync", tok)
                dt = rec.dma("sync",
                             kv_in[layer][row_base + o * 128:
                                          row_base + (o + 1) * 128, :], buf,
                             sem="kvo" + str(o % 2))
                state[o % 2] = dt
            return tok
        return evac

    def mk_evac_sbuf(dst_fn, bias_ap=None):
        def evac(o, half, bank, mmtok):
            rec.wait("vector", mmtok)
            if bias_ap is not None:
                return V(lambda e, o=o, half=half, bank=bank: e.tensor_scalar(
                    dst_fn(o, half), bank[:, 0:512], bias_ap[:, o:o + 1],
                    None, ALU.add))
            return V(lambda e, o=o, half=half, bank=bank: e.tensor_copy(
                dst_fn(o, half), bank[:, 0:512]))
        return evac

    qsv = qs.ap().rearrange("p (blk tl gg) -> p blk tl gg", blk=2, tl=512, gg=G)

    def stream_rhs(g, half):
        return stream[:, g * 1024 + half * 512: g * 1024 + (half + 1) * 512]

    def ph_proj_L0(agA_tok):
        rec.wait("sync", agA_tok)
        kv_dma_toks.clear()
        proj16(lambda o: packA_stage(natA("Wk", o), 2048), stream_rhs,
               mk_evac_kv(0, 0, with_ck=True))
        proj16(lambda o: packA_stage(natA("Wv", o), 2048), stream_rhs,
               mk_evac_sbuf(lambda o, half: outT[:, o * 1024 + half * 512:
                                                 o * 1024 + (half + 1) * 512]))
        vptok = rec.now("dve")
        rec.wait("tensor", vptok)
        rec.wait("sync", rec.now("kvo0"))
        rec.wait("sync", rec.now("kvo1"))
        prev_dma = [None, None]
        slot_ev = [None, None]
        for t in range(8):
            half = t % 2
            vrow = wsb[:, half * 2048:(half + 1) * 2048]
            if prev_dma[half] is not None:
                rec.wait("vector", prev_dma[half])
            evs = []
            for g in range(G):
                slot = pbb[2 + (g % 2)][:, 0:128]
                if g >= 2:
                    rec.wait("tensor", evs[g - 2])
                elif slot_ev[g % 2] is not None:
                    rec.wait("tensor", slot_ev[g % 2])
                tt = T(lambda e, g=g, t=t, slot=slot: e.transpose(
                    slot, outT[:, g * 1024 + t * 128:g * 1024 + (t + 1) * 128],
                    identb[:]))
                rec.wait("vector", tt)
                evs.append(V(lambda e, g=g, slot=slot, vrow=vrow: e.tensor_copy(
                    vrow[:, g * 128:(g + 1) * 128], slot)))
            slot_ev = [evs[-2], evs[-1]]
            rec.wait("sync", evs[-1])
            vreg0 = kvi_flat[0][ds(2048 * 1024, 2097152)].rearrange(
                "(p hf h sb j) -> p hf h sb j", p=128, hf=2, h=16, sb=4)
            dt = rec.dma("sync", vreg0[:, t // 4, :, t % 4, :],
                         vrow.rearrange("p (h j) -> p h j", h=16),
                         sem="kvo" + str(half))
            prev_dma[half] = dt
        cktok = rec.now("dve")
        rec.wait("gpsimd", cktok)
        dt = rec.dma("gpsimd", kvi_flat[0][ds(KVBASE[0]["c"] * 1024, 2048)]
                     .rearrange("(g p) -> p g", p=128), ck,
                     allow_slow_non_contiguous=True)
        rec.wait("gpsimd", dt)
        def q_evac(o, half, bank, mmtok):
            rec.wait("vector", mmtok)
            return V(lambda e, o=o, half=half, bank=bank: e.tensor_scalar(
                qsv[:, half, :, o], bank[:, 0:512], bq_f[:, o:o + 1],
                None, ALU.add))
        proj16(lambda o: packA_stage(natA("Wq", o), 2048), stream_rhs, q_evac)
        rec.wait("gpsimd", rec.now("dve"))
        for sname in ("kvo0", "kvo1", "dkv", "dsp"):
            rec.wait("gpsimd", rec.now(sname))
        ag = rec.op("gpsimd", lambda e: e.collective_compute(
            "AllGather", ALU.bypass, replica_groups=[list(range(R))],
            ins=[kv_in[0][:, :]], outs=[kv_out[0][:, :]]), inc="ck0")
        return ag

    def ph_proj_L1(agB_tok):
        kv_dma_toks.clear()
        for dst, src in ((ksb, upk_i), (vsb, upv_i)):
            sf = src.rearrange("h d j -> (h d j)")
            ap = sf.rearrange("(h g p j) -> p h g j", h=NH, g=4, p=128)
            rec.dma("gpsimd", dst.ap().rearrange(
                "p (h g j) -> p h g j", h=NH, g=4), ap, sem="dup")
        up_tok = rec.now("dup")
        proj16(lambda o: packA_stage(natA("wq1", o), 2048), stream_rhs,
               mk_evac_kv(1, KVBASE[1]["q"], with_ck=False, bias_ap=wqb_f))
        proj16(lambda o: packA_stage(natA("wk1", o), 2048), stream_rhs,
               mk_evac_sbuf(lambda o, half: outT[:, o * 1024 + half * 512:
                                                 o * 1024 + (half + 1) * 512]))
        def k1_rhs(g, half):
            return outT[:, g * 1024 + half * 512: g * 1024 + (half + 1) * 512]
        proj16(lambda o: packA_stage(wlA(0, o), 2048), k1_rhs,
               mk_evac_sbuf(lambda o, half: qs[:, 8192 + o * 1024 + half * 512:
                                               8192 + o * 1024 + (half + 1) * 512]),
               nout=4)
        rec.wait("tensor", up_tok)
        rec.wait("tensor", rec.now("dve"))
        evk = mk_evac_kv(1, KVBASE[1]["k"], with_ck=True)
        bank_tok = [None, None]
        for h in range(NH):
            mmtok = None
            for half in range(2):
                bnk = (2 * h + half) % 2
                bank = pb[bnk]
                if bank_tok[bnk] is not None:
                    rec.wait("tensor", bank_tok[bnk])
                for g in range(4):
                    mmtok = T(lambda e, h=h, g=g, half=half, bank=bank:
                              e.matmul(bank[:, 0:512],
                                       ksb[:, h * 512 + g * 128:
                                           h * 512 + (g + 1) * 128],
                                       qs[:, 8192 + g * 1024 + half * 512:
                                          8192 + g * 1024 + (half + 1) * 512],
                                       start=(g == 0), stop=(g == 3)))
                bank_tok[bnk] = evk(h, half, bank, mmtok)
        proj16(lambda o: packA_stage(natA("wv1", o), 2048), stream_rhs,
               mk_evac_sbuf(lambda o, half: outT[:, o * 1024 + half * 512:
                                                 o * 1024 + (half + 1) * 512]))
        rec.wait("sync", rec.now("pe"))
        rec.wait("tensor", rec.now("dve"))
        vl_tok = [None, None]
        vlb_tok = [None, None]
        for o in range(4):
            sbuf_half = o % 2
            stg = wsa[:, sbuf_half * 2048:(sbuf_half + 1) * 2048]
            if vl_tok[sbuf_half] is not None:
                rec.wait("sync", vl_tok[sbuf_half])
            ld = rec.dma("sync", stg, packA_stage(wlA(1, o), 2048),
                         sem="sta" + str(sbuf_half))
            mmtok = None
            for half in range(2):
                bnk = (2 * o + half) % 2
                bank = pb[bnk]
                if vlb_tok[bnk] is not None:
                    rec.wait("tensor", vlb_tok[bnk])
                rec.wait("tensor", ld)
                for g in range(G):
                    mmtok = T(lambda e, stg=stg, g=g, half=half, o=o,
                              bank=bank: e.matmul(
                        bank[:, 0:512], stg[:, g * 128:(g + 1) * 128],
                        outT[:, g * 1024 + half * 512:
                             g * 1024 + (half + 1) * 512],
                        start=(g == 0), stop=(g == G - 1)))
                rec.wait("vector", mmtok)
                ev = V(lambda e, o=o, half=half, bank=bank: e.tensor_copy(
                    qs[:, 12288 + o * 1024 + half * 512:
                       12288 + o * 1024 + (half + 1) * 512], bank[:, 0:512]))
                vlb_tok[bnk] = ev
            rec.wait("tensor", rec.now("dve"))
            c_mm = None
            for g in range(G):
                c_mm = T(lambda e, stg=stg, g=g: e.matmul(
                    pb[5][:, 0:1], stg[:, g * 128:(g + 1) * 128],
                    wvb_b[:, g:g + 1], start=(g == 0), stop=(g == G - 1)))
            rec.wait("vector", c_mm)
            V(lambda e, o=o: e.tensor_scalar(
                v1c[:, o:o + 1], pb[5][:, 0:1], wvlb_f[:, o:o + 1],
                None, ALU.add))
            vgap()
            vl_tok[sbuf_half] = c_mm
        cst = V(lambda e: e.tensor_copy(v1cb, v1c))
        rec.wait("tensor", cst)
        rec.wait("tensor", rec.now("dve"))
        prev_ev = {}
        for h in range(NH):
            for t in range(8):
                bank = pb[(h * 8 + t) % 2]
                key = (h * 8 + t) % 2
                if key in prev_ev:
                    rec.wait("tensor", prev_ev[key])
                mmtok = None
                for g in range(4):
                    mmtok = T(lambda e, h=h, t=t, g=g, bank=bank: e.matmul(
                        bank[:, 0:128],
                        qs[:, 12288 + g * 1024 + t * 128:
                           12288 + g * 1024 + (t + 1) * 128],
                        vsb[:, h * 512 + g * 128: h * 512 + (g + 1) * 128],
                        start=(g == 0), stop=(g == 3)))
                rec.wait("vector", mmtok)
                prev_ev[key] = V(lambda e, h=h, t=t, bank=bank: e.tensor_copy(
                    outT[:, t * 2048 + h * 128: t * 2048 + (h + 1) * 128],
                    bank[:, 0:128]))
            rec.wait("tensor", rec.now("dve"))
            cm = None
            for g in range(4):
                cm = T(lambda e, h=h, g=g: e.matmul(
                    pb[5][:, 0:1],
                    vsb[:, h * 512 + g * 128: h * 512 + (g + 1) * 128],
                    v1cb[:, g:g + 1], start=(g == 0), stop=(g == 3)))
            rec.wait("vector", cm)
            V(lambda e, h=h: e.tensor_scalar(
                cc1[:, h:h + 1], pb[5][:, 0:1], upvb_f[:, h:h + 1],
                None, ALU.add))
            vgap()
        vtok = rec.now("dve")
        rec.wait("sync", vtok)
        vreg1 = kvi_flat[1][ds(KVBASE[1]["v"] * 1024, 2097152)].rearrange(
            "(p hf h sb j) -> p hf h sb j", p=128, hf=2, h=16, sb=4)
        for t in range(8):
            dt = rec.dma("sync", vreg1[:, t // 4, :, t % 4, :],
                         outT[:, t * 2048:(t + 1) * 2048].rearrange(
                             "p (h j) -> p h j", h=16), sem="dkv")
        cktok = rec.now("dve")
        rec.wait("gpsimd", cktok)
        dt = rec.dma("gpsimd", kvi_flat[1][ds(KVBASE[1]["c"] * 1024, 2048)]
                     .rearrange("(g p) -> p g", p=128), ck,
                     allow_slow_non_contiguous=True)
        rec.wait("gpsimd", dt)
        for sname in ("kvo0", "kvo1", "dkv", "dsp"):
            rec.wait("gpsimd", rec.now(sname))
        ag = rec.op("gpsimd", lambda e: e.collective_compute(
            "AllGather", ALU.bypass, replica_groups=[list(range(R))],
            ins=[kv_in[1][:, :]], outs=[kv_out[1][:, :]]), inc="ck1")
        return ag

    def ph_ccombine(layer, ag_tok):
        rec.wait("gpsimd", ag_tok)
        base = KVBASE[layer]["c"]
        rows = KVBASE[layer]["rows"]
        dts = []
        for r in range(R):
            dts.append(rec.dma(
                "gpsimd", absq[:, r * 16:(r + 1) * 16],
                kvo_flat[layer][ds((r * rows + base) * 1024, 2048)]
                .rearrange("(g p) -> p g", p=128), sem="dcm",
                allow_slow_non_contiguous=True))
        rec.wait("gpsimd", dts[-1])
        rec.wait("vector", dts[-1])
        V(lambda e: e.tensor_max(absq[:, 128:144], absq[:, 0:16], absq[:, 16:32]))
        V(lambda e: e.tensor_max(absq[:, 144:160], absq[:, 32:48], absq[:, 48:64]))
        V(lambda e: e.tensor_max(absq[:, 160:176], absq[:, 64:80], absq[:, 80:96]))
        V(lambda e: e.tensor_max(absq[:, 176:192], absq[:, 96:112], absq[:, 112:128]))
        vgap()
        V(lambda e: e.tensor_max(absq[:, 192:208], absq[:, 128:144], absq[:, 144:160]))
        V(lambda e: e.tensor_max(absq[:, 208:224], absq[:, 160:176], absq[:, 176:192]))
        vgap()
        cf = V(lambda e: e.tensor_max(cfeat, absq[:, 192:208], absq[:, 208:224]))
        cg = vgap()
        rec.wait("gpsimd", cg)
        cs_flat = cscr.rearrange("r c -> (r c)")
        st = rec.dma("gpsimd", cs_flat.rearrange("(g p) -> p g", p=128),
                     cfeat, sem="dcm", allow_slow_non_contiguous=True)
        rec.wait("gpsimd", st)
        cs1 = cs_flat.rearrange("(n b) -> n b", b=1)
        gather(cmy[:, 0:1], cs1, ("cmy", layer, 0))
        t2 = gather(cmy[:, 1:2], cs1, ("cmy", layer, 1))
        rec.wait("gpsimd", t2)
        return t2, rec.now("gp")

    def ph_attention(layer, ag_tok):
        dgp_tok, gp_tok = ph_ccombine(layer, ag_tok)
        rec.wait("tensor", dgp_tok)
        rec.wait("scalar", dgp_tok)
        SC = [pb[0], pb[1], pb[2]]
        PPRE3 = [ppre[:, 0:512], ppre[:, 512:1024], ascr[:, 1024:1536]]
        PBF3 = [pbf[:, 0:512], pbf[:, 512:1024], t1b[:, 0:512]]
        PVB = [pb[3], pb[6]]
        MHB = [ascr[:, 0:512], ascr[:, 512:1024]]
        MROW = [ascr[0:1, 1536:2048], t1[0:1, 512:1024]]
        rrow = ascr[0:1, 2048:2560]
        rbbv = ascr[:, 2048:2560]
        outv_toks = []
        state = {}

        def qtile_of(hi, qt):
            if layer == 0:
                return qs[:, hi * 8192 + qt * 512: hi * 8192 + (qt + 1) * 512]
            return qs[:, qt * 512:(qt + 1) * 512]

        def emit_gather_group(hi, m):
            sem = f"dl{m}"
            gather(ksb[:, m * 512:(m + 1) * 512], kvo512[layer],
                   ("k", layer, hi, m), sem=sem)
            if layer == 1:
                gather(qs[:, m * 512:(m + 1) * 512], kvo512[layer],
                       ("q", layer, hi, m), sem=sem)
            gather(vsb[:, m * 512:(m + 1) * 512], kvo512[layer],
                   ("v", layer, hi, m), sem=sem)
            return rec.now(sem)

        def mk_prologue(hi, qt, gtok):
            par = qt % 2
            qtile = qtile_of(hi, qt)
            toks = {}

            def p_ab():
                if layer == 1:
                    rec.wait("scalar", gtok[qt])
                if "mh" in state:
                    rec.wait("scalar", state["mh"])
                toks["ab"] = Aop(lambda e: e.activation(
                    absq[:, 0:512], qtile, AF.Abs))

            def p_mh():
                rec.wait("tensor", toks["ab"])
                if "mts" in state:
                    rec.wait("tensor", state["mts"])
                mh = T(lambda e: e.matmul(
                    pb[5][0:1, 0:512], cmy[:, hi:hi + 1], absq[:, 0:512],
                    start=True, stop=True))
                state["mh"] = mh
                rec.wait("vector", mh)
                if ("bc", par) in state:
                    rec.wait("vector", state[("bc", par)])
                V(lambda e: e.tensor_scalar(
                    MROW[par], pb[5][0:1, 0:512], SCALE, 1.0,
                    ALU.mult, ALU.add))
                toks["mg"] = vgap()
                state["mts"] = toks["mg"]

            def p_bc():
                rec.wait("tensor", toks["mg"])
                if "mhcp" in state:
                    rec.wait("tensor", state["mhcp"])
                bc = T(lambda e: e.matmul(
                    pb[4][:, 0:512], ones_row, MROW[par],
                    start=True, stop=True))
                state[("bc", par)] = bc
                rec.wait("vector", bc)
                V(lambda e: e.tensor_copy(MHB[par], pb[4][:, 0:512]))
                state["mhcp"] = vgap()

            return [p_ab, p_mh, p_bc]

        next_gtok = []
        for hi in range(2):
            if hi == 0:
                gtok = []
                for m in range(16):
                    gtok.append(emit_gather_group(hi, m))
            else:
                gtok = next_gtok
                next_gtok = []

            units = []
            for qt in range(16):
                n = 4 * (qt + 1)
                for uu in range(n):
                    units.append((qt, uu, uu // 4, uu % 4,
                                  uu // 4 == qt, uu == 0, uu == n - 1))
            NU = len(units)
            qk_toks = {}
            sub_toks = {}
            exa_toks = {}
            ptoks = {}
            pvl_toks = {}
            fo_toks = {}
            recip_tok = None
            last_m = [-1]

            def emit_qk(j):
                qtj, uuj, mj, sbj, diagj, firstj, lastj = units[j]
                slot = SC[j % 3]
                if j >= 3:
                    rec.wait("tensor", sub_toks[j - 3])
                if mj > last_m[0]:
                    rec.wait("tensor", gtok[mj])
                    last_m[0] = mj
                qtile = qtile_of(hi, qtj)
                if layer == 1 and firstj:
                    rec.wait("tensor", gtok[qtj])
                qk_toks[j] = T(lambda e, uuj=uuj, slot=slot, qtile=qtile:
                               e.matmul(slot[:, 0:512],
                                        ksb[:, uuj * 128:(uuj + 1) * 128],
                                        qtile, start=True, stop=True))

            pro = mk_prologue(hi, 0, gtok)
            for p in pro:
                p()
            pending_pro = None
            pro_pos = []
            emit_qk(0)
            if NU > 1:
                emit_qk(1)
            for j, (qt, uu, m, sb, diag, first, last) in enumerate(units):
                nunits = 4 * (qt + 1)
                if first and qt < 15:
                    pending_pro = mk_prologue(hi, qt + 1, gtok)
                    p0 = max(0, nunits - 6)
                    p1 = max(p0 + 1, nunits - 4)
                    p2 = max(p1 + 1, nunits - 2)
                    pro_pos = [p0, p1, p2]
                if pending_pro and uu in pro_pos:
                    pi = pro_pos.index(uu)
                    pending_pro[pi]()
                    if pi == 2:
                        pending_pro = None
                if j + 2 < NU:
                    emit_qk(j + 2)
                rec.wait("vector", qk_toks[j])
                if j - 3 in exa_toks:
                    rec.wait("vector", exa_toks[j - 3])
                st = V(lambda e, j=j, qt=qt: e.scalar_tensor_tensor(
                    PPRE3[j % 3], SC[j % 3][:, 0:512], SCALE, MHB[qt % 2],
                    ALU.mult, ALU.subtract))
                sub_toks[j] = st
                rec.wait("scalar", st)
                if j - 3 in pvl_toks:
                    rec.wait("scalar", pvl_toks[j - 3])
                ex = Aop(lambda e, j=j: e.activation(
                    PBF3[j % 3], PPRE3[j % 3], AF.Exp))
                exa_toks[j] = ex
                ptok = ex
                if diag:
                    rec.wait("gpsimd", ex)
                    ptok = Gop(lambda e, j=j, sb=sb: e.affine_select(
                        out=PBF3[j % 3], in_=PBF3[j % 3],
                        compare_op=ALU.is_ge, fill=0.0, base=-128 * sb,
                        pattern=[[1, 512]], channel_multiplier=-1))
                ptoks[j] = ptok
                rec.wait("tensor", ptok)
                if first:
                    if qt - 2 in fo_toks:
                        rec.wait("tensor", fo_toks[qt - 2])
                    if recip_tok is not None:
                        rec.wait("tensor", recip_tok)
                T(lambda e, j=j, uu=uu, qt=qt, first=first, last=last:
                  e.matmul(PVB[qt % 2][:, 0:512],
                           vsb[:, uu * 128:(uu + 1) * 128], PBF3[j % 3],
                           start=first, stop=last))
                pvl = T(lambda e, j=j, first=first, last=last: e.matmul(
                    pb[7][0:1, 0:512], ones_b, PBF3[j % 3],
                    start=first, stop=last))
                pvl_toks[j] = pvl
                if hi == 0 and qt == 15 and sb == 3:
                    rec.wait("gpsimd", pvl)
                    if layer == 1:
                        rec.wait("gpsimd", exa_toks[j])
                    next_gtok.append(emit_gather_group(1, m))
                if last:
                    rec.wait("vector", pvl)
                    V(lambda e: e.reciprocal(rrow, pb[7][0:1, 0:512]))
                    rg = vgap()
                    recip_tok = rg
                    rec.wait("tensor", rg)
                    slot = SC[j % 3]
                    bc2 = T(lambda e, slot=slot: e.matmul(
                        slot[:, 0:512], ones_row, rrow,
                        start=True, stop=True))
                    rec.wait("vector", bc2)
                    rbc = V(lambda e, slot=slot: e.tensor_copy(
                        rbbv, slot[:, 0:512]))
                    sub_toks[j] = rbc
                    vgap()
                    fo = V(lambda e, hi=hi, qt=qt: e.tensor_tensor(
                        outT[:, hi * 8192 + qt * 512:
                             hi * 8192 + (qt + 1) * 512],
                        PVB[qt % 2][:, 0:512], rbbv, ALU.mult))
                    fo_toks[qt] = fo
                    outv_toks.append(fo)
        return outv_toks

    def ph_wo(layer, outv_toks):
        for t in outv_toks[-2:]:
            rec.wait("sync", t)
        rec.wait("sync", rec.now("dve"))
        d1 = rec.dma("sync", ao_in[0:128, :], outT[:, 0:8192], sem="dkv")
        d2 = rec.dma("sync", ao_in[128:256, :], outT[:, 8192:16384], sem="dkv")
        rec.wait("gpsimd", rec.now("dkv"))
        ag = rec.op("gpsimd", lambda e: e.collective_compute(
            "AllGather", ALU.bypass, replica_groups=[list(range(R))],
            ins=[ao_in[:, :]], outs=[ao_out[:, :]]), inc="cao")
        rec.wait("gpsimd", ag)
        for jk in range(16):
            for cc in range(2):
                s2 = gather(outT[:, jk * 1024 + cc * 512:
                                 jk * 1024 + (cc + 1) * 512],
                            ao512, ("ao", layer, jk, cc))
        rec.wait("gpsimd", s2)
        rec.wait("tensor", s2)
        rec.wait("sync", s2)
        mat = "Wo" if layer == 0 else "wo1"
        bias_src = bo_f if layer == 0 else wob_f
        cbv = cb0 if layer == 0 else cb1
        if layer == 1:
            cv = V(lambda e: e.tensor_copy(cvb, cc1))
            rec.wait("tensor", cv)
        bank_tok = [None, None]
        stage_tok = [None, None]
        rec.wait("sync", rec.now("pe"))
        rec.wait("tensor", rec.now("dve"))
        for o in range(G):
            sh = o % 2
            stg = wsa[:, sh * 2048:(sh + 1) * 2048]
            if stage_tok[sh] is not None:
                rec.wait("sync", stage_tok[sh])
            ld = rec.dma("sync", stg, packA_stage(natA(mat, o), 2048),
                         sem="sta" + str(sh))
            rec.wait("tensor", ld)
            if o > 0:
                rec.wait("tensor", cb_ts)
            cm = None
            for g in range(G):
                cm = T(lambda e, stg=stg, g=g: e.matmul(
                    pb[5][:, 0:1], stg[:, g * 128:(g + 1) * 128],
                    cvb[:, g:g + 1], start=(g == 0), stop=(g == G - 1)))
            rec.wait("vector", cm)
            cb_ts = V(lambda e, o=o, cbv=cbv, bias_src=bias_src: e.tensor_scalar(
                cbv[:, o:o + 1], pb[5][:, 0:1], bias_src[:, o:o + 1],
                None, ALU.add))
            vgap()
            mmtok = None
            for half in range(2):
                bnk = (2 * o + half) % 2
                bank = pb[bnk]
                if bank_tok[bnk] is not None:
                    rec.wait("tensor", bank_tok[bnk])
                for jk in range(G):
                    gh = ghead(jk)
                    mmtok = T(lambda e, stg=stg, gh=gh, jk=jk, half=half,
                              bank=bank: e.matmul(
                        bank[:, 0:512], stg[:, gh * 128:(gh + 1) * 128],
                        outT[:, jk * 1024 + half * 512:
                             jk * 1024 + (half + 1) * 512],
                        start=(jk == 0), stop=(jk == G - 1)))
                rec.wait("vector", mmtok)
                bank_tok[bnk] = V(
                    lambda e, o=o, half=half, bank=bank, cbv=cbv:
                    e.scalar_tensor_tensor(
                        ybuf[:, o * 1024 + half * 512:
                             o * 1024 + (half + 1) * 512],
                        bank[:, 0:512], cbv[:, o:o + 1],
                        stream[:, o * 1024 + half * 512:
                               o * 1024 + (half + 1) * 512],
                        ALU.add, ALU.add))
            stage_tok[sh] = bank_tok[(2 * o + 1) % 2]

    def fview(g):
        if g < 8:
            return qs32[:, g * 1024:(g + 1) * 1024]
        if g < 12:
            return ksb32[:, (g - 8) * 1024:(g - 7) * 1024]
        return vsb32[:, (g - 12) * 1024:(g - 11) * 1024]

    def ph_ln(final=False):
        barrier()
        for half in range(2):
            sq_toks = []
            for g in range(G):
                if g >= 2:
                    rec.wait("scalar", sq_toks[g - 2])
                sq = Aop(lambda e, g=g, half=half: e.activation(
                    ysqv[:, (g % 2) * 512:(g % 2) * 512 + 512],
                    ybuf[:, g * 1024 + half * 512:
                         g * 1024 + (half + 1) * 512], AF.Square))
                rec.wait("tensor", sq)
                T(lambda e, g=g, half=half: e.matmul(
                    pb[6][0:1, 0:512], ones_b,
                    ybuf[:, g * 1024 + half * 512:
                         g * 1024 + (half + 1) * 512],
                    start=(g == 0), stop=(g == G - 1)))
                s2t = T(lambda e, g=g: e.matmul(
                    pb[7][0:1, 0:512], ones_b,
                    ysqv[:, (g % 2) * 512:(g % 2) * 512 + 512],
                    start=(g == 0), stop=(g == G - 1)))
                sq_toks.append(s2t)
            stt = sq_toks[-1]
            rec.wait("vector", stt)
            rec.wait("vector", rec.now("act"))
            r4 = pb[4][0:1, 0:512]
            r5 = pb[5][0:1, 0:512]
            Atmp = t1[0:1, 0:512]
            Btmp = t1[0:1, 512:1024]
            mu = muK[:, 0:512]
            V(lambda e: e.tensor_scalar(mu, pb[6][0:1, 0:512], 1.0 / D,
                                        None, ALU.mult))
            V(lambda e: e.tensor_scalar(Atmp, pb[7][0:1, 0:512], 1.0 / D,
                                        None, ALU.mult))
            vgap()
            V(lambda e, mu=mu: e.tensor_tensor(Btmp, mu, mu, ALU.mult))
            vgap()
            V(lambda e: e.tensor_tensor(r4, Atmp, Btmp, ALU.subtract))
            vgap()
            V(lambda e: e.tensor_scalar(Btmp, r4, EPS_LN, None, ALU.add))
            vr = vgap()
            rec.wait("scalar", vr)
            sd = Aop(lambda e: e.activation(r4, Btmp, AF.Sqrt))
            rec.wait("vector", sd)
            V(lambda e: e.reciprocal(Atmp, r4))
            vgap()
            V(lambda e: e.tensor_tensor(r4, Atmp, Atmp, ALU.mult))
            vgap()
            V(lambda e: e.tensor_tensor(r5, Btmp, r4, ALU.mult))
            vgap()
            V(lambda e: e.tensor_scalar(Btmp, r5, -0.5, 1.5, ALU.mult,
                                        ALU.add))
            vgap()
            V(lambda e, half=half: e.tensor_tensor(
                rstd_row[:, half * 512:(half + 1) * 512], Atmp, Btmp,
                ALU.mult))
            vgap()
            V(lambda e, half=half, mu=mu: e.tensor_tensor(
                m2_row[:, half * 512:(half + 1) * 512], mu,
                rstd_row[:, half * 512:(half + 1) * 512], ALU.mult))
            mfin = vgap()
            rec.wait("tensor", mfin)
            b1 = T(lambda e, half=half: e.matmul(
                pb[4][:, 0:512], ones_row,
                rstd_row[:, half * 512:(half + 1) * 512],
                start=True, stop=True))
            b2 = T(lambda e, half=half: e.matmul(
                pb[5][:, 0:512], ones_row,
                m2_row[:, half * 512:(half + 1) * 512],
                start=True, stop=True))
            rec.wait("vector", b2)
            zbuf = [t1[:, 0:512], t1[:, 512:1024],
                    ppre[:, 0:512], ppre[:, 512:1024]]
            z_toks = []
            o_toks = []

            def emit_z(g):
                if g >= 4:
                    rec.wait("vector", o_toks[g - 4])
                z_toks.append(V(lambda e, g=g, half=half: e.tensor_tensor(
                    zbuf[g % 4], ybuf[:, g * 1024 + half * 512:
                                      g * 1024 + (half + 1) * 512],
                    pb[4][:, 0:512], ALU.mult)))

            def emit_out(g):
                rec.wait("vector", z_toks[g])
                if final:
                    dst = fview(g)[:, half * 512:(half + 1) * 512]
                else:
                    dst = stream[:, g * 1024 + half * 512:
                                 g * 1024 + (half + 1) * 512]
                o_toks.append(V(lambda e, g=g, dst=dst: e.tensor_tensor(
                    dst, zbuf[g % 4], pb[5][:, 0:512], ALU.subtract)))

            emit_z(0)
            emit_z(1)
            for g in range(G):
                if g + 2 < G:
                    emit_z(g + 2)
                emit_out(g)
        barrier()

    def ph_ffn0(stop=None):
        hT = outT[0:64, 0:1024]
        Rt = outT32[:, 2048:2176]
        m1t = outT32[:, 2176:2184]
        m2t = outT32[:, 2184:2192]
        eqm = outT32[:, 2192:2320]
        msk = outT32[:, 2320:2448]
        expl = outT32[:, 2448:2576]
        gat = outT32[:, 2576:2704]
        den = outT32[:, 2704:2712]
        rcd = outT32[:, 2712:2720]
        cfb = outT[:, 5536:5536 + 2048]
        coT = outT[0:16, 7680:8704]
        xab = outT[:, 8704:8704 + 2048]
        rec.wait("vector", rec.now("pe"))
        onetok = V(lambda e: e.memset(outT[64:65, 0:1024], 1.0))
        for half in range(2):
            mm = None
            for g in range(G):
                mm = T(lambda e, g=g, half=half: e.matmul(
                    pb[4][0:64, 0:512], rwT[:, g * 64:(g + 1) * 64],
                    stream_rhs(g, half), start=(g == 0), stop=(g == G - 1)))
            rec.wait("vector", mm)
            V(lambda e, half=half: e.tensor_scalar(
                t1[0:64, 0:512], pb[4][0:64, 0:512], rb1_f[0:64, 0:1],
                None, ALU.add))
            sv = vgap()
            rec.wait("scalar", sv)
            hs = Aop(lambda e, half=half: e.activation(
                hT[:, half * 512:(half + 1) * 512], t1[0:64, 0:512], AF.Silu))
            rec.wait("tensor", hs)
        if stop == "router":
            return
        rec.wait("tensor", onetok)
        evs = []
        for tt in range(8):
            slot = pb[2 + (tt % 2)][:, 0:16]
            if tt >= 2:
                rec.wait("tensor", evs[tt - 2])
            tk = T(lambda e, tt=tt, slot=slot: e.matmul(
                slot, outT[0:65, tt * 128:(tt + 1) * 128],
                rwT[0:65, 1024:1040], start=True, stop=True))
            rec.wait("vector", tk)
            evs.append(V(lambda e, tt=tt, slot=slot: e.tensor_copy(
                Rt[:, tt * 16:(tt + 1) * 16], slot)))
        vgap()
        if stop == "ltr":
            return
        R3 = Rt.rearrange("p (t e) -> p t e", t=8)
        V(lambda e: e.reduce_max(m1t, R3, AX.X, negate=True))
        vgap()
        m1b = m1t.rearrange("p (t o) -> p t o", o=1).broadcast_to([128, 8, 16])
        V(lambda e: e.tensor_tensor(eqm.rearrange("p (t e) -> p t e", t=8),
                                    R3, m1b, ALU.add))
        vgap()
        V(lambda e: e.tensor_scalar(msk, eqm, 0.0, None, ALU.is_equal))
        vgap()
        V(lambda e: e.scalar_tensor_tensor(expl, msk, NEG, Rt, ALU.mult,
                                           ALU.add))
        vgap()
        V(lambda e: e.reduce_max(m2t, expl.rearrange("p (t e) -> p t e", t=8),
                                 AX.X))
        vgap()
        rec.wait("scalar", rec.now("dve"))
        ex = Aop(lambda e: e.activation(expl, eqm, AF.Exp))
        rec.wait("vector", ex)
        m2b = m2t.rearrange("p (t o) -> p t o", o=1).broadcast_to([128, 8, 16])
        V(lambda e: e.tensor_tensor(msk.rearrange("p (t e) -> p t e", t=8),
                                    R3, m2b, ALU.is_ge))
        vgap()
        V(lambda e: e.tensor_tensor(gat, expl, msk, ALU.mult))
        vgap()
        V(lambda e: e.reduce_sum(den, gat.rearrange("p (t e) -> p t e", t=8),
                                 AX.X))
        vgap()
        V(lambda e: e.reciprocal(rcd, den))
        vgap()
        if stop == "rmid":
            return
        denb = rcd.rearrange("p (t o) -> p t o", o=1).broadcast_to([128, 8, 16])
        vgap()
        cfo = V(lambda e: e.tensor_tensor(
            msk.rearrange("p (t e) -> p t e", t=8),
            gat.rearrange("p (t e) -> p t e", t=8), denb, ALU.mult))
        rec.wait("tensor", cfo)
        evs = []
        for tt in range(8):
            slot = pb[2 + (tt % 2)][0:16, 0:128]
            if tt >= 2:
                rec.wait("tensor", evs[tt - 2])
            tk = T(lambda e, tt=tt, slot=slot: e.transpose(
                slot, msk[:, tt * 16:(tt + 1) * 16], identf[:]))
            rec.wait("vector", tk)
            evs.append(V(lambda e, tt=tt, slot=slot: e.tensor_copy(
                coT[:, tt * 128:(tt + 1) * 128], slot)))
        ctok = vgap()
        if stop == "rmath":
            return
        rec.wait("tensor", ctok)
        cpt = None
        for erb in range(2):
            for half in range(2):
                if cpt is not None:
                    rec.wait("tensor", cpt)
                bm = T(lambda e, erb=erb, half=half: e.matmul(
                    pb[3][:, 0:512], ebf[0:16, erb * 128:(erb + 1) * 128],
                    coT[:, half * 512:(half + 1) * 512],
                    start=True, stop=True))
                rec.wait("vector", bm)
                cpt = V(lambda e, erb=erb, half=half: e.tensor_copy(
                    cfb[:, erb * 1024 + half * 512:
                        erb * 1024 + (half + 1) * 512], pb[3][:, 0:512]))
        vgap()
        if stop == "expand":
            return
        rec.wait("sync", rec.now("pe"))
        abtok = rec.dma("sync", wsb[:, 0:2048],
                        abT_flat[ds(0, BLK)].rearrange("(p c) -> p c", p=128),
                        sem="stb0")
        abtok2 = rec.dma("sync", wsb[:, 2048:4096],
                         abT_flat[ds(BLK, BLK)].rearrange("(p c) -> p c", p=128),
                         sem="stb1")
        cftok = rec.now("dve")
        xat = None
        rec.wait("tensor", abtok)
        rec.wait("tensor", abtok2)
        for erb in range(2):
            for half in range(2):
                if xat is not None:
                    rec.wait("tensor", xat)
                mm = None
                for g in range(G):
                    mm = T(lambda e, erb=erb, g=g, half=half: e.matmul(
                        pb[2][:, 0:512],
                        wsb[:, erb * 2048 + g * 128: erb * 2048 + (g + 1) * 128],
                        stream_rhs(g, half), start=(g == 0), stop=(g == G - 1)))
                rec.wait("vector", mm)
                xat = V(lambda e, erb=erb, half=half: e.tensor_tensor(
                    xab[:, erb * 1024 + half * 512:
                        erb * 1024 + (half + 1) * 512], pb[2][:, 0:512],
                    cfb[:, erb * 1024 + half * 512:
                        erb * 1024 + (half + 1) * 512], ALU.mult))
        wxtok = rec.now("dve")
        if stop == "xa":
            return
        rec.wait("sync", wxtok)
        btok = rec.dma("sync",
                       ksb[:, 0:4096].rearrange("p (o erb j) -> p o erb j",
                                                o=16, erb=2),
                       abT_flat[ds(2 * BLK, 2 * BLK)].rearrange(
                           "(o erb p j) -> p o erb j", p=128, o=16, erb=2),
                       sem="std")
        rec.wait("tensor", btok)
        rec.wait("tensor", wxtok)
        bank_tok = [None, None]
        stage_tok = [None, None]
        rec.wait("tensor", rec.now("dve"))
        for o in range(G):
            sh = o % 2
            stg = wsa[:, sh * 2048:(sh + 1) * 2048]
            if stage_tok[sh] is not None:
                rec.wait("sync", stage_tok[sh])
            ld = rec.dma("sync", stg, packA_stage(wtA(o), 2048),
                         sem="sta" + str(sh))
            rec.wait("tensor", ld)
            mmtok = None
            for half in range(2):
                bnk = (2 * o + half) % 2
                bank = pb[bnk]
                if bank_tok[bnk] is not None:
                    rec.wait("tensor", bank_tok[bnk])
                for g in range(G):
                    T(lambda e, stg=stg, g=g, half=half, bank=bank: e.matmul(
                        bank[:, 0:512], stg[:, g * 128:(g + 1) * 128],
                        stream_rhs(g, half), start=(g == 0), stop=False))
                for erb in range(2):
                    mmtok = T(lambda e, o=o, erb=erb, half=half, bank=bank:
                              e.matmul(
                        bank[:, 0:512],
                        ksb[:, o * 256 + erb * 128: o * 256 + (erb + 1) * 128],
                        xab[:, erb * 1024 + half * 512:
                            erb * 1024 + (half + 1) * 512],
                        start=False, stop=(erb == 1)))
                rec.wait("vector", mmtok)
                bank_tok[bnk] = V(
                    lambda e, o=o, half=half, bank=bank: e.scalar_tensor_tensor(
                        ybuf[:, o * 1024 + half * 512:
                             o * 1024 + (half + 1) * 512],
                        bank[:, 0:512], fb_f[:, o:o + 1],
                        stream[:, o * 1024 + half * 512:
                               o * 1024 + (half + 1) * 512],
                        ALU.add, ALU.add))
            stage_tok[sh] = bank_tok[(2 * o + 1) % 2]

    def hid_ap(hc, half):
        if hc < 16:
            return qs[:, hc * 1024 + half * 512: hc * 1024 + (half + 1) * 512]
        if hc < 24:
            c = (hc - 16) * 1024 + half * 512
            return ksb[:, c:c + 512]
        if hc < 32:
            c = (hc - 24) * 1024 + half * 512
            return vsb[:, c:c + 512]
        c = (hc - 32) * 1024 + half * 512
        return outT[:, c:c + 512]

    def ph_ffn1(agB_tok):
        rec.wait("sync", agB_tok)
        rec.wait("sync", rec.now("pe"))
        rec.wait("scalar", rec.now("pe"))
        rec.wait("tensor", rec.now("dve"))
        rec.wait("tensor", rec.now("act"))
        sig_tok = {}
        h_toks = {}
        w1tok = [None, None]
        w2tok = [None, None]
        for hc in range(NHC):
            sh = hc % 2
            stg1 = wsa[:, sh * 2048:(sh + 1) * 2048]
            stg2 = wsb[:, sh * 2048:(sh + 1) * 2048]
            if w1tok[sh] is not None:
                rec.wait("sync", w1tok[sh])
                rec.wait("scalar", w2tok[sh])
            l1 = rec.dma("sync", stg1, packB_stage(w12B(0, hc), 2048),
                         sem="sta" + str(sh))
            l2 = rec.dma("scalar", stg2, packB_stage(w12B(1, hc), 2048),
                         sem="stb" + str(sh))
            b1 = pb[2 * (hc % 2)]
            b2 = pb[2 * (hc % 2) + 1]
            for half in range(2):
                prev = (hc, 0) if half == 1 else (hc - 2, 1)
                if prev in sig_tok:
                    rec.wait("tensor", sig_tok[prev])
                    rec.wait("tensor", h_toks[prev])
                if (hc - 1, half) in h_toks:
                    rec.wait("scalar", h_toks[(hc - 1, half)])
                rec.wait("tensor", l1)
                mm1 = None
                for g in range(G):
                    mm1 = T(lambda e, stg1=stg1, g=g, half=half, b1=b1:
                            e.matmul(
                        b1[:, 0:512], stg1[:, g * 128:(g + 1) * 128],
                        stream_rhs(g, half), start=(g == 0), stop=(g == G - 1)))
                rec.wait("tensor", l2)
                mm2 = None
                for g in range(G):
                    mm2 = T(lambda e, stg2=stg2, g=g, half=half, b2=b2:
                            e.matmul(
                        b2[:, 0:512], stg2[:, g * 128:(g + 1) * 128],
                        stream_rhs(g, half), start=(g == 0), stop=(g == G - 1)))
                rec.wait("scalar", mm1)
                sg = Aop(lambda e, half=half, b1=b1: e.activation(
                    pbf[:, half * 512:(half + 1) * 512], b1[:, 0:512],
                    AF.Sigmoid))
                sig_tok[(hc, half)] = sg
                rec.wait("vector", mm2)
                rec.wait("vector", sg)
                h_toks[(hc, half)] = V(
                    lambda e, hc=hc, half=half, b2=b2: e.tensor_tensor(
                        hid_ap(hc, half), pbf[:, half * 512:(half + 1) * 512],
                        b2[:, 0:512], ALU.mult))
            w1tok[sh] = h_toks[(hc, 1)]
            w2tok[sh] = h_toks[(hc, 1)]
        hfin = rec.now("dve")
        rec.wait("sync", hfin)
        rec.wait("tensor", hfin)
        w3v = outT[:, 12288:16384]
        bank_tok = [None, None]
        st_tok = [None, None]
        for o in range(G):
            for qtr in range(4):
                sh = qtr % 2
                stg = w3v[:, sh * 2048: sh * 2048 + 1408]
                if st_tok[sh] is not None:
                    rec.wait("sync", st_tok[sh])
                ld = rec.dma("sync", stg,
                             packB_stage(w3B(o), NHC * 128).rearrange(
                                 "p (hc j) -> p hc j", hc=NHC)
                             [:, qtr * 11:(qtr + 1) * 11, :],
                             sem="stc" + str(sh))
                rec.wait("tensor", ld)
                mmt = None
                for hl in range(11):
                    hc = qtr * 11 + hl
                    for half in range(2):
                        bank = pb[2 + half]
                        if hc == 0 and bank_tok[half] is not None:
                            rec.wait("tensor", bank_tok[half])
                        mmt = T(lambda e, stg=stg, hl=hl, hc=hc, half=half,
                                bank=bank: e.matmul(
                            bank[:, 0:512], stg[:, hl * 128:(hl + 1) * 128],
                            hid_ap(hc, half), start=(hc == 0),
                            stop=(hc == NHC - 1)))
                st_tok[sh] = mmt
            rec.wait("vector", mmt)
            for half in range(2):
                bank = pb[2 + half]
                bank_tok[half] = V(
                    lambda e, o=o, half=half, bank=bank: e.tensor_tensor(
                        ybuf[:, o * 1024 + half * 512:
                             o * 1024 + (half + 1) * 512],
                        bank[:, 0:512],
                        stream[:, o * 1024 + half * 512:
                               o * 1024 + (half + 1) * 512], ALU.add))
        barrier()

    def ph_store():
        evs = []
        dts = []
        for t in range(8):
            buf = xstg if t % 2 == 0 else xstg2
            if t >= 2:
                rec.wait("vector", dts[t - 2])
                rec.wait("tensor", dts[t - 2])
            for g in range(G):
                slot = pb[g % 4][:, 0:128]
                idx = t * G + g
                if idx >= 4:
                    rec.wait("tensor", evs[idx - 4])
                tk = T(lambda e, g=g, t=t, slot=slot: e.transpose(
                    slot, fview(g)[:, t * 128:(t + 1) * 128], identf[:]))
                rec.wait("vector", tk)
                evs.append(V(lambda e, g=g, slot=slot, buf=buf: e.tensor_copy(
                    buf[:, g * 128:(g + 1) * 128], slot)))
            rec.wait("sync", evs[-1])
            dts.append(rec.dma("sync", y_out[t * 128:(t + 1) * 128, :], buf,
                               sem="dst" + str(t % 2)))
        tap_srcs = {"stream": (stream, G * 1024), "ybuf": (ybuf, G * 1024),
                    "qs": (qs, G * 1024), "ksb": (ksb, 8192),
                    "vsb": (vsb, 8192), "outT": (outT, G * 1024)}
        for tname in tap_outs:
            src, w = tap_srcs[tname]
            dts.append(rec.dma("gpsimd", tap_outs[tname][:, 0:w],
                               src[:, 0:w], sem="dtp"))
        for sname in ("dst0", "dst1", "dtp"):
            rec.wait("gpsimd", rec.now(sname))

    def mark(name):
        PHASE_MARKS.append((name, rec.cnt["pe"], rec.cnt["dve"],
                            rec.cnt["act"], rec.cnt["gp"]))

    def run_phases(upto=None):
        del PHASE_MARKS[:]
        mark("consts")
        ph_consts()
        mark("prep_A")
        agA = ph_prep_A()
        mark("prep_AB")
        ph_prep_AB()
        mark("prep_B")
        agB = ph_prep_B()
        mark("load_x")
        ph_load_x()
        if upto == "prep":
            barrier(); return
        mark("proj0")
        ag0 = ph_proj_L0(agA)
        if upto == "proj0":
            barrier(); return
        mark("attn0")
        ot = ph_attention(0, ag0)
        if upto == "attn0":
            barrier(); return
        mark("wo0")
        ph_wo(0, ot)
        if upto == "wo0":
            barrier(); return
        mark("ln0")
        ph_ln(final=False)
        if upto == "ln0":
            barrier(); return
        if upto is not None and upto.startswith("ffn0:"):
            ph_ffn0(stop=upto.split(":")[1])
            barrier(); return
        mark("ffn0")
        ph_ffn0()
        if upto == "ffn0":
            barrier(); return
        mark("ln1")
        ph_ln(final=False)
        if upto == "ln1":
            barrier(); return
        mark("proj1")
        ag1 = ph_proj_L1(agB)
        if upto == "proj1":
            barrier(); return
        mark("attn1")
        ot = ph_attention(1, ag1)
        if upto == "attn1":
            barrier(); return
        mark("wo1")
        ph_wo(1, ot)
        if upto == "wo1":
            barrier(); return
        mark("ln2")
        ph_ln(final=False)
        mark("ffn1")
        ph_ffn1(agB)
        if upto == "ffn1":
            barrier(); return
        mark("ln3")
        ph_ln(final=True)
        mark("store")
        ph_store()
        mark("end")

    run_phases(upto)
    if upto is not None:
        for sname in ("dst0", "dst1", "dtp"):
            rec.wait("gpsimd", rec.now(sname))

    with nc.Block() as block:
        for engname, deco in (("tensor", block.tensor), ("vector", block.vector),
                              ("scalar", block.scalar), ("gpsimd", block.gpsimd),
                              ("sync", block.sync)):
            def runner(e, ops=rec.ops[engname]):
                for f in ops:
                    f(e)
            deco(runner)

    ctx.close()
    return nc


_NC_CACHE = {}


def _get_nc(tapkey=()):
    if tapkey not in _NC_CACHE:
        _NC_CACHE[tapkey] = build_program(taps=tapkey)
    return _NC_CACHE[tapkey]


def make_in_maps(inputs):
    ins = {k: np.ascontiguousarray(np.asarray(v)) for k, v in inputs.items()}
    x = ins["L0_attn_x"].reshape(S, D)
    in_maps = []
    for c in range(R):
        m = {}
        m["x_own"] = np.concatenate(
            [x[c * 512:(c + 1) * 512], x[(15 - c) * 512:(16 - c) * 512]], 0)
        m["p_Wq"] = ins["L0_attn_Wq"][:, c * 256:(c + 1) * 256]
        m["p_Wk"] = ins["L0_attn_Wk"][:, c * 256:(c + 1) * 256]
        m["p_Wv"] = ins["L0_attn_Wv"][:, c * 256:(c + 1) * 256]
        m["p_Wo"] = ins["L0_attn_Wo"][:, c * 256:(c + 1) * 256]
        m["p_wq1"] = ins["L1_attn_wq_w"][:, c * 256:(c + 1) * 256]
        m["p_wk1"] = ins["L1_attn_wk_w"][:, c * 256:(c + 1) * 256]
        m["p_wv1"] = ins["L1_attn_wv_w"][:, c * 256:(c + 1) * 256]
        m["p_wo1"] = ins["L1_attn_wo_w"][:, c * 256:(c + 1) * 256]
        if c < 4:
            m["p_wl"] = ins["L1_attn_wkl_w"][:, c * 128:(c + 1) * 128]
        else:
            m["p_wl"] = ins["L1_attn_wvl_w"][:, (c - 4) * 128:(c - 3) * 128]
        m["p_WT"] = ins["L0_ffn_W"][c * 256:(c + 1) * 256, :]
        st, nreal = w12_rank_blocks(c)
        w1s = ins["L1_ffn_w1"][st * 128:(st + nreal) * 128, :]
        w2s = ins["L1_ffn_w2"][st * 128:(st + nreal) * 128, :]
        if nreal < W1B:
            pad = np.zeros(((W1B - nreal) * 128, D), np.float32)
            w1s = np.concatenate([w1s, pad], 0)
            w2s = np.concatenate([w2s, pad], 0)
        m["p_w1"] = w1s
        m["p_w2"] = w2s
        m["p_w3"] = ins["L1_ffn_w3"][c * 256:(c + 1) * 256, :]
        m["bq"] = ins["L0_attn_bq"]
        m["bo"] = ins["L0_attn_bo"]
        m["bv"] = ins["L0_attn_bv"]
        m["f_bias"] = ins["L0_ffn_bias"]
        m["rw1"] = ins["L0_ffn_rw1"]
        m["rb1"] = ins["L0_ffn_rb1"]
        m["rw2"] = ins["L0_ffn_rw2"]
        m["rb2"] = np.ascontiguousarray(ins["L0_ffn_rb2"])
        m["gates"] = np.ascontiguousarray(ins["L0_ffn_gates"])
        m["A_in"] = ins["L0_ffn_A"].reshape(256, D)
        m["B_in"] = ins["L0_ffn_B"]
        m["wq_b"] = ins["L1_attn_wq_b"]
        m["wv_b"] = ins["L1_attn_wv_b"]
        m["wvl_b"] = ins["L1_attn_wvl_b"]
        m["upk_w"] = ins["L1_attn_upk_w"]
        m["upv_w"] = ins["L1_attn_upv_w"]
        m["upv_b"] = ins["L1_attn_upv_b"]
        m["wo_b"] = ins["L1_attn_wo_b"]
        gx = np.zeros((128, GIDX_N), np.int32)
        P = np.arange(128)
        rows_l = [KV0_ROWS, KV1_ROWS]
        for L in (0, 1):
            rws = rows_l[L]
            for hi in (0, 1):
                h = c if hi == 0 else 15 - c
                for mm2 in range(16):
                    rr, half = rank_of_chunk(mm2)
                    kb = KVBASE[L]["k"]
                    vb = KVBASE[L]["v"]
                    gx[:, GIDX_COLS[("k", L, hi, mm2)]] = \
                        (rr * rws + kb + h * 128 + P) * 2 + half
                    gx[:, GIDX_COLS[("v", L, hi, mm2)]] = \
                        (rr * rws + vb) * 2 + P * 32 + half * 16 + h
                    if L == 1:
                        qb = KVBASE[L]["q"]
                        gx[:, GIDX_COLS[("q", L, hi, mm2)]] = \
                            (rr * rws + qb + h * 128 + P) * 2 + half
                gx[:, GIDX_COLS[("cmy", L, hi)]] = h * 128 + P
            for jk in range(16):
                for cc2 in (0, 1):
                    chunk = c if cc2 == 0 else 15 - c
                    row = (jk // 2) * 256 + (jk % 2) * 128 + P
                    gx[:, GIDX_COLS[("ao", L, jk, cc2)]] = row * 16 + chunk
        m["gidx"] = gx
        in_maps.append({k: np.ascontiguousarray(v) for k, v in m.items()})
    return in_maps


def kernel(**inputs):
    nc = _get_nc(())
    in_maps = make_in_maps(inputs)
    res = run_bass_kernel_spmd(nc, in_maps, core_ids=list(range(R)))
    out = np.zeros((S, D), np.float32)
    for c in range(R):
        y = np.asarray(res.results[c]["y_out"])
        out[c * 512:(c + 1) * 512] = y[0:512]
        out[(15 - c) * 512:(16 - c) * 512] = y[512:1024]
    return out.reshape(1, S, D)
